# version 11
# baseline (speedup 1.0000x reference)
import numpy as np
import ml_dtypes
from contextlib import ExitStack

import concourse.bass as bass
import concourse.mybir as mybir
from concourse.bass_utils import run_bass_kernel_spmd

BFNP = ml_dtypes.bfloat16
F32 = mybir.dt.float32
BF16 = mybir.dt.bfloat16
AX = mybir.AxisListType.X
ALU = mybir.AluOpType
ACT = mybir.ActivationFunctionType

NCORES = 8
TCORE = 2048
C = 512
NCH = TCORE // C
NST = C // 128
E = 2048
I = 8192
NET = E // 128
NIT = I // 128
H = 16
D = 128
M = 1024
NMT = M // 128
VW = 136
RS2048 = float(1.0 / np.sqrt(2048.0))
RS8192 = float(1.0 / np.sqrt(8192.0))
LN_EPS = 1e-5
RMS_EPS = 1e-6

ENG_SEM = {"tensor": "tsem", "scalar": "ssem", "vector": "vsem", "gpsimd": "bsem"}


def _hadamard(n):
    h = np.array([[1.0]], dtype=np.float64)
    while h.shape[0] < n:
        h = np.block([[h, h], [h, -h]])
    return h


class P:

    def __init__(self):
        self.plan = {k: [] for k in ("sync", "tensor", "scalar", "vector", "gpsimd")}
        self.cnt = {}
        self.lastw = {}
        self.sems = {}

    def emit(self, eng, build, inc=None, incby=1):
        if inc is not None:
            self.cnt[inc] = self.cnt.get(inc, 0) + incby

            def fn(e, b=build, s=inc, n=incby):
                b(e).then_inc(self.sems[s], n)
        else:
            def fn(e, b=build):
                b(e)
        self.plan[eng].append(fn)
        return self.cnt.get(inc, 0)

    def wait(self, eng, sem, val):
        if not val or val <= 0:
            return
        key = (eng, sem)
        if self.lastw.get(key, 0) >= val:
            return
        self.lastw[key] = val
        self.plan[eng].append(lambda e, s=sem, v=val: e.wait_ge(self.sems[s], v))

    def drain(self, eng):
        self.plan[eng].append(lambda e: e.drain())

    def release(self, eng):
        sem = ENG_SEM[eng]
        return self.emit(eng, lambda e: e.drain(), inc=sem)

    def c(self, sem):
        return self.cnt.get(sem, 0)


def build_nc():
    nc = bass.Bass("TRN2", target_bir_lowering=False)
    p = P()

    x_in = nc.dram_tensor("x", [TCORE, E], F32, kind="ExternalInput").ap()
    w_in = nc.dram_tensor("w_in", [E, E], F32, kind="ExternalInput").ap()
    w_q = nc.dram_tensor("w_q", [D, D], F32, kind="ExternalInput").ap()
    w_k = nc.dram_tensor("w_k", [D, D], F32, kind="ExternalInput").ap()
    w_v = nc.dram_tensor("w_v", [D, D], F32, kind="ExternalInput").ap()
    pats = nc.dram_tensor("pats", [H, M, D], F32, kind="ExternalInput").ap()
    nqw = nc.dram_tensor("nqw", [E], F32, kind="ExternalInput").ap()
    nrw = nc.dram_tensor("nrw", [E], F32, kind="ExternalInput").ap()
    beta = nc.dram_tensor("beta", [1], F32, kind="ExternalInput").ap()
    su_ug = nc.dram_tensor("su_ug", [E], F32, kind="ExternalInput").ap()
    su_dp = nc.dram_tensor("su_dp", [I], F32, kind="ExternalInput").ap()
    wsc_ug = nc.dram_tensor("wsc_ug", [2 * I], F32, kind="ExternalInput").ap()
    wsc_dp = nc.dram_tensor("wsc_dp", [E], F32, kind="ExternalInput").ap()
    w_up = nc.dram_tensor("w_up", [I, E], F32, kind="ExternalInput").ap()
    w_gate = nc.dram_tensor("w_gate", [I, E], F32, kind="ExternalInput").ap()
    w_down = nc.dram_tensor("w_down", [E, I], F32, kind="ExternalInput").ap()
    h128 = nc.dram_tensor("h128", [128, 128], BF16, kind="ExternalInput").ap()
    ident = nc.dram_tensor("ident", [128, 128], BF16, kind="ExternalInput").ap()
    ones1 = nc.dram_tensor("ones1", [1, 128], BF16, kind="ExternalInput").ap()
    ones1f = nc.dram_tensor("ones1f", [1, 128], F32, kind="ExternalInput").ap()

    y_out = nc.dram_tensor("y", [TCORE, E], F32, kind="ExternalOutput").ap()

    upT = nc.dram_tensor("upT", [32, NET, 128, 256], BF16, kind="Internal").ap()
    gateT = nc.dram_tensor("gateT", [32, NET, 128, 256], BF16, kind="Internal").ap()
    winT = nc.dram_tensor("winT", [8, NET, 128, 256], BF16, kind="Internal").ap()
    dnT = nc.dram_tensor("dnT", [I, E], BF16, kind="Internal").ap()
    kT = nc.dram_tensor("kT", [H, 128, M], BF16, kind="Internal").ap()
    vT = nc.dram_tensor("vT", [H, 128, NMT * VW], BF16, kind="Internal").ap()

    with ExitStack() as ctx:
        for s in ("wsem", "gsem", "tsem", "ssem", "vsem", "osem", "bsem",
                  "wprep", "wprep1", "castg", "castg1", "dnw", "patg", "gkv",
                  "sl0", "sl1", "sl2", "sl3", "kv0", "kv1"):
            p.sems[s] = ctx.enter_context(nc.semaphore(s))

        x_b = ctx.enter_context(nc.sbuf_tensor("x_b", [128, NST * E], F32))
        tmA = ctx.enter_context(nc.sbuf_tensor("tmA", [128, NST * E], BF16))
        fmA = ctx.enter_context(nc.sbuf_tensor("fmA", [128, NET * C], BF16))
        fwtmp = ctx.enter_context(nc.sbuf_tensor("fwtmp", [128, C], BF16))
        arena = ctx.enter_context(nc.sbuf_tensor("arena", [128, NIT * C], BF16))
        wbuf = ctx.enter_context(nc.sbuf_tensor("wbuf", [128, 4 * 4096], BF16))
        sq = ctx.enter_context(nc.sbuf_tensor("sq", [128, E], F32))
        silu_t = ctx.enter_context(nc.sbuf_tensor("silu_t", [128, C], BF16))
        stats = ctx.enter_context(nc.sbuf_tensor("stats", [128, 128], F32))
        h_sb = ctx.enter_context(nc.sbuf_tensor("h_sb", [128, 128], BF16))
        id_sb = ctx.enter_context(nc.sbuf_tensor("id_sb", [128, 128], BF16))
        on_sb = ctx.enter_context(nc.sbuf_tensor("on_sb", [1, 128], BF16))
        onf_sb = ctx.enter_context(nc.sbuf_tensor("onf_sb", [1, 128], F32))
        su_sb = ctx.enter_context(nc.sbuf_tensor("su_sb", [128, NET], F32))
        wscu_su = ctx.enter_context(nc.sbuf_tensor("wscu_su", [128, NIT], F32))
        wscg_sb = ctx.enter_context(nc.sbuf_tensor("wscg_sb", [128, NIT], F32))
        wscb = ctx.enter_context(nc.sbuf_tensor("wscb", [128, E], F32))
        wnq_b = ctx.enter_context(nc.sbuf_tensor("wnq_b", [128, E], BF16))
        wret_b = ctx.enter_context(nc.sbuf_tensor("wret_b", [128, E], BF16))
        wqT = ctx.enter_context(nc.sbuf_tensor("wqT", [128, 128], BF16))
        wkT = ctx.enter_context(nc.sbuf_tensor("wkT", [128, 128], BF16))
        wvT = ctx.enter_context(nc.sbuf_tensor("wvT", [128, 128], BF16))
        bsig = ctx.enter_context(nc.sbuf_tensor("bsig", [128, 4], F32))

        pf = [ctx.enter_context(nc.psum_tensor(f"pf{i}", [128, 512], F32)) for i in range(6)]
        pt = [ctx.enter_context(nc.psum_tensor(f"pt{i}", [128, 1024], BF16)) for i in range(2)]

        pf_free = [("ssem", 0)] * 6
        pt_free = [("ssem", 0)] * 2

        def mm(out_ap, lhsT_ap, rhs_ap, start, stop):
            return p.emit(
                "tensor",
                lambda e, o=out_ap, l=lhsT_ap, r=rhs_ap, st=start, sp=stop:
                    e.matmul(o, l, r, start=st, stop=sp),
                inc="tsem")

        def tp(out_ap, in_ap):
            return p.emit(
                "tensor",
                lambda e, o=out_ap, i_=in_ap: e.transpose(o, i_, id_sb[:]),
                inc="tsem")

        def sload(dst, src, slow=False):
            return p.emit("sync", lambda e, d=dst, s=src, sl=slow: e.dma_start(
                d, s, allow_slow_non_contiguous=sl), inc="wsem", incby=16)

        sload(h_sb[:], h128)
        sload(id_sb[:], ident)
        sload(on_sb[:], ones1)
        sload(onf_sb[:], ones1f)
        sload(su_sb[:], su_ug.rearrange("(a b) -> b a", b=128), slow=True)
        sload(wscu_su[:], wsc_ug[0:I].rearrange("(a b) -> b a", b=128), slow=True)
        sload(wscg_sb[:], wsc_ug[I:2 * I].rearrange("(a b) -> b a", b=128), slow=True)
        sload(stats[0:1, 112:113], beta.rearrange("(a b) -> a b", b=1))
        sload(x_b[0:1, 0:E], nqw.rearrange("(a b) -> a b", a=1))
        sload(x_b[0:1, E:2 * E], nrw.rearrange("(a b) -> a b", a=1))
        sload(x_b[0:1, 2 * E:3 * E], wsc_dp.rearrange("(a b) -> a b", a=1))
        sload(sq[:, 1024:1024 + NIT], su_dp.rearrange("(a b) -> b a", b=128), slow=True)
        sload(sq[:, 1200:1328], w_q)
        sload(sq[:, 1328:1456], w_k)
        w_all = sload(sq[:, 1456:1584], w_v)

        p.emit("vector", lambda e: e.memset(stats[:, 120:121], -0.5))
        p.emit("vector", lambda e: e.memset(stats[:, 121:122], float(LN_EPS)))
        p.emit("vector", lambda e: e.memset(stats[:, 122:123], float(RMS_EPS)))
        p.emit("vector", lambda e: e.memset(stats[:, 123:124], -1.0))
        v_consts = p.release("vector")
        p.wait("vector", "wsem", w_all)
        p.emit("vector", lambda e: e.tensor_scalar_mul(su_sb[:], su_sb[:], RS2048))
        p.emit("vector", lambda e: e.tensor_mul(wscu_su[:], wscu_su[:], sq[:, 1024:1024 + NIT]))
        p.wait("scalar", "wsem", w_all)
        p.wait("scalar", "vsem", v_consts)
        p.emit("scalar", lambda e: e.activation(
            stats[0:1, 113:114], stats[0:1, 112:113], ACT.Exp, scale=stats[0:1, 123:124]))
        p.emit("scalar", lambda e: e.mul(x_b[0:1, 2 * E:3 * E], x_b[0:1, 2 * E:3 * E], 0.5 * RS8192))
        s_rows0 = p.release("scalar")
        p.wait("vector", "ssem", s_rows0)
        p.emit("vector", lambda e: e.tensor_scalar_add(stats[0:1, 114:115], stats[0:1, 113:114], 1.0))
        p.drain("vector")
        p.emit("vector", lambda e: e.reciprocal(stats[0:1, 115:116], stats[0:1, 114:115]))
        p.emit("vector", lambda e: e.tensor_copy(tmA[0:1, 0:E], x_b[0:1, 0:E]))
        p.emit("vector", lambda e: e.tensor_copy(tmA[:, 2 * E:2 * E + 128], sq[:, 1200:1328]))
        p.emit("vector", lambda e: e.tensor_copy(tmA[:, 2 * E + 128:2 * E + 256], sq[:, 1328:1456]))
        p.emit("vector", lambda e: e.tensor_copy(tmA[:, 2 * E + 256:2 * E + 384], sq[:, 1456:1584]))
        p.emit("vector", lambda e: e.tensor_copy(tmA[0:1, E:2 * E], x_b[0:1, E:2 * E]))
        v_prep0 = p.release("vector")

        p.wait("tensor", "vsem", v_prep0)
        p.wait("tensor", "ssem", s_rows0)
        for j in range(4):
            mm(pf[0][:, 0:512], on_sb[:], tmA[0:1, j * 512:(j + 1) * 512], True, True)
            mm(pf[1][:, 0:512], on_sb[:], tmA[0:1, E + j * 512:E + (j + 1) * 512], True, True)
            t_b = mm(pf[2][:, 0:512], onf_sb[:], x_b[0:1, 2 * E + j * 512:2 * E + (j + 1) * 512], True, True)
            p.wait("scalar", "tsem", t_b)
            p.emit("scalar", lambda e, j=j: e.copy(wnq_b[:, j * 512:(j + 1) * 512], pf[0][:, 0:512]))
            p.emit("scalar", lambda e, j=j: e.copy(wret_b[:, j * 512:(j + 1) * 512], pf[1][:, 0:512]))
            s_b = p.emit("scalar", lambda e, j=j: e.copy(wscb[:, j * 512:(j + 1) * 512], pf[2][:, 0:512]), inc="ssem")
            p.wait("tensor", "ssem", s_b)
        t_bs = mm(pf[3][:, 0:1], onf_sb[:], stats[0:1, 115:116], True, True)
        p.wait("scalar", "tsem", t_bs)
        s_bs = p.emit("scalar", lambda e: e.copy(bsig[:, 0:1], pf[3][:, 0:1]), inc="ssem")

        tp(pt[0][:, 0:128], tmA[:, 2 * E:2 * E + 128])
        tp(pt[0][:, 128:256], tmA[:, 2 * E + 128:2 * E + 256])
        t_w3 = tp(pt[0][:, 256:384], tmA[:, 2 * E + 256:2 * E + 384])
        p.wait("vector", "tsem", t_w3)
        p.wait("vector", "ssem", s_bs)
        p.emit("vector", lambda e: e.tensor_scalar_mul(wqT[:], pt[0][:, 0:128], bsig[:, 0:1]))
        p.emit("vector", lambda e: e.tensor_copy(wkT[:], pt[0][:, 128:256]))
        v_w3 = p.emit("vector", lambda e: e.tensor_copy(wvT[:], pt[0][:, 256:384]), inc="vsem")
        pt_free[0] = ("vsem", v_w3)
        pf_free[3] = ("ssem", s_bs)
        xb_free_t = p.c("tsem")

        pat_guard = [0, 0]
        for h in range(H):
            sl = h % 2
            poff = sl * 1024
            toff = 2048 + sl * 1024
            p.wait("gpsimd", "tsem", pat_guard[sl])
            g_pat = p.emit(
                "gpsimd",
                lambda e, h=h, poff=poff: e.dma_start(
                    fmA[:, poff:poff + 1024].rearrange("p (mt d) -> p mt d", d=128),
                    pats[h].rearrange("(mt p) d -> p mt d", p=128)),
                inc="patg", incby=16)
            p.wait("tensor", "patg", g_pat)
            for eg in range(2):
                bk = eg % 2
                sem, val = pt_free[bk]
                p.wait("tensor", sem, val)
                t_last = 0
                for k in range(8):
                    mt = eg * 8 + k
                    if mt >= 8:
                        break
                    t_last = tp(pt[bk][:, k * 128:(k + 1) * 128],
                                fmA[:, poff + mt * 128:poff + (mt + 1) * 128])
                if eg == 0:
                    p.wait("scalar", "tsem", t_last)
                    sv = p.emit("scalar",
                                lambda e, toff=toff, bk=bk: e.copy(
                                    fmA[:, toff:toff + 1024], pt[bk][:, 0:1024]),
                                inc="ssem")
                    pt_free[bk] = ("ssem", sv)
                    break
            s_patT = p.c("ssem")
            for bix in (0, 1):
                sem, val = pf_free[bix]
                p.wait("tensor", sem, val)
            p.wait("tensor", "ssem", s_patT)
            mm(pf[0][:, 0:512], wkT[:], fmA[:, toff:toff + 512], True, True)
            t_k = mm(pf[1][:, 0:512], wkT[:], fmA[:, toff + 512:toff + 1024], True, True)
            p.wait("vector", "tsem", t_k)
            p.drain("vector")
            p.emit("vector", lambda e: e.reduce_sum(stats[:, 0:1], pf[0][:, 0:512], axis=AX))
            p.emit("vector", lambda e: e.reduce_sum(stats[:, 1:2], pf[1][:, 0:512], axis=AX))
            p.drain("vector")
            p.emit("vector", lambda e: e.tensor_add(stats[:, 2:3], stats[:, 0:1], stats[:, 1:2]))
            p.drain("vector")
            p.emit("vector", lambda e: e.tensor_scalar_mul(stats[:, 3:4], stats[:, 2:3], 1.0 / M))
            p.drain("vector")
            p.emit("vector", lambda e: e.tensor_scalar(
                arena[:, 0:512], pf[0][:, 0:512], stats[:, 3:4], None, ALU.subtract), inc="vsem")
            v_k = p.emit("vector", lambda e: e.tensor_scalar(
                arena[:, 512:1024], pf[1][:, 0:512], stats[:, 3:4], None, ALU.subtract), inc="vsem")
            pf_free[0] = ("vsem", v_k)
            pf_free[1] = ("vsem", v_k)
            p.wait("gpsimd", "vsem", v_k)
            g_k = p.emit("gpsimd", lambda e, h=h: e.dma_start(kT[h], arena[:, 0:1024]),
                         inc="gkv", incby=16)
            p.drain("vector")
            vz = p.emit("vector", lambda e: e.memset(arena[:, 1024:1024 + NMT * VW], 0.0), inc="vsem")
            p.drain("vector")
            ones_ap = arena[:, 1024:1024 + NMT * VW].rearrange(
                "p (mt c) -> p mt c", c=VW)[:, :, 128:129]
            vm_last = p.emit("vector", lambda e, ones_ap=ones_ap: e.memset(ones_ap, 1.0), inc="vsem")
            p.wait("scalar", "vsem", vm_last)
            t_v = 0
            for mtq in range(0, 8, 4):
                for k in range(4):
                    mt = mtq + k
                    sem, val = pf_free[k]
                    p.wait("tensor", sem, val)
                    t_v = mm(pf[k][:, 0:128], fmA[:, toff + mt * 128:toff + (mt + 1) * 128],
                             wvT[:], True, True)
                p.wait("scalar", "tsem", t_v)
                s_v = 0
                for k in range(4):
                    mt = mtq + k
                    s_v = p.emit("scalar", lambda e, mt=mt, k=k: e.copy(
                        arena[:, 1024 + mt * VW:1024 + mt * VW + 128], pf[k][:, 0:128]), inc="ssem")
                for k in range(4):
                    pf_free[k] = ("ssem", s_v)
            p.wait("gpsimd", "ssem", p.c("ssem"))
            g_v = p.emit("gpsimd", lambda e, h=h: e.dma_start(vT[h], arena[:, 1024:1024 + NMT * VW]),
                         inc="gkv", incby=16)
            pat_guard[sl] = p.c("tsem")
            p.wait("vector", "gkv", p.c("gkv"))
            p.wait("scalar", "gkv", p.c("gkv"))

        gkv_all = p.c("gkv")
        p.wait("vector", "gkv", gkv_all)
        p.wait("gpsimd", "gkv", gkv_all)
        p.wait("sync", "tsem", xb_free_t)
        p.wait("sync", "vsem", v_prep0)
        wT_off = [12288, 4096]
        cast_guard = [0, 0]
        write_guard = [0, 0]
        stage_guard = [("vsem", 0), ("bsem", 0)]

        def prep_wT(w_src, dst, nib):
            for ib in range(nib):
                sset = ib % 2
                base = 0 if sset == 0 else 8192
                soff = sset * 4096
                woff = wT_off[sset]
                wT_view = arena[:, woff:woff + 4096].rearrange(
                    "p (g two c) -> p g two c", two=2, c=128)
                csem = "castg" if sset == 0 else "castg1"
                ceng = "vector" if sset == 0 else "gpsimd"
                cinc = "vsem" if sset == 0 else "bsem"
                sem, val = stage_guard[sset]
                p.wait("sync", sem, val)
                p.emit(
                    "sync",
                    lambda e, ib=ib, soff=soff, w=w_src: e.dma_start(
                        x_b[:, soff:soff + 4096].rearrange("p (il c) -> p il c", c=2048),
                        w[ib * 256:(ib + 1) * 256, :].rearrange("(il p) c -> p il c", p=128)),
                    inc=csem, incby=16)
                g_c = p.c(csem)
                p.wait(ceng, csem, g_c)
                p.wait(ceng, "tsem", cast_guard[sset])
                v_c = p.emit(ceng, lambda e, base=base, soff=soff: e.tensor_copy(
                    arena[:, base:base + 4096], x_b[:, soff:soff + 4096]), inc=cinc)
                stage_guard[sset] = (cinc, v_c)
                p.wait("tensor", cinc, v_c)
                for il in range(2):
                    coff = base + il * 2048
                    for eg in range(2):
                        bk = eg % 2
                        sem, val = pt_free[bk]
                        p.wait("tensor", sem, val)
                        t_last = 0
                        for k in range(8):
                            et = eg * 8 + k
                            t_last = tp(pt[bk][:, k * 128:(k + 1) * 128],
                                        arena[:, coff + et * 128:coff + (et + 1) * 128])
                        p.wait("scalar", "tsem", t_last)
                        p.wait("scalar", "wprep" if sset == 0 else "wprep1", write_guard[sset])
                        s_t = p.emit(
                            "scalar",
                            lambda e, eg=eg, il=il, bk=bk, wT_view=wT_view: e.copy(
                                wT_view[:, eg * 8:(eg + 1) * 8, il, :], pt[bk][:, 0:1024]),
                            inc="ssem")
                        pt_free[bk] = ("ssem", s_t)
                cast_guard[sset] = p.c("tsem")
                p.wait("sync", "ssem", p.c("ssem"))
                write_guard[sset] = p.emit(
                    "sync",
                    lambda e, ib=ib, d=dst, woff=woff: e.dma_start(
                        d[ib].rearrange("et p c -> p et c"),
                        arena[:, woff:woff + 4096].rearrange("p (et c) -> p et c", c=256)),
                    inc="wprep" if sset == 0 else "wprep1", incby=16)

        prep_wT(w_up, upT, 32)
        prep_wT(w_gate, gateT, 32)
        prep_wT(w_in, winT, 8)

        p3_cast_guard = p.c("tsem")
        p3_z_guard = p.c("dnw")
        wprep_all, wprep1_all = p.c("wprep"), p.c("wprep1")
        for eng in ("vector", "gpsimd"):
            p.wait(eng, "wprep", wprep_all)
            p.wait(eng, "wprep1", wprep1_all)

        def pong_ap(it):
            if it < 32:
                return fmA[:, it * 256:(it + 1) * 256]
            return tmA[:, (it - 32) * 256:(it - 31) * 256]

        bt_rel = {"vsem": p.c("vsem"), "bsem": p.c("bsem")}
        for eb in range(8):
            cast_tok = {}
            for q in range(4):
                j, h2 = q // 2, q % 2
                sset = q % 2
                soff = sset * 4096
                csem = "castg" if sset == 0 else "castg1"
                ceng = "vector" if sset == 0 else "gpsimd"
                cinc = "vsem" if sset == 0 else "bsem"
                sem, val = stage_guard[sset]
                p.wait("sync", sem, val)
                p.emit(
                    "sync",
                    lambda e, eb=eb, j=j, h2=h2, soff=soff: e.dma_start(
                        x_b[:, soff:soff + 4096],
                        w_down[eb * 256 + j * 128:eb * 256 + (j + 1) * 128,
                               h2 * 4096:(h2 + 1) * 4096]),
                    inc=csem, incby=16)
                g_c = p.c(csem)
                p.wait(ceng, csem, g_c)
                p.wait(ceng, "tsem", p3_cast_guard)
                v_c = p.emit(ceng, lambda e, j=j, h2=h2, soff=soff: e.tensor_copy(
                    arena[:, j * 8192 + h2 * 4096:j * 8192 + (h2 + 1) * 4096],
                    x_b[:, soff:soff + 4096]), inc=cinc)
                stage_guard[sset] = (cinc, v_c)
                cast_tok[cinc] = v_c
            for s_, v_ in cast_tok.items():
                p.wait("tensor", s_, v_)
            for tg in range(16):
                bk = tg % 2
                sem, val = pt_free[bk]
                p.wait("tensor", sem, val)
                t_last = 0
                for k in range(8):
                    tix = tg * 8 + k
                    it, j = tix // 2, tix % 2
                    t_last = tp(pt[bk][:, k * 128:(k + 1) * 128],
                                arena[:, j * 8192 + it * 128:j * 8192 + (it + 1) * 128])
                p.wait("scalar", "tsem", t_last)
                p.wait("scalar", "dnw", p3_z_guard)
                s_t = p.emit(
                    "scalar",
                    lambda e, tg=tg, bk=bk: e.copy(
                        arena[:, 16384 + tg * 1024:16384 + (tg + 1) * 1024], pt[bk][:, 0:1024]),
                    inc="ssem")
                pt_free[bk] = ("ssem", s_t)
            p3_cast_guard = p.c("tsem")
            p.wait("tensor", "ssem", p.c("ssem"))
            for itq in range(0, NIT, 4):
                t_s = 0
                for k in range(4):
                    it = itq + k
                    sem, val = pf_free[k]
                    p.wait("tensor", sem, val)
                    t_s = mm(pf[k][:, 0:256], h_sb[:],
                             arena[:, 16384 + it * 256:16384 + (it + 1) * 256],
                             True, True)
                p.wait("scalar", "tsem", t_s)
                s_s = 0
                for k in range(4):
                    it = itq + k
                    s_s = p.emit("scalar", lambda e, it=it, k=k: e.copy(
                        arena[:, 16384 + it * 256:16384 + (it + 1) * 256], pf[k][:, 0:256]),
                        inc="ssem")
                for k in range(4):
                    pf_free[k] = ("ssem", s_s)
            zs = p.c("ssem")
            for eng in ("vector", "gpsimd"):
                p.wait(eng, "ssem", zs)
                p.wait(eng, "dnw", p3_z_guard)
            p.wait("vector", "bsem", bt_rel["bsem"])
            p.wait("gpsimd", "vsem", bt_rel["vsem"])
            v_rel = b_rel = 0
            for st in range(6):
                from_z = (st % 2 == 0)
                idx = 0
                for u in range(NIT):
                    v = u ^ (1 << st)
                    if v < u:
                        continue
                    if from_z:
                        iu = arena[:, 16384 + u * 256:16384 + (u + 1) * 256]
                        iv = arena[:, 16384 + v * 256:16384 + (v + 1) * 256]
                        ou, ov = pong_ap(u), pong_ap(v)
                    else:
                        iu, iv = pong_ap(u), pong_ap(v)
                        ou = arena[:, 16384 + u * 256:16384 + (u + 1) * 256]
                        ov = arena[:, 16384 + v * 256:16384 + (v + 1) * 256]
                    beng = "vector" if idx % 2 == 0 else "gpsimd"
                    idx += 1
                    p.emit(beng, lambda e, iu=iu, iv=iv, ou=ou: e.tensor_add(ou, iu, iv))
                    p.emit(beng, lambda e, iu=iu, iv=iv, ov=ov: e.tensor_sub(ov, iu, iv))
                v_rel = p.release("vector")
                b_rel = p.release("gpsimd")
                if st < 5:
                    p.wait("vector", "bsem", b_rel)
                    p.wait("gpsimd", "vsem", v_rel)
            bt_rel = {"vsem": v_rel, "bsem": b_rel}
            p.wait("sync", "vsem", v_rel)
            p.wait("sync", "bsem", b_rel)
            for grp in range(8):
                p.emit(
                    "sync",
                    lambda e, eb=eb, grp=grp: e.dma_start(
                        dnT[grp * 1024:(grp + 1) * 1024, eb * 256:(eb + 1) * 256]
                        .rearrange("(k p) c -> p k c", p=128),
                        arena[:, 16384 + grp * 2048:16384 + (grp + 1) * 2048]
                        .rearrange("p (k c) -> p k c", c=256)),
                    inc="dnw", incby=16)
            p3_z_guard = p.c("dnw")

        t_relb = p.release("tensor")
        s_relb = p.release("scalar")
        v_relb = p.release("vector")
        b_relb = p.release("gpsimd")
        pd = dict(wsem=p.c("wsem"), gsem=p.c("gsem"), tsem=t_relb, ssem=s_relb, vsem=v_relb,
                  bsem=b_relb,
                  wprep=p.c("wprep"), wprep1=p.c("wprep1"), castg=p.c("castg"), castg1=p.c("castg1"), dnw=p.c("dnw"),
                  patg=p.c("patg"), gkv=p.c("gkv"))
        for eng in ("sync", "tensor", "scalar", "vector", "gpsimd"):
            for s, v in pd.items():
                p.wait(eng, s, v)

        A_NQ = 0
        A_AF = 8192
        A_E = 16384
        A_KB = 24576
        A_VB = 26624

        pf_free = [("ssem", p.c("ssem"))] * 6
        pt_free = [("ssem", p.c("ssem"))] * 2
        wbuf_tok = [("tsem", p.c("tsem"))] * 4
        kv_tok = [("tsem", p.c("tsem"))] * 2
        e_tok = [p.c("tsem"), p.c("tsem")]

        def layer_norm(eps):
            v_in = p.c("vsem")
            p.wait("scalar", "vsem", v_in)
            eps_ap = stats[:, 121:122] if eps == LN_EPS else stats[:, 122:123]
            p.drain("vector")
            for st in range(NST):
                xs = x_b[:, st * E:(st + 1) * E]
                for j in range(4):
                    p.emit("vector", lambda e, xs=xs, st=st, j=j: e.bn_stats(
                        stats[:, st * 24 + j * 6:st * 24 + (j + 1) * 6],
                        xs[:, j * 512:(j + 1) * 512]))
            p.drain("vector")
            v_mu = 0
            for st in range(NST):
                v_mu = p.emit("vector", lambda e, st=st: e.bn_aggr(
                    stats[:, 96 + 2 * st:98 + 2 * st], stats[:, st * 24:(st + 1) * 24]),
                    inc="vsem")
            p.wait("scalar", "vsem", v_mu)
            var_v = stats[:, 97:105].rearrange("p (a b) -> p a b", b=2)[:, :, 0:1]
            ln_v = stats[:, 104:108].rearrange("p (a b) -> p a b", b=1)
            rs_v = stats[:, 108:112].rearrange("p (a b) -> p a b", b=1)
            p.drain("scalar")
            p.emit("scalar", lambda e, eps_ap=eps_ap: e.activation(ln_v, var_v, ACT.Ln, bias=eps_ap))
            p.drain("scalar")
            s_r = p.emit("scalar", lambda e: e.activation(
                rs_v, ln_v, ACT.Exp, scale=stats[:, 120:121]), inc="ssem")
            p.wait("vector", "ssem", s_r)
            v_a = 0
            for st in range(NST):
                xs = x_b[:, st * E:(st + 1) * E]
                v_a = p.emit("vector", lambda e, xs=xs, st=st: e.tensor_scalar(
                    tmA[:, st * E:(st + 1) * E], xs, stats[:, 96 + 2 * st:97 + 2 * st],
                    stats[:, 108 + st:109 + st], ALU.subtract, ALU.mult), inc="vsem")
            return v_a

        def transpose_tm_to_fm(src_off, scaled, v_ready):
            p.wait("tensor", "vsem", v_ready)
            s_last = 0
            for et in range(NET):
                bk = et % 2
                sem, val = pt_free[bk]
                p.wait("tensor", sem, val)
                t_last = 0
                for st in range(NST):
                    src = tmA if src_off is None else arena
                    off = (0 if src_off is None else src_off) + st * E + et * 128
                    t_last = tp(pt[bk][:, st * 128:(st + 1) * 128], src[:, off:off + 128])
                p.wait("scalar", "tsem", t_last)
                if scaled:
                    s_last = p.emit("scalar", lambda e, et=et, bk=bk: e.activation(
                        fmA[:, et * C:(et + 1) * C], pt[bk][:, 0:C],
                        ACT.Copy, scale=su_sb[:, et:et + 1]), inc="ssem")
                else:
                    s_last = p.emit("scalar", lambda e, et=et, bk=bk: e.copy(
                        fmA[:, et * C:(et + 1) * C], pt[bk][:, 0:C]), inc="ssem")
                pt_free[bk] = ("ssem", s_last)
            return s_last

        def fwht2048_fm():
            p.wait("tensor", "ssem", p.c("ssem"))
            for et in range(NET):
                bix = et % 2
                sem, val = pf_free[bix]
                p.wait("tensor", sem, val)
                t_s = mm(pf[bix][:, 0:C], h_sb[:], fmA[:, et * C:(et + 1) * C], True, True)
                p.wait("scalar", "tsem", t_s)
                s_s = p.emit("scalar", lambda e, et=et, bix=bix: e.copy(
                    fmA[:, et * C:(et + 1) * C], pf[bix][:, 0:C]), inc="ssem")
                pf_free[bix] = ("ssem", s_s)
            p.wait("vector", "ssem", p.c("ssem"))
            p.wait("vector", "tsem", p.c("tsem"))
            ready = [0] * NET
            for st in range(4):
                src_fm = (st % 2 == 0)
                for u in range(NET):
                    v = u ^ (1 << st)
                    if v < u:
                        continue
                    if src_fm:
                        iu, iv = fmA[:, u * C:(u + 1) * C], fmA[:, v * C:(v + 1) * C]
                        ou, ov = arena[:, u * C:(u + 1) * C], arena[:, v * C:(v + 1) * C]
                    else:
                        iu, iv = arena[:, u * C:(u + 1) * C], arena[:, v * C:(v + 1) * C]
                        ou, ov = fmA[:, u * C:(u + 1) * C], fmA[:, v * C:(v + 1) * C]
                    p.emit("vector", lambda e, iu=iu, iv=iv, ou=ou: e.tensor_add(ou, iu, iv))
                    last = (st == 3)
                    vv = p.emit("vector", lambda e, iu=iu, iv=iv, ov=ov: e.tensor_sub(ov, iu, iv),
                                inc="vsem" if last else None)
                    if last:
                        ready[u] = vv
                        ready[v] = vv
                p.drain("vector")
            return ready

        def ffn(fm_ready):
            korder = sorted(range(NET), key=lambda k: fm_ready[k])
            wb = [wbuf[:, k * 4096:(k + 1) * 4096] for k in range(4)]
            last_v_g = [p.c("vsem")]
            for ib in range(32):
                ub, gb = wb[(ib % 2) * 2], wb[(ib % 2) * 2 + 1]
                for m_i, (buf, srcT) in enumerate(((ub, upT), (gb, gateT))):
                    slot = (ib % 2) * 2 + m_i
                    sem, val = wbuf_tok[slot]
                    p.wait("sync", sem, val)
                    p.emit("sync", lambda e, ib=ib, buf=buf, srcT=srcT: e.dma_start(
                        buf.rearrange("p (et c) -> p et c", c=256),
                        srcT[ib].rearrange("et p c -> p et c")), inc=f"sl{slot}", incby=16)
                    p.wait("tensor", f"sl{slot}", p.c(f"sl{slot}"))
                for il in range(2):
                    it = ib * 2 + il
                    pu_i, pg_i = (it % 2) * 2, (it % 2) * 2 + 1
                    for bix in (pu_i, pg_i):
                        sem, val = pf_free[bix]
                        p.wait("tensor", sem, val)
                    t_u = 0
                    for ki, et in enumerate(korder):
                        p.wait("tensor", "vsem", fm_ready[et])
                        t_u = mm(pf[pu_i][:, 0:C], ub[:, et * 256 + il * 128:et * 256 + (il + 1) * 128],
                                 fmA[:, et * C:(et + 1) * C], ki == 0, ki == NET - 1)
                    t_g = 0
                    for ki, et in enumerate(korder):
                        t_g = mm(pf[pg_i][:, 0:C], gb[:, et * 256 + il * 128:et * 256 + (il + 1) * 128],
                                 fmA[:, et * C:(et + 1) * C], ki == 0, ki == NET - 1)
                    if il == 1:
                        wbuf_tok[(ib % 2) * 2] = ("tsem", t_g)
                        wbuf_tok[(ib % 2) * 2 + 1] = ("tsem", t_g)
                    p.wait("scalar", "tsem", t_g)
                    p.wait("scalar", "vsem", last_v_g[0])
                    s_si = p.emit("scalar", lambda e, it=it, pg_i=pg_i: e.activation(
                        silu_t[:, 0:C], pf[pg_i][:, 0:C], ACT.Silu, scale=wscg_sb[:, it:it + 1]),
                        inc="ssem")
                    p.wait("vector", "ssem", s_si)
                    p.wait("vector", "tsem", t_u)
                    v_g = p.emit("vector", lambda e, it=it, pu_i=pu_i: e.scalar_tensor_tensor(
                        arena[:, it * C:(it + 1) * C], pf[pu_i][:, 0:C], wscu_su[:, it:it + 1],
                        silu_t[:, 0:C], ALU.mult, ALU.mult), inc="vsem")
                    pf_free[pu_i] = ("vsem", v_g)
                    pf_free[pg_i] = ("ssem", s_si)
                    last_v_g[0] = v_g
            v_g_all = p.c("vsem")
            db = [wbuf[:, k * 4096:(k + 1) * 4096] for k in range(4)]
            p.wait("tensor", "vsem", v_g_all)
            v_d = 0
            for ec in range(4):
                for st in range(NST):
                    sem, val = pf_free[st]
                    p.wait("tensor", sem, val)
                t_d = 0
                for kb in range(8):
                    bi = kb % 4
                    sem, val = wbuf_tok[bi]
                    p.wait("sync", sem, val)
                    wv = p.emit("sync", lambda e, ec=ec, kb=kb, b=db[bi]: e.dma_start(
                        b.rearrange("p (k c) -> p k c", c=512),
                        dnT[kb * 1024:(kb + 1) * 1024, ec * 512:(ec + 1) * 512]
                        .rearrange("(k p) c -> p k c", p=128)), inc=f"sl{bi}", incby=16)
                    p.wait("tensor", f"sl{bi}", wv)
                    for st in range(NST):
                        for k in range(8):
                            it = kb * 8 + k
                            t_d = mm(pf[st][:, 0:512],
                                     arena[:, it * C + st * 128:it * C + (st + 1) * 128],
                                     db[bi][:, k * 512:(k + 1) * 512],
                                     it == 0, it == NIT - 1)
                    wbuf_tok[bi] = ("tsem", t_d)
                p.wait("vector", "tsem", t_d)
                for st in range(NST):
                    p.drain("vector")
                    p.emit("vector", lambda e, st=st, ec=ec: e.tensor_mul(
                        sq[:, 0:512], pf[st][:, 0:512], wscb[:, ec * 512:(ec + 1) * 512]))
                    p.drain("vector")
                    v_d = p.emit("vector", lambda e, st=st, ec=ec: e.tensor_add(
                        x_b[:, st * E + ec * 512:st * E + (ec + 1) * 512],
                        x_b[:, st * E + ec * 512:st * E + (ec + 1) * 512],
                        sq[:, 0:512]), inc="vsem")
                    pf_free[st] = ("vsem", v_d)
            p.drain("vector")
            return v_d

        def rms_apply(src_ap_fn, dst_off, w_bcast, v_ready, out_is_nq):
            p.wait("scalar", "vsem", v_ready)
            p.wait("vector", "vsem", v_ready)
            p.drain("vector")
            for st in range(NST):
                a_s = src_ap_fn(st)
                for j in range(4):
                    p.emit("vector", lambda e, a_s=a_s, st=st, j=j: e.bn_stats(
                        stats[:, st * 24 + j * 6:st * 24 + (j + 1) * 6],
                        a_s[:, j * 512:(j + 1) * 512]))
            p.drain("vector")
            for st in range(NST):
                p.emit("vector", lambda e, st=st: e.bn_aggr(
                    stats[:, 96 + 2 * st:98 + 2 * st], stats[:, st * 24:(st + 1) * 24]))
            p.drain("vector")
            mean_v = stats[:, 96:104].rearrange("p (a b) -> p a b", b=2)[:, :, 0:1]
            var_v = stats[:, 97:105].rearrange("p (a b) -> p a b", b=2)[:, :, 0:1]
            msq_v = stats[:, 104:108].rearrange("p (a b) -> p a b", b=1)
            rs_v = stats[:, 108:112].rearrange("p (a b) -> p a b", b=1)
            v_ss = p.emit("vector", lambda e: e.tensor_tensor(
                msq_v, mean_v, mean_v, ALU.mult), inc="vsem")
            p.drain("vector")
            v_ss = p.emit("vector", lambda e: e.tensor_tensor(
                msq_v, msq_v, var_v, ALU.add), inc="vsem")
            p.wait("scalar", "vsem", v_ss)
            p.drain("scalar")
            p.emit("scalar", lambda e: e.activation(msq_v, msq_v, ACT.Ln, bias=stats[:, 122:123]))
            p.drain("scalar")
            s_r = p.emit("scalar", lambda e: e.activation(
                rs_v, msq_v, ACT.Exp, scale=stats[:, 120:121]), inc="ssem")
            p.wait("vector", "ssem", s_r)
            last = 0
            for st in range(NST):
                a_s = src_ap_fn(st)
                if out_is_nq:
                    last = p.emit("vector", lambda e, a_s=a_s, st=st: e.scalar_tensor_tensor(
                        arena[:, dst_off + st * E:dst_off + (st + 1) * E], a_s,
                        stats[:, 108 + st:109 + st], w_bcast[:], ALU.mult, ALU.mult), inc="vsem")
                else:
                    p.drain("vector")
                    p.emit("vector", lambda e, a_s=a_s, st=st: e.scalar_tensor_tensor(
                        sq[:], a_s, stats[:, 108 + st:109 + st], w_bcast[:], ALU.mult, ALU.mult))
                    p.drain("vector")
                    last = p.emit("vector", lambda e, st=st: e.tensor_add(
                        x_b[:, st * E:(st + 1) * E], x_b[:, st * E:(st + 1) * E], sq[:]), inc="vsem")
            return last

        def attn(v_nq_ready):
            s_tr = transpose_tm_to_fm(A_NQ, False, v_nq_ready)
            wb = [wbuf[:, k * 4096:(k + 1) * 4096] for k in range(2)]
            p.wait("tensor", "ssem", s_tr)
            for eb in range(8):
                bi = eb % 2
                sem, val = wbuf_tok[bi]
                p.wait("sync", sem, val)
                p.emit("sync", lambda e, eb=eb, b=wb[bi]: e.dma_start(
                    b.rearrange("p (et c) -> p et c", c=256),
                    winT[eb].rearrange("et p c -> p et c")), inc=f"sl{bi}", incby=16)
                p.wait("tensor", f"sl{bi}", p.c(f"sl{bi}"))
                for il in range(2):
                    ep = eb * 2 + il
                    bix = 4 + (ep % 2)
                    sem, val = pf_free[bix]
                    p.wait("tensor", sem, val)
                    t_p = 0
                    for et in range(NET):
                        t_p = mm(pf[bix][:, 0:C], wb[bi][:, et * 256 + il * 128:et * 256 + (il + 1) * 128],
                                 fmA[:, et * C:(et + 1) * C], et == 0, et == NET - 1)
                    if il == 1:
                        wbuf_tok[bi] = ("tsem", t_p)
                    p.wait("scalar", "tsem", t_p)
                    s_p = p.emit("scalar", lambda e, ep=ep, bix=bix: e.copy(
                        arena[:, A_NQ + ep * C:A_NQ + (ep + 1) * C], pf[bix][:, 0:C]), inc="ssem")
                    pf_free[bix] = ("ssem", s_p)
            p.wait("tensor", "ssem", p.c("ssem"))
            for h in range(H):
                bix = 4 + (h % 2)
                sem, val = pf_free[bix]
                p.wait("tensor", sem, val)
                t_q = mm(pf[bix][:, 0:C], wqT[:], arena[:, A_NQ + h * C:A_NQ + (h + 1) * C],
                         True, True)
                p.wait("scalar", "tsem", t_q)
                s_q = p.emit("scalar", lambda e, h=h, bix=bix: e.copy(
                    fmA[:, h * C:(h + 1) * C], pf[bix][:, 0:C]), inc="ssem")
                pf_free[bix] = ("ssem", s_q)
            p.wait("tensor", "ssem", p.c("ssem"))
            kv_free = kv_tok
            e_guard = e_tok
            s_e_done = [0] * H
            v_o = 0

            def emit_scores(h):
                sl = h % 2
                koff = A_KB + sl * 1024
                voff = A_VB + sl * 1088
                sem, val = kv_free[sl]
                p.wait("sync", sem, val)
                p.emit("sync", lambda e, h=h, koff=koff: e.dma_start(
                    arena[:, koff:koff + 1024], kT[h]), inc=f"kv{sl}", incby=16)
                w_kv = p.emit("sync", lambda e, h=h, voff=voff: e.dma_start(
                    arena[:, voff:voff + 1088], vT[h]), inc=f"kv{sl}", incby=16)
                p.wait("tensor", f"kv{sl}", w_kv)
                eoff = A_E + sl * 4096
                p.wait("scalar", "tsem", e_guard[sl])
                for mtq in range(0, NMT, 4):
                    t_sc = 0
                    for k in range(4):
                        mt = mtq + k
                        sem, val = pf_free[k]
                        p.wait("tensor", sem, val)
                        t_sc = mm(pf[k][:, 0:C], arena[:, koff + mt * 128:koff + (mt + 1) * 128],
                                  fmA[:, h * C:(h + 1) * C], True, True)
                    p.wait("scalar", "tsem", t_sc)
                    s_e = 0
                    for k in range(4):
                        mt = mtq + k
                        s_e = p.emit("scalar", lambda e, mt=mt, k=k, eoff=eoff: e.activation(
                            arena[:, eoff + mt * C:eoff + (mt + 1) * C], pf[k][:, 0:C], ACT.Exp),
                            inc="ssem")
                    for k in range(4):
                        pf_free[k] = ("ssem", s_e)
                    if mtq == 0 and h > 0:
                        emit_out(h - 1)
                s_e_done[h] = p.c("ssem")

            def emit_out(h):
                nonlocal v_o
                sl = h % 2
                voff = A_VB + sl * 1088
                eoff = A_E + sl * 4096
                p.wait("tensor", "ssem", s_e_done[h])
                t_o = 0
                for st in range(NST):
                    bix = 4 + st // 2
                    coff = (st % 2) * 256
                    if st % 2 == 0:
                        sem, val = pf_free[bix]
                        p.wait("tensor", sem, val)
                    for mt in range(NMT):
                        t_o = mm(pf[bix][:, coff:coff + VW],
                                 arena[:, eoff + mt * C + st * 128:eoff + mt * C + (st + 1) * 128],
                                 arena[:, voff + mt * VW:voff + (mt + 1) * VW],
                                 mt == 0, mt == NMT - 1)
                p.wait("vector", "tsem", t_o)
                p.drain("vector")
                for st in range(NST):
                    bix = 4 + st // 2
                    coff = (st % 2) * 256
                    p.emit("vector", lambda e, st=st, bix=bix, coff=coff: e.reciprocal(
                        stats[:, 116 + st:117 + st], pf[bix][:, coff + 128:coff + 129]))
                p.drain("vector")
                for st in range(NST):
                    bix = 4 + st // 2
                    coff = (st % 2) * 256
                    v_o = p.emit("vector", lambda e, h=h, st=st, bix=bix, coff=coff: e.tensor_scalar(
                        arena[:, A_AF + st * E + h * 128:A_AF + st * E + (h + 1) * 128],
                        pf[bix][:, coff:coff + 128], stats[:, 116 + st:117 + st], None, ALU.mult), inc="vsem")
                for st in (0, 2):
                    pf_free[4 + st // 2] = ("vsem", v_o)
                kv_free[sl] = ("tsem", p.c("tsem"))
                e_guard[sl] = p.c("tsem")

            for h in range(H):
                emit_scores(h)
                if h > 0:
                    emit_out(h - 1)
            emit_out(H - 1)
            return rms_apply(lambda st: arena[:, A_AF + st * E:A_AF + (st + 1) * E],
                             0, wret_b, v_o, out_is_nq=False)

        out_writes = 0
        for ch in range(NCH):
            v_endprev = p.release("vector")
            s_endprev = p.release("scalar")
            p.wait("gpsimd", "osem", out_writes * 16)
            p.wait("gpsimd", "vsem", v_endprev)
            p.wait("gpsimd", "ssem", s_endprev)
            g_in = 0
            for st in range(NST):
                g_in = p.emit("gpsimd", lambda e, ch=ch, st=st: e.dma_start(
                    x_b[:, st * E:(st + 1) * E],
                    x_in[ch * C + st * 128:ch * C + (st + 1) * 128, :]), inc="gsem", incby=16)
            p.wait("vector", "gsem", g_in)
            p.wait("scalar", "gsem", g_in)
            p.wait("vector", "tsem", p.c("tsem"))

            v_ln = layer_norm(LN_EPS)
            transpose_tm_to_fm(None, True, v_ln)
            fm_ready = fwht2048_fm()
            ffn(fm_ready)

            v_ln2 = layer_norm(LN_EPS)
            v_y = 0
            for st in range(NST):
                p.drain("vector")
                v_y = p.emit("vector", lambda e, st=st: e.tensor_add(
                    x_b[:, st * E:(st + 1) * E], x_b[:, st * E:(st + 1) * E],
                    tmA[:, st * E:(st + 1) * E]), inc="vsem")
            p.wait("vector", "tsem", p.c("tsem"))
            rs_v = stats[:, 108:112].rearrange("p (a b) -> p a b", b=1)
            var_v = stats[:, 97:105].rearrange("p (a b) -> p a b", b=2)[:, :, 0:1]
            w1_v = stats[:, 64:68].rearrange("p (a b) -> p a b", b=1)
            u_v = stats[:, 68:72].rearrange("p (a b) -> p a b", b=1)
            p.drain("vector")
            p.emit("vector", lambda e: e.tensor_tensor(w1_v, rs_v, rs_v, ALU.mult))
            p.drain("vector")
            v_w1 = p.emit("vector", lambda e: e.tensor_tensor(w1_v, w1_v, var_v, ALU.mult), inc="vsem")
            p.wait("scalar", "vsem", v_w1)
            p.drain("scalar")
            p.emit("scalar", lambda e: e.activation(u_v, w1_v, ACT.Ln, bias=stats[:, 122:123]))
            p.drain("scalar")
            s_u = p.emit("scalar", lambda e: e.activation(
                u_v, u_v, ACT.Exp, scale=stats[:, 120:121]), inc="ssem")
            p.wait("vector", "ssem", s_u)
            p.wait("vector", "vsem", v_y)
            v_nq = 0
            for st in range(NST):
                v_nq = p.emit("vector", lambda e, st=st: e.scalar_tensor_tensor(
                    arena[:, A_NQ + st * E:A_NQ + (st + 1) * E], tmA[:, st * E:(st + 1) * E],
                    stats[:, 68 + st:69 + st], wnq_b[:], ALU.mult, ALU.mult), inc="vsem")
            attn(v_nq)

            v_ln3 = layer_norm(LN_EPS)
            transpose_tm_to_fm(None, True, v_ln3)
            fm_ready3 = fwht2048_fm()
            v_ffn3 = ffn(fm_ready3)

            p.wait("gpsimd", "vsem", v_ffn3)
            for st in range(NST):
                p.emit("gpsimd", lambda e, ch=ch, st=st: e.dma_start(
                    y_out[ch * C + st * 128:ch * C + (st + 1) * 128, :],
                    x_b[:, st * E:(st + 1) * E]), inc="osem", incby=16)
                out_writes += 1

        total_out = out_writes * 16

        with nc.Block() as block:
            @block.sync
            def _(eng):
                for fn in p.plan["sync"]:
                    fn(eng)

            @block.tensor
            def _(eng):
                for fn in p.plan["tensor"]:
                    fn(eng)

            @block.scalar
            def _(eng):
                for fn in p.plan["scalar"]:
                    fn(eng)

            @block.vector
            def _(eng):
                for fn in p.plan["vector"]:
                    fn(eng)

            @block.gpsimd
            def _(eng):
                for fn in p.plan["gpsimd"]:
                    fn(eng)
                eng.wait_ge(p.sems["osem"], total_out)

    return nc


_NC_CACHE = None


def kernel(L0_attn_query_input, L0_attn_W_in, L0_attn_W_q, L0_attn_W_k, L0_attn_W_v,
           L0_attn_stored_patterns, L0_attn_norm_query_w, L0_attn_norm_retrieved_w,
           L0_attn_beta, L0_ffn_SU_ug, L0_ffn_SU_dp, L0_ffn_Wscale_ug, L0_ffn_Wscale_dp,
           L0_ffn_W_up, L0_ffn_W_gate, L0_ffn_W_down):
    global _NC_CACHE
    x = np.ascontiguousarray(np.asarray(L0_attn_query_input, dtype=np.float32).reshape(16384, E))
    consts = dict(
        h128=np.ascontiguousarray(_hadamard(128).astype(BFNP)),
        ident=np.ascontiguousarray(np.eye(128, dtype=np.float32).astype(BFNP)),
        ones1=np.ones((1, 128), dtype=BFNP),
        ones1f=np.ones((1, 128), dtype=np.float32),
    )
    weights = dict(
        w_in=np.ascontiguousarray(np.asarray(L0_attn_W_in, dtype=np.float32)),
        w_q=np.ascontiguousarray(np.asarray(L0_attn_W_q, dtype=np.float32)),
        w_k=np.ascontiguousarray(np.asarray(L0_attn_W_k, dtype=np.float32)),
        w_v=np.ascontiguousarray(np.asarray(L0_attn_W_v, dtype=np.float32)),
        pats=np.ascontiguousarray(np.asarray(L0_attn_stored_patterns, dtype=np.float32)),
        nqw=np.ascontiguousarray(np.asarray(L0_attn_norm_query_w, dtype=np.float32)),
        nrw=np.ascontiguousarray(np.asarray(L0_attn_norm_retrieved_w, dtype=np.float32)),
        beta=np.asarray(L0_attn_beta, dtype=np.float32).reshape(1),
        su_ug=np.ascontiguousarray(np.asarray(L0_ffn_SU_ug, dtype=np.float32)),
        su_dp=np.ascontiguousarray(np.asarray(L0_ffn_SU_dp, dtype=np.float32)),
        wsc_ug=np.ascontiguousarray(np.asarray(L0_ffn_Wscale_ug, dtype=np.float32)),
        wsc_dp=np.ascontiguousarray(np.asarray(L0_ffn_Wscale_dp, dtype=np.float32)),
        w_up=np.ascontiguousarray(np.asarray(L0_ffn_W_up, dtype=np.float32)),
        w_gate=np.ascontiguousarray(np.asarray(L0_ffn_W_gate, dtype=np.float32)),
        w_down=np.ascontiguousarray(np.asarray(L0_ffn_W_down, dtype=np.float32)),
    )
    if _NC_CACHE is None:
        _NC_CACHE = build_nc()
    nc = _NC_CACHE
    in_maps = []
    for c in range(NCORES):
        m = {"x": np.ascontiguousarray(x[c * TCORE:(c + 1) * TCORE])}
        m.update(weights)
        m.update(consts)
        in_maps.append(m)
    res = run_bass_kernel_spmd(nc, in_maps, core_ids=list(range(NCORES)))
    out = np.concatenate([res.results[c]["y"] for c in range(NCORES)], axis=0)
    return np.ascontiguousarray(out.reshape(1, 16384, E).astype(np.float32))


if __name__ == "__main__":
    build_nc()
    print("built ok")


# version 13
# speedup vs baseline: 1.0682x; 1.0682x over previous
import numpy as np
import ml_dtypes
from contextlib import ExitStack

import concourse.bass as bass
import concourse.mybir as mybir
from concourse.bass_utils import run_bass_kernel_spmd

BFNP = ml_dtypes.bfloat16
F32 = mybir.dt.float32
BF16 = mybir.dt.bfloat16
AX = mybir.AxisListType.X
ALU = mybir.AluOpType
ACT = mybir.ActivationFunctionType

NCORES = 8
TCORE = 2048
C = 512
NCH = TCORE // C
NST = C // 128
E = 2048
I = 8192
NET = E // 128
NIT = I // 128
H = 16
D = 128
M = 1024
NMT = M // 128
VW = 136
RS2048 = float(1.0 / np.sqrt(2048.0))
RS8192 = float(1.0 / np.sqrt(8192.0))
LN_EPS = 1e-5
RMS_EPS = 1e-6

ENG_SEM = {"tensor": "tsem", "scalar": "ssem", "vector": "vsem", "gpsimd": "bsem"}


def _hadamard(n):
    h = np.array([[1.0]], dtype=np.float64)
    while h.shape[0] < n:
        h = np.block([[h, h], [h, -h]])
    return h


class P:

    def __init__(self):
        self.plan = {k: [] for k in ("sync", "tensor", "scalar", "vector", "gpsimd")}
        self.cnt = {}
        self.lastw = {}
        self.sems = {}

    def emit(self, eng, build, inc=None, incby=1):
        if inc is not None:
            self.cnt[inc] = self.cnt.get(inc, 0) + incby

            def fn(e, b=build, s=inc, n=incby):
                b(e).then_inc(self.sems[s], n)
        else:
            def fn(e, b=build):
                b(e)
        self.plan[eng].append(fn)
        return self.cnt.get(inc, 0)

    def wait(self, eng, sem, val):
        if not val or val <= 0:
            return
        key = (eng, sem)
        if self.lastw.get(key, 0) >= val:
            return
        self.lastw[key] = val
        self.plan[eng].append(lambda e, s=sem, v=val: e.wait_ge(self.sems[s], v))

    def drain(self, eng):
        self.plan[eng].append(lambda e: e.drain())

    def release(self, eng):
        sem = ENG_SEM[eng]
        return self.emit(eng, lambda e: e.drain(), inc=sem)

    def c(self, sem):
        return self.cnt.get(sem, 0)


def build_nc():
    nc = bass.Bass("TRN2", target_bir_lowering=False)
    p = P()

    x_in = nc.dram_tensor("x", [TCORE, E], F32, kind="ExternalInput").ap()
    w_in = nc.dram_tensor("w_in", [E, E], F32, kind="ExternalInput").ap()
    w_q = nc.dram_tensor("w_q", [D, D], F32, kind="ExternalInput").ap()
    w_k = nc.dram_tensor("w_k", [D, D], F32, kind="ExternalInput").ap()
    w_v = nc.dram_tensor("w_v", [D, D], F32, kind="ExternalInput").ap()
    pats = nc.dram_tensor("pats", [H, M, D], F32, kind="ExternalInput").ap()
    nqw = nc.dram_tensor("nqw", [E], F32, kind="ExternalInput").ap()
    nrw = nc.dram_tensor("nrw", [E], F32, kind="ExternalInput").ap()
    beta = nc.dram_tensor("beta", [1], F32, kind="ExternalInput").ap()
    su_ug = nc.dram_tensor("su_ug", [E], F32, kind="ExternalInput").ap()
    su_dp = nc.dram_tensor("su_dp", [I], F32, kind="ExternalInput").ap()
    wsc_ug = nc.dram_tensor("wsc_ug", [2 * I], F32, kind="ExternalInput").ap()
    wsc_dp = nc.dram_tensor("wsc_dp", [E], F32, kind="ExternalInput").ap()
    w_up = nc.dram_tensor("w_up", [I, E], F32, kind="ExternalInput").ap()
    w_gate = nc.dram_tensor("w_gate", [I, E], F32, kind="ExternalInput").ap()
    w_down = nc.dram_tensor("w_down", [E, I], F32, kind="ExternalInput").ap()
    h128 = nc.dram_tensor("h128", [128, 128], BF16, kind="ExternalInput").ap()
    k2h64 = nc.dram_tensor("k2h64", [128, 128], BF16, kind="ExternalInput").ap()
    ident = nc.dram_tensor("ident", [128, 128], BF16, kind="ExternalInput").ap()
    ones1 = nc.dram_tensor("ones1", [1, 128], BF16, kind="ExternalInput").ap()
    ones1f = nc.dram_tensor("ones1f", [1, 128], F32, kind="ExternalInput").ap()

    y_out = nc.dram_tensor("y", [TCORE, E], F32, kind="ExternalOutput").ap()

    upT = nc.dram_tensor("upT", [32, NET, 128, 256], BF16, kind="Internal").ap()
    gateT = nc.dram_tensor("gateT", [32, NET, 128, 256], BF16, kind="Internal").ap()
    winT = nc.dram_tensor("winT", [8, NET, 128, 256], BF16, kind="Internal").ap()
    dnT = nc.dram_tensor("dnT", [I, E], BF16, kind="Internal").ap()
    kT = nc.dram_tensor("kT", [H, 128, M], BF16, kind="Internal").ap()
    vT = nc.dram_tensor("vT", [H, 128, NMT * VW], BF16, kind="Internal").ap()

    with ExitStack() as ctx:
        for s in ("wsem", "gsem", "tsem", "ssem", "vsem", "osem", "bsem",
                  "wprep", "wprep1", "castg", "castg1", "dnw", "patg", "gkv",
                  "sl0", "sl1", "sl2", "sl3", "kv0", "kv1"):
            p.sems[s] = ctx.enter_context(nc.semaphore(s))

        x_b = ctx.enter_context(nc.sbuf_tensor("x_b", [128, NST * E], F32))
        tmA = ctx.enter_context(nc.sbuf_tensor("tmA", [128, NST * E], BF16))
        fmA = ctx.enter_context(nc.sbuf_tensor("fmA", [128, NET * C], BF16))
        fwtmp = ctx.enter_context(nc.sbuf_tensor("fwtmp", [128, C], BF16))
        arena = ctx.enter_context(nc.sbuf_tensor("arena", [128, NIT * C], BF16))
        wbuf = ctx.enter_context(nc.sbuf_tensor("wbuf", [128, 4 * 4096], BF16))
        sq = ctx.enter_context(nc.sbuf_tensor("sq", [128, E], F32))
        silu_t = ctx.enter_context(nc.sbuf_tensor("silu_t", [128, C], BF16))
        stats = ctx.enter_context(nc.sbuf_tensor("stats", [128, 128], F32))
        h_sb = ctx.enter_context(nc.sbuf_tensor("h_sb", [128, 128], BF16))
        k2_sb = ctx.enter_context(nc.sbuf_tensor("k2_sb", [128, 128], BF16))
        id_sb = ctx.enter_context(nc.sbuf_tensor("id_sb", [128, 128], BF16))
        on_sb = ctx.enter_context(nc.sbuf_tensor("on_sb", [1, 128], BF16))
        onf_sb = ctx.enter_context(nc.sbuf_tensor("onf_sb", [1, 128], F32))
        su_sb = ctx.enter_context(nc.sbuf_tensor("su_sb", [128, NET], F32))
        wscu_su = ctx.enter_context(nc.sbuf_tensor("wscu_su", [128, NIT], F32))
        wscg_sb = ctx.enter_context(nc.sbuf_tensor("wscg_sb", [128, NIT], F32))
        wscb = ctx.enter_context(nc.sbuf_tensor("wscb", [128, E], F32))
        wnq_b = ctx.enter_context(nc.sbuf_tensor("wnq_b", [128, E], BF16))
        wret_b = ctx.enter_context(nc.sbuf_tensor("wret_b", [128, E], BF16))
        wqT = ctx.enter_context(nc.sbuf_tensor("wqT", [128, 128], BF16))
        wkT = ctx.enter_context(nc.sbuf_tensor("wkT", [128, 128], BF16))
        wvT = ctx.enter_context(nc.sbuf_tensor("wvT", [128, 128], BF16))
        bsig = ctx.enter_context(nc.sbuf_tensor("bsig", [128, 4], F32))

        pf = [ctx.enter_context(nc.psum_tensor(f"pf{i}", [128, 512], F32)) for i in range(6)]
        pt = [ctx.enter_context(nc.psum_tensor(f"pt{i}", [128, 1024], BF16)) for i in range(2)]

        pf_free = [("ssem", 0)] * 6
        pt_free = [("ssem", 0)] * 2

        def mm(out_ap, lhsT_ap, rhs_ap, start, stop):
            return p.emit(
                "tensor",
                lambda e, o=out_ap, l=lhsT_ap, r=rhs_ap, st=start, sp=stop:
                    e.matmul(o, l, r, start=st, stop=sp),
                inc="tsem")

        def tp(out_ap, in_ap):
            return p.emit(
                "tensor",
                lambda e, o=out_ap, i_=in_ap: e.transpose(o, i_, id_sb[:]),
                inc="tsem")

        def sload(dst, src, slow=False):
            return p.emit("sync", lambda e, d=dst, s=src, sl=slow: e.dma_start(
                d, s, allow_slow_non_contiguous=sl), inc="wsem", incby=16)

        sload(h_sb[:], h128)
        sload(k2_sb[:], k2h64)
        sload(id_sb[:], ident)
        sload(on_sb[:], ones1)
        sload(onf_sb[:], ones1f)
        sload(su_sb[:], su_ug.rearrange("(a b) -> b a", b=128), slow=True)
        sload(wscu_su[:], wsc_ug[0:I].rearrange("(a b) -> b a", b=128), slow=True)
        sload(wscg_sb[:], wsc_ug[I:2 * I].rearrange("(a b) -> b a", b=128), slow=True)
        sload(stats[0:1, 112:113], beta.rearrange("(a b) -> a b", b=1))
        sload(x_b[0:1, 0:E], nqw.rearrange("(a b) -> a b", a=1))
        sload(x_b[0:1, E:2 * E], nrw.rearrange("(a b) -> a b", a=1))
        sload(x_b[0:1, 2 * E:3 * E], wsc_dp.rearrange("(a b) -> a b", a=1))
        sload(sq[:, 1024:1024 + NIT], su_dp.rearrange("(a b) -> b a", b=128), slow=True)
        sload(sq[:, 1200:1328], w_q)
        sload(sq[:, 1328:1456], w_k)
        w_all = sload(sq[:, 1456:1584], w_v)

        p.emit("vector", lambda e: e.memset(stats[:, 120:121], -0.5))
        p.emit("vector", lambda e: e.memset(stats[:, 121:122], float(LN_EPS)))
        p.emit("vector", lambda e: e.memset(stats[:, 122:123], float(RMS_EPS)))
        p.emit("vector", lambda e: e.memset(stats[:, 123:124], -1.0))
        v_consts = p.release("vector")
        p.wait("vector", "wsem", w_all)
        p.emit("vector", lambda e: e.tensor_scalar_mul(su_sb[:], su_sb[:], RS2048))
        p.emit("vector", lambda e: e.tensor_mul(wscu_su[:], wscu_su[:], sq[:, 1024:1024 + NIT]))
        p.wait("scalar", "wsem", w_all)
        p.wait("scalar", "vsem", v_consts)
        p.emit("scalar", lambda e: e.activation(
            stats[0:1, 113:114], stats[0:1, 112:113], ACT.Exp, scale=stats[0:1, 123:124]))
        p.emit("scalar", lambda e: e.mul(x_b[0:1, 2 * E:3 * E], x_b[0:1, 2 * E:3 * E], 0.5 * RS8192))
        s_rows0 = p.release("scalar")
        p.wait("vector", "ssem", s_rows0)
        p.emit("vector", lambda e: e.tensor_scalar_add(stats[0:1, 114:115], stats[0:1, 113:114], 1.0))
        p.drain("vector")
        p.emit("vector", lambda e: e.reciprocal(stats[0:1, 115:116], stats[0:1, 114:115]))
        p.emit("vector", lambda e: e.tensor_copy(tmA[0:1, 0:E], x_b[0:1, 0:E]))
        p.emit("vector", lambda e: e.tensor_copy(tmA[:, 2 * E:2 * E + 128], sq[:, 1200:1328]))
        p.emit("vector", lambda e: e.tensor_copy(tmA[:, 2 * E + 128:2 * E + 256], sq[:, 1328:1456]))
        p.emit("vector", lambda e: e.tensor_copy(tmA[:, 2 * E + 256:2 * E + 384], sq[:, 1456:1584]))
        p.emit("vector", lambda e: e.tensor_copy(tmA[0:1, E:2 * E], x_b[0:1, E:2 * E]))
        v_prep0 = p.release("vector")

        p.wait("tensor", "vsem", v_prep0)
        p.wait("tensor", "ssem", s_rows0)
        for j in range(4):
            mm(pf[0][:, 0:512], on_sb[:], tmA[0:1, j * 512:(j + 1) * 512], True, True)
            mm(pf[1][:, 0:512], on_sb[:], tmA[0:1, E + j * 512:E + (j + 1) * 512], True, True)
            t_b = mm(pf[2][:, 0:512], onf_sb[:], x_b[0:1, 2 * E + j * 512:2 * E + (j + 1) * 512], True, True)
            p.wait("scalar", "tsem", t_b)
            p.emit("scalar", lambda e, j=j: e.copy(wnq_b[:, j * 512:(j + 1) * 512], pf[0][:, 0:512]))
            p.emit("scalar", lambda e, j=j: e.copy(wret_b[:, j * 512:(j + 1) * 512], pf[1][:, 0:512]))
            s_b = p.emit("scalar", lambda e, j=j: e.copy(wscb[:, j * 512:(j + 1) * 512], pf[2][:, 0:512]), inc="ssem")
            p.wait("tensor", "ssem", s_b)
        t_bs = mm(pf[3][:, 0:1], onf_sb[:], stats[0:1, 115:116], True, True)
        p.wait("scalar", "tsem", t_bs)
        s_bs = p.emit("scalar", lambda e: e.copy(bsig[:, 0:1], pf[3][:, 0:1]), inc="ssem")

        tp(pt[0][:, 0:128], tmA[:, 2 * E:2 * E + 128])
        tp(pt[0][:, 128:256], tmA[:, 2 * E + 128:2 * E + 256])
        t_w3 = tp(pt[0][:, 256:384], tmA[:, 2 * E + 256:2 * E + 384])
        p.wait("vector", "tsem", t_w3)
        p.wait("vector", "ssem", s_bs)
        p.emit("vector", lambda e: e.tensor_scalar_mul(wqT[:], pt[0][:, 0:128], bsig[:, 0:1]))
        p.emit("vector", lambda e: e.tensor_copy(wkT[:], pt[0][:, 128:256]))
        v_w3 = p.emit("vector", lambda e: e.tensor_copy(wvT[:], pt[0][:, 256:384]), inc="vsem")
        pt_free[0] = ("vsem", v_w3)
        pf_free[3] = ("ssem", s_bs)
        xb_free_t = p.c("tsem")

        pat_guard = [0, 0]
        for h in range(H):
            sl = h % 2
            poff = sl * 1024
            toff = 2048 + sl * 1024
            p.wait("gpsimd", "tsem", pat_guard[sl])
            g_pat = p.emit(
                "gpsimd",
                lambda e, h=h, poff=poff: e.dma_start(
                    fmA[:, poff:poff + 1024].rearrange("p (mt d) -> p mt d", d=128),
                    pats[h].rearrange("(mt p) d -> p mt d", p=128)),
                inc="patg", incby=16)
            p.wait("tensor", "patg", g_pat)
            for eg in range(2):
                bk = eg % 2
                sem, val = pt_free[bk]
                p.wait("tensor", sem, val)
                t_last = 0
                for k in range(8):
                    mt = eg * 8 + k
                    if mt >= 8:
                        break
                    t_last = tp(pt[bk][:, k * 128:(k + 1) * 128],
                                fmA[:, poff + mt * 128:poff + (mt + 1) * 128])
                if eg == 0:
                    p.wait("scalar", "tsem", t_last)
                    sv = p.emit("scalar",
                                lambda e, toff=toff, bk=bk: e.copy(
                                    fmA[:, toff:toff + 1024], pt[bk][:, 0:1024]),
                                inc="ssem")
                    pt_free[bk] = ("ssem", sv)
                    break
            s_patT = p.c("ssem")
            for bix in (0, 1):
                sem, val = pf_free[bix]
                p.wait("tensor", sem, val)
            p.wait("tensor", "ssem", s_patT)
            mm(pf[0][:, 0:512], wkT[:], fmA[:, toff:toff + 512], True, True)
            t_k = mm(pf[1][:, 0:512], wkT[:], fmA[:, toff + 512:toff + 1024], True, True)
            p.wait("vector", "tsem", t_k)
            p.drain("vector")
            p.emit("vector", lambda e: e.reduce_sum(stats[:, 0:1], pf[0][:, 0:512], axis=AX))
            p.emit("vector", lambda e: e.reduce_sum(stats[:, 1:2], pf[1][:, 0:512], axis=AX))
            p.drain("vector")
            p.emit("vector", lambda e: e.tensor_add(stats[:, 2:3], stats[:, 0:1], stats[:, 1:2]))
            p.drain("vector")
            p.emit("vector", lambda e: e.tensor_scalar_mul(stats[:, 3:4], stats[:, 2:3], 1.0 / M))
            p.drain("vector")
            p.emit("vector", lambda e: e.tensor_scalar(
                arena[:, 0:512], pf[0][:, 0:512], stats[:, 3:4], None, ALU.subtract), inc="vsem")
            v_k = p.emit("vector", lambda e: e.tensor_scalar(
                arena[:, 512:1024], pf[1][:, 0:512], stats[:, 3:4], None, ALU.subtract), inc="vsem")
            pf_free[0] = ("vsem", v_k)
            pf_free[1] = ("vsem", v_k)
            p.wait("gpsimd", "vsem", v_k)
            g_k = p.emit("gpsimd", lambda e, h=h: e.dma_start(kT[h], arena[:, 0:1024]),
                         inc="gkv", incby=16)
            p.drain("vector")
            vz = p.emit("vector", lambda e: e.memset(arena[:, 1024:1024 + NMT * VW], 0.0), inc="vsem")
            p.drain("vector")
            ones_ap = arena[:, 1024:1024 + NMT * VW].rearrange(
                "p (mt c) -> p mt c", c=VW)[:, :, 128:129]
            vm_last = p.emit("vector", lambda e, ones_ap=ones_ap: e.memset(ones_ap, 1.0), inc="vsem")
            p.wait("scalar", "vsem", vm_last)
            t_v = 0
            for mtq in range(0, 8, 4):
                for k in range(4):
                    mt = mtq + k
                    sem, val = pf_free[k]
                    p.wait("tensor", sem, val)
                    t_v = mm(pf[k][:, 0:128], fmA[:, toff + mt * 128:toff + (mt + 1) * 128],
                             wvT[:], True, True)
                p.wait("scalar", "tsem", t_v)
                s_v = 0
                for k in range(4):
                    mt = mtq + k
                    s_v = p.emit("scalar", lambda e, mt=mt, k=k: e.copy(
                        arena[:, 1024 + mt * VW:1024 + mt * VW + 128], pf[k][:, 0:128]), inc="ssem")
                for k in range(4):
                    pf_free[k] = ("ssem", s_v)
            p.wait("gpsimd", "ssem", p.c("ssem"))
            g_v = p.emit("gpsimd", lambda e, h=h: e.dma_start(vT[h], arena[:, 1024:1024 + NMT * VW]),
                         inc="gkv", incby=16)
            pat_guard[sl] = p.c("tsem")
            p.wait("vector", "gkv", p.c("gkv"))
            p.wait("scalar", "gkv", p.c("gkv"))

        gkv_all = p.c("gkv")
        p.wait("vector", "gkv", gkv_all)
        p.wait("sync", "tsem", xb_free_t)
        p.wait("sync", "vsem", v_prep0)
        wT_off = [12288, 4096]
        cast_guard = [0, 0]
        write_guard = [0, 0]
        stage_guard = [("vsem", 0), ("vsem", 0)]

        def prep_wT(w_src, dst, nib):
            for ib in range(nib):
                sset = ib % 2
                base = 0 if sset == 0 else 8192
                soff = sset * 4096
                woff = wT_off[sset]
                wT_view = arena[:, woff:woff + 4096].rearrange(
                    "p (g two c) -> p g two c", two=2, c=128)
                csem = "castg" if sset == 0 else "castg1"
                sem, val = stage_guard[sset]
                p.wait("sync", sem, val)
                p.emit(
                    "sync",
                    lambda e, ib=ib, soff=soff, w=w_src: e.dma_start(
                        x_b[:, soff:soff + 4096].rearrange("p (il c) -> p il c", c=2048),
                        w[ib * 256:(ib + 1) * 256, :].rearrange("(il p) c -> p il c", p=128)),
                    inc=csem, incby=16)
                g_c = p.c(csem)
                p.wait("vector", csem, g_c)
                p.wait("vector", "tsem", cast_guard[sset])
                v_c = p.emit("vector", lambda e, base=base, soff=soff: e.tensor_copy(
                    arena[:, base:base + 4096], x_b[:, soff:soff + 4096]), inc="vsem")
                stage_guard[sset] = ("vsem", v_c)
                p.wait("tensor", "vsem", v_c)
                for il in range(2):
                    coff = base + il * 2048
                    for eg in range(2):
                        bk = eg % 2
                        sem, val = pt_free[bk]
                        p.wait("tensor", sem, val)
                        t_last = 0
                        for k in range(8):
                            et = eg * 8 + k
                            t_last = tp(pt[bk][:, k * 128:(k + 1) * 128],
                                        arena[:, coff + et * 128:coff + (et + 1) * 128])
                        p.wait("scalar", "tsem", t_last)
                        p.wait("scalar", "wprep" if sset == 0 else "wprep1", write_guard[sset])
                        s_t = p.emit(
                            "scalar",
                            lambda e, eg=eg, il=il, bk=bk, wT_view=wT_view: e.copy(
                                wT_view[:, eg * 8:(eg + 1) * 8, il, :], pt[bk][:, 0:1024]),
                            inc="ssem")
                        pt_free[bk] = ("ssem", s_t)
                cast_guard[sset] = p.c("tsem")
                p.wait("sync", "ssem", p.c("ssem"))
                write_guard[sset] = p.emit(
                    "sync",
                    lambda e, ib=ib, d=dst, woff=woff: e.dma_start(
                        d[ib].rearrange("et p c -> p et c"),
                        arena[:, woff:woff + 4096].rearrange("p (et c) -> p et c", c=256)),
                    inc="wprep" if sset == 0 else "wprep1", incby=16)

        prep_wT(w_up, upT, 32)
        prep_wT(w_gate, gateT, 32)
        prep_wT(w_in, winT, 8)

        p3_cast_guard = p.c("tsem")
        p3_z_guard = p.c("dnw")
        p.wait("vector", "wprep", p.c("wprep"))
        p.wait("vector", "wprep1", p.c("wprep1"))
        z1v = arena[:, 0:16384].rearrange("p (e a) -> p a e", a=64)
        zf4 = arena[:, 16384:32768].rearrange("p (a eh de) -> p eh de a", eh=128, de=2)
        eng_rr = [0]

        def evac(dst, src, twait, eng=None, dnw_guard=None):
            if eng is None:
                eng = "scalar" if eng_rr[0] % 2 == 0 else "vector"
                eng_rr[0] += 1
            p.wait(eng, "tsem", twait)
            if dnw_guard is not None:
                p.wait(eng, "dnw", dnw_guard)
            if eng == "scalar":
                return ("ssem", p.emit(
                    "scalar", lambda e, d=dst, s=src: e.copy(d, s), inc="ssem"))
            return ("vsem", p.emit(
                "vector", lambda e, d=dst, s=src: e.tensor_copy(d, s), inc="vsem"))

        def pe_wait_evacs():
            p.wait("tensor", "ssem", p.c("ssem"))
            p.wait("tensor", "vsem", p.c("vsem"))

        for eb in range(8):
            cast_last = 0
            for q in range(4):
                j, h2 = q // 2, q % 2
                sset = q % 2
                soff = sset * 4096
                csem = "castg" if sset == 0 else "castg1"
                sem, val = stage_guard[sset]
                p.wait("sync", sem, val)
                p.emit(
                    "sync",
                    lambda e, eb=eb, j=j, h2=h2, soff=soff: e.dma_start(
                        x_b[:, soff:soff + 4096],
                        w_down[eb * 256 + j * 128:eb * 256 + (j + 1) * 128,
                               h2 * 4096:(h2 + 1) * 4096]),
                    inc=csem, incby=16)
                g_c = p.c(csem)
                p.wait("vector", csem, g_c)
                p.wait("vector", "tsem", p3_cast_guard)
                cast_last = p.emit("vector", lambda e, j=j, h2=h2, soff=soff: e.tensor_copy(
                    arena[:, j * 8192 + h2 * 4096:j * 8192 + (h2 + 1) * 4096],
                    x_b[:, soff:soff + 4096]), inc="vsem")
                stage_guard[sset] = ("vsem", cast_last)
            p.wait("tensor", "vsem", cast_last)
            for tg in range(16):
                bk = tg % 2
                sem, val = pt_free[bk]
                p.wait("tensor", sem, val)
                t_last = 0
                for k in range(8):
                    tix = tg * 8 + k
                    it, j = tix // 2, tix % 2
                    t_last = tp(pt[bk][:, k * 128:(k + 1) * 128],
                                arena[:, j * 8192 + it * 128:j * 8192 + (it + 1) * 128])
                pt_free[bk] = evac(
                    arena[:, 16384 + tg * 1024:16384 + (tg + 1) * 1024],
                    pt[bk][:, 0:1024], t_last, eng="scalar", dnw_guard=p3_z_guard)
            pe_wait_evacs()
            for ag in range(0, 64, 2):
                bix = (ag // 2) % 4
                sem, val = pf_free[bix]
                p.wait("tensor", sem, val)
                t_s = mm(pf[bix][:, 0:512], h_sb[:],
                         arena[:, 16384 + ag * 256:16384 + (ag + 2) * 256], True, True)
                pf_free[bix] = evac(
                    z1v[:, ag:ag + 2, :], pf[bix][:, 0:512], t_s)
            pe_wait_evacs()
            for g in range(16):
                bk = g % 2
                sem, val = pt_free[bk]
                p.wait("tensor", sem, val)
                t_last = 0
                for k in range(8):
                    e0h = g * 8 + k
                    t_last = tp(pt[bk][:, k * 128:(k + 1) * 128],
                                arena[:, e0h * 128:(e0h + 1) * 128])
                pt_free[bk] = evac(
                    arena[:, 16384 + g * 1024:16384 + (g + 1) * 1024],
                    pt[bk][:, 0:1024], t_last, eng="scalar")
            pe_wait_evacs()
            for t2 in range(0, 64, 2):
                bix = (t2 // 2) % 4
                sem, val = pf_free[bix]
                p.wait("tensor", sem, val)
                t_s = mm(pf[bix][:, 0:512], k2_sb[:],
                         arena[:, 16384 + t2 * 256:16384 + (t2 + 2) * 256], True, True)
                pf_free[bix] = evac(
                    arena[:, t2 * 256:(t2 + 2) * 256], pf[bix][:, 0:512], t_s)
            pe_wait_evacs()
            for g in range(16):
                bk = g % 2
                sem, val = pt_free[bk]
                p.wait("tensor", sem, val)
                t_last = 0
                for k in range(8):
                    e0h = g * 8 + k
                    t_last = tp(pt[bk][:, k * 128:(k + 1) * 128],
                                arena[:, e0h * 128:(e0h + 1) * 128])
                pt_free[bk] = evac(zf4[:, g * 8:(g + 1) * 8, :, :], pt[bk][:, 0:1024],
                                   t_last, eng="scalar")
            p3_cast_guard = p.c("tsem")
            p.wait("sync", "ssem", p.c("ssem"))
            p.wait("sync", "vsem", p.c("vsem"))
            for grp in range(8):
                p.emit(
                    "sync",
                    lambda e, eb=eb, grp=grp: e.dma_start(
                        dnT[grp * 1024:(grp + 1) * 1024, eb * 256:(eb + 1) * 256]
                        .rearrange("(k p) c -> p k c", p=128),
                        arena[:, 16384 + grp * 2048:16384 + (grp + 1) * 2048]
                        .rearrange("p (k c) -> p k c", c=256)),
                    inc="dnw", incby=16)
            p3_z_guard = p.c("dnw")

        t_relb = p.release("tensor")
        s_relb = p.release("scalar")
        v_relb = p.release("vector")
        b_relb = p.release("gpsimd")
        pd = dict(wsem=p.c("wsem"), gsem=p.c("gsem"), tsem=t_relb, ssem=s_relb, vsem=v_relb,
                  bsem=b_relb,
                  wprep=p.c("wprep"), wprep1=p.c("wprep1"), castg=p.c("castg"), castg1=p.c("castg1"), dnw=p.c("dnw"),
                  patg=p.c("patg"), gkv=p.c("gkv"))
        for eng in ("sync", "tensor", "scalar", "vector", "gpsimd"):
            for s, v in pd.items():
                p.wait(eng, s, v)

        A_NQ = 0
        A_AF = 8192
        A_E = 16384
        A_KB = 24576
        A_VB = 26624

        pf_free = [("ssem", p.c("ssem"))] * 6
        pt_free = [("ssem", p.c("ssem"))] * 2
        wbuf_tok = [("tsem", p.c("tsem"))] * 4
        kv_tok = [("tsem", p.c("tsem"))] * 2
        e_tok = [p.c("tsem"), p.c("tsem")]

        def layer_norm(eps):
            v_in = p.c("vsem")
            p.wait("scalar", "vsem", v_in)
            eps_ap = stats[:, 121:122] if eps == LN_EPS else stats[:, 122:123]
            p.drain("vector")
            for st in range(NST):
                xs = x_b[:, st * E:(st + 1) * E]
                for j in range(4):
                    p.emit("vector", lambda e, xs=xs, st=st, j=j: e.bn_stats(
                        stats[:, st * 24 + j * 6:st * 24 + (j + 1) * 6],
                        xs[:, j * 512:(j + 1) * 512]))
            p.drain("vector")
            v_mu = 0
            for st in range(NST):
                v_mu = p.emit("vector", lambda e, st=st: e.bn_aggr(
                    stats[:, 96 + 2 * st:98 + 2 * st], stats[:, st * 24:(st + 1) * 24]),
                    inc="vsem")
            p.wait("scalar", "vsem", v_mu)
            var_v = stats[:, 97:105].rearrange("p (a b) -> p a b", b=2)[:, :, 0:1]
            ln_v = stats[:, 104:108].rearrange("p (a b) -> p a b", b=1)
            rs_v = stats[:, 108:112].rearrange("p (a b) -> p a b", b=1)
            p.drain("scalar")
            p.emit("scalar", lambda e, eps_ap=eps_ap: e.activation(ln_v, var_v, ACT.Ln, bias=eps_ap))
            p.drain("scalar")
            s_r = p.emit("scalar", lambda e: e.activation(
                rs_v, ln_v, ACT.Exp, scale=stats[:, 120:121]), inc="ssem")
            p.wait("vector", "ssem", s_r)
            v_a = 0
            for st in range(NST):
                xs = x_b[:, st * E:(st + 1) * E]
                v_a = p.emit("vector", lambda e, xs=xs, st=st: e.tensor_scalar(
                    tmA[:, st * E:(st + 1) * E], xs, stats[:, 96 + 2 * st:97 + 2 * st],
                    stats[:, 108 + st:109 + st], ALU.subtract, ALU.mult), inc="vsem")
            return v_a

        def transpose_tm_to_fm(src_off, scaled, v_ready):
            p.wait("tensor", "vsem", v_ready)
            s_last = 0
            for et in range(NET):
                bk = et % 2
                sem, val = pt_free[bk]
                p.wait("tensor", sem, val)
                t_last = 0
                for st in range(NST):
                    src = tmA if src_off is None else arena
                    off = (0 if src_off is None else src_off) + st * E + et * 128
                    t_last = tp(pt[bk][:, st * 128:(st + 1) * 128], src[:, off:off + 128])
                p.wait("scalar", "tsem", t_last)
                if scaled:
                    s_last = p.emit("scalar", lambda e, et=et, bk=bk: e.activation(
                        fmA[:, et * C:(et + 1) * C], pt[bk][:, 0:C],
                        ACT.Copy, scale=su_sb[:, et:et + 1]), inc="ssem")
                else:
                    s_last = p.emit("scalar", lambda e, et=et, bk=bk: e.copy(
                        fmA[:, et * C:(et + 1) * C], pt[bk][:, 0:C]), inc="ssem")
                pt_free[bk] = ("ssem", s_last)
            return s_last

        def fwht2048_fm():
            p.wait("tensor", "ssem", p.c("ssem"))
            for et in range(NET):
                bix = et % 2
                sem, val = pf_free[bix]
                p.wait("tensor", sem, val)
                t_s = mm(pf[bix][:, 0:C], h_sb[:], fmA[:, et * C:(et + 1) * C], True, True)
                p.wait("scalar", "tsem", t_s)
                s_s = p.emit("scalar", lambda e, et=et, bix=bix: e.copy(
                    fmA[:, et * C:(et + 1) * C], pf[bix][:, 0:C]), inc="ssem")
                pf_free[bix] = ("ssem", s_s)
            p.wait("vector", "ssem", p.c("ssem"))
            p.wait("vector", "tsem", p.c("tsem"))
            ready = [0] * NET
            for st in range(4):
                src_fm = (st % 2 == 0)
                for u in range(NET):
                    v = u ^ (1 << st)
                    if v < u:
                        continue
                    if src_fm:
                        iu, iv = fmA[:, u * C:(u + 1) * C], fmA[:, v * C:(v + 1) * C]
                        ou, ov = arena[:, u * C:(u + 1) * C], arena[:, v * C:(v + 1) * C]
                    else:
                        iu, iv = arena[:, u * C:(u + 1) * C], arena[:, v * C:(v + 1) * C]
                        ou, ov = fmA[:, u * C:(u + 1) * C], fmA[:, v * C:(v + 1) * C]
                    p.emit("vector", lambda e, iu=iu, iv=iv, ou=ou: e.tensor_add(ou, iu, iv))
                    last = (st == 3)
                    vv = p.emit("vector", lambda e, iu=iu, iv=iv, ov=ov: e.tensor_sub(ov, iu, iv),
                                inc="vsem" if last else None)
                    if last:
                        ready[u] = vv
                        ready[v] = vv
                p.drain("vector")
            return ready

        def ffn(fm_ready):
            korder = sorted(range(NET), key=lambda k: fm_ready[k])
            wb = [wbuf[:, k * 4096:(k + 1) * 4096] for k in range(4)]
            last_v_g = [p.c("vsem")]
            for ib in range(32):
                ub, gb = wb[(ib % 2) * 2], wb[(ib % 2) * 2 + 1]
                for m_i, (buf, srcT) in enumerate(((ub, upT), (gb, gateT))):
                    slot = (ib % 2) * 2 + m_i
                    sem, val = wbuf_tok[slot]
                    p.wait("sync", sem, val)
                    p.emit("sync", lambda e, ib=ib, buf=buf, srcT=srcT: e.dma_start(
                        buf.rearrange("p (et c) -> p et c", c=256),
                        srcT[ib].rearrange("et p c -> p et c")), inc=f"sl{slot}", incby=16)
                    p.wait("tensor", f"sl{slot}", p.c(f"sl{slot}"))
                for il in range(2):
                    it = ib * 2 + il
                    pu_i, pg_i = (it % 2) * 2, (it % 2) * 2 + 1
                    for bix in (pu_i, pg_i):
                        sem, val = pf_free[bix]
                        p.wait("tensor", sem, val)
                    t_u = 0
                    for ki, et in enumerate(korder):
                        p.wait("tensor", "vsem", fm_ready[et])
                        t_u = mm(pf[pu_i][:, 0:C], ub[:, et * 256 + il * 128:et * 256 + (il + 1) * 128],
                                 fmA[:, et * C:(et + 1) * C], ki == 0, ki == NET - 1)
                    t_g = 0
                    for ki, et in enumerate(korder):
                        t_g = mm(pf[pg_i][:, 0:C], gb[:, et * 256 + il * 128:et * 256 + (il + 1) * 128],
                                 fmA[:, et * C:(et + 1) * C], ki == 0, ki == NET - 1)
                    if il == 1:
                        wbuf_tok[(ib % 2) * 2] = ("tsem", t_g)
                        wbuf_tok[(ib % 2) * 2 + 1] = ("tsem", t_g)
                    p.wait("scalar", "tsem", t_g)
                    p.wait("scalar", "vsem", last_v_g[0])
                    s_si = p.emit("scalar", lambda e, it=it, pg_i=pg_i: e.activation(
                        silu_t[:, 0:C], pf[pg_i][:, 0:C], ACT.Silu, scale=wscg_sb[:, it:it + 1]),
                        inc="ssem")
                    p.wait("vector", "ssem", s_si)
                    p.wait("vector", "tsem", t_u)
                    v_g = p.emit("vector", lambda e, it=it, pu_i=pu_i: e.scalar_tensor_tensor(
                        arena[:, it * C:(it + 1) * C], pf[pu_i][:, 0:C], wscu_su[:, it:it + 1],
                        silu_t[:, 0:C], ALU.mult, ALU.mult), inc="vsem")
                    pf_free[pu_i] = ("vsem", v_g)
                    pf_free[pg_i] = ("ssem", s_si)
                    last_v_g[0] = v_g
            v_g_all = p.c("vsem")
            db = [wbuf[:, k * 4096:(k + 1) * 4096] for k in range(4)]
            p.wait("tensor", "vsem", v_g_all)
            v_d = 0
            for ec in range(4):
                for st in range(NST):
                    sem, val = pf_free[st]
                    p.wait("tensor", sem, val)
                t_d = 0
                for kb in range(8):
                    bi = kb % 4
                    sem, val = wbuf_tok[bi]
                    p.wait("sync", sem, val)
                    wv = p.emit("sync", lambda e, ec=ec, kb=kb, b=db[bi]: e.dma_start(
                        b.rearrange("p (k c) -> p k c", c=512),
                        dnT[kb * 1024:(kb + 1) * 1024, ec * 512:(ec + 1) * 512]
                        .rearrange("(k p) c -> p k c", p=128)), inc=f"sl{bi}", incby=16)
                    p.wait("tensor", f"sl{bi}", wv)
                    for st in range(NST):
                        for k in range(8):
                            it = kb * 8 + k
                            t_d = mm(pf[st][:, 0:512],
                                     arena[:, it * C + st * 128:it * C + (st + 1) * 128],
                                     db[bi][:, k * 512:(k + 1) * 512],
                                     it == 0, it == NIT - 1)
                    wbuf_tok[bi] = ("tsem", t_d)
                p.wait("vector", "tsem", t_d)
                for st in range(NST):
                    p.drain("vector")
                    p.emit("vector", lambda e, st=st, ec=ec: e.tensor_mul(
                        sq[:, 0:512], pf[st][:, 0:512], wscb[:, ec * 512:(ec + 1) * 512]))
                    p.drain("vector")
                    v_d = p.emit("vector", lambda e, st=st, ec=ec: e.tensor_add(
                        x_b[:, st * E + ec * 512:st * E + (ec + 1) * 512],
                        x_b[:, st * E + ec * 512:st * E + (ec + 1) * 512],
                        sq[:, 0:512]), inc="vsem")
                    pf_free[st] = ("vsem", v_d)
            p.drain("vector")
            return v_d

        def rms_apply(src_ap_fn, dst_off, w_bcast, v_ready, out_is_nq):
            p.wait("scalar", "vsem", v_ready)
            p.wait("vector", "vsem", v_ready)
            p.drain("vector")
            for st in range(NST):
                a_s = src_ap_fn(st)
                for j in range(4):
                    p.emit("vector", lambda e, a_s=a_s, st=st, j=j: e.bn_stats(
                        stats[:, st * 24 + j * 6:st * 24 + (j + 1) * 6],
                        a_s[:, j * 512:(j + 1) * 512]))
            p.drain("vector")
            for st in range(NST):
                p.emit("vector", lambda e, st=st: e.bn_aggr(
                    stats[:, 96 + 2 * st:98 + 2 * st], stats[:, st * 24:(st + 1) * 24]))
            p.drain("vector")
            mean_v = stats[:, 96:104].rearrange("p (a b) -> p a b", b=2)[:, :, 0:1]
            var_v = stats[:, 97:105].rearrange("p (a b) -> p a b", b=2)[:, :, 0:1]
            msq_v = stats[:, 104:108].rearrange("p (a b) -> p a b", b=1)
            rs_v = stats[:, 108:112].rearrange("p (a b) -> p a b", b=1)
            v_ss = p.emit("vector", lambda e: e.tensor_tensor(
                msq_v, mean_v, mean_v, ALU.mult), inc="vsem")
            p.drain("vector")
            v_ss = p.emit("vector", lambda e: e.tensor_tensor(
                msq_v, msq_v, var_v, ALU.add), inc="vsem")
            p.wait("scalar", "vsem", v_ss)
            p.drain("scalar")
            p.emit("scalar", lambda e: e.activation(msq_v, msq_v, ACT.Ln, bias=stats[:, 122:123]))
            p.drain("scalar")
            s_r = p.emit("scalar", lambda e: e.activation(
                rs_v, msq_v, ACT.Exp, scale=stats[:, 120:121]), inc="ssem")
            p.wait("vector", "ssem", s_r)
            last = 0
            for st in range(NST):
                a_s = src_ap_fn(st)
                if out_is_nq:
                    last = p.emit("vector", lambda e, a_s=a_s, st=st: e.scalar_tensor_tensor(
                        arena[:, dst_off + st * E:dst_off + (st + 1) * E], a_s,
                        stats[:, 108 + st:109 + st], w_bcast[:], ALU.mult, ALU.mult), inc="vsem")
                else:
                    p.drain("vector")
                    p.emit("vector", lambda e, a_s=a_s, st=st: e.scalar_tensor_tensor(
                        sq[:], a_s, stats[:, 108 + st:109 + st], w_bcast[:], ALU.mult, ALU.mult))
                    p.drain("vector")
                    last = p.emit("vector", lambda e, st=st: e.tensor_add(
                        x_b[:, st * E:(st + 1) * E], x_b[:, st * E:(st + 1) * E], sq[:]), inc="vsem")
            return last

        def attn(v_nq_ready):
            s_tr = transpose_tm_to_fm(A_NQ, False, v_nq_ready)
            wb = [wbuf[:, k * 4096:(k + 1) * 4096] for k in range(2)]
            p.wait("tensor", "ssem", s_tr)
            for eb in range(8):
                bi = eb % 2
                sem, val = wbuf_tok[bi]
                p.wait("sync", sem, val)
                p.emit("sync", lambda e, eb=eb, b=wb[bi]: e.dma_start(
                    b.rearrange("p (et c) -> p et c", c=256),
                    winT[eb].rearrange("et p c -> p et c")), inc=f"sl{bi}", incby=16)
                p.wait("tensor", f"sl{bi}", p.c(f"sl{bi}"))
                for il in range(2):
                    ep = eb * 2 + il
                    bix = 4 + (ep % 2)
                    sem, val = pf_free[bix]
                    p.wait("tensor", sem, val)
                    t_p = 0
                    for et in range(NET):
                        t_p = mm(pf[bix][:, 0:C], wb[bi][:, et * 256 + il * 128:et * 256 + (il + 1) * 128],
                                 fmA[:, et * C:(et + 1) * C], et == 0, et == NET - 1)
                    if il == 1:
                        wbuf_tok[bi] = ("tsem", t_p)
                    p.wait("scalar", "tsem", t_p)
                    s_p = p.emit("scalar", lambda e, ep=ep, bix=bix: e.copy(
                        arena[:, A_NQ + ep * C:A_NQ + (ep + 1) * C], pf[bix][:, 0:C]), inc="ssem")
                    pf_free[bix] = ("ssem", s_p)
            p.wait("tensor", "ssem", p.c("ssem"))
            for h in range(H):
                bix = 4 + (h % 2)
                sem, val = pf_free[bix]
                p.wait("tensor", sem, val)
                t_q = mm(pf[bix][:, 0:C], wqT[:], arena[:, A_NQ + h * C:A_NQ + (h + 1) * C],
                         True, True)
                p.wait("scalar", "tsem", t_q)
                s_q = p.emit("scalar", lambda e, h=h, bix=bix: e.copy(
                    fmA[:, h * C:(h + 1) * C], pf[bix][:, 0:C]), inc="ssem")
                pf_free[bix] = ("ssem", s_q)
            p.wait("tensor", "ssem", p.c("ssem"))
            kv_free = kv_tok
            e_guard = e_tok
            s_e_done = [0] * H
            v_o = 0

            def emit_scores(h):
                sl = h % 2
                koff = A_KB + sl * 1024
                voff = A_VB + sl * 1088
                sem, val = kv_free[sl]
                p.wait("sync", sem, val)
                p.emit("sync", lambda e, h=h, koff=koff: e.dma_start(
                    arena[:, koff:koff + 1024], kT[h]), inc=f"kv{sl}", incby=16)
                w_kv = p.emit("sync", lambda e, h=h, voff=voff: e.dma_start(
                    arena[:, voff:voff + 1088], vT[h]), inc=f"kv{sl}", incby=16)
                p.wait("tensor", f"kv{sl}", w_kv)
                eoff = A_E + sl * 4096
                p.wait("scalar", "tsem", e_guard[sl])
                for mtq in range(0, NMT, 4):
                    t_sc = 0
                    for k in range(4):
                        mt = mtq + k
                        sem, val = pf_free[k]
                        p.wait("tensor", sem, val)
                        t_sc = mm(pf[k][:, 0:C], arena[:, koff + mt * 128:koff + (mt + 1) * 128],
                                  fmA[:, h * C:(h + 1) * C], True, True)
                    p.wait("scalar", "tsem", t_sc)
                    s_e = 0
                    for k in range(4):
                        mt = mtq + k
                        s_e = p.emit("scalar", lambda e, mt=mt, k=k, eoff=eoff: e.activation(
                            arena[:, eoff + mt * C:eoff + (mt + 1) * C], pf[k][:, 0:C], ACT.Exp),
                            inc="ssem")
                    for k in range(4):
                        pf_free[k] = ("ssem", s_e)
                    if mtq == 0 and h > 0:
                        emit_out(h - 1)
                s_e_done[h] = p.c("ssem")

            def emit_out(h):
                nonlocal v_o
                sl = h % 2
                voff = A_VB + sl * 1088
                eoff = A_E + sl * 4096
                p.wait("tensor", "ssem", s_e_done[h])
                t_o = 0
                for st in range(NST):
                    bix = 4 + st // 2
                    coff = (st % 2) * 256
                    if st % 2 == 0:
                        sem, val = pf_free[bix]
                        p.wait("tensor", sem, val)
                    for mt in range(NMT):
                        t_o = mm(pf[bix][:, coff:coff + VW],
                                 arena[:, eoff + mt * C + st * 128:eoff + mt * C + (st + 1) * 128],
                                 arena[:, voff + mt * VW:voff + (mt + 1) * VW],
                                 mt == 0, mt == NMT - 1)
                p.wait("vector", "tsem", t_o)
                p.drain("vector")
                for st in range(NST):
                    bix = 4 + st // 2
                    coff = (st % 2) * 256
                    p.emit("vector", lambda e, st=st, bix=bix, coff=coff: e.reciprocal(
                        stats[:, 116 + st:117 + st], pf[bix][:, coff + 128:coff + 129]))
                p.drain("vector")
                for st in range(NST):
                    bix = 4 + st // 2
                    coff = (st % 2) * 256
                    v_o = p.emit("vector", lambda e, h=h, st=st, bix=bix, coff=coff: e.tensor_scalar(
                        arena[:, A_AF + st * E + h * 128:A_AF + st * E + (h + 1) * 128],
                        pf[bix][:, coff:coff + 128], stats[:, 116 + st:117 + st], None, ALU.mult), inc="vsem")
                for st in (0, 2):
                    pf_free[4 + st // 2] = ("vsem", v_o)
                kv_free[sl] = ("tsem", p.c("tsem"))
                e_guard[sl] = p.c("tsem")

            for h in range(H):
                emit_scores(h)
                if h > 0:
                    emit_out(h - 1)
            emit_out(H - 1)
            return rms_apply(lambda st: arena[:, A_AF + st * E:A_AF + (st + 1) * E],
                             0, wret_b, v_o, out_is_nq=False)

        out_writes = 0
        for ch in range(NCH):
            v_endprev = p.release("vector")
            s_endprev = p.release("scalar")
            p.wait("gpsimd", "osem", out_writes * 16)
            p.wait("gpsimd", "vsem", v_endprev)
            p.wait("gpsimd", "ssem", s_endprev)
            g_in = 0
            for st in range(NST):
                g_in = p.emit("gpsimd", lambda e, ch=ch, st=st: e.dma_start(
                    x_b[:, st * E:(st + 1) * E],
                    x_in[ch * C + st * 128:ch * C + (st + 1) * 128, :]), inc="gsem", incby=16)
            p.wait("vector", "gsem", g_in)
            p.wait("scalar", "gsem", g_in)
            p.wait("vector", "tsem", p.c("tsem"))

            v_ln = layer_norm(LN_EPS)
            transpose_tm_to_fm(None, True, v_ln)
            fm_ready = fwht2048_fm()
            ffn(fm_ready)

            v_ln2 = layer_norm(LN_EPS)
            v_y = 0
            for st in range(NST):
                p.drain("vector")
                v_y = p.emit("vector", lambda e, st=st: e.tensor_add(
                    x_b[:, st * E:(st + 1) * E], x_b[:, st * E:(st + 1) * E],
                    tmA[:, st * E:(st + 1) * E]), inc="vsem")
            p.wait("vector", "tsem", p.c("tsem"))
            rs_v = stats[:, 108:112].rearrange("p (a b) -> p a b", b=1)
            var_v = stats[:, 97:105].rearrange("p (a b) -> p a b", b=2)[:, :, 0:1]
            w1_v = stats[:, 64:68].rearrange("p (a b) -> p a b", b=1)
            u_v = stats[:, 68:72].rearrange("p (a b) -> p a b", b=1)
            p.drain("vector")
            p.emit("vector", lambda e: e.tensor_tensor(w1_v, rs_v, rs_v, ALU.mult))
            p.drain("vector")
            v_w1 = p.emit("vector", lambda e: e.tensor_tensor(w1_v, w1_v, var_v, ALU.mult), inc="vsem")
            p.wait("scalar", "vsem", v_w1)
            p.drain("scalar")
            p.emit("scalar", lambda e: e.activation(u_v, w1_v, ACT.Ln, bias=stats[:, 122:123]))
            p.drain("scalar")
            s_u = p.emit("scalar", lambda e: e.activation(
                u_v, u_v, ACT.Exp, scale=stats[:, 120:121]), inc="ssem")
            p.wait("vector", "ssem", s_u)
            p.wait("vector", "vsem", v_y)
            v_nq = 0
            for st in range(NST):
                v_nq = p.emit("vector", lambda e, st=st: e.scalar_tensor_tensor(
                    arena[:, A_NQ + st * E:A_NQ + (st + 1) * E], tmA[:, st * E:(st + 1) * E],
                    stats[:, 68 + st:69 + st], wnq_b[:], ALU.mult, ALU.mult), inc="vsem")
            attn(v_nq)

            v_ln3 = layer_norm(LN_EPS)
            transpose_tm_to_fm(None, True, v_ln3)
            fm_ready3 = fwht2048_fm()
            v_ffn3 = ffn(fm_ready3)

            p.wait("gpsimd", "vsem", v_ffn3)
            for st in range(NST):
                p.emit("gpsimd", lambda e, ch=ch, st=st: e.dma_start(
                    y_out[ch * C + st * 128:ch * C + (st + 1) * 128, :],
                    x_b[:, st * E:(st + 1) * E]), inc="osem", incby=16)
                out_writes += 1

        total_out = out_writes * 16

        with nc.Block() as block:
            @block.sync
            def _(eng):
                for fn in p.plan["sync"]:
                    fn(eng)

            @block.tensor
            def _(eng):
                for fn in p.plan["tensor"]:
                    fn(eng)

            @block.scalar
            def _(eng):
                for fn in p.plan["scalar"]:
                    fn(eng)

            @block.vector
            def _(eng):
                for fn in p.plan["vector"]:
                    fn(eng)

            @block.gpsimd
            def _(eng):
                for fn in p.plan["gpsimd"]:
                    fn(eng)
                eng.wait_ge(p.sems["osem"], total_out)

    return nc


_NC_CACHE = None


def kernel(L0_attn_query_input, L0_attn_W_in, L0_attn_W_q, L0_attn_W_k, L0_attn_W_v,
           L0_attn_stored_patterns, L0_attn_norm_query_w, L0_attn_norm_retrieved_w,
           L0_attn_beta, L0_ffn_SU_ug, L0_ffn_SU_dp, L0_ffn_Wscale_ug, L0_ffn_Wscale_dp,
           L0_ffn_W_up, L0_ffn_W_gate, L0_ffn_W_down):
    global _NC_CACHE
    x = np.ascontiguousarray(np.asarray(L0_attn_query_input, dtype=np.float32).reshape(16384, E))
    consts = dict(
        h128=np.ascontiguousarray(_hadamard(128).astype(BFNP)),
        k2h64=np.ascontiguousarray(np.kron(np.eye(2), _hadamard(64)).astype(BFNP)),
        ident=np.ascontiguousarray(np.eye(128, dtype=np.float32).astype(BFNP)),
        ones1=np.ones((1, 128), dtype=BFNP),
        ones1f=np.ones((1, 128), dtype=np.float32),
    )
    weights = dict(
        w_in=np.ascontiguousarray(np.asarray(L0_attn_W_in, dtype=np.float32)),
        w_q=np.ascontiguousarray(np.asarray(L0_attn_W_q, dtype=np.float32)),
        w_k=np.ascontiguousarray(np.asarray(L0_attn_W_k, dtype=np.float32)),
        w_v=np.ascontiguousarray(np.asarray(L0_attn_W_v, dtype=np.float32)),
        pats=np.ascontiguousarray(np.asarray(L0_attn_stored_patterns, dtype=np.float32)),
        nqw=np.ascontiguousarray(np.asarray(L0_attn_norm_query_w, dtype=np.float32)),
        nrw=np.ascontiguousarray(np.asarray(L0_attn_norm_retrieved_w, dtype=np.float32)),
        beta=np.asarray(L0_attn_beta, dtype=np.float32).reshape(1),
        su_ug=np.ascontiguousarray(np.asarray(L0_ffn_SU_ug, dtype=np.float32)),
        su_dp=np.ascontiguousarray(np.asarray(L0_ffn_SU_dp, dtype=np.float32)),
        wsc_ug=np.ascontiguousarray(np.asarray(L0_ffn_Wscale_ug, dtype=np.float32)),
        wsc_dp=np.ascontiguousarray(np.asarray(L0_ffn_Wscale_dp, dtype=np.float32)),
        w_up=np.ascontiguousarray(np.asarray(L0_ffn_W_up, dtype=np.float32)),
        w_gate=np.ascontiguousarray(np.asarray(L0_ffn_W_gate, dtype=np.float32)),
        w_down=np.ascontiguousarray(np.asarray(L0_ffn_W_down, dtype=np.float32)),
    )
    if _NC_CACHE is None:
        _NC_CACHE = build_nc()
    nc = _NC_CACHE
    in_maps = []
    for c in range(NCORES):
        m = {"x": np.ascontiguousarray(x[c * TCORE:(c + 1) * TCORE])}
        m.update(weights)
        m.update(consts)
        in_maps.append(m)
    res = run_bass_kernel_spmd(nc, in_maps, core_ids=list(range(NCORES)))
    out = np.concatenate([res.results[c]["y"] for c in range(NCORES)], axis=0)
    return np.ascontiguousarray(out.reshape(1, 16384, E).astype(np.float32))


if __name__ == "__main__":
    build_nc()
    print("built ok")


# version 15
# speedup vs baseline: 1.1167x; 1.0454x over previous
import numpy as np
import ml_dtypes
from contextlib import ExitStack

import concourse.bass as bass
import concourse.mybir as mybir
from concourse.bass_utils import run_bass_kernel_spmd

BFNP = ml_dtypes.bfloat16
F32 = mybir.dt.float32
BF16 = mybir.dt.bfloat16
AX = mybir.AxisListType.X
ALU = mybir.AluOpType
ACT = mybir.ActivationFunctionType

NCORES = 8
TCORE = 2048
C = 512
NCH = TCORE // C
NST = C // 128
E = 2048
I = 8192
NET = E // 128
NIT = I // 128
H = 16
D = 128
M = 1024
NMT = M // 128
VW = 136
RS2048 = float(1.0 / np.sqrt(2048.0))
RS8192 = float(1.0 / np.sqrt(8192.0))
LN_EPS = 1e-5
RMS_EPS = 1e-6

ENG_SEM = {"tensor": "tsem", "scalar": "ssem", "vector": "vsem", "gpsimd": "bsem"}


def _hadamard(n):
    h = np.array([[1.0]], dtype=np.float64)
    while h.shape[0] < n:
        h = np.block([[h, h], [h, -h]])
    return h


class P:

    def __init__(self):
        self.plan = {k: [] for k in ("sync", "tensor", "scalar", "vector", "gpsimd")}
        self.cnt = {}
        self.lastw = {}
        self.sems = {}

    def emit(self, eng, build, inc=None, incby=1):
        if inc is not None:
            self.cnt[inc] = self.cnt.get(inc, 0) + incby

            def fn(e, b=build, s=inc, n=incby):
                b(e).then_inc(self.sems[s], n)
        else:
            def fn(e, b=build):
                b(e)
        self.plan[eng].append(fn)
        return self.cnt.get(inc, 0)

    def wait(self, eng, sem, val):
        if not val or val <= 0:
            return
        key = (eng, sem)
        if self.lastw.get(key, 0) >= val:
            return
        self.lastw[key] = val
        self.plan[eng].append(lambda e, s=sem, v=val: e.wait_ge(self.sems[s], v))

    def drain(self, eng):
        self.plan[eng].append(lambda e: e.drain())

    def release(self, eng):
        sem = ENG_SEM[eng]
        return self.emit(eng, lambda e: e.drain(), inc=sem)

    def c(self, sem):
        return self.cnt.get(sem, 0)


def build_nc():
    nc = bass.Bass("TRN2", target_bir_lowering=False)
    p = P()

    x_in = nc.dram_tensor("x", [TCORE, E], F32, kind="ExternalInput").ap()
    w_in = nc.dram_tensor("w_in", [E, E], F32, kind="ExternalInput").ap()
    w_q = nc.dram_tensor("w_q", [D, D], F32, kind="ExternalInput").ap()
    w_k = nc.dram_tensor("w_k", [D, D], F32, kind="ExternalInput").ap()
    w_v = nc.dram_tensor("w_v", [D, D], F32, kind="ExternalInput").ap()
    pats = nc.dram_tensor("pats", [H, M, D], F32, kind="ExternalInput").ap()
    nqw = nc.dram_tensor("nqw", [E], F32, kind="ExternalInput").ap()
    nrw = nc.dram_tensor("nrw", [E], F32, kind="ExternalInput").ap()
    beta = nc.dram_tensor("beta", [1], F32, kind="ExternalInput").ap()
    su_ug = nc.dram_tensor("su_ug", [E], F32, kind="ExternalInput").ap()
    su_dp = nc.dram_tensor("su_dp", [I], F32, kind="ExternalInput").ap()
    wsc_ug = nc.dram_tensor("wsc_ug", [2 * I], F32, kind="ExternalInput").ap()
    wsc_dp = nc.dram_tensor("wsc_dp", [E], F32, kind="ExternalInput").ap()
    w_up = nc.dram_tensor("w_up", [I, E], F32, kind="ExternalInput").ap()
    w_gate = nc.dram_tensor("w_gate", [I, E], F32, kind="ExternalInput").ap()
    w_down = nc.dram_tensor("w_down", [E, I], F32, kind="ExternalInput").ap()
    h128 = nc.dram_tensor("h128", [128, 128], BF16, kind="ExternalInput").ap()
    k2h64 = nc.dram_tensor("k2h64", [128, 128], BF16, kind="ExternalInput").ap()
    ident = nc.dram_tensor("ident", [128, 128], BF16, kind="ExternalInput").ap()
    ones1 = nc.dram_tensor("ones1", [1, 128], BF16, kind="ExternalInput").ap()
    ones1f = nc.dram_tensor("ones1f", [1, 128], F32, kind="ExternalInput").ap()

    y_out = nc.dram_tensor("y", [TCORE, E], F32, kind="ExternalOutput").ap()

    upT = nc.dram_tensor("upT", [32, NET, 128, 256], BF16, kind="Internal").ap()
    gateT = nc.dram_tensor("gateT", [32, NET, 128, 256], BF16, kind="Internal").ap()
    winT = nc.dram_tensor("winT", [8, NET, 128, 256], BF16, kind="Internal").ap()
    dnT = nc.dram_tensor("dnT", [I, E], BF16, kind="Internal").ap()
    kT = nc.dram_tensor("kT", [H, 128, M], BF16, kind="Internal").ap()
    vT = nc.dram_tensor("vT", [H, 128, NMT * VW], BF16, kind="Internal").ap()

    with ExitStack() as ctx:
        for s in ("wsem", "gsem", "tsem", "ssem", "vsem", "osem", "bsem",
                  "wprep", "wprep1", "castg", "castg1", "dnw", "patg", "gkv",
                  "sl0", "sl1", "sl2", "sl3", "kv0", "kv1"):
            p.sems[s] = ctx.enter_context(nc.semaphore(s))

        x_b = ctx.enter_context(nc.sbuf_tensor("x_b", [128, NST * E], F32))
        tmA = ctx.enter_context(nc.sbuf_tensor("tmA", [128, NST * E], BF16))
        fmA = ctx.enter_context(nc.sbuf_tensor("fmA", [128, NET * C], BF16))
        fwtmp = ctx.enter_context(nc.sbuf_tensor("fwtmp", [128, C], BF16))
        arena = ctx.enter_context(nc.sbuf_tensor("arena", [128, NIT * C], BF16))
        wbuf = ctx.enter_context(nc.sbuf_tensor("wbuf", [128, 4 * 4096], BF16))
        sq = ctx.enter_context(nc.sbuf_tensor("sq", [128, E], F32))
        silu_t = ctx.enter_context(nc.sbuf_tensor("silu_t", [128, C], BF16))
        stats = ctx.enter_context(nc.sbuf_tensor("stats", [128, 128], F32))
        h_sb = ctx.enter_context(nc.sbuf_tensor("h_sb", [128, 128], BF16))
        k2_sb = ctx.enter_context(nc.sbuf_tensor("k2_sb", [128, 128], BF16))
        id_sb = ctx.enter_context(nc.sbuf_tensor("id_sb", [128, 128], BF16))
        on_sb = ctx.enter_context(nc.sbuf_tensor("on_sb", [1, 128], BF16))
        onf_sb = ctx.enter_context(nc.sbuf_tensor("onf_sb", [1, 128], F32))
        su_sb = ctx.enter_context(nc.sbuf_tensor("su_sb", [128, NET], F32))
        wscu_su = ctx.enter_context(nc.sbuf_tensor("wscu_su", [128, NIT], F32))
        wscg_sb = ctx.enter_context(nc.sbuf_tensor("wscg_sb", [128, NIT], F32))
        wscb = ctx.enter_context(nc.sbuf_tensor("wscb", [128, E], F32))
        wnq_b = ctx.enter_context(nc.sbuf_tensor("wnq_b", [128, E], BF16))
        wret_b = ctx.enter_context(nc.sbuf_tensor("wret_b", [128, E], BF16))
        wqT = ctx.enter_context(nc.sbuf_tensor("wqT", [128, 128], BF16))
        wkT = ctx.enter_context(nc.sbuf_tensor("wkT", [128, 128], BF16))
        wvT = ctx.enter_context(nc.sbuf_tensor("wvT", [128, 128], BF16))
        bsig = ctx.enter_context(nc.sbuf_tensor("bsig", [128, 4], F32))

        pf = [ctx.enter_context(nc.psum_tensor(f"pf{i}", [128, 512], F32)) for i in range(6)]
        pt = [ctx.enter_context(nc.psum_tensor(f"pt{i}", [128, 1024], BF16)) for i in range(2)]

        pf_free = [("ssem", 0)] * 6
        pt_free = [("ssem", 0)] * 2

        def mm(out_ap, lhsT_ap, rhs_ap, start, stop):
            return p.emit(
                "tensor",
                lambda e, o=out_ap, l=lhsT_ap, r=rhs_ap, st=start, sp=stop:
                    e.matmul(o, l, r, start=st, stop=sp),
                inc="tsem")

        def tp(out_ap, in_ap):
            return p.emit(
                "tensor",
                lambda e, o=out_ap, i_=in_ap: e.transpose(o, i_, id_sb[:]),
                inc="tsem")

        def sload(dst, src, slow=False):
            return p.emit("sync", lambda e, d=dst, s=src, sl=slow: e.dma_start(
                d, s, allow_slow_non_contiguous=sl), inc="wsem", incby=16)

        sload(h_sb[:], h128)
        sload(k2_sb[:], k2h64)
        sload(id_sb[:], ident)
        sload(on_sb[:], ones1)
        sload(onf_sb[:], ones1f)
        sload(su_sb[:], su_ug.rearrange("(a b) -> b a", b=128), slow=True)
        sload(wscu_su[:], wsc_ug[0:I].rearrange("(a b) -> b a", b=128), slow=True)
        sload(wscg_sb[:], wsc_ug[I:2 * I].rearrange("(a b) -> b a", b=128), slow=True)
        sload(stats[0:1, 112:113], beta.rearrange("(a b) -> a b", b=1))
        sload(x_b[0:1, 0:E], nqw.rearrange("(a b) -> a b", a=1))
        sload(x_b[0:1, E:2 * E], nrw.rearrange("(a b) -> a b", a=1))
        sload(x_b[0:1, 2 * E:3 * E], wsc_dp.rearrange("(a b) -> a b", a=1))
        sload(sq[:, 1024:1024 + NIT], su_dp.rearrange("(a b) -> b a", b=128), slow=True)
        sload(sq[:, 1200:1328], w_q)
        sload(sq[:, 1328:1456], w_k)
        w_all = sload(sq[:, 1456:1584], w_v)

        p.emit("vector", lambda e: e.memset(stats[:, 120:121], -0.5))
        p.emit("vector", lambda e: e.memset(stats[:, 121:122], float(LN_EPS)))
        p.emit("vector", lambda e: e.memset(stats[:, 122:123], float(RMS_EPS)))
        p.emit("vector", lambda e: e.memset(stats[:, 123:124], -1.0))
        v_consts = p.release("vector")
        p.wait("vector", "wsem", w_all)
        p.emit("vector", lambda e: e.tensor_scalar_mul(su_sb[:], su_sb[:], RS2048))
        p.emit("vector", lambda e: e.tensor_mul(wscu_su[:], wscu_su[:], sq[:, 1024:1024 + NIT]))
        p.wait("scalar", "wsem", w_all)
        p.wait("scalar", "vsem", v_consts)
        p.emit("scalar", lambda e: e.activation(
            stats[0:1, 113:114], stats[0:1, 112:113], ACT.Exp, scale=stats[0:1, 123:124]))
        p.emit("scalar", lambda e: e.mul(x_b[0:1, 2 * E:3 * E], x_b[0:1, 2 * E:3 * E], 0.5 * RS8192))
        s_rows0 = p.release("scalar")
        p.wait("vector", "ssem", s_rows0)
        p.emit("vector", lambda e: e.tensor_scalar_add(stats[0:1, 114:115], stats[0:1, 113:114], 1.0))
        p.drain("vector")
        p.emit("vector", lambda e: e.reciprocal(stats[0:1, 115:116], stats[0:1, 114:115]))
        p.emit("vector", lambda e: e.tensor_copy(tmA[0:1, 0:E], x_b[0:1, 0:E]))
        p.emit("vector", lambda e: e.tensor_copy(tmA[:, 2 * E:2 * E + 128], sq[:, 1200:1328]))
        p.emit("vector", lambda e: e.tensor_copy(tmA[:, 2 * E + 128:2 * E + 256], sq[:, 1328:1456]))
        p.emit("vector", lambda e: e.tensor_copy(tmA[:, 2 * E + 256:2 * E + 384], sq[:, 1456:1584]))
        p.emit("vector", lambda e: e.tensor_copy(tmA[0:1, E:2 * E], x_b[0:1, E:2 * E]))
        v_prep0 = p.release("vector")

        p.wait("tensor", "vsem", v_prep0)
        p.wait("tensor", "ssem", s_rows0)
        for j in range(4):
            mm(pf[0][:, 0:512], on_sb[:], tmA[0:1, j * 512:(j + 1) * 512], True, True)
            mm(pf[1][:, 0:512], on_sb[:], tmA[0:1, E + j * 512:E + (j + 1) * 512], True, True)
            t_b = mm(pf[2][:, 0:512], onf_sb[:], x_b[0:1, 2 * E + j * 512:2 * E + (j + 1) * 512], True, True)
            p.wait("scalar", "tsem", t_b)
            p.emit("scalar", lambda e, j=j: e.copy(wnq_b[:, j * 512:(j + 1) * 512], pf[0][:, 0:512]))
            p.emit("scalar", lambda e, j=j: e.copy(wret_b[:, j * 512:(j + 1) * 512], pf[1][:, 0:512]))
            s_b = p.emit("scalar", lambda e, j=j: e.copy(wscb[:, j * 512:(j + 1) * 512], pf[2][:, 0:512]), inc="ssem")
            p.wait("tensor", "ssem", s_b)
        t_bs = mm(pf[3][:, 0:1], onf_sb[:], stats[0:1, 115:116], True, True)
        p.wait("scalar", "tsem", t_bs)
        s_bs = p.emit("scalar", lambda e: e.copy(bsig[:, 0:1], pf[3][:, 0:1]), inc="ssem")

        tp(pt[0][:, 0:128], tmA[:, 2 * E:2 * E + 128])
        tp(pt[0][:, 128:256], tmA[:, 2 * E + 128:2 * E + 256])
        t_w3 = tp(pt[0][:, 256:384], tmA[:, 2 * E + 256:2 * E + 384])
        p.wait("vector", "tsem", t_w3)
        p.wait("vector", "ssem", s_bs)
        p.emit("vector", lambda e: e.tensor_scalar_mul(wqT[:], pt[0][:, 0:128], bsig[:, 0:1]))
        p.emit("vector", lambda e: e.tensor_copy(wkT[:], pt[0][:, 128:256]))
        v_w3 = p.emit("vector", lambda e: e.tensor_copy(wvT[:], pt[0][:, 256:384]), inc="vsem")
        pt_free[0] = ("vsem", v_w3)
        pf_free[3] = ("ssem", s_bs)
        xb_free_t = p.c("tsem")

        pat_guard = [0, 0]
        for h in range(H):
            sl = h % 2
            poff = sl * 1024
            toff = 2048 + sl * 1024
            p.wait("gpsimd", "tsem", pat_guard[sl])
            g_pat = p.emit(
                "gpsimd",
                lambda e, h=h, poff=poff: e.dma_start(
                    fmA[:, poff:poff + 1024].rearrange("p (mt d) -> p mt d", d=128),
                    pats[h].rearrange("(mt p) d -> p mt d", p=128)),
                inc="patg", incby=16)
            p.wait("tensor", "patg", g_pat)
            for eg in range(2):
                bk = eg % 2
                sem, val = pt_free[bk]
                p.wait("tensor", sem, val)
                t_last = 0
                for k in range(8):
                    mt = eg * 8 + k
                    if mt >= 8:
                        break
                    t_last = tp(pt[bk][:, k * 128:(k + 1) * 128],
                                fmA[:, poff + mt * 128:poff + (mt + 1) * 128])
                if eg == 0:
                    p.wait("scalar", "tsem", t_last)
                    sv = p.emit("scalar",
                                lambda e, toff=toff, bk=bk: e.copy(
                                    fmA[:, toff:toff + 1024], pt[bk][:, 0:1024]),
                                inc="ssem")
                    pt_free[bk] = ("ssem", sv)
                    break
            s_patT = p.c("ssem")
            for bix in (0, 1):
                sem, val = pf_free[bix]
                p.wait("tensor", sem, val)
            p.wait("tensor", "ssem", s_patT)
            mm(pf[0][:, 0:512], wkT[:], fmA[:, toff:toff + 512], True, True)
            t_k = mm(pf[1][:, 0:512], wkT[:], fmA[:, toff + 512:toff + 1024], True, True)
            p.wait("vector", "tsem", t_k)
            p.drain("vector")
            p.emit("vector", lambda e: e.reduce_sum(stats[:, 0:1], pf[0][:, 0:512], axis=AX))
            p.emit("vector", lambda e: e.reduce_sum(stats[:, 1:2], pf[1][:, 0:512], axis=AX))
            p.drain("vector")
            p.emit("vector", lambda e: e.tensor_add(stats[:, 2:3], stats[:, 0:1], stats[:, 1:2]))
            p.drain("vector")
            p.emit("vector", lambda e: e.tensor_scalar_mul(stats[:, 3:4], stats[:, 2:3], 1.0 / M))
            p.drain("vector")
            p.emit("vector", lambda e: e.tensor_scalar(
                arena[:, 0:512], pf[0][:, 0:512], stats[:, 3:4], None, ALU.subtract), inc="vsem")
            v_k = p.emit("vector", lambda e: e.tensor_scalar(
                arena[:, 512:1024], pf[1][:, 0:512], stats[:, 3:4], None, ALU.subtract), inc="vsem")
            pf_free[0] = ("vsem", v_k)
            pf_free[1] = ("vsem", v_k)
            p.wait("gpsimd", "vsem", v_k)
            g_k = p.emit("gpsimd", lambda e, h=h: e.dma_start(kT[h], arena[:, 0:1024]),
                         inc="gkv", incby=16)
            p.drain("vector")
            vz = p.emit("vector", lambda e: e.memset(arena[:, 1024:1024 + NMT * VW], 0.0), inc="vsem")
            p.drain("vector")
            ones_ap = arena[:, 1024:1024 + NMT * VW].rearrange(
                "p (mt c) -> p mt c", c=VW)[:, :, 128:129]
            vm_last = p.emit("vector", lambda e, ones_ap=ones_ap: e.memset(ones_ap, 1.0), inc="vsem")
            p.wait("scalar", "vsem", vm_last)
            t_v = 0
            for mtq in range(0, 8, 4):
                for k in range(4):
                    mt = mtq + k
                    sem, val = pf_free[k]
                    p.wait("tensor", sem, val)
                    t_v = mm(pf[k][:, 0:128], fmA[:, toff + mt * 128:toff + (mt + 1) * 128],
                             wvT[:], True, True)
                p.wait("scalar", "tsem", t_v)
                s_v = 0
                for k in range(4):
                    mt = mtq + k
                    s_v = p.emit("scalar", lambda e, mt=mt, k=k: e.copy(
                        arena[:, 1024 + mt * VW:1024 + mt * VW + 128], pf[k][:, 0:128]), inc="ssem")
                for k in range(4):
                    pf_free[k] = ("ssem", s_v)
            p.wait("gpsimd", "ssem", p.c("ssem"))
            g_v = p.emit("gpsimd", lambda e, h=h: e.dma_start(vT[h], arena[:, 1024:1024 + NMT * VW]),
                         inc="gkv", incby=16)
            pat_guard[sl] = p.c("tsem")
            p.wait("vector", "gkv", p.c("gkv"))
            p.wait("scalar", "gkv", p.c("gkv"))

        gkv_all = p.c("gkv")
        p.wait("vector", "gkv", gkv_all)
        p.wait("sync", "tsem", xb_free_t)
        p.wait("sync", "vsem", v_prep0)
        wT_off = [12288, 4096]
        cast_guard = [0, 0]
        write_guard = [0, 0]
        stage_guard = [("vsem", 0), ("vsem", 0)]

        def prep_wT(w_src, dst, nib):
            for ib in range(nib):
                sset = ib % 2
                base = 0 if sset == 0 else 8192
                soff = sset * 4096
                woff = wT_off[sset]
                wT_view = arena[:, woff:woff + 4096].rearrange(
                    "p (g two c) -> p g two c", two=2, c=128)
                csem = "castg" if sset == 0 else "castg1"
                sem, val = stage_guard[sset]
                p.wait("sync", sem, val)
                p.emit(
                    "sync",
                    lambda e, ib=ib, soff=soff, w=w_src: e.dma_start(
                        x_b[:, soff:soff + 4096].rearrange("p (il c) -> p il c", c=2048),
                        w[ib * 256:(ib + 1) * 256, :].rearrange("(il p) c -> p il c", p=128)),
                    inc=csem, incby=16)
                g_c = p.c(csem)
                p.wait("vector", csem, g_c)
                p.wait("vector", "tsem", cast_guard[sset])
                v_c = p.emit("vector", lambda e, base=base, soff=soff: e.tensor_copy(
                    arena[:, base:base + 4096], x_b[:, soff:soff + 4096]), inc="vsem")
                stage_guard[sset] = ("vsem", v_c)
                p.wait("tensor", "vsem", v_c)
                for il in range(2):
                    coff = base + il * 2048
                    for eg in range(2):
                        bk = eg % 2
                        sem, val = pt_free[bk]
                        p.wait("tensor", sem, val)
                        t_last = 0
                        for k in range(8):
                            et = eg * 8 + k
                            t_last = tp(pt[bk][:, k * 128:(k + 1) * 128],
                                        arena[:, coff + et * 128:coff + (et + 1) * 128])
                        p.wait("scalar", "tsem", t_last)
                        p.wait("scalar", "wprep" if sset == 0 else "wprep1", write_guard[sset])
                        s_t = p.emit(
                            "scalar",
                            lambda e, eg=eg, il=il, bk=bk, wT_view=wT_view: e.copy(
                                wT_view[:, eg * 8:(eg + 1) * 8, il, :], pt[bk][:, 0:1024]),
                            inc="ssem")
                        pt_free[bk] = ("ssem", s_t)
                cast_guard[sset] = p.c("tsem")
                p.wait("sync", "ssem", p.c("ssem"))
                write_guard[sset] = p.emit(
                    "sync",
                    lambda e, ib=ib, d=dst, woff=woff: e.dma_start(
                        d[ib].rearrange("et p c -> p et c"),
                        arena[:, woff:woff + 4096].rearrange("p (et c) -> p et c", c=256)),
                    inc="wprep" if sset == 0 else "wprep1", incby=16)

        prep_wT(w_up, upT, 32)
        prep_wT(w_gate, gateT, 32)
        prep_wT(w_in, winT, 8)

        p3_cast_guard = p.c("tsem")
        p3_z_guard = p.c("dnw")
        p.wait("vector", "wprep", p.c("wprep"))
        p.wait("vector", "wprep1", p.c("wprep1"))
        z1v = arena[:, 0:16384].rearrange("p (e a) -> p a e", a=64)
        zf4 = arena[:, 16384:32768].rearrange("p (a eh de) -> p eh de a", eh=128, de=2)
        eng_rr = [0]

        def evac(dst, src, twait, eng=None, dnw_guard=None):
            if eng is None:
                eng = "scalar" if eng_rr[0] % 2 == 0 else "vector"
                eng_rr[0] += 1
            p.wait(eng, "tsem", twait)
            if dnw_guard is not None:
                p.wait(eng, "dnw", dnw_guard)
            if eng == "scalar":
                return ("ssem", p.emit(
                    "scalar", lambda e, d=dst, s=src: e.copy(d, s), inc="ssem"))
            return ("vsem", p.emit(
                "vector", lambda e, d=dst, s=src: e.tensor_copy(d, s), inc="vsem"))

        def pe_wait_evacs():
            p.wait("tensor", "ssem", p.c("ssem"))
            p.wait("tensor", "vsem", p.c("vsem"))

        ld_tok = {}
        cast_tok = {}

        def p3_loads(eb, rng):
            for i in rng:
                j, h2, hh = (i >> 2) & 1, (i >> 1) & 1, i & 1
                s = i % 4
                sem, val = stage_guard[s] if i < 4 else cast_tok[(eb, i - 4)]
                p.wait("sync", sem, val)
                p.emit(
                    "sync",
                    lambda e, eb=eb, j=j, h2=h2, hh=hh, s=s: e.dma_start(
                        x_b[:, s * 2048:(s + 1) * 2048],
                        w_down[eb * 256 + j * 128:eb * 256 + (j + 1) * 128,
                               h2 * 4096 + hh * 2048:h2 * 4096 + (hh + 1) * 2048]),
                    inc="castg", incby=16)
                ld_tok[(eb, i)] = ("castg", p.c("castg"))

        def p3_casts(eb, rng, guard):
            last = 0
            for i in rng:
                j, h2, hh = (i >> 2) & 1, (i >> 1) & 1, i & 1
                s = i % 4
                sem, val = ld_tok[(eb, i)]
                p.wait("vector", sem, val)
                p.wait("vector", "tsem", guard)
                natb = fmA if j == 0 else tmA
                last = p.emit(
                    "vector", lambda e, natb=natb, h2=h2, hh=hh, s=s: e.tensor_copy(
                        natb[:, h2 * 4096 + hh * 2048:h2 * 4096 + (hh + 1) * 2048],
                        x_b[:, s * 2048:(s + 1) * 2048]), inc="vsem")
                cast_tok[(eb, i)] = ("vsem", last)
                stage_guard[s] = ("vsem", last)
            return last

        stage_guard = [stage_guard[0], stage_guard[0],
                       stage_guard[1], stage_guard[1]]
        p3_loads(0, range(4))
        p3_casts(0, range(4), p3_cast_guard)
        p3_loads(0, range(4, 8))
        nat_ready = p3_casts(0, range(4, 8), p3_cast_guard)
        for eb in range(8):
            if eb < 7:
                p3_loads(eb + 1, range(4))
            p.wait("tensor", "vsem", nat_ready)
            r0_tok = [None] * 16
            r1_last = [0]

            def emit_r0(tg):
                bk = tg % 2
                sem, val = pt_free[bk]
                p.wait("tensor", sem, val)
                t_last = 0
                for k in range(8):
                    tix = tg * 8 + k
                    it, j = tix // 2, tix % 2
                    natb = fmA if j == 0 else tmA
                    t_last = tp(pt[bk][:, k * 128:(k + 1) * 128],
                                natb[:, it * 128:(it + 1) * 128])
                tok = evac(arena[:, 16384 + tg * 1024:16384 + (tg + 1) * 1024],
                           pt[bk][:, 0:1024], t_last, eng="scalar",
                           dnw_guard=p3_z_guard)
                pt_free[bk] = tok
                r0_tok[tg] = tok

            def emit_r1(l):
                bix = l % 4
                sem, val = pf_free[bix]
                p.wait("tensor", sem, val)
                need = r0_tok[(2 * l + 1) // 4]
                p.wait("tensor", need[0], need[1])
                t_s = mm(pf[bix][:, 0:512], h_sb[:],
                         arena[:, 16384 + l * 512:16384 + (l + 1) * 512], True, True)
                pf_free[bix] = evac(z1v[:, 2 * l:2 * l + 2, :], pf[bix][:, 0:512], t_s)

            for tg in range(16):
                emit_r0(tg)
                if tg >= 1:
                    emit_r1(2 * (tg - 1))
                    emit_r1(2 * (tg - 1) + 1)
            cast_guard_next = p.c("tsem")
            emit_r1(30)
            emit_r1(31)
            pe_wait_evacs()
            r2_tok = [None] * 16

            def emit_r2(g):
                bk = g % 2
                sem, val = pt_free[bk]
                p.wait("tensor", sem, val)
                t_last = 0
                for k in range(8):
                    e0h = g * 8 + k
                    t_last = tp(pt[bk][:, k * 128:(k + 1) * 128],
                                arena[:, e0h * 128:(e0h + 1) * 128])
                tok = evac(arena[:, 16384 + g * 1024:16384 + (g + 1) * 1024],
                           pt[bk][:, 0:1024], t_last, eng="scalar")
                pt_free[bk] = tok
                r2_tok[g] = tok

            def emit_r3(s):
                bix = s % 4
                sem, val = pf_free[bix]
                p.wait("tensor", sem, val)
                need = r2_tok[(4 * s + 3) // 8]
                p.wait("tensor", need[0], need[1])
                t_s = mm(pf[bix][:, 0:512], k2_sb[:],
                         arena[:, 16384 + s * 512:16384 + (s + 1) * 512], True, True)
                pf_free[bix] = evac(arena[:, s * 512:(s + 1) * 512],
                                    pf[bix][:, 0:512], t_s)

            for g in range(16):
                emit_r2(g)
                if g >= 1:
                    emit_r3(2 * (g - 1))
                    emit_r3(2 * (g - 1) + 1)
            emit_r3(30)
            emit_r3(31)
            pe_wait_evacs()
            if eb < 7:
                p3_casts(eb + 1, range(4), cast_guard_next)
                p3_loads(eb + 1, range(4, 8))
                nat_ready = p3_casts(eb + 1, range(4, 8), cast_guard_next)
            for g in range(16):
                bk = g % 2
                sem, val = pt_free[bk]
                p.wait("tensor", sem, val)
                t_last = 0
                for k in range(8):
                    e0h = g * 8 + k
                    t_last = tp(pt[bk][:, k * 128:(k + 1) * 128],
                                arena[:, e0h * 128:(e0h + 1) * 128])
                pt_free[bk] = evac(zf4[:, g * 8:(g + 1) * 8, :, :], pt[bk][:, 0:1024],
                                   t_last, eng="scalar")
            p3_cast_guard = cast_guard_next
            p.wait("sync", "ssem", p.c("ssem"))
            p.wait("sync", "vsem", p.c("vsem"))
            for grp in range(8):
                p.emit(
                    "sync",
                    lambda e, eb=eb, grp=grp: e.dma_start(
                        dnT[grp * 1024:(grp + 1) * 1024, eb * 256:(eb + 1) * 256]
                        .rearrange("(k p) c -> p k c", p=128),
                        arena[:, 16384 + grp * 2048:16384 + (grp + 1) * 2048]
                        .rearrange("p (k c) -> p k c", c=256)),
                    inc="dnw", incby=16)
            p3_z_guard = p.c("dnw")

        t_relb = p.release("tensor")
        s_relb = p.release("scalar")
        v_relb = p.release("vector")
        b_relb = p.release("gpsimd")
        pd = dict(wsem=p.c("wsem"), gsem=p.c("gsem"), tsem=t_relb, ssem=s_relb, vsem=v_relb,
                  bsem=b_relb,
                  wprep=p.c("wprep"), wprep1=p.c("wprep1"), castg=p.c("castg"), castg1=p.c("castg1"), dnw=p.c("dnw"),
                  patg=p.c("patg"), gkv=p.c("gkv"))
        for eng in ("sync", "tensor", "scalar", "vector", "gpsimd"):
            for s, v in pd.items():
                p.wait(eng, s, v)

        A_NQ = 0
        A_AF = 8192
        A_E = 16384
        A_KB = 24576
        A_VB = 26624

        pf_free = [("ssem", p.c("ssem"))] * 6
        pt_free = [("ssem", p.c("ssem"))] * 2
        wbuf_tok = [("tsem", p.c("tsem"))] * 4
        kv_tok = [("tsem", p.c("tsem"))] * 2
        e_tok = [p.c("tsem"), p.c("tsem")]

        def layer_norm(eps):
            v_in = p.c("vsem")
            p.wait("scalar", "vsem", v_in)
            eps_ap = stats[:, 121:122] if eps == LN_EPS else stats[:, 122:123]
            p.drain("vector")
            for st in range(NST):
                xs = x_b[:, st * E:(st + 1) * E]
                for j in range(4):
                    p.emit("vector", lambda e, xs=xs, st=st, j=j: e.bn_stats(
                        stats[:, st * 24 + j * 6:st * 24 + (j + 1) * 6],
                        xs[:, j * 512:(j + 1) * 512]))
            p.drain("vector")
            v_mu = 0
            for st in range(NST):
                v_mu = p.emit("vector", lambda e, st=st: e.bn_aggr(
                    stats[:, 96 + 2 * st:98 + 2 * st], stats[:, st * 24:(st + 1) * 24]),
                    inc="vsem")
            p.wait("scalar", "vsem", v_mu)
            var_v = stats[:, 97:105].rearrange("p (a b) -> p a b", b=2)[:, :, 0:1]
            ln_v = stats[:, 104:108].rearrange("p (a b) -> p a b", b=1)
            rs_v = stats[:, 108:112].rearrange("p (a b) -> p a b", b=1)
            p.drain("scalar")
            p.emit("scalar", lambda e, eps_ap=eps_ap: e.activation(ln_v, var_v, ACT.Ln, bias=eps_ap))
            p.drain("scalar")
            s_r = p.emit("scalar", lambda e: e.activation(
                rs_v, ln_v, ACT.Exp, scale=stats[:, 120:121]), inc="ssem")
            p.wait("vector", "ssem", s_r)
            v_a = 0
            for st in range(NST):
                xs = x_b[:, st * E:(st + 1) * E]
                v_a = p.emit("vector", lambda e, xs=xs, st=st: e.tensor_scalar(
                    tmA[:, st * E:(st + 1) * E], xs, stats[:, 96 + 2 * st:97 + 2 * st],
                    stats[:, 108 + st:109 + st], ALU.subtract, ALU.mult), inc="vsem")
            return v_a

        def transpose_tm_to_fm(src_off, scaled, v_ready):
            p.wait("tensor", "vsem", v_ready)
            s_last = 0
            for et in range(NET):
                bk = et % 2
                sem, val = pt_free[bk]
                p.wait("tensor", sem, val)
                t_last = 0
                for st in range(NST):
                    src = tmA if src_off is None else arena
                    off = (0 if src_off is None else src_off) + st * E + et * 128
                    t_last = tp(pt[bk][:, st * 128:(st + 1) * 128], src[:, off:off + 128])
                p.wait("scalar", "tsem", t_last)
                if scaled:
                    s_last = p.emit("scalar", lambda e, et=et, bk=bk: e.activation(
                        fmA[:, et * C:(et + 1) * C], pt[bk][:, 0:C],
                        ACT.Copy, scale=su_sb[:, et:et + 1]), inc="ssem")
                else:
                    s_last = p.emit("scalar", lambda e, et=et, bk=bk: e.copy(
                        fmA[:, et * C:(et + 1) * C], pt[bk][:, 0:C]), inc="ssem")
                pt_free[bk] = ("ssem", s_last)
            return s_last

        def fwht2048_fm():
            p.wait("tensor", "ssem", p.c("ssem"))
            for et in range(NET):
                bix = et % 2
                sem, val = pf_free[bix]
                p.wait("tensor", sem, val)
                t_s = mm(pf[bix][:, 0:C], h_sb[:], fmA[:, et * C:(et + 1) * C], True, True)
                p.wait("scalar", "tsem", t_s)
                s_s = p.emit("scalar", lambda e, et=et, bix=bix: e.copy(
                    fmA[:, et * C:(et + 1) * C], pf[bix][:, 0:C]), inc="ssem")
                pf_free[bix] = ("ssem", s_s)
            p.wait("vector", "ssem", p.c("ssem"))
            p.wait("vector", "tsem", p.c("tsem"))
            ready = [0] * NET
            for st in range(4):
                src_fm = (st % 2 == 0)
                for u in range(NET):
                    v = u ^ (1 << st)
                    if v < u:
                        continue
                    if src_fm:
                        iu, iv = fmA[:, u * C:(u + 1) * C], fmA[:, v * C:(v + 1) * C]
                        ou, ov = arena[:, u * C:(u + 1) * C], arena[:, v * C:(v + 1) * C]
                    else:
                        iu, iv = arena[:, u * C:(u + 1) * C], arena[:, v * C:(v + 1) * C]
                        ou, ov = fmA[:, u * C:(u + 1) * C], fmA[:, v * C:(v + 1) * C]
                    p.emit("vector", lambda e, iu=iu, iv=iv, ou=ou: e.tensor_add(ou, iu, iv))
                    last = (st == 3)
                    vv = p.emit("vector", lambda e, iu=iu, iv=iv, ov=ov: e.tensor_sub(ov, iu, iv),
                                inc="vsem" if last else None)
                    if last:
                        ready[u] = vv
                        ready[v] = vv
                p.drain("vector")
            return ready

        def ffn(fm_ready):
            korder = sorted(range(NET), key=lambda k: fm_ready[k])
            wb = [wbuf[:, k * 4096:(k + 1) * 4096] for k in range(4)]
            last_v_g = [p.c("vsem")]
            for ib in range(32):
                ub, gb = wb[(ib % 2) * 2], wb[(ib % 2) * 2 + 1]
                for m_i, (buf, srcT) in enumerate(((ub, upT), (gb, gateT))):
                    slot = (ib % 2) * 2 + m_i
                    sem, val = wbuf_tok[slot]
                    p.wait("sync", sem, val)
                    p.emit("sync", lambda e, ib=ib, buf=buf, srcT=srcT: e.dma_start(
                        buf.rearrange("p (et c) -> p et c", c=256),
                        srcT[ib].rearrange("et p c -> p et c")), inc=f"sl{slot}", incby=16)
                    p.wait("tensor", f"sl{slot}", p.c(f"sl{slot}"))
                for il in range(2):
                    it = ib * 2 + il
                    pu_i, pg_i = (it % 2) * 2, (it % 2) * 2 + 1
                    for bix in (pu_i, pg_i):
                        sem, val = pf_free[bix]
                        p.wait("tensor", sem, val)
                    t_u = 0
                    for ki, et in enumerate(korder):
                        p.wait("tensor", "vsem", fm_ready[et])
                        t_u = mm(pf[pu_i][:, 0:C], ub[:, et * 256 + il * 128:et * 256 + (il + 1) * 128],
                                 fmA[:, et * C:(et + 1) * C], ki == 0, ki == NET - 1)
                    t_g = 0
                    for ki, et in enumerate(korder):
                        t_g = mm(pf[pg_i][:, 0:C], gb[:, et * 256 + il * 128:et * 256 + (il + 1) * 128],
                                 fmA[:, et * C:(et + 1) * C], ki == 0, ki == NET - 1)
                    if il == 1:
                        wbuf_tok[(ib % 2) * 2] = ("tsem", t_g)
                        wbuf_tok[(ib % 2) * 2 + 1] = ("tsem", t_g)
                    p.wait("scalar", "tsem", t_g)
                    p.wait("scalar", "vsem", last_v_g[0])
                    s_si = p.emit("scalar", lambda e, it=it, pg_i=pg_i: e.activation(
                        silu_t[:, 0:C], pf[pg_i][:, 0:C], ACT.Silu, scale=wscg_sb[:, it:it + 1]),
                        inc="ssem")
                    p.wait("vector", "ssem", s_si)
                    p.wait("vector", "tsem", t_u)
                    v_g = p.emit("vector", lambda e, it=it, pu_i=pu_i: e.scalar_tensor_tensor(
                        arena[:, it * C:(it + 1) * C], pf[pu_i][:, 0:C], wscu_su[:, it:it + 1],
                        silu_t[:, 0:C], ALU.mult, ALU.mult), inc="vsem")
                    pf_free[pu_i] = ("vsem", v_g)
                    pf_free[pg_i] = ("ssem", s_si)
                    last_v_g[0] = v_g
            v_g_all = p.c("vsem")
            db = [wbuf[:, k * 4096:(k + 1) * 4096] for k in range(4)]
            p.wait("tensor", "vsem", v_g_all)
            v_d = 0
            for ec in range(4):
                for st in range(NST):
                    sem, val = pf_free[st]
                    p.wait("tensor", sem, val)
                t_d = 0
                for kb in range(8):
                    bi = kb % 4
                    sem, val = wbuf_tok[bi]
                    p.wait("sync", sem, val)
                    wv = p.emit("sync", lambda e, ec=ec, kb=kb, b=db[bi]: e.dma_start(
                        b.rearrange("p (k c) -> p k c", c=512),
                        dnT[kb * 1024:(kb + 1) * 1024, ec * 512:(ec + 1) * 512]
                        .rearrange("(k p) c -> p k c", p=128)), inc=f"sl{bi}", incby=16)
                    p.wait("tensor", f"sl{bi}", wv)
                    for st in range(NST):
                        for k in range(8):
                            it = kb * 8 + k
                            t_d = mm(pf[st][:, 0:512],
                                     arena[:, it * C + st * 128:it * C + (st + 1) * 128],
                                     db[bi][:, k * 512:(k + 1) * 512],
                                     it == 0, it == NIT - 1)
                    wbuf_tok[bi] = ("tsem", t_d)
                p.wait("vector", "tsem", t_d)
                for st in range(NST):
                    p.drain("vector")
                    p.emit("vector", lambda e, st=st, ec=ec: e.tensor_mul(
                        sq[:, 0:512], pf[st][:, 0:512], wscb[:, ec * 512:(ec + 1) * 512]))
                    p.drain("vector")
                    v_d = p.emit("vector", lambda e, st=st, ec=ec: e.tensor_add(
                        x_b[:, st * E + ec * 512:st * E + (ec + 1) * 512],
                        x_b[:, st * E + ec * 512:st * E + (ec + 1) * 512],
                        sq[:, 0:512]), inc="vsem")
                    pf_free[st] = ("vsem", v_d)
            p.drain("vector")
            return v_d

        def rms_apply(src_ap_fn, dst_off, w_bcast, v_ready, out_is_nq):
            p.wait("scalar", "vsem", v_ready)
            p.wait("vector", "vsem", v_ready)
            p.drain("vector")
            for st in range(NST):
                a_s = src_ap_fn(st)
                for j in range(4):
                    p.emit("vector", lambda e, a_s=a_s, st=st, j=j: e.bn_stats(
                        stats[:, st * 24 + j * 6:st * 24 + (j + 1) * 6],
                        a_s[:, j * 512:(j + 1) * 512]))
            p.drain("vector")
            for st in range(NST):
                p.emit("vector", lambda e, st=st: e.bn_aggr(
                    stats[:, 96 + 2 * st:98 + 2 * st], stats[:, st * 24:(st + 1) * 24]))
            p.drain("vector")
            mean_v = stats[:, 96:104].rearrange("p (a b) -> p a b", b=2)[:, :, 0:1]
            var_v = stats[:, 97:105].rearrange("p (a b) -> p a b", b=2)[:, :, 0:1]
            msq_v = stats[:, 104:108].rearrange("p (a b) -> p a b", b=1)
            rs_v = stats[:, 108:112].rearrange("p (a b) -> p a b", b=1)
            v_ss = p.emit("vector", lambda e: e.tensor_tensor(
                msq_v, mean_v, mean_v, ALU.mult), inc="vsem")
            p.drain("vector")
            v_ss = p.emit("vector", lambda e: e.tensor_tensor(
                msq_v, msq_v, var_v, ALU.add), inc="vsem")
            p.wait("scalar", "vsem", v_ss)
            p.drain("scalar")
            p.emit("scalar", lambda e: e.activation(msq_v, msq_v, ACT.Ln, bias=stats[:, 122:123]))
            p.drain("scalar")
            s_r = p.emit("scalar", lambda e: e.activation(
                rs_v, msq_v, ACT.Exp, scale=stats[:, 120:121]), inc="ssem")
            p.wait("vector", "ssem", s_r)
            last = 0
            for st in range(NST):
                a_s = src_ap_fn(st)
                if out_is_nq:
                    last = p.emit("vector", lambda e, a_s=a_s, st=st: e.scalar_tensor_tensor(
                        arena[:, dst_off + st * E:dst_off + (st + 1) * E], a_s,
                        stats[:, 108 + st:109 + st], w_bcast[:], ALU.mult, ALU.mult), inc="vsem")
                else:
                    p.drain("vector")
                    p.emit("vector", lambda e, a_s=a_s, st=st: e.scalar_tensor_tensor(
                        sq[:], a_s, stats[:, 108 + st:109 + st], w_bcast[:], ALU.mult, ALU.mult))
                    p.drain("vector")
                    last = p.emit("vector", lambda e, st=st: e.tensor_add(
                        x_b[:, st * E:(st + 1) * E], x_b[:, st * E:(st + 1) * E], sq[:]), inc="vsem")
            return last

        def attn(v_nq_ready):
            s_tr = transpose_tm_to_fm(A_NQ, False, v_nq_ready)
            wb = [wbuf[:, k * 4096:(k + 1) * 4096] for k in range(2)]
            p.wait("tensor", "ssem", s_tr)
            for eb in range(8):
                bi = eb % 2
                sem, val = wbuf_tok[bi]
                p.wait("sync", sem, val)
                p.emit("sync", lambda e, eb=eb, b=wb[bi]: e.dma_start(
                    b.rearrange("p (et c) -> p et c", c=256),
                    winT[eb].rearrange("et p c -> p et c")), inc=f"sl{bi}", incby=16)
                p.wait("tensor", f"sl{bi}", p.c(f"sl{bi}"))
                for il in range(2):
                    ep = eb * 2 + il
                    bix = 4 + (ep % 2)
                    sem, val = pf_free[bix]
                    p.wait("tensor", sem, val)
                    t_p = 0
                    for et in range(NET):
                        t_p = mm(pf[bix][:, 0:C], wb[bi][:, et * 256 + il * 128:et * 256 + (il + 1) * 128],
                                 fmA[:, et * C:(et + 1) * C], et == 0, et == NET - 1)
                    if il == 1:
                        wbuf_tok[bi] = ("tsem", t_p)
                    p.wait("scalar", "tsem", t_p)
                    s_p = p.emit("scalar", lambda e, ep=ep, bix=bix: e.copy(
                        arena[:, A_NQ + ep * C:A_NQ + (ep + 1) * C], pf[bix][:, 0:C]), inc="ssem")
                    pf_free[bix] = ("ssem", s_p)
            p.wait("tensor", "ssem", p.c("ssem"))
            for h in range(H):
                bix = 4 + (h % 2)
                sem, val = pf_free[bix]
                p.wait("tensor", sem, val)
                t_q = mm(pf[bix][:, 0:C], wqT[:], arena[:, A_NQ + h * C:A_NQ + (h + 1) * C],
                         True, True)
                p.wait("scalar", "tsem", t_q)
                s_q = p.emit("scalar", lambda e, h=h, bix=bix: e.copy(
                    fmA[:, h * C:(h + 1) * C], pf[bix][:, 0:C]), inc="ssem")
                pf_free[bix] = ("ssem", s_q)
            p.wait("tensor", "ssem", p.c("ssem"))
            kv_free = kv_tok
            e_guard = e_tok
            s_e_done = [0] * H
            v_o = 0

            def emit_scores(h):
                sl = h % 2
                koff = A_KB + sl * 1024
                voff = A_VB + sl * 1088
                sem, val = kv_free[sl]
                p.wait("sync", sem, val)
                p.emit("sync", lambda e, h=h, koff=koff: e.dma_start(
                    arena[:, koff:koff + 1024], kT[h]), inc=f"kv{sl}", incby=16)
                w_kv = p.emit("sync", lambda e, h=h, voff=voff: e.dma_start(
                    arena[:, voff:voff + 1088], vT[h]), inc=f"kv{sl}", incby=16)
                p.wait("tensor", f"kv{sl}", w_kv)
                eoff = A_E + sl * 4096
                p.wait("scalar", "tsem", e_guard[sl])
                for mtq in range(0, NMT, 4):
                    t_sc = 0
                    for k in range(4):
                        mt = mtq + k
                        sem, val = pf_free[k]
                        p.wait("tensor", sem, val)
                        t_sc = mm(pf[k][:, 0:C], arena[:, koff + mt * 128:koff + (mt + 1) * 128],
                                  fmA[:, h * C:(h + 1) * C], True, True)
                    p.wait("scalar", "tsem", t_sc)
                    s_e = 0
                    for k in range(4):
                        mt = mtq + k
                        s_e = p.emit("scalar", lambda e, mt=mt, k=k, eoff=eoff: e.activation(
                            arena[:, eoff + mt * C:eoff + (mt + 1) * C], pf[k][:, 0:C], ACT.Exp),
                            inc="ssem")
                    for k in range(4):
                        pf_free[k] = ("ssem", s_e)
                    if mtq == 0 and h > 0:
                        emit_out(h - 1)
                s_e_done[h] = p.c("ssem")

            def emit_out(h):
                nonlocal v_o
                sl = h % 2
                voff = A_VB + sl * 1088
                eoff = A_E + sl * 4096
                p.wait("tensor", "ssem", s_e_done[h])
                t_o = 0
                for st in range(NST):
                    bix = 4 + st // 2
                    coff = (st % 2) * 256
                    if st % 2 == 0:
                        sem, val = pf_free[bix]
                        p.wait("tensor", sem, val)
                    for mt in range(NMT):
                        t_o = mm(pf[bix][:, coff:coff + VW],
                                 arena[:, eoff + mt * C + st * 128:eoff + mt * C + (st + 1) * 128],
                                 arena[:, voff + mt * VW:voff + (mt + 1) * VW],
                                 mt == 0, mt == NMT - 1)
                p.wait("vector", "tsem", t_o)
                p.drain("vector")
                for st in range(NST):
                    bix = 4 + st // 2
                    coff = (st % 2) * 256
                    p.emit("vector", lambda e, st=st, bix=bix, coff=coff: e.reciprocal(
                        stats[:, 116 + st:117 + st], pf[bix][:, coff + 128:coff + 129]))
                p.drain("vector")
                for st in range(NST):
                    bix = 4 + st // 2
                    coff = (st % 2) * 256
                    v_o = p.emit("vector", lambda e, h=h, st=st, bix=bix, coff=coff: e.tensor_scalar(
                        arena[:, A_AF + st * E + h * 128:A_AF + st * E + (h + 1) * 128],
                        pf[bix][:, coff:coff + 128], stats[:, 116 + st:117 + st], None, ALU.mult), inc="vsem")
                for st in (0, 2):
                    pf_free[4 + st // 2] = ("vsem", v_o)
                kv_free[sl] = ("tsem", p.c("tsem"))
                e_guard[sl] = p.c("tsem")

            for h in range(H):
                emit_scores(h)
                if h > 0:
                    emit_out(h - 1)
            emit_out(H - 1)
            return rms_apply(lambda st: arena[:, A_AF + st * E:A_AF + (st + 1) * E],
                             0, wret_b, v_o, out_is_nq=False)

        out_writes = 0
        for ch in range(NCH):
            v_endprev = p.release("vector")
            s_endprev = p.release("scalar")
            p.wait("gpsimd", "osem", out_writes * 16)
            p.wait("gpsimd", "vsem", v_endprev)
            p.wait("gpsimd", "ssem", s_endprev)
            g_in = 0
            for st in range(NST):
                g_in = p.emit("gpsimd", lambda e, ch=ch, st=st: e.dma_start(
                    x_b[:, st * E:(st + 1) * E],
                    x_in[ch * C + st * 128:ch * C + (st + 1) * 128, :]), inc="gsem", incby=16)
            p.wait("vector", "gsem", g_in)
            p.wait("scalar", "gsem", g_in)
            p.wait("vector", "tsem", p.c("tsem"))

            v_ln = layer_norm(LN_EPS)
            transpose_tm_to_fm(None, True, v_ln)
            fm_ready = fwht2048_fm()
            ffn(fm_ready)

            v_ln2 = layer_norm(LN_EPS)
            v_y = 0
            for st in range(NST):
                p.drain("vector")
                v_y = p.emit("vector", lambda e, st=st: e.tensor_add(
                    x_b[:, st * E:(st + 1) * E], x_b[:, st * E:(st + 1) * E],
                    tmA[:, st * E:(st + 1) * E]), inc="vsem")
            p.wait("vector", "tsem", p.c("tsem"))
            rs_v = stats[:, 108:112].rearrange("p (a b) -> p a b", b=1)
            var_v = stats[:, 97:105].rearrange("p (a b) -> p a b", b=2)[:, :, 0:1]
            w1_v = stats[:, 64:68].rearrange("p (a b) -> p a b", b=1)
            u_v = stats[:, 68:72].rearrange("p (a b) -> p a b", b=1)
            p.drain("vector")
            p.emit("vector", lambda e: e.tensor_tensor(w1_v, rs_v, rs_v, ALU.mult))
            p.drain("vector")
            v_w1 = p.emit("vector", lambda e: e.tensor_tensor(w1_v, w1_v, var_v, ALU.mult), inc="vsem")
            p.wait("scalar", "vsem", v_w1)
            p.drain("scalar")
            p.emit("scalar", lambda e: e.activation(u_v, w1_v, ACT.Ln, bias=stats[:, 122:123]))
            p.drain("scalar")
            s_u = p.emit("scalar", lambda e: e.activation(
                u_v, u_v, ACT.Exp, scale=stats[:, 120:121]), inc="ssem")
            p.wait("vector", "ssem", s_u)
            p.wait("vector", "vsem", v_y)
            v_nq = 0
            for st in range(NST):
                v_nq = p.emit("vector", lambda e, st=st: e.scalar_tensor_tensor(
                    arena[:, A_NQ + st * E:A_NQ + (st + 1) * E], tmA[:, st * E:(st + 1) * E],
                    stats[:, 68 + st:69 + st], wnq_b[:], ALU.mult, ALU.mult), inc="vsem")
            attn(v_nq)

            v_ln3 = layer_norm(LN_EPS)
            transpose_tm_to_fm(None, True, v_ln3)
            fm_ready3 = fwht2048_fm()
            v_ffn3 = ffn(fm_ready3)

            p.wait("gpsimd", "vsem", v_ffn3)
            for st in range(NST):
                p.emit("gpsimd", lambda e, ch=ch, st=st: e.dma_start(
                    y_out[ch * C + st * 128:ch * C + (st + 1) * 128, :],
                    x_b[:, st * E:(st + 1) * E]), inc="osem", incby=16)
                out_writes += 1

        total_out = out_writes * 16

        with nc.Block() as block:
            @block.sync
            def _(eng):
                for fn in p.plan["sync"]:
                    fn(eng)

            @block.tensor
            def _(eng):
                for fn in p.plan["tensor"]:
                    fn(eng)

            @block.scalar
            def _(eng):
                for fn in p.plan["scalar"]:
                    fn(eng)

            @block.vector
            def _(eng):
                for fn in p.plan["vector"]:
                    fn(eng)

            @block.gpsimd
            def _(eng):
                for fn in p.plan["gpsimd"]:
                    fn(eng)
                eng.wait_ge(p.sems["osem"], total_out)

    return nc


_NC_CACHE = None


def kernel(L0_attn_query_input, L0_attn_W_in, L0_attn_W_q, L0_attn_W_k, L0_attn_W_v,
           L0_attn_stored_patterns, L0_attn_norm_query_w, L0_attn_norm_retrieved_w,
           L0_attn_beta, L0_ffn_SU_ug, L0_ffn_SU_dp, L0_ffn_Wscale_ug, L0_ffn_Wscale_dp,
           L0_ffn_W_up, L0_ffn_W_gate, L0_ffn_W_down):
    global _NC_CACHE
    x = np.ascontiguousarray(np.asarray(L0_attn_query_input, dtype=np.float32).reshape(16384, E))
    consts = dict(
        h128=np.ascontiguousarray(_hadamard(128).astype(BFNP)),
        k2h64=np.ascontiguousarray(np.kron(np.eye(2), _hadamard(64)).astype(BFNP)),
        ident=np.ascontiguousarray(np.eye(128, dtype=np.float32).astype(BFNP)),
        ones1=np.ones((1, 128), dtype=BFNP),
        ones1f=np.ones((1, 128), dtype=np.float32),
    )
    weights = dict(
        w_in=np.ascontiguousarray(np.asarray(L0_attn_W_in, dtype=np.float32)),
        w_q=np.ascontiguousarray(np.asarray(L0_attn_W_q, dtype=np.float32)),
        w_k=np.ascontiguousarray(np.asarray(L0_attn_W_k, dtype=np.float32)),
        w_v=np.ascontiguousarray(np.asarray(L0_attn_W_v, dtype=np.float32)),
        pats=np.ascontiguousarray(np.asarray(L0_attn_stored_patterns, dtype=np.float32)),
        nqw=np.ascontiguousarray(np.asarray(L0_attn_norm_query_w, dtype=np.float32)),
        nrw=np.ascontiguousarray(np.asarray(L0_attn_norm_retrieved_w, dtype=np.float32)),
        beta=np.asarray(L0_attn_beta, dtype=np.float32).reshape(1),
        su_ug=np.ascontiguousarray(np.asarray(L0_ffn_SU_ug, dtype=np.float32)),
        su_dp=np.ascontiguousarray(np.asarray(L0_ffn_SU_dp, dtype=np.float32)),
        wsc_ug=np.ascontiguousarray(np.asarray(L0_ffn_Wscale_ug, dtype=np.float32)),
        wsc_dp=np.ascontiguousarray(np.asarray(L0_ffn_Wscale_dp, dtype=np.float32)),
        w_up=np.ascontiguousarray(np.asarray(L0_ffn_W_up, dtype=np.float32)),
        w_gate=np.ascontiguousarray(np.asarray(L0_ffn_W_gate, dtype=np.float32)),
        w_down=np.ascontiguousarray(np.asarray(L0_ffn_W_down, dtype=np.float32)),
    )
    if _NC_CACHE is None:
        _NC_CACHE = build_nc()
    nc = _NC_CACHE
    in_maps = []
    for c in range(NCORES):
        m = {"x": np.ascontiguousarray(x[c * TCORE:(c + 1) * TCORE])}
        m.update(weights)
        m.update(consts)
        in_maps.append(m)
    res = run_bass_kernel_spmd(nc, in_maps, core_ids=list(range(NCORES)))
    out = np.concatenate([res.results[c]["y"] for c in range(NCORES)], axis=0)
    return np.ascontiguousarray(out.reshape(1, 16384, E).astype(np.float32))


if __name__ == "__main__":
    build_nc()
    print("built ok")


# version 16
# speedup vs baseline: 1.1224x; 1.0052x over previous
import numpy as np
import ml_dtypes
from contextlib import ExitStack

import concourse.bass as bass
import concourse.mybir as mybir
from concourse.bass_utils import run_bass_kernel_spmd

BFNP = ml_dtypes.bfloat16
F32 = mybir.dt.float32
BF16 = mybir.dt.bfloat16
AX = mybir.AxisListType.X
ALU = mybir.AluOpType
ACT = mybir.ActivationFunctionType

NCORES = 8
TCORE = 2048
C = 512
NCH = TCORE // C
NST = C // 128
E = 2048
I = 8192
NET = E // 128
NIT = I // 128
H = 16
D = 128
M = 1024
NMT = M // 128
VW = 136
RS2048 = float(1.0 / np.sqrt(2048.0))
RS8192 = float(1.0 / np.sqrt(8192.0))
LN_EPS = 1e-5
RMS_EPS = 1e-6

ENG_SEM = {"tensor": "tsem", "scalar": "ssem", "vector": "vsem", "gpsimd": "bsem"}


def _hadamard(n):
    h = np.array([[1.0]], dtype=np.float64)
    while h.shape[0] < n:
        h = np.block([[h, h], [h, -h]])
    return h


class P:

    def __init__(self):
        self.plan = {k: [] for k in ("sync", "tensor", "scalar", "vector", "gpsimd")}
        self.cnt = {}
        self.lastw = {}
        self.sems = {}

    def emit(self, eng, build, inc=None, incby=1):
        if inc is not None:
            self.cnt[inc] = self.cnt.get(inc, 0) + incby

            def fn(e, b=build, s=inc, n=incby):
                b(e).then_inc(self.sems[s], n)
        else:
            def fn(e, b=build):
                b(e)
        self.plan[eng].append(fn)
        return self.cnt.get(inc, 0)

    def wait(self, eng, sem, val):
        if not val or val <= 0:
            return
        key = (eng, sem)
        if self.lastw.get(key, 0) >= val:
            return
        self.lastw[key] = val
        self.plan[eng].append(lambda e, s=sem, v=val: e.wait_ge(self.sems[s], v))

    def drain(self, eng):
        self.plan[eng].append(lambda e: e.drain())

    def release(self, eng):
        sem = ENG_SEM[eng]
        return self.emit(eng, lambda e: e.drain(), inc=sem)

    def c(self, sem):
        return self.cnt.get(sem, 0)


def build_nc():
    nc = bass.Bass("TRN2", target_bir_lowering=False)
    p = P()

    x_in = nc.dram_tensor("x", [TCORE, E], F32, kind="ExternalInput").ap()
    w_in = nc.dram_tensor("w_in", [E, E], F32, kind="ExternalInput").ap()
    w_q = nc.dram_tensor("w_q", [D, D], F32, kind="ExternalInput").ap()
    w_k = nc.dram_tensor("w_k", [D, D], F32, kind="ExternalInput").ap()
    w_v = nc.dram_tensor("w_v", [D, D], F32, kind="ExternalInput").ap()
    pats = nc.dram_tensor("pats", [H, M, D], F32, kind="ExternalInput").ap()
    nqw = nc.dram_tensor("nqw", [E], F32, kind="ExternalInput").ap()
    nrw = nc.dram_tensor("nrw", [E], F32, kind="ExternalInput").ap()
    beta = nc.dram_tensor("beta", [1], F32, kind="ExternalInput").ap()
    su_ug = nc.dram_tensor("su_ug", [E], F32, kind="ExternalInput").ap()
    su_dp = nc.dram_tensor("su_dp", [I], F32, kind="ExternalInput").ap()
    wsc_ug = nc.dram_tensor("wsc_ug", [2 * I], F32, kind="ExternalInput").ap()
    wsc_dp = nc.dram_tensor("wsc_dp", [E], F32, kind="ExternalInput").ap()
    w_up = nc.dram_tensor("w_up", [I, E], F32, kind="ExternalInput").ap()
    w_gate = nc.dram_tensor("w_gate", [I, E], F32, kind="ExternalInput").ap()
    w_down = nc.dram_tensor("w_down", [E, I], F32, kind="ExternalInput").ap()
    h128 = nc.dram_tensor("h128", [128, 128], BF16, kind="ExternalInput").ap()
    k2h64 = nc.dram_tensor("k2h64", [128, 128], BF16, kind="ExternalInput").ap()
    ident = nc.dram_tensor("ident", [128, 128], BF16, kind="ExternalInput").ap()
    identf = nc.dram_tensor("identf", [128, 128], F32, kind="ExternalInput").ap()
    ones1 = nc.dram_tensor("ones1", [1, 128], BF16, kind="ExternalInput").ap()
    ones1f = nc.dram_tensor("ones1f", [1, 128], F32, kind="ExternalInput").ap()

    y_out = nc.dram_tensor("y", [TCORE, E], F32, kind="ExternalOutput").ap()

    upT = nc.dram_tensor("upT", [32, NET, 128, 256], BF16, kind="Internal").ap()
    gateT = nc.dram_tensor("gateT", [32, NET, 128, 256], BF16, kind="Internal").ap()
    winT = nc.dram_tensor("winT", [8, NET, 128, 256], BF16, kind="Internal").ap()
    dnT = nc.dram_tensor("dnT", [I, E], BF16, kind="Internal").ap()
    kT = nc.dram_tensor("kT", [H, 128, M], BF16, kind="Internal").ap()
    vT = nc.dram_tensor("vT", [H, 128, NMT * VW], BF16, kind="Internal").ap()

    with ExitStack() as ctx:
        for s in ("wsem", "gsem", "tsem", "ssem", "vsem", "osem", "bsem",
                  "wprep", "wprep1", "castg", "castg1", "dnw", "patg", "gkv",
                  "sl0", "sl1", "sl2", "sl3", "kv0", "kv1"):
            p.sems[s] = ctx.enter_context(nc.semaphore(s))

        x_b = ctx.enter_context(nc.sbuf_tensor("x_b", [128, NST * E], F32))
        tmA = ctx.enter_context(nc.sbuf_tensor("tmA", [128, NST * E], BF16))
        fmA = ctx.enter_context(nc.sbuf_tensor("fmA", [128, NET * C], BF16))
        fwtmp = ctx.enter_context(nc.sbuf_tensor("fwtmp", [128, C], BF16))
        arena = ctx.enter_context(nc.sbuf_tensor("arena", [128, NIT * C], BF16))
        wbuf = ctx.enter_context(nc.sbuf_tensor("wbuf", [128, 4 * 4096], BF16))
        sq = ctx.enter_context(nc.sbuf_tensor("sq", [128, E], F32))
        silu_t = ctx.enter_context(nc.sbuf_tensor("silu_t", [128, C], BF16))
        stats = ctx.enter_context(nc.sbuf_tensor("stats", [128, 128], F32))
        h_sb = ctx.enter_context(nc.sbuf_tensor("h_sb", [128, 128], BF16))
        k2_sb = ctx.enter_context(nc.sbuf_tensor("k2_sb", [128, 128], BF16))
        id_sb = ctx.enter_context(nc.sbuf_tensor("id_sb", [128, 128], BF16))
        idf_sb = ctx.enter_context(nc.sbuf_tensor("idf_sb", [128, 128], F32))
        on_sb = ctx.enter_context(nc.sbuf_tensor("on_sb", [1, 128], BF16))
        onf_sb = ctx.enter_context(nc.sbuf_tensor("onf_sb", [1, 128], F32))
        su_sb = ctx.enter_context(nc.sbuf_tensor("su_sb", [128, NET], F32))
        wscu_su = ctx.enter_context(nc.sbuf_tensor("wscu_su", [128, NIT], F32))
        wscg_sb = ctx.enter_context(nc.sbuf_tensor("wscg_sb", [128, NIT], F32))
        wscb = ctx.enter_context(nc.sbuf_tensor("wscb", [128, E], F32))
        wnq_b = ctx.enter_context(nc.sbuf_tensor("wnq_b", [128, E], BF16))
        wret_b = ctx.enter_context(nc.sbuf_tensor("wret_b", [128, E], BF16))
        wqT = ctx.enter_context(nc.sbuf_tensor("wqT", [128, 128], BF16))
        wkT = ctx.enter_context(nc.sbuf_tensor("wkT", [128, 128], BF16))
        wvT = ctx.enter_context(nc.sbuf_tensor("wvT", [128, 128], BF16))
        bsig = ctx.enter_context(nc.sbuf_tensor("bsig", [128, 4], F32))

        pf = [ctx.enter_context(nc.psum_tensor(f"pf{i}", [128, 512], F32)) for i in range(6)]
        pt = [ctx.enter_context(nc.psum_tensor(f"pt{i}", [128, 1024], BF16)) for i in range(2)]

        pf_free = [("ssem", 0)] * 6
        pt_free = [("ssem", 0)] * 2

        def mm(out_ap, lhsT_ap, rhs_ap, start, stop):
            return p.emit(
                "tensor",
                lambda e, o=out_ap, l=lhsT_ap, r=rhs_ap, st=start, sp=stop:
                    e.matmul(o, l, r, start=st, stop=sp),
                inc="tsem")

        def tp(out_ap, in_ap):
            return p.emit(
                "tensor",
                lambda e, o=out_ap, i_=in_ap: e.transpose(o, i_, id_sb[:]),
                inc="tsem")

        def tpf(out_ap, in_ap):
            return p.emit(
                "tensor",
                lambda e, o=out_ap, i_=in_ap: e.transpose(o, i_, idf_sb[:]),
                inc="tsem")

        eng_rr = [0]

        def evac(dst, src, twait, eng=None, extra=None):
            if eng is None:
                eng = "scalar" if eng_rr[0] % 2 == 0 else "vector"
                eng_rr[0] += 1
            p.wait(eng, "tsem", twait)
            if extra is not None:
                p.wait(eng, extra[0], extra[1])
            if eng == "scalar":
                return ("ssem", p.emit(
                    "scalar", lambda e, d=dst, s=src: e.copy(d, s), inc="ssem"))
            return ("vsem", p.emit(
                "vector", lambda e, d=dst, s=src: e.tensor_copy(d, s), inc="vsem"))

        def sload(dst, src, slow=False):
            return p.emit("sync", lambda e, d=dst, s=src, sl=slow: e.dma_start(
                d, s, allow_slow_non_contiguous=sl), inc="wsem", incby=16)

        sload(h_sb[:], h128)
        sload(k2_sb[:], k2h64)
        sload(id_sb[:], ident)
        sload(idf_sb[:], identf)
        sload(on_sb[:], ones1)
        sload(onf_sb[:], ones1f)
        sload(su_sb[:], su_ug.rearrange("(a b) -> b a", b=128), slow=True)
        sload(wscu_su[:], wsc_ug[0:I].rearrange("(a b) -> b a", b=128), slow=True)
        sload(wscg_sb[:], wsc_ug[I:2 * I].rearrange("(a b) -> b a", b=128), slow=True)
        sload(stats[0:1, 112:113], beta.rearrange("(a b) -> a b", b=1))
        sload(x_b[0:1, 0:E], nqw.rearrange("(a b) -> a b", a=1))
        sload(x_b[0:1, E:2 * E], nrw.rearrange("(a b) -> a b", a=1))
        sload(x_b[0:1, 2 * E:3 * E], wsc_dp.rearrange("(a b) -> a b", a=1))
        sload(sq[:, 1024:1024 + NIT], su_dp.rearrange("(a b) -> b a", b=128), slow=True)
        sload(sq[:, 1200:1328], w_q)
        sload(sq[:, 1328:1456], w_k)
        w_all = sload(sq[:, 1456:1584], w_v)

        p.emit("vector", lambda e: e.memset(stats[:, 120:121], -0.5))
        p.emit("vector", lambda e: e.memset(stats[:, 121:122], float(LN_EPS)))
        p.emit("vector", lambda e: e.memset(stats[:, 122:123], float(RMS_EPS)))
        p.emit("vector", lambda e: e.memset(stats[:, 123:124], -1.0))
        v_consts = p.release("vector")
        p.wait("vector", "wsem", w_all)
        p.emit("vector", lambda e: e.tensor_scalar_mul(su_sb[:], su_sb[:], RS2048))
        p.emit("vector", lambda e: e.tensor_mul(wscu_su[:], wscu_su[:], sq[:, 1024:1024 + NIT]))
        p.wait("scalar", "wsem", w_all)
        p.wait("scalar", "vsem", v_consts)
        p.emit("scalar", lambda e: e.activation(
            stats[0:1, 113:114], stats[0:1, 112:113], ACT.Exp, scale=stats[0:1, 123:124]))
        p.emit("scalar", lambda e: e.mul(x_b[0:1, 2 * E:3 * E], x_b[0:1, 2 * E:3 * E], 0.5 * RS8192))
        s_rows0 = p.release("scalar")
        p.wait("vector", "ssem", s_rows0)
        p.emit("vector", lambda e: e.tensor_scalar_add(stats[0:1, 114:115], stats[0:1, 113:114], 1.0))
        p.drain("vector")
        p.emit("vector", lambda e: e.reciprocal(stats[0:1, 115:116], stats[0:1, 114:115]))
        p.emit("vector", lambda e: e.tensor_copy(tmA[0:1, 0:E], x_b[0:1, 0:E]))
        p.emit("vector", lambda e: e.tensor_copy(tmA[:, 2 * E:2 * E + 128], sq[:, 1200:1328]))
        p.emit("vector", lambda e: e.tensor_copy(tmA[:, 2 * E + 128:2 * E + 256], sq[:, 1328:1456]))
        p.emit("vector", lambda e: e.tensor_copy(tmA[:, 2 * E + 256:2 * E + 384], sq[:, 1456:1584]))
        p.emit("vector", lambda e: e.tensor_copy(tmA[0:1, E:2 * E], x_b[0:1, E:2 * E]))
        v_prep0 = p.release("vector")

        p.wait("tensor", "vsem", v_prep0)
        p.wait("tensor", "ssem", s_rows0)
        for j in range(4):
            mm(pf[0][:, 0:512], on_sb[:], tmA[0:1, j * 512:(j + 1) * 512], True, True)
            mm(pf[1][:, 0:512], on_sb[:], tmA[0:1, E + j * 512:E + (j + 1) * 512], True, True)
            t_b = mm(pf[2][:, 0:512], onf_sb[:], x_b[0:1, 2 * E + j * 512:2 * E + (j + 1) * 512], True, True)
            p.wait("scalar", "tsem", t_b)
            p.emit("scalar", lambda e, j=j: e.copy(wnq_b[:, j * 512:(j + 1) * 512], pf[0][:, 0:512]))
            p.emit("scalar", lambda e, j=j: e.copy(wret_b[:, j * 512:(j + 1) * 512], pf[1][:, 0:512]))
            s_b = p.emit("scalar", lambda e, j=j: e.copy(wscb[:, j * 512:(j + 1) * 512], pf[2][:, 0:512]), inc="ssem")
            p.wait("tensor", "ssem", s_b)
        t_bs = mm(pf[3][:, 0:1], onf_sb[:], stats[0:1, 115:116], True, True)
        p.wait("scalar", "tsem", t_bs)
        s_bs = p.emit("scalar", lambda e: e.copy(bsig[:, 0:1], pf[3][:, 0:1]), inc="ssem")

        tp(pt[0][:, 0:128], tmA[:, 2 * E:2 * E + 128])
        tp(pt[0][:, 128:256], tmA[:, 2 * E + 128:2 * E + 256])
        t_w3 = tp(pt[0][:, 256:384], tmA[:, 2 * E + 256:2 * E + 384])
        p.wait("vector", "tsem", t_w3)
        p.wait("vector", "ssem", s_bs)
        p.emit("vector", lambda e: e.tensor_scalar_mul(wqT[:], pt[0][:, 0:128], bsig[:, 0:1]))
        p.emit("vector", lambda e: e.tensor_copy(wkT[:], pt[0][:, 128:256]))
        v_w3 = p.emit("vector", lambda e: e.tensor_copy(wvT[:], pt[0][:, 256:384]), inc="vsem")
        pt_free[0] = ("vsem", v_w3)
        pf_free[3] = ("ssem", s_bs)
        xb_free_t = p.c("tsem")

        pat_guard = [0, 0]
        for h in range(H):
            sl = h % 2
            poff = sl * 1024
            toff = 2048 + sl * 1024
            p.wait("gpsimd", "tsem", pat_guard[sl])
            g_pat = p.emit(
                "gpsimd",
                lambda e, h=h, poff=poff: e.dma_start(
                    fmA[:, poff:poff + 1024].rearrange("p (mt d) -> p mt d", d=128),
                    pats[h].rearrange("(mt p) d -> p mt d", p=128)),
                inc="patg", incby=16)
            p.wait("tensor", "patg", g_pat)
            for eg in range(2):
                bk = eg % 2
                sem, val = pt_free[bk]
                p.wait("tensor", sem, val)
                t_last = 0
                for k in range(8):
                    mt = eg * 8 + k
                    if mt >= 8:
                        break
                    t_last = tp(pt[bk][:, k * 128:(k + 1) * 128],
                                fmA[:, poff + mt * 128:poff + (mt + 1) * 128])
                if eg == 0:
                    p.wait("scalar", "tsem", t_last)
                    sv = p.emit("scalar",
                                lambda e, toff=toff, bk=bk: e.copy(
                                    fmA[:, toff:toff + 1024], pt[bk][:, 0:1024]),
                                inc="ssem")
                    pt_free[bk] = ("ssem", sv)
                    break
            s_patT = p.c("ssem")
            for bix in (0, 1):
                sem, val = pf_free[bix]
                p.wait("tensor", sem, val)
            p.wait("tensor", "ssem", s_patT)
            mm(pf[0][:, 0:512], wkT[:], fmA[:, toff:toff + 512], True, True)
            t_k = mm(pf[1][:, 0:512], wkT[:], fmA[:, toff + 512:toff + 1024], True, True)
            p.wait("vector", "tsem", t_k)
            p.drain("vector")
            p.emit("vector", lambda e: e.reduce_sum(stats[:, 0:1], pf[0][:, 0:512], axis=AX))
            p.emit("vector", lambda e: e.reduce_sum(stats[:, 1:2], pf[1][:, 0:512], axis=AX))
            p.drain("vector")
            p.emit("vector", lambda e: e.tensor_add(stats[:, 2:3], stats[:, 0:1], stats[:, 1:2]))
            p.drain("vector")
            p.emit("vector", lambda e: e.tensor_scalar_mul(stats[:, 3:4], stats[:, 2:3], 1.0 / M))
            p.drain("vector")
            p.emit("vector", lambda e: e.tensor_scalar(
                arena[:, 0:512], pf[0][:, 0:512], stats[:, 3:4], None, ALU.subtract), inc="vsem")
            v_k = p.emit("vector", lambda e: e.tensor_scalar(
                arena[:, 512:1024], pf[1][:, 0:512], stats[:, 3:4], None, ALU.subtract), inc="vsem")
            pf_free[0] = ("vsem", v_k)
            pf_free[1] = ("vsem", v_k)
            p.wait("gpsimd", "vsem", v_k)
            g_k = p.emit("gpsimd", lambda e, h=h: e.dma_start(kT[h], arena[:, 0:1024]),
                         inc="gkv", incby=16)
            p.drain("vector")
            vz = p.emit("vector", lambda e: e.memset(arena[:, 1024:1024 + NMT * VW], 0.0), inc="vsem")
            p.drain("vector")
            ones_ap = arena[:, 1024:1024 + NMT * VW].rearrange(
                "p (mt c) -> p mt c", c=VW)[:, :, 128:129]
            vm_last = p.emit("vector", lambda e, ones_ap=ones_ap: e.memset(ones_ap, 1.0), inc="vsem")
            p.wait("scalar", "vsem", vm_last)
            t_v = 0
            for mtq in range(0, 8, 4):
                for k in range(4):
                    mt = mtq + k
                    sem, val = pf_free[k]
                    p.wait("tensor", sem, val)
                    t_v = mm(pf[k][:, 0:128], fmA[:, toff + mt * 128:toff + (mt + 1) * 128],
                             wvT[:], True, True)
                p.wait("scalar", "tsem", t_v)
                s_v = 0
                for k in range(4):
                    mt = mtq + k
                    s_v = p.emit("scalar", lambda e, mt=mt, k=k: e.copy(
                        arena[:, 1024 + mt * VW:1024 + mt * VW + 128], pf[k][:, 0:128]), inc="ssem")
                for k in range(4):
                    pf_free[k] = ("ssem", s_v)
            p.wait("gpsimd", "ssem", p.c("ssem"))
            g_v = p.emit("gpsimd", lambda e, h=h: e.dma_start(vT[h], arena[:, 1024:1024 + NMT * VW]),
                         inc="gkv", incby=16)
            pat_guard[sl] = p.c("tsem")
            p.wait("vector", "gkv", p.c("gkv"))
            p.wait("scalar", "gkv", p.c("gkv"))

        gkv_all = p.c("gkv")
        p.wait("sync", "tsem", xb_free_t)
        p.wait("sync", "vsem", v_prep0)
        wT_off = [12288, 4096]
        write_guard = [0, 0]
        stage_guard = [("vsem", 0), ("vsem", 0)]

        def prep_wT(w_src, dst, nib):
            for ib in range(nib):
                sset = ib % 2
                soff = sset * 4096
                woff = wT_off[sset]
                wT_view = arena[:, woff:woff + 4096].rearrange(
                    "p (g two c) -> p g two c", two=2, c=128)
                csem = "castg" if sset == 0 else "castg1"
                wpsem = "wprep" if sset == 0 else "wprep1"
                sem, val = stage_guard[sset]
                p.wait("sync", sem, val)
                p.emit(
                    "sync",
                    lambda e, ib=ib, soff=soff, w=w_src: e.dma_start(
                        x_b[:, soff:soff + 4096].rearrange("p (il c) -> p il c", c=2048),
                        w[ib * 256:(ib + 1) * 256, :].rearrange("(il p) c -> p il c", p=128)),
                    inc=csem, incby=16)
                g_c = p.c(csem)
                p.wait("tensor", csem, g_c)
                for il in range(2):
                    for eg4 in range(4):
                        bix = eg4 % 4
                        sem2, val2 = pf_free[bix]
                        p.wait("tensor", sem2, val2)
                        t_last = 0
                        for k in range(4):
                            et = eg4 * 4 + k
                            t_last = tpf(pf[bix][:, k * 128:(k + 1) * 128],
                                         x_b[:, soff + il * 2048 + et * 128:
                                             soff + il * 2048 + (et + 1) * 128])
                        pf_free[bix] = evac(
                            wT_view[:, eg4 * 4:(eg4 + 1) * 4, il, :],
                            pf[bix][:, 0:512], t_last,
                            extra=(wpsem, write_guard[sset]))
                stage_guard[sset] = ("tsem", p.c("tsem"))
                p.wait("sync", "ssem", p.c("ssem"))
                p.wait("sync", "vsem", p.c("vsem"))
                write_guard[sset] = p.emit(
                    "sync",
                    lambda e, ib=ib, d=dst, woff=woff: e.dma_start(
                        d[ib].rearrange("et p c -> p et c"),
                        arena[:, woff:woff + 4096].rearrange("p (et c) -> p et c", c=256)),
                    inc=wpsem, incby=16)

        prep_wT(w_up, upT, 32)
        prep_wT(w_gate, gateT, 32)
        prep_wT(w_in, winT, 8)

        p3_z_guard = p.c("dnw")
        for eng_ in ("scalar", "vector"):
            p.wait(eng_, "wprep", p.c("wprep"))
            p.wait(eng_, "wprep1", p.c("wprep1"))
            p.wait(eng_, "gkv", gkv_all)
        z04 = arena[:, 16384:32768].rearrange("p (a two c) -> p a two c", two=2, c=128)
        z1v = arena[:, 0:16384].rearrange("p (e a) -> p a e", a=64)
        zf4 = arena[:, 16384:32768].rearrange("p (a eh de) -> p eh de a", eh=128, de=2)

        def pe_wait_evacs():
            p.wait("tensor", "ssem", p.c("ssem"))
            p.wait("tensor", "vsem", p.c("vsem"))

        ld_tok = {}

        def p3_load(eb, i):
            sem, val = stage_guard[i % 4]
            p.wait("sync", sem, val)
            j, h2, hh = (i >> 2) & 1, (i >> 1) & 1, i & 1
            p.emit(
                "sync",
                lambda e, eb=eb, j=j, h2=h2, hh=hh, s=i % 4: e.dma_start(
                    x_b[:, s * 2048:(s + 1) * 2048],
                    w_down[eb * 256 + j * 128:eb * 256 + (j + 1) * 128,
                           h2 * 4096 + hh * 2048:h2 * 4096 + (hh + 1) * 2048]),
                inc="castg", incby=16)
            ld_tok[(eb, i)] = ("castg", p.c("castg"))

        stage_guard = [stage_guard[0], stage_guard[0],
                       stage_guard[1], stage_guard[1]]
        for i in range(4):
            p3_load(0, i)
        for eb in range(8):
            r0_tok = {}

            def emit_r0q(q):
                j, h2, hh = (q >> 2) & 1, (q >> 1) & 1, q & 1
                s = q % 4
                it0 = h2 * 32 + hh * 16
                sem, val = ld_tok[(eb, q)]
                p.wait("tensor", sem, val)
                t_last = 0
                for g4 in range(4):
                    bix = g4 % 4
                    semb, valb = pf_free[bix]
                    p.wait("tensor", semb, valb)
                    t_last = 0
                    for k in range(4):
                        kk = g4 * 4 + k
                        t_last = tpf(pf[bix][:, k * 128:(k + 1) * 128],
                                     x_b[:, s * 2048 + kk * 128:s * 2048 + (kk + 1) * 128])
                    a0 = it0 + g4 * 4
                    tok = evac(z04[:, a0:a0 + 4, j, :], pf[bix][:, 0:512], t_last,
                               extra=("dnw", p3_z_guard))
                    pf_free[bix] = tok
                    r0_tok[(j, a0 // 4)] = tok
                stage_guard[s] = ("tsem", t_last)

            def emit_r1(l):
                bix = 4 + (l % 2)
                sem, val = pf_free[bix]
                p.wait("tensor", sem, val)
                for j in range(2):
                    need = r0_tok[(j, l // 2)]
                    p.wait("tensor", need[0], need[1])
                t_s = mm(pf[bix][:, 0:512], h_sb[:],
                         arena[:, 16384 + l * 512:16384 + (l + 1) * 512], True, True)
                pf_free[bix] = evac(z1v[:, 2 * l:2 * l + 2, :], pf[bix][:, 0:512], t_s)

            for q in range(8):
                emit_r0q(q)
                if q < 4:
                    p3_load(eb, q + 4)
                elif eb < 7:
                    p3_load(eb + 1, q - 4)
                if q >= 4:
                    for l in range(8 * (q - 4), 8 * (q - 3)):
                        emit_r1(l)
            pe_wait_evacs()
            r2_tok = [None] * 16

            def emit_r2(g):
                bk = g % 2
                sem, val = pt_free[bk]
                p.wait("tensor", sem, val)
                t_last = 0
                for k in range(8):
                    e0h = g * 8 + k
                    t_last = tp(pt[bk][:, k * 128:(k + 1) * 128],
                                arena[:, e0h * 128:(e0h + 1) * 128])
                tok = evac(arena[:, 16384 + g * 1024:16384 + (g + 1) * 1024],
                           pt[bk][:, 0:1024], t_last)
                pt_free[bk] = tok
                r2_tok[g] = tok

            def emit_r3(s):
                bix = s % 4
                sem, val = pf_free[bix]
                p.wait("tensor", sem, val)
                need = r2_tok[(4 * s + 3) // 8]
                p.wait("tensor", need[0], need[1])
                t_s = mm(pf[bix][:, 0:512], k2_sb[:],
                         arena[:, 16384 + s * 512:16384 + (s + 1) * 512], True, True)
                pf_free[bix] = evac(arena[:, s * 512:(s + 1) * 512],
                                    pf[bix][:, 0:512], t_s)

            for g in range(16):
                emit_r2(g)
                if g >= 1:
                    emit_r3(2 * (g - 1))
                    emit_r3(2 * (g - 1) + 1)
            emit_r3(30)
            emit_r3(31)
            pe_wait_evacs()
            for g in range(16):
                bk = g % 2
                sem, val = pt_free[bk]
                p.wait("tensor", sem, val)
                t_last = 0
                for k in range(8):
                    e0h = g * 8 + k
                    t_last = tp(pt[bk][:, k * 128:(k + 1) * 128],
                                arena[:, e0h * 128:(e0h + 1) * 128])
                pt_free[bk] = evac(zf4[:, g * 8:(g + 1) * 8, :, :], pt[bk][:, 0:1024],
                                   t_last, extra=("dnw", p3_z_guard))
            p.wait("sync", "ssem", p.c("ssem"))
            p.wait("sync", "vsem", p.c("vsem"))
            for grp in range(8):
                p.emit(
                    "sync",
                    lambda e, eb=eb, grp=grp: e.dma_start(
                        dnT[grp * 1024:(grp + 1) * 1024, eb * 256:(eb + 1) * 256]
                        .rearrange("(k p) c -> p k c", p=128),
                        arena[:, 16384 + grp * 2048:16384 + (grp + 1) * 2048]
                        .rearrange("p (k c) -> p k c", c=256)),
                    inc="dnw", incby=16)
            p3_z_guard = p.c("dnw")

        t_relb = p.release("tensor")
        s_relb = p.release("scalar")
        v_relb = p.release("vector")
        b_relb = p.release("gpsimd")
        pd = dict(wsem=p.c("wsem"), gsem=p.c("gsem"), tsem=t_relb, ssem=s_relb, vsem=v_relb,
                  bsem=b_relb,
                  wprep=p.c("wprep"), wprep1=p.c("wprep1"), castg=p.c("castg"), castg1=p.c("castg1"), dnw=p.c("dnw"),
                  patg=p.c("patg"), gkv=p.c("gkv"))
        for eng in ("sync", "tensor", "scalar", "vector", "gpsimd"):
            for s, v in pd.items():
                p.wait(eng, s, v)

        A_NQ = 0
        A_AF = 8192
        A_E = 16384
        A_KB = 24576
        A_VB = 26624

        pf_free = [("ssem", p.c("ssem"))] * 6
        pt_free = [("ssem", p.c("ssem"))] * 2
        wbuf_tok = [("tsem", p.c("tsem"))] * 4
        kv_tok = [("tsem", p.c("tsem"))] * 2
        e_tok = [p.c("tsem"), p.c("tsem")]

        def layer_norm(eps):
            v_in = p.c("vsem")
            p.wait("scalar", "vsem", v_in)
            eps_ap = stats[:, 121:122] if eps == LN_EPS else stats[:, 122:123]
            p.drain("vector")
            for st in range(NST):
                xs = x_b[:, st * E:(st + 1) * E]
                for j in range(4):
                    p.emit("vector", lambda e, xs=xs, st=st, j=j: e.bn_stats(
                        stats[:, st * 24 + j * 6:st * 24 + (j + 1) * 6],
                        xs[:, j * 512:(j + 1) * 512]))
            p.drain("vector")
            v_mu = 0
            for st in range(NST):
                v_mu = p.emit("vector", lambda e, st=st: e.bn_aggr(
                    stats[:, 96 + 2 * st:98 + 2 * st], stats[:, st * 24:(st + 1) * 24]),
                    inc="vsem")
            p.wait("scalar", "vsem", v_mu)
            var_v = stats[:, 97:105].rearrange("p (a b) -> p a b", b=2)[:, :, 0:1]
            ln_v = stats[:, 104:108].rearrange("p (a b) -> p a b", b=1)
            rs_v = stats[:, 108:112].rearrange("p (a b) -> p a b", b=1)
            p.drain("scalar")
            p.emit("scalar", lambda e, eps_ap=eps_ap: e.activation(ln_v, var_v, ACT.Ln, bias=eps_ap))
            p.drain("scalar")
            s_r = p.emit("scalar", lambda e: e.activation(
                rs_v, ln_v, ACT.Exp, scale=stats[:, 120:121]), inc="ssem")
            p.wait("vector", "ssem", s_r)
            v_a = 0
            for st in range(NST):
                xs = x_b[:, st * E:(st + 1) * E]
                v_a = p.emit("vector", lambda e, xs=xs, st=st: e.tensor_scalar(
                    tmA[:, st * E:(st + 1) * E], xs, stats[:, 96 + 2 * st:97 + 2 * st],
                    stats[:, 108 + st:109 + st], ALU.subtract, ALU.mult), inc="vsem")
            return v_a

        def transpose_tm_to_fm(src_off, scaled, v_ready):
            p.wait("tensor", "vsem", v_ready)
            s_last = 0
            for et in range(NET):
                bk = et % 2
                sem, val = pt_free[bk]
                p.wait("tensor", sem, val)
                t_last = 0
                for st in range(NST):
                    src = tmA if src_off is None else arena
                    off = (0 if src_off is None else src_off) + st * E + et * 128
                    t_last = tp(pt[bk][:, st * 128:(st + 1) * 128], src[:, off:off + 128])
                p.wait("scalar", "tsem", t_last)
                if scaled:
                    s_last = p.emit("scalar", lambda e, et=et, bk=bk: e.activation(
                        fmA[:, et * C:(et + 1) * C], pt[bk][:, 0:C],
                        ACT.Copy, scale=su_sb[:, et:et + 1]), inc="ssem")
                else:
                    s_last = p.emit("scalar", lambda e, et=et, bk=bk: e.copy(
                        fmA[:, et * C:(et + 1) * C], pt[bk][:, 0:C]), inc="ssem")
                pt_free[bk] = ("ssem", s_last)
            return s_last

        def fwht2048_fm():
            p.wait("tensor", "ssem", p.c("ssem"))
            for et in range(NET):
                bix = et % 2
                sem, val = pf_free[bix]
                p.wait("tensor", sem, val)
                t_s = mm(pf[bix][:, 0:C], h_sb[:], fmA[:, et * C:(et + 1) * C], True, True)
                p.wait("scalar", "tsem", t_s)
                s_s = p.emit("scalar", lambda e, et=et, bix=bix: e.copy(
                    fmA[:, et * C:(et + 1) * C], pf[bix][:, 0:C]), inc="ssem")
                pf_free[bix] = ("ssem", s_s)
            p.wait("vector", "ssem", p.c("ssem"))
            p.wait("vector", "tsem", p.c("tsem"))
            ready = [0] * NET
            for st in range(4):
                src_fm = (st % 2 == 0)
                for u in range(NET):
                    v = u ^ (1 << st)
                    if v < u:
                        continue
                    if src_fm:
                        iu, iv = fmA[:, u * C:(u + 1) * C], fmA[:, v * C:(v + 1) * C]
                        ou, ov = arena[:, u * C:(u + 1) * C], arena[:, v * C:(v + 1) * C]
                    else:
                        iu, iv = arena[:, u * C:(u + 1) * C], arena[:, v * C:(v + 1) * C]
                        ou, ov = fmA[:, u * C:(u + 1) * C], fmA[:, v * C:(v + 1) * C]
                    p.emit("vector", lambda e, iu=iu, iv=iv, ou=ou: e.tensor_add(ou, iu, iv))
                    last = (st == 3)
                    vv = p.emit("vector", lambda e, iu=iu, iv=iv, ov=ov: e.tensor_sub(ov, iu, iv),
                                inc="vsem" if last else None)
                    if last:
                        ready[u] = vv
                        ready[v] = vv
                p.drain("vector")
            return ready

        def ffn(fm_ready):
            korder = sorted(range(NET), key=lambda k: fm_ready[k])
            wb = [wbuf[:, k * 4096:(k + 1) * 4096] for k in range(4)]
            last_v_g = [p.c("vsem")]
            for ib in range(32):
                ub, gb = wb[(ib % 2) * 2], wb[(ib % 2) * 2 + 1]
                for m_i, (buf, srcT) in enumerate(((ub, upT), (gb, gateT))):
                    slot = (ib % 2) * 2 + m_i
                    sem, val = wbuf_tok[slot]
                    p.wait("sync", sem, val)
                    p.emit("sync", lambda e, ib=ib, buf=buf, srcT=srcT: e.dma_start(
                        buf.rearrange("p (et c) -> p et c", c=256),
                        srcT[ib].rearrange("et p c -> p et c")), inc=f"sl{slot}", incby=16)
                    p.wait("tensor", f"sl{slot}", p.c(f"sl{slot}"))
                for il in range(2):
                    it = ib * 2 + il
                    pu_i, pg_i = (it % 2) * 2, (it % 2) * 2 + 1
                    for bix in (pu_i, pg_i):
                        sem, val = pf_free[bix]
                        p.wait("tensor", sem, val)
                    t_u = 0
                    for ki, et in enumerate(korder):
                        p.wait("tensor", "vsem", fm_ready[et])
                        t_u = mm(pf[pu_i][:, 0:C], ub[:, et * 256 + il * 128:et * 256 + (il + 1) * 128],
                                 fmA[:, et * C:(et + 1) * C], ki == 0, ki == NET - 1)
                    t_g = 0
                    for ki, et in enumerate(korder):
                        t_g = mm(pf[pg_i][:, 0:C], gb[:, et * 256 + il * 128:et * 256 + (il + 1) * 128],
                                 fmA[:, et * C:(et + 1) * C], ki == 0, ki == NET - 1)
                    if il == 1:
                        wbuf_tok[(ib % 2) * 2] = ("tsem", t_g)
                        wbuf_tok[(ib % 2) * 2 + 1] = ("tsem", t_g)
                    p.wait("scalar", "tsem", t_g)
                    p.wait("scalar", "vsem", last_v_g[0])
                    s_si = p.emit("scalar", lambda e, it=it, pg_i=pg_i: e.activation(
                        silu_t[:, 0:C], pf[pg_i][:, 0:C], ACT.Silu, scale=wscg_sb[:, it:it + 1]),
                        inc="ssem")
                    p.wait("vector", "ssem", s_si)
                    p.wait("vector", "tsem", t_u)
                    v_g = p.emit("vector", lambda e, it=it, pu_i=pu_i: e.scalar_tensor_tensor(
                        arena[:, it * C:(it + 1) * C], pf[pu_i][:, 0:C], wscu_su[:, it:it + 1],
                        silu_t[:, 0:C], ALU.mult, ALU.mult), inc="vsem")
                    pf_free[pu_i] = ("vsem", v_g)
                    pf_free[pg_i] = ("ssem", s_si)
                    last_v_g[0] = v_g
            v_g_all = p.c("vsem")
            db = [wbuf[:, k * 4096:(k + 1) * 4096] for k in range(4)]
            p.wait("tensor", "vsem", v_g_all)
            v_d = 0
            for ec in range(4):
                for st in range(NST):
                    sem, val = pf_free[st]
                    p.wait("tensor", sem, val)
                t_d = 0
                for kb in range(8):
                    bi = kb % 4
                    sem, val = wbuf_tok[bi]
                    p.wait("sync", sem, val)
                    wv = p.emit("sync", lambda e, ec=ec, kb=kb, b=db[bi]: e.dma_start(
                        b.rearrange("p (k c) -> p k c", c=512),
                        dnT[kb * 1024:(kb + 1) * 1024, ec * 512:(ec + 1) * 512]
                        .rearrange("(k p) c -> p k c", p=128)), inc=f"sl{bi}", incby=16)
                    p.wait("tensor", f"sl{bi}", wv)
                    for st in range(NST):
                        for k in range(8):
                            it = kb * 8 + k
                            t_d = mm(pf[st][:, 0:512],
                                     arena[:, it * C + st * 128:it * C + (st + 1) * 128],
                                     db[bi][:, k * 512:(k + 1) * 512],
                                     it == 0, it == NIT - 1)
                    wbuf_tok[bi] = ("tsem", t_d)
                p.wait("vector", "tsem", t_d)
                for st in range(NST):
                    p.drain("vector")
                    p.emit("vector", lambda e, st=st, ec=ec: e.tensor_mul(
                        sq[:, 0:512], pf[st][:, 0:512], wscb[:, ec * 512:(ec + 1) * 512]))
                    p.drain("vector")
                    v_d = p.emit("vector", lambda e, st=st, ec=ec: e.tensor_add(
                        x_b[:, st * E + ec * 512:st * E + (ec + 1) * 512],
                        x_b[:, st * E + ec * 512:st * E + (ec + 1) * 512],
                        sq[:, 0:512]), inc="vsem")
                    pf_free[st] = ("vsem", v_d)
            p.drain("vector")
            return v_d

        def rms_apply(src_ap_fn, dst_off, w_bcast, v_ready, out_is_nq):
            p.wait("scalar", "vsem", v_ready)
            p.wait("vector", "vsem", v_ready)
            p.drain("vector")
            for st in range(NST):
                a_s = src_ap_fn(st)
                for j in range(4):
                    p.emit("vector", lambda e, a_s=a_s, st=st, j=j: e.bn_stats(
                        stats[:, st * 24 + j * 6:st * 24 + (j + 1) * 6],
                        a_s[:, j * 512:(j + 1) * 512]))
            p.drain("vector")
            for st in range(NST):
                p.emit("vector", lambda e, st=st: e.bn_aggr(
                    stats[:, 96 + 2 * st:98 + 2 * st], stats[:, st * 24:(st + 1) * 24]))
            p.drain("vector")
            mean_v = stats[:, 96:104].rearrange("p (a b) -> p a b", b=2)[:, :, 0:1]
            var_v = stats[:, 97:105].rearrange("p (a b) -> p a b", b=2)[:, :, 0:1]
            msq_v = stats[:, 104:108].rearrange("p (a b) -> p a b", b=1)
            rs_v = stats[:, 108:112].rearrange("p (a b) -> p a b", b=1)
            v_ss = p.emit("vector", lambda e: e.tensor_tensor(
                msq_v, mean_v, mean_v, ALU.mult), inc="vsem")
            p.drain("vector")
            v_ss = p.emit("vector", lambda e: e.tensor_tensor(
                msq_v, msq_v, var_v, ALU.add), inc="vsem")
            p.wait("scalar", "vsem", v_ss)
            p.drain("scalar")
            p.emit("scalar", lambda e: e.activation(msq_v, msq_v, ACT.Ln, bias=stats[:, 122:123]))
            p.drain("scalar")
            s_r = p.emit("scalar", lambda e: e.activation(
                rs_v, msq_v, ACT.Exp, scale=stats[:, 120:121]), inc="ssem")
            p.wait("vector", "ssem", s_r)
            last = 0
            for st in range(NST):
                a_s = src_ap_fn(st)
                if out_is_nq:
                    last = p.emit("vector", lambda e, a_s=a_s, st=st: e.scalar_tensor_tensor(
                        arena[:, dst_off + st * E:dst_off + (st + 1) * E], a_s,
                        stats[:, 108 + st:109 + st], w_bcast[:], ALU.mult, ALU.mult), inc="vsem")
                else:
                    p.drain("vector")
                    p.emit("vector", lambda e, a_s=a_s, st=st: e.scalar_tensor_tensor(
                        sq[:], a_s, stats[:, 108 + st:109 + st], w_bcast[:], ALU.mult, ALU.mult))
                    p.drain("vector")
                    last = p.emit("vector", lambda e, st=st: e.tensor_add(
                        x_b[:, st * E:(st + 1) * E], x_b[:, st * E:(st + 1) * E], sq[:]), inc="vsem")
            return last

        def attn(v_nq_ready):
            s_tr = transpose_tm_to_fm(A_NQ, False, v_nq_ready)
            wb = [wbuf[:, k * 4096:(k + 1) * 4096] for k in range(2)]
            p.wait("tensor", "ssem", s_tr)
            for eb in range(8):
                bi = eb % 2
                sem, val = wbuf_tok[bi]
                p.wait("sync", sem, val)
                p.emit("sync", lambda e, eb=eb, b=wb[bi]: e.dma_start(
                    b.rearrange("p (et c) -> p et c", c=256),
                    winT[eb].rearrange("et p c -> p et c")), inc=f"sl{bi}", incby=16)
                p.wait("tensor", f"sl{bi}", p.c(f"sl{bi}"))
                for il in range(2):
                    ep = eb * 2 + il
                    bix = 4 + (ep % 2)
                    sem, val = pf_free[bix]
                    p.wait("tensor", sem, val)
                    t_p = 0
                    for et in range(NET):
                        t_p = mm(pf[bix][:, 0:C], wb[bi][:, et * 256 + il * 128:et * 256 + (il + 1) * 128],
                                 fmA[:, et * C:(et + 1) * C], et == 0, et == NET - 1)
                    if il == 1:
                        wbuf_tok[bi] = ("tsem", t_p)
                    p.wait("scalar", "tsem", t_p)
                    s_p = p.emit("scalar", lambda e, ep=ep, bix=bix: e.copy(
                        arena[:, A_NQ + ep * C:A_NQ + (ep + 1) * C], pf[bix][:, 0:C]), inc="ssem")
                    pf_free[bix] = ("ssem", s_p)
            p.wait("tensor", "ssem", p.c("ssem"))
            for h in range(H):
                bix = 4 + (h % 2)
                sem, val = pf_free[bix]
                p.wait("tensor", sem, val)
                t_q = mm(pf[bix][:, 0:C], wqT[:], arena[:, A_NQ + h * C:A_NQ + (h + 1) * C],
                         True, True)
                p.wait("scalar", "tsem", t_q)
                s_q = p.emit("scalar", lambda e, h=h, bix=bix: e.copy(
                    fmA[:, h * C:(h + 1) * C], pf[bix][:, 0:C]), inc="ssem")
                pf_free[bix] = ("ssem", s_q)
            p.wait("tensor", "ssem", p.c("ssem"))
            kv_free = kv_tok
            e_guard = e_tok
            s_e_done = [0] * H
            v_o = 0

            def emit_scores(h):
                sl = h % 2
                koff = A_KB + sl * 1024
                voff = A_VB + sl * 1088
                sem, val = kv_free[sl]
                p.wait("sync", sem, val)
                p.emit("sync", lambda e, h=h, koff=koff: e.dma_start(
                    arena[:, koff:koff + 1024], kT[h]), inc=f"kv{sl}", incby=16)
                w_kv = p.emit("sync", lambda e, h=h, voff=voff: e.dma_start(
                    arena[:, voff:voff + 1088], vT[h]), inc=f"kv{sl}", incby=16)
                p.wait("tensor", f"kv{sl}", w_kv)
                eoff = A_E + sl * 4096
                p.wait("scalar", "tsem", e_guard[sl])
                for mtq in range(0, NMT, 4):
                    t_sc = 0
                    for k in range(4):
                        mt = mtq + k
                        sem, val = pf_free[k]
                        p.wait("tensor", sem, val)
                        t_sc = mm(pf[k][:, 0:C], arena[:, koff + mt * 128:koff + (mt + 1) * 128],
                                  fmA[:, h * C:(h + 1) * C], True, True)
                    p.wait("scalar", "tsem", t_sc)
                    s_e = 0
                    for k in range(4):
                        mt = mtq + k
                        s_e = p.emit("scalar", lambda e, mt=mt, k=k, eoff=eoff: e.activation(
                            arena[:, eoff + mt * C:eoff + (mt + 1) * C], pf[k][:, 0:C], ACT.Exp),
                            inc="ssem")
                    for k in range(4):
                        pf_free[k] = ("ssem", s_e)
                    if mtq == 0 and h > 0:
                        emit_out(h - 1)
                s_e_done[h] = p.c("ssem")

            def emit_out(h):
                nonlocal v_o
                sl = h % 2
                voff = A_VB + sl * 1088
                eoff = A_E + sl * 4096
                p.wait("tensor", "ssem", s_e_done[h])
                t_o = 0
                for st in range(NST):
                    bix = 4 + st // 2
                    coff = (st % 2) * 256
                    if st % 2 == 0:
                        sem, val = pf_free[bix]
                        p.wait("tensor", sem, val)
                    for mt in range(NMT):
                        t_o = mm(pf[bix][:, coff:coff + VW],
                                 arena[:, eoff + mt * C + st * 128:eoff + mt * C + (st + 1) * 128],
                                 arena[:, voff + mt * VW:voff + (mt + 1) * VW],
                                 mt == 0, mt == NMT - 1)
                p.wait("vector", "tsem", t_o)
                p.drain("vector")
                for st in range(NST):
                    bix = 4 + st // 2
                    coff = (st % 2) * 256
                    p.emit("vector", lambda e, st=st, bix=bix, coff=coff: e.reciprocal(
                        stats[:, 116 + st:117 + st], pf[bix][:, coff + 128:coff + 129]))
                p.drain("vector")
                for st in range(NST):
                    bix = 4 + st // 2
                    coff = (st % 2) * 256
                    v_o = p.emit("vector", lambda e, h=h, st=st, bix=bix, coff=coff: e.tensor_scalar(
                        arena[:, A_AF + st * E + h * 128:A_AF + st * E + (h + 1) * 128],
                        pf[bix][:, coff:coff + 128], stats[:, 116 + st:117 + st], None, ALU.mult), inc="vsem")
                for st in (0, 2):
                    pf_free[4 + st // 2] = ("vsem", v_o)
                kv_free[sl] = ("tsem", p.c("tsem"))
                e_guard[sl] = p.c("tsem")

            for h in range(H):
                emit_scores(h)
                if h > 0:
                    emit_out(h - 1)
            emit_out(H - 1)
            return rms_apply(lambda st: arena[:, A_AF + st * E:A_AF + (st + 1) * E],
                             0, wret_b, v_o, out_is_nq=False)

        out_writes = 0
        for ch in range(NCH):
            v_endprev = p.release("vector")
            s_endprev = p.release("scalar")
            p.wait("gpsimd", "osem", out_writes * 16)
            p.wait("gpsimd", "vsem", v_endprev)
            p.wait("gpsimd", "ssem", s_endprev)
            g_in = 0
            for st in range(NST):
                g_in = p.emit("gpsimd", lambda e, ch=ch, st=st: e.dma_start(
                    x_b[:, st * E:(st + 1) * E],
                    x_in[ch * C + st * 128:ch * C + (st + 1) * 128, :]), inc="gsem", incby=16)
            p.wait("vector", "gsem", g_in)
            p.wait("scalar", "gsem", g_in)
            p.wait("vector", "tsem", p.c("tsem"))

            v_ln = layer_norm(LN_EPS)
            transpose_tm_to_fm(None, True, v_ln)
            fm_ready = fwht2048_fm()
            ffn(fm_ready)

            v_ln2 = layer_norm(LN_EPS)
            v_y = 0
            for st in range(NST):
                p.drain("vector")
                v_y = p.emit("vector", lambda e, st=st: e.tensor_add(
                    x_b[:, st * E:(st + 1) * E], x_b[:, st * E:(st + 1) * E],
                    tmA[:, st * E:(st + 1) * E]), inc="vsem")
            p.wait("vector", "tsem", p.c("tsem"))
            rs_v = stats[:, 108:112].rearrange("p (a b) -> p a b", b=1)
            var_v = stats[:, 97:105].rearrange("p (a b) -> p a b", b=2)[:, :, 0:1]
            w1_v = stats[:, 64:68].rearrange("p (a b) -> p a b", b=1)
            u_v = stats[:, 68:72].rearrange("p (a b) -> p a b", b=1)
            p.drain("vector")
            p.emit("vector", lambda e: e.tensor_tensor(w1_v, rs_v, rs_v, ALU.mult))
            p.drain("vector")
            v_w1 = p.emit("vector", lambda e: e.tensor_tensor(w1_v, w1_v, var_v, ALU.mult), inc="vsem")
            p.wait("scalar", "vsem", v_w1)
            p.drain("scalar")
            p.emit("scalar", lambda e: e.activation(u_v, w1_v, ACT.Ln, bias=stats[:, 122:123]))
            p.drain("scalar")
            s_u = p.emit("scalar", lambda e: e.activation(
                u_v, u_v, ACT.Exp, scale=stats[:, 120:121]), inc="ssem")
            p.wait("vector", "ssem", s_u)
            p.wait("vector", "vsem", v_y)
            v_nq = 0
            for st in range(NST):
                v_nq = p.emit("vector", lambda e, st=st: e.scalar_tensor_tensor(
                    arena[:, A_NQ + st * E:A_NQ + (st + 1) * E], tmA[:, st * E:(st + 1) * E],
                    stats[:, 68 + st:69 + st], wnq_b[:], ALU.mult, ALU.mult), inc="vsem")
            attn(v_nq)

            v_ln3 = layer_norm(LN_EPS)
            transpose_tm_to_fm(None, True, v_ln3)
            fm_ready3 = fwht2048_fm()
            v_ffn3 = ffn(fm_ready3)

            p.wait("gpsimd", "vsem", v_ffn3)
            for st in range(NST):
                p.emit("gpsimd", lambda e, ch=ch, st=st: e.dma_start(
                    y_out[ch * C + st * 128:ch * C + (st + 1) * 128, :],
                    x_b[:, st * E:(st + 1) * E]), inc="osem", incby=16)
                out_writes += 1

        total_out = out_writes * 16

        with nc.Block() as block:
            @block.sync
            def _(eng):
                for fn in p.plan["sync"]:
                    fn(eng)

            @block.tensor
            def _(eng):
                for fn in p.plan["tensor"]:
                    fn(eng)

            @block.scalar
            def _(eng):
                for fn in p.plan["scalar"]:
                    fn(eng)

            @block.vector
            def _(eng):
                for fn in p.plan["vector"]:
                    fn(eng)

            @block.gpsimd
            def _(eng):
                for fn in p.plan["gpsimd"]:
                    fn(eng)
                eng.wait_ge(p.sems["osem"], total_out)

    return nc


_NC_CACHE = None


def kernel(L0_attn_query_input, L0_attn_W_in, L0_attn_W_q, L0_attn_W_k, L0_attn_W_v,
           L0_attn_stored_patterns, L0_attn_norm_query_w, L0_attn_norm_retrieved_w,
           L0_attn_beta, L0_ffn_SU_ug, L0_ffn_SU_dp, L0_ffn_Wscale_ug, L0_ffn_Wscale_dp,
           L0_ffn_W_up, L0_ffn_W_gate, L0_ffn_W_down):
    global _NC_CACHE
    x = np.ascontiguousarray(np.asarray(L0_attn_query_input, dtype=np.float32).reshape(16384, E))
    consts = dict(
        h128=np.ascontiguousarray(_hadamard(128).astype(BFNP)),
        k2h64=np.ascontiguousarray(np.kron(np.eye(2), _hadamard(64)).astype(BFNP)),
        ident=np.ascontiguousarray(np.eye(128, dtype=np.float32).astype(BFNP)),
        identf=np.ascontiguousarray(np.eye(128, dtype=np.float32)),
        ones1=np.ones((1, 128), dtype=BFNP),
        ones1f=np.ones((1, 128), dtype=np.float32),
    )
    weights = dict(
        w_in=np.ascontiguousarray(np.asarray(L0_attn_W_in, dtype=np.float32)),
        w_q=np.ascontiguousarray(np.asarray(L0_attn_W_q, dtype=np.float32)),
        w_k=np.ascontiguousarray(np.asarray(L0_attn_W_k, dtype=np.float32)),
        w_v=np.ascontiguousarray(np.asarray(L0_attn_W_v, dtype=np.float32)),
        pats=np.ascontiguousarray(np.asarray(L0_attn_stored_patterns, dtype=np.float32)),
        nqw=np.ascontiguousarray(np.asarray(L0_attn_norm_query_w, dtype=np.float32)),
        nrw=np.ascontiguousarray(np.asarray(L0_attn_norm_retrieved_w, dtype=np.float32)),
        beta=np.asarray(L0_attn_beta, dtype=np.float32).reshape(1),
        su_ug=np.ascontiguousarray(np.asarray(L0_ffn_SU_ug, dtype=np.float32)),
        su_dp=np.ascontiguousarray(np.asarray(L0_ffn_SU_dp, dtype=np.float32)),
        wsc_ug=np.ascontiguousarray(np.asarray(L0_ffn_Wscale_ug, dtype=np.float32)),
        wsc_dp=np.ascontiguousarray(np.asarray(L0_ffn_Wscale_dp, dtype=np.float32)),
        w_up=np.ascontiguousarray(np.asarray(L0_ffn_W_up, dtype=np.float32)),
        w_gate=np.ascontiguousarray(np.asarray(L0_ffn_W_gate, dtype=np.float32)),
        w_down=np.ascontiguousarray(np.asarray(L0_ffn_W_down, dtype=np.float32)),
    )
    if _NC_CACHE is None:
        _NC_CACHE = build_nc()
    nc = _NC_CACHE
    in_maps = []
    for c in range(NCORES):
        m = {"x": np.ascontiguousarray(x[c * TCORE:(c + 1) * TCORE])}
        m.update(weights)
        m.update(consts)
        in_maps.append(m)
    res = run_bass_kernel_spmd(nc, in_maps, core_ids=list(range(NCORES)))
    out = np.concatenate([res.results[c]["y"] for c in range(NCORES)], axis=0)
    return np.ascontiguousarray(out.reshape(1, 16384, E).astype(np.float32))


if __name__ == "__main__":
    build_nc()
    print("built ok")


# version 17
# speedup vs baseline: 1.1481x; 1.0229x over previous
import numpy as np
import ml_dtypes
from contextlib import ExitStack

import concourse.bass as bass
import concourse.mybir as mybir
from concourse.bass_utils import run_bass_kernel_spmd

BFNP = ml_dtypes.bfloat16
F32 = mybir.dt.float32
BF16 = mybir.dt.bfloat16
AX = mybir.AxisListType.X
ALU = mybir.AluOpType
ACT = mybir.ActivationFunctionType

NCORES = 8
TCORE = 2048
C = 512
NCH = TCORE // C
NST = C // 128
E = 2048
I = 8192
NET = E // 128
NIT = I // 128
H = 16
D = 128
M = 1024
NMT = M // 128
VW = 136
RS2048 = float(1.0 / np.sqrt(2048.0))
RS8192 = float(1.0 / np.sqrt(8192.0))
LN_EPS = 1e-5
RMS_EPS = 1e-6

ENG_SEM = {"tensor": "tsem", "scalar": "ssem", "vector": "vsem"}


def _hadamard(n):
    h = np.array([[1.0]], dtype=np.float64)
    while h.shape[0] < n:
        h = np.block([[h, h], [h, -h]])
    return h


class P:

    def __init__(self):
        self.plan = {k: [] for k in ("sync", "tensor", "scalar", "vector", "gpsimd")}
        self.cnt = {}
        self.lastw = {}
        self.sems = {}

    def emit(self, eng, build, inc=None, incby=1):
        if inc is not None:
            self.cnt[inc] = self.cnt.get(inc, 0) + incby

            def fn(e, b=build, s=inc, n=incby):
                b(e).then_inc(self.sems[s], n)
        else:
            def fn(e, b=build):
                b(e)
        self.plan[eng].append(fn)
        return self.cnt.get(inc, 0)

    def wait(self, eng, sem, val):
        if not val or val <= 0:
            return
        key = (eng, sem)
        if self.lastw.get(key, 0) >= val:
            return
        self.lastw[key] = val
        self.plan[eng].append(lambda e, s=sem, v=val: e.wait_ge(self.sems[s], v))

    def drain(self, eng):
        self.plan[eng].append(lambda e: e.drain())

    def release(self, eng):
        sem = ENG_SEM[eng]
        return self.emit(eng, lambda e: e.drain(), inc=sem)

    def c(self, sem):
        return self.cnt.get(sem, 0)


def build_nc():
    nc = bass.Bass("TRN2", target_bir_lowering=False)
    p = P()

    x_in = nc.dram_tensor("x", [TCORE, E], F32, kind="ExternalInput").ap()
    w_in = nc.dram_tensor("w_in", [E, E], F32, kind="ExternalInput").ap()
    w_q = nc.dram_tensor("w_q", [D, D], F32, kind="ExternalInput").ap()
    w_k = nc.dram_tensor("w_k", [D, D], F32, kind="ExternalInput").ap()
    w_v = nc.dram_tensor("w_v", [D, D], F32, kind="ExternalInput").ap()
    pats = nc.dram_tensor("pats", [H, M, D], F32, kind="ExternalInput").ap()
    nqw = nc.dram_tensor("nqw", [E], F32, kind="ExternalInput").ap()
    nrw = nc.dram_tensor("nrw", [E], F32, kind="ExternalInput").ap()
    beta = nc.dram_tensor("beta", [1], F32, kind="ExternalInput").ap()
    su_ug = nc.dram_tensor("su_ug", [E], F32, kind="ExternalInput").ap()
    su_dp = nc.dram_tensor("su_dp", [I], F32, kind="ExternalInput").ap()
    wsc_ug = nc.dram_tensor("wsc_ug", [2 * I], F32, kind="ExternalInput").ap()
    wsc_dp = nc.dram_tensor("wsc_dp", [E], F32, kind="ExternalInput").ap()
    w_up = nc.dram_tensor("w_up", [I, E], F32, kind="ExternalInput").ap()
    w_gate = nc.dram_tensor("w_gate", [I, E], F32, kind="ExternalInput").ap()
    w_down = nc.dram_tensor("w_down", [E, I], F32, kind="ExternalInput").ap()
    h128 = nc.dram_tensor("h128", [128, 128], BF16, kind="ExternalInput").ap()
    ident = nc.dram_tensor("ident", [128, 128], BF16, kind="ExternalInput").ap()
    ones1 = nc.dram_tensor("ones1", [1, 128], BF16, kind="ExternalInput").ap()
    ones1f = nc.dram_tensor("ones1f", [1, 128], F32, kind="ExternalInput").ap()

    y_out = nc.dram_tensor("y", [TCORE, E], F32, kind="ExternalOutput").ap()

    upT = nc.dram_tensor("upT", [32, NET, 128, 256], BF16, kind="Internal").ap()
    gateT = nc.dram_tensor("gateT", [32, NET, 128, 256], BF16, kind="Internal").ap()
    winT = nc.dram_tensor("winT", [8, NET, 128, 256], BF16, kind="Internal").ap()
    dnT = nc.dram_tensor("dnT", [I, E], BF16, kind="Internal").ap()
    kT = nc.dram_tensor("kT", [H, 128, M], BF16, kind="Internal").ap()
    vT = nc.dram_tensor("vT", [H, 128, NMT * VW], BF16, kind="Internal").ap()

    with ExitStack() as ctx:
        for s in ("wsem", "gsem", "tsem", "ssem", "vsem", "osem",
                  "wprep", "wprep1", "castg", "castg1", "dnw", "patg", "gkv",
                  "sl0", "sl1", "sl2", "sl3", "kv0", "kv1"):
            p.sems[s] = ctx.enter_context(nc.semaphore(s))

        x_b = ctx.enter_context(nc.sbuf_tensor("x_b", [128, NST * E], F32))
        tmA = ctx.enter_context(nc.sbuf_tensor("tmA", [128, NST * E], BF16))
        fmA = ctx.enter_context(nc.sbuf_tensor("fmA", [128, NET * C], BF16))
        fwtmp = ctx.enter_context(nc.sbuf_tensor("fwtmp", [128, C], BF16))
        arena = ctx.enter_context(nc.sbuf_tensor("arena", [128, NIT * C], BF16))
        wbuf = ctx.enter_context(nc.sbuf_tensor("wbuf", [128, 4 * 4096], BF16))
        sq = ctx.enter_context(nc.sbuf_tensor("sq", [128, E], F32))
        silu_t = ctx.enter_context(nc.sbuf_tensor("silu_t", [128, C], BF16))
        stats = ctx.enter_context(nc.sbuf_tensor("stats", [128, 128], F32))
        h_sb = ctx.enter_context(nc.sbuf_tensor("h_sb", [128, 128], BF16))
        id_sb = ctx.enter_context(nc.sbuf_tensor("id_sb", [128, 128], BF16))
        on_sb = ctx.enter_context(nc.sbuf_tensor("on_sb", [1, 128], BF16))
        onf_sb = ctx.enter_context(nc.sbuf_tensor("onf_sb", [1, 128], F32))
        su_sb = ctx.enter_context(nc.sbuf_tensor("su_sb", [128, NET], F32))
        wscu_su = ctx.enter_context(nc.sbuf_tensor("wscu_su", [128, NIT], F32))
        wscg_sb = ctx.enter_context(nc.sbuf_tensor("wscg_sb", [128, NIT], F32))
        wscb = ctx.enter_context(nc.sbuf_tensor("wscb", [128, E], F32))
        wnq_b = ctx.enter_context(nc.sbuf_tensor("wnq_b", [128, E], BF16))
        wret_b = ctx.enter_context(nc.sbuf_tensor("wret_b", [128, E], BF16))
        wqT = ctx.enter_context(nc.sbuf_tensor("wqT", [128, 128], BF16))
        wkT = ctx.enter_context(nc.sbuf_tensor("wkT", [128, 128], BF16))
        wvT = ctx.enter_context(nc.sbuf_tensor("wvT", [128, 128], BF16))
        bsig = ctx.enter_context(nc.sbuf_tensor("bsig", [128, 4], F32))

        pf = [ctx.enter_context(nc.psum_tensor(f"pf{i}", [128, 512], F32)) for i in range(6)]
        pt = [ctx.enter_context(nc.psum_tensor(f"pt{i}", [128, 1024], BF16)) for i in range(2)]

        pf_free = [("ssem", 0)] * 6
        pt_free = [("ssem", 0)] * 2

        def mm(out_ap, lhsT_ap, rhs_ap, start, stop):
            return p.emit(
                "tensor",
                lambda e, o=out_ap, l=lhsT_ap, r=rhs_ap, st=start, sp=stop:
                    e.matmul(o, l, r, start=st, stop=sp),
                inc="tsem")

        def tp(out_ap, in_ap):
            return p.emit(
                "tensor",
                lambda e, o=out_ap, i_=in_ap: e.transpose(o, i_, id_sb[:]),
                inc="tsem")

        def sload(dst, src, slow=False):
            return p.emit("sync", lambda e, d=dst, s=src, sl=slow: e.dma_start(
                d, s, allow_slow_non_contiguous=sl), inc="wsem", incby=16)

        sload(h_sb[:], h128)
        sload(id_sb[:], ident)
        sload(on_sb[:], ones1)
        sload(onf_sb[:], ones1f)
        sload(su_sb[:], su_ug.rearrange("(a b) -> b a", b=128), slow=True)
        sload(wscu_su[:], wsc_ug[0:I].rearrange("(a b) -> b a", b=128), slow=True)
        sload(wscg_sb[:], wsc_ug[I:2 * I].rearrange("(a b) -> b a", b=128), slow=True)
        sload(stats[0:1, 112:113], beta.rearrange("(a b) -> a b", b=1))
        sload(x_b[0:1, 0:E], nqw.rearrange("(a b) -> a b", a=1))
        sload(x_b[0:1, E:2 * E], nrw.rearrange("(a b) -> a b", a=1))
        sload(x_b[0:1, 2 * E:3 * E], wsc_dp.rearrange("(a b) -> a b", a=1))
        sload(sq[:, 1024:1024 + NIT], su_dp.rearrange("(a b) -> b a", b=128), slow=True)
        sload(sq[:, 1200:1328], w_q)
        sload(sq[:, 1328:1456], w_k)
        w_all = sload(sq[:, 1456:1584], w_v)

        p.emit("vector", lambda e: e.memset(stats[:, 120:121], -0.5))
        p.emit("vector", lambda e: e.memset(stats[:, 121:122], float(LN_EPS)))
        p.emit("vector", lambda e: e.memset(stats[:, 122:123], float(RMS_EPS)))
        p.emit("vector", lambda e: e.memset(stats[:, 123:124], -1.0))
        v_consts = p.release("vector")
        p.wait("vector", "wsem", w_all)
        p.emit("vector", lambda e: e.tensor_scalar_mul(su_sb[:], su_sb[:], RS2048))
        p.emit("vector", lambda e: e.tensor_mul(wscu_su[:], wscu_su[:], sq[:, 1024:1024 + NIT]))
        p.wait("scalar", "wsem", w_all)
        p.wait("scalar", "vsem", v_consts)
        p.emit("scalar", lambda e: e.activation(
            stats[0:1, 113:114], stats[0:1, 112:113], ACT.Exp, scale=stats[0:1, 123:124]))
        p.emit("scalar", lambda e: e.mul(x_b[0:1, 2 * E:3 * E], x_b[0:1, 2 * E:3 * E], 0.5 * RS8192))
        s_rows0 = p.release("scalar")
        p.wait("vector", "ssem", s_rows0)
        p.emit("vector", lambda e: e.tensor_scalar_add(stats[0:1, 114:115], stats[0:1, 113:114], 1.0))
        p.drain("vector")
        p.emit("vector", lambda e: e.reciprocal(stats[0:1, 115:116], stats[0:1, 114:115]))
        p.emit("vector", lambda e: e.tensor_copy(tmA[0:1, 0:E], x_b[0:1, 0:E]))
        p.emit("vector", lambda e: e.tensor_copy(tmA[:, 2 * E:2 * E + 128], sq[:, 1200:1328]))
        p.emit("vector", lambda e: e.tensor_copy(tmA[:, 2 * E + 128:2 * E + 256], sq[:, 1328:1456]))
        p.emit("vector", lambda e: e.tensor_copy(tmA[:, 2 * E + 256:2 * E + 384], sq[:, 1456:1584]))
        p.emit("vector", lambda e: e.tensor_copy(tmA[0:1, E:2 * E], x_b[0:1, E:2 * E]))
        v_prep0 = p.release("vector")

        p.wait("tensor", "vsem", v_prep0)
        p.wait("tensor", "ssem", s_rows0)
        for j in range(4):
            mm(pf[0][:, 0:512], on_sb[:], tmA[0:1, j * 512:(j + 1) * 512], True, True)
            mm(pf[1][:, 0:512], on_sb[:], tmA[0:1, E + j * 512:E + (j + 1) * 512], True, True)
            t_b = mm(pf[2][:, 0:512], onf_sb[:], x_b[0:1, 2 * E + j * 512:2 * E + (j + 1) * 512], True, True)
            p.wait("scalar", "tsem", t_b)
            p.emit("scalar", lambda e, j=j: e.copy(wnq_b[:, j * 512:(j + 1) * 512], pf[0][:, 0:512]))
            p.emit("scalar", lambda e, j=j: e.copy(wret_b[:, j * 512:(j + 1) * 512], pf[1][:, 0:512]))
            s_b = p.emit("scalar", lambda e, j=j: e.copy(wscb[:, j * 512:(j + 1) * 512], pf[2][:, 0:512]), inc="ssem")
            p.wait("tensor", "ssem", s_b)
        t_bs = mm(pf[3][:, 0:1], onf_sb[:], stats[0:1, 115:116], True, True)
        p.wait("scalar", "tsem", t_bs)
        s_bs = p.emit("scalar", lambda e: e.copy(bsig[:, 0:1], pf[3][:, 0:1]), inc="ssem")

        tp(pt[0][:, 0:128], tmA[:, 2 * E:2 * E + 128])
        tp(pt[0][:, 128:256], tmA[:, 2 * E + 128:2 * E + 256])
        t_w3 = tp(pt[0][:, 256:384], tmA[:, 2 * E + 256:2 * E + 384])
        p.wait("vector", "tsem", t_w3)
        p.wait("vector", "ssem", s_bs)
        p.emit("vector", lambda e: e.tensor_scalar_mul(wqT[:], pt[0][:, 0:128], bsig[:, 0:1]))
        p.emit("vector", lambda e: e.tensor_copy(wkT[:], pt[0][:, 128:256]))
        v_w3 = p.emit("vector", lambda e: e.tensor_copy(wvT[:], pt[0][:, 256:384]), inc="vsem")
        pt_free[0] = ("vsem", v_w3)
        pf_free[3] = ("ssem", s_bs)

        pat_guard = [0, 0]
        for h in range(H):
            sl = h % 2
            poff = sl * 1024
            toff = 2048 + sl * 1024
            p.wait("gpsimd", "tsem", pat_guard[sl])
            g_pat = p.emit(
                "gpsimd",
                lambda e, h=h, poff=poff: e.dma_start(
                    fmA[:, poff:poff + 1024].rearrange("p (mt d) -> p mt d", d=128),
                    pats[h].rearrange("(mt p) d -> p mt d", p=128)),
                inc="patg", incby=16)
            p.wait("tensor", "patg", g_pat)
            for eg in range(2):
                bk = eg % 2
                sem, val = pt_free[bk]
                p.wait("tensor", sem, val)
                t_last = 0
                for k in range(8):
                    mt = eg * 8 + k
                    if mt >= 8:
                        break
                    t_last = tp(pt[bk][:, k * 128:(k + 1) * 128],
                                fmA[:, poff + mt * 128:poff + (mt + 1) * 128])
                if eg == 0:
                    p.wait("scalar", "tsem", t_last)
                    sv = p.emit("scalar",
                                lambda e, toff=toff, bk=bk: e.copy(
                                    fmA[:, toff:toff + 1024], pt[bk][:, 0:1024]),
                                inc="ssem")
                    pt_free[bk] = ("ssem", sv)
                    break
            s_patT = p.c("ssem")
            for bix in (0, 1):
                sem, val = pf_free[bix]
                p.wait("tensor", sem, val)
            p.wait("tensor", "ssem", s_patT)
            mm(pf[0][:, 0:512], wkT[:], fmA[:, toff:toff + 512], True, True)
            t_k = mm(pf[1][:, 0:512], wkT[:], fmA[:, toff + 512:toff + 1024], True, True)
            p.wait("vector", "tsem", t_k)
            p.drain("vector")
            p.emit("vector", lambda e: e.reduce_sum(stats[:, 0:1], pf[0][:, 0:512], axis=AX))
            p.emit("vector", lambda e: e.reduce_sum(stats[:, 1:2], pf[1][:, 0:512], axis=AX))
            p.drain("vector")
            p.emit("vector", lambda e: e.tensor_add(stats[:, 2:3], stats[:, 0:1], stats[:, 1:2]))
            p.drain("vector")
            p.emit("vector", lambda e: e.tensor_scalar_mul(stats[:, 3:4], stats[:, 2:3], 1.0 / M))
            p.drain("vector")
            p.emit("vector", lambda e: e.tensor_scalar(
                arena[:, 0:512], pf[0][:, 0:512], stats[:, 3:4], None, ALU.subtract), inc="vsem")
            v_k = p.emit("vector", lambda e: e.tensor_scalar(
                arena[:, 512:1024], pf[1][:, 0:512], stats[:, 3:4], None, ALU.subtract), inc="vsem")
            pf_free[0] = ("vsem", v_k)
            pf_free[1] = ("vsem", v_k)
            p.wait("gpsimd", "vsem", v_k)
            g_k = p.emit("gpsimd", lambda e, h=h: e.dma_start(kT[h], arena[:, 0:1024]),
                         inc="gkv", incby=16)
            p.drain("vector")
            vz = p.emit("vector", lambda e: e.memset(arena[:, 1024:1024 + NMT * VW], 0.0), inc="vsem")
            p.drain("vector")
            ones_ap = arena[:, 1024:1024 + NMT * VW].rearrange(
                "p (mt c) -> p mt c", c=VW)[:, :, 128:129]
            vm_last = p.emit("vector", lambda e, ones_ap=ones_ap: e.memset(ones_ap, 1.0), inc="vsem")
            p.wait("scalar", "vsem", vm_last)
            t_v = 0
            for mtq in range(0, 8, 4):
                for k in range(4):
                    mt = mtq + k
                    sem, val = pf_free[k]
                    p.wait("tensor", sem, val)
                    t_v = mm(pf[k][:, 0:128], fmA[:, toff + mt * 128:toff + (mt + 1) * 128],
                             wvT[:], True, True)
                p.wait("scalar", "tsem", t_v)
                s_v = 0
                for k in range(4):
                    mt = mtq + k
                    s_v = p.emit("scalar", lambda e, mt=mt, k=k: e.copy(
                        arena[:, 1024 + mt * VW:1024 + mt * VW + 128], pf[k][:, 0:128]), inc="ssem")
                for k in range(4):
                    pf_free[k] = ("ssem", s_v)
            p.wait("gpsimd", "ssem", p.c("ssem"))
            g_v = p.emit("gpsimd", lambda e, h=h: e.dma_start(vT[h], arena[:, 1024:1024 + NMT * VW]),
                         inc="gkv", incby=16)
            pat_guard[sl] = p.c("tsem")
            p.wait("vector", "gkv", p.c("gkv"))
            p.wait("scalar", "gkv", p.c("gkv"))

        p.wait("gpsimd", "gkv", p.c("gkv"))
        wT_off = [12288, 4096]
        cast_guard = [0, 0]
        write_guard = [0, 0]

        def prep_wT(w_src, dst, nib):
            for ib in range(nib):
                sset = ib % 2
                base = 0 if sset == 0 else 8192
                woff = wT_off[sset]
                wT_view = arena[:, woff:woff + 4096].rearrange(
                    "p (g two c) -> p g two c", two=2, c=128)
                csem = "castg" if sset == 0 else "castg1"
                p.wait("gpsimd", "tsem", cast_guard[sset])
                p.emit(
                    "gpsimd",
                    lambda e, ib=ib, base=base, w=w_src: e.dma_start(
                        arena[:, base:base + 4096].rearrange("p (il c) -> p il c", c=2048),
                        w[ib * 256:(ib + 1) * 256, :].rearrange("(il p) c -> p il c", p=128)),
                    inc=csem, incby=16)
                g_c = p.c(csem)
                p.wait("tensor", csem, g_c)
                for il in range(2):
                    coff = base + il * 2048
                    for eg in range(2):
                        bk = eg % 2
                        sem, val = pt_free[bk]
                        p.wait("tensor", sem, val)
                        t_last = 0
                        for k in range(8):
                            et = eg * 8 + k
                            t_last = tp(pt[bk][:, k * 128:(k + 1) * 128],
                                        arena[:, coff + et * 128:coff + (et + 1) * 128])
                        p.wait("scalar", "tsem", t_last)
                        p.wait("scalar", "wprep" if sset == 0 else "wprep1", write_guard[sset])
                        s_t = p.emit(
                            "scalar",
                            lambda e, eg=eg, il=il, bk=bk, wT_view=wT_view: e.copy(
                                wT_view[:, eg * 8:(eg + 1) * 8, il, :], pt[bk][:, 0:1024]),
                            inc="ssem")
                        pt_free[bk] = ("ssem", s_t)
                cast_guard[sset] = p.c("tsem")
                p.wait("sync", "ssem", p.c("ssem"))
                write_guard[sset] = p.emit(
                    "sync",
                    lambda e, ib=ib, d=dst, woff=woff: e.dma_start(
                        d[ib].rearrange("et p c -> p et c"),
                        arena[:, woff:woff + 4096].rearrange("p (et c) -> p et c", c=256)),
                    inc="wprep" if sset == 0 else "wprep1", incby=16)

        prep_wT(w_up, upT, 32)
        prep_wT(w_gate, gateT, 32)
        prep_wT(w_in, winT, 8)

        p3_cast_guard = p.c("tsem")
        p3_z_guard = p.c("dnw")
        p.wait("gpsimd", "wprep", p.c("wprep"))
        p.wait("gpsimd", "wprep1", p.c("wprep1"))
        def pong_ap(it):
            if it < 32:
                return fmA[:, it * 256:(it + 1) * 256]
            return tmA[:, (it - 32) * 256:(it - 31) * 256]

        for eb in range(8):
            p.wait("gpsimd", "tsem", p3_cast_guard)
            for j in range(2):
                p.emit(
                    "gpsimd",
                    lambda e, eb=eb, j=j: e.dma_start(
                        arena[:, j * 8192:(j + 1) * 8192],
                        w_down[eb * 256 + j * 128:eb * 256 + (j + 1) * 128, :]),
                    inc="castg", incby=16)
            g_c = p.c("castg")
            p.wait("tensor", "castg", g_c)
            for tg in range(16):
                bk = tg % 2
                sem, val = pt_free[bk]
                p.wait("tensor", sem, val)
                t_last = 0
                for k in range(8):
                    tix = tg * 8 + k
                    it, j = tix // 2, tix % 2
                    t_last = tp(pt[bk][:, k * 128:(k + 1) * 128],
                                arena[:, j * 8192 + it * 128:j * 8192 + (it + 1) * 128])
                p.wait("scalar", "tsem", t_last)
                p.wait("scalar", "dnw", p3_z_guard)
                s_t = p.emit(
                    "scalar",
                    lambda e, tg=tg, bk=bk: e.copy(
                        arena[:, 16384 + tg * 1024:16384 + (tg + 1) * 1024], pt[bk][:, 0:1024]),
                    inc="ssem")
                pt_free[bk] = ("ssem", s_t)
            p3_cast_guard = p.c("tsem")
            p.wait("tensor", "ssem", p.c("ssem"))
            for itq in range(0, NIT, 4):
                t_s = 0
                for k in range(4):
                    it = itq + k
                    sem, val = pf_free[k]
                    p.wait("tensor", sem, val)
                    t_s = mm(pf[k][:, 0:256], h_sb[:],
                             arena[:, 16384 + it * 256:16384 + (it + 1) * 256],
                             True, True)
                p.wait("scalar", "tsem", t_s)
                s_s = 0
                for k in range(4):
                    it = itq + k
                    s_s = p.emit("scalar", lambda e, it=it, k=k: e.copy(
                        arena[:, 16384 + it * 256:16384 + (it + 1) * 256], pf[k][:, 0:256]),
                        inc="ssem")
                for k in range(4):
                    pf_free[k] = ("ssem", s_s)
            p.wait("vector", "ssem", p.c("ssem"))
            p.wait("vector", "dnw", p3_z_guard)
            for st in range(6):
                from_z = (st % 2 == 0)
                for u in range(NIT):
                    v = u ^ (1 << st)
                    if v < u:
                        continue
                    if from_z:
                        iu = arena[:, 16384 + u * 256:16384 + (u + 1) * 256]
                        iv = arena[:, 16384 + v * 256:16384 + (v + 1) * 256]
                        ou, ov = pong_ap(u), pong_ap(v)
                    else:
                        iu, iv = pong_ap(u), pong_ap(v)
                        ou = arena[:, 16384 + u * 256:16384 + (u + 1) * 256]
                        ov = arena[:, 16384 + v * 256:16384 + (v + 1) * 256]
                    p.emit("vector", lambda e, iu=iu, iv=iv, ou=ou: e.tensor_add(ou, iu, iv))
                    p.emit("vector", lambda e, iu=iu, iv=iv, ov=ov: e.tensor_sub(ov, iu, iv))
                p.drain("vector")
            v_bt = p.release("vector")
            p.wait("gpsimd", "vsem", v_bt)
            for grp in range(8):
                p.emit(
                    "gpsimd",
                    lambda e, eb=eb, grp=grp: e.dma_start(
                        dnT[grp * 1024:(grp + 1) * 1024, eb * 256:(eb + 1) * 256]
                        .rearrange("(k p) c -> p k c", p=128),
                        arena[:, 16384 + grp * 2048:16384 + (grp + 1) * 2048]
                        .rearrange("p (k c) -> p k c", c=256)),
                    inc="dnw", incby=16)
            p3_z_guard = p.c("dnw")

        t_rel = p.release("tensor")
        s_rel = p.release("scalar")
        v_rel = p.release("vector")
        pd = dict(wsem=p.c("wsem"), gsem=p.c("gsem"), tsem=t_rel, ssem=s_rel, vsem=v_rel,
                  wprep=p.c("wprep"), wprep1=p.c("wprep1"), castg=p.c("castg"), castg1=p.c("castg1"), dnw=p.c("dnw"),
                  patg=p.c("patg"), gkv=p.c("gkv"))
        for eng in ("sync", "tensor", "scalar", "vector", "gpsimd"):
            for s, v in pd.items():
                p.wait(eng, s, v)

        A_NQ = 0
        A_AF = 8192
        A_E = 16384
        A_KB = 24576
        A_VB = 26624

        pf_free = [("ssem", p.c("ssem"))] * 6
        pt_free = [("ssem", p.c("ssem"))] * 2
        wbuf_tok = [("tsem", p.c("tsem"))] * 4
        kv_tok = [("tsem", p.c("tsem"))] * 2
        e_tok = [p.c("tsem"), p.c("tsem")]

        def layer_norm(eps):
            v_in = p.c("vsem")
            p.wait("scalar", "vsem", v_in)
            eps_ap = stats[:, 121:122] if eps == LN_EPS else stats[:, 122:123]
            p.drain("vector")
            for st in range(NST):
                xs = x_b[:, st * E:(st + 1) * E]
                for j in range(4):
                    p.emit("vector", lambda e, xs=xs, st=st, j=j: e.bn_stats(
                        stats[:, st * 24 + j * 6:st * 24 + (j + 1) * 6],
                        xs[:, j * 512:(j + 1) * 512]))
            p.drain("vector")
            v_mu = 0
            for st in range(NST):
                v_mu = p.emit("vector", lambda e, st=st: e.bn_aggr(
                    stats[:, 96 + 2 * st:98 + 2 * st], stats[:, st * 24:(st + 1) * 24]),
                    inc="vsem")
            p.wait("scalar", "vsem", v_mu)
            var_v = stats[:, 97:105].rearrange("p (a b) -> p a b", b=2)[:, :, 0:1]
            ln_v = stats[:, 104:108].rearrange("p (a b) -> p a b", b=1)
            rs_v = stats[:, 108:112].rearrange("p (a b) -> p a b", b=1)
            p.drain("scalar")
            p.emit("scalar", lambda e, eps_ap=eps_ap: e.activation(ln_v, var_v, ACT.Ln, bias=eps_ap))
            p.drain("scalar")
            s_r = p.emit("scalar", lambda e: e.activation(
                rs_v, ln_v, ACT.Exp, scale=stats[:, 120:121]), inc="ssem")
            p.wait("vector", "ssem", s_r)
            v_a = 0
            for st in range(NST):
                xs = x_b[:, st * E:(st + 1) * E]
                v_a = p.emit("vector", lambda e, xs=xs, st=st: e.tensor_scalar(
                    tmA[:, st * E:(st + 1) * E], xs, stats[:, 96 + 2 * st:97 + 2 * st],
                    stats[:, 108 + st:109 + st], ALU.subtract, ALU.mult), inc="vsem")
            return v_a

        def transpose_tm_to_fm(src_off, scaled, v_ready):
            p.wait("tensor", "vsem", v_ready)
            s_last = 0
            for et in range(NET):
                bk = et % 2
                sem, val = pt_free[bk]
                p.wait("tensor", sem, val)
                t_last = 0
                for st in range(NST):
                    src = tmA if src_off is None else arena
                    off = (0 if src_off is None else src_off) + st * E + et * 128
                    t_last = tp(pt[bk][:, st * 128:(st + 1) * 128], src[:, off:off + 128])
                p.wait("scalar", "tsem", t_last)
                if scaled:
                    s_last = p.emit("scalar", lambda e, et=et, bk=bk: e.activation(
                        fmA[:, et * C:(et + 1) * C], pt[bk][:, 0:C],
                        ACT.Copy, scale=su_sb[:, et:et + 1]), inc="ssem")
                else:
                    s_last = p.emit("scalar", lambda e, et=et, bk=bk: e.copy(
                        fmA[:, et * C:(et + 1) * C], pt[bk][:, 0:C]), inc="ssem")
                pt_free[bk] = ("ssem", s_last)
            return s_last

        def fwht2048_fm():
            p.wait("tensor", "ssem", p.c("ssem"))
            for et in range(NET):
                bix = et % 2
                sem, val = pf_free[bix]
                p.wait("tensor", sem, val)
                t_s = mm(pf[bix][:, 0:C], h_sb[:], fmA[:, et * C:(et + 1) * C], True, True)
                p.wait("scalar", "tsem", t_s)
                s_s = p.emit("scalar", lambda e, et=et, bix=bix: e.copy(
                    fmA[:, et * C:(et + 1) * C], pf[bix][:, 0:C]), inc="ssem")
                pf_free[bix] = ("ssem", s_s)
            p.wait("vector", "ssem", p.c("ssem"))
            p.wait("vector", "tsem", p.c("tsem"))
            ready = [0] * NET
            for st in range(4):
                src_fm = (st % 2 == 0)
                for u in range(NET):
                    v = u ^ (1 << st)
                    if v < u:
                        continue
                    if src_fm:
                        iu, iv = fmA[:, u * C:(u + 1) * C], fmA[:, v * C:(v + 1) * C]
                        ou, ov = arena[:, u * C:(u + 1) * C], arena[:, v * C:(v + 1) * C]
                    else:
                        iu, iv = arena[:, u * C:(u + 1) * C], arena[:, v * C:(v + 1) * C]
                        ou, ov = fmA[:, u * C:(u + 1) * C], fmA[:, v * C:(v + 1) * C]
                    p.emit("vector", lambda e, iu=iu, iv=iv, ou=ou: e.tensor_add(ou, iu, iv))
                    last = (st == 3)
                    vv = p.emit("vector", lambda e, iu=iu, iv=iv, ov=ov: e.tensor_sub(ov, iu, iv),
                                inc="vsem" if last else None)
                    if last:
                        ready[u] = vv
                        ready[v] = vv
                p.drain("vector")
            return ready

        ow = [0]

        def ffn(fm_ready, write_ch=None):
            korder = sorted(range(NET), key=lambda k: fm_ready[k])
            wb = [wbuf[:, k * 4096:(k + 1) * 4096] for k in range(4)]
            last_v_g = [p.c("vsem")]
            for ib in range(32):
                ub, gb = wb[(ib % 2) * 2], wb[(ib % 2) * 2 + 1]
                for m_i, (buf, srcT) in enumerate(((ub, upT), (gb, gateT))):
                    slot = (ib % 2) * 2 + m_i
                    sem, val = wbuf_tok[slot]
                    p.wait("sync", sem, val)
                    p.emit("sync", lambda e, ib=ib, buf=buf, srcT=srcT: e.dma_start(
                        buf.rearrange("p (et c) -> p et c", c=256),
                        srcT[ib].rearrange("et p c -> p et c")), inc=f"sl{slot}", incby=16)
                    p.wait("tensor", f"sl{slot}", p.c(f"sl{slot}"))
                for il in range(2):
                    it = ib * 2 + il
                    pu_i, pg_i = (it % 2) * 2, (it % 2) * 2 + 1
                    for bix in (pu_i, pg_i):
                        sem, val = pf_free[bix]
                        p.wait("tensor", sem, val)
                    t_u = 0
                    for ki, et in enumerate(korder):
                        p.wait("tensor", "vsem", fm_ready[et])
                        t_u = mm(pf[pu_i][:, 0:C], ub[:, et * 256 + il * 128:et * 256 + (il + 1) * 128],
                                 fmA[:, et * C:(et + 1) * C], ki == 0, ki == NET - 1)
                    t_g = 0
                    for ki, et in enumerate(korder):
                        t_g = mm(pf[pg_i][:, 0:C], gb[:, et * 256 + il * 128:et * 256 + (il + 1) * 128],
                                 fmA[:, et * C:(et + 1) * C], ki == 0, ki == NET - 1)
                    if il == 1:
                        wbuf_tok[(ib % 2) * 2] = ("tsem", t_g)
                        wbuf_tok[(ib % 2) * 2 + 1] = ("tsem", t_g)
                    p.wait("scalar", "tsem", t_g)
                    p.wait("scalar", "vsem", last_v_g[0])
                    s_si = p.emit("scalar", lambda e, it=it, pg_i=pg_i: e.activation(
                        silu_t[:, 0:C], pf[pg_i][:, 0:C], ACT.Silu, scale=wscg_sb[:, it:it + 1]),
                        inc="ssem")
                    p.wait("vector", "ssem", s_si)
                    p.wait("vector", "tsem", t_u)
                    v_g = p.emit("vector", lambda e, it=it, pu_i=pu_i: e.scalar_tensor_tensor(
                        arena[:, it * C:(it + 1) * C], pf[pu_i][:, 0:C], wscu_su[:, it:it + 1],
                        silu_t[:, 0:C], ALU.mult, ALU.mult), inc="vsem")
                    pf_free[pu_i] = ("vsem", v_g)
                    pf_free[pg_i] = ("ssem", s_si)
                    last_v_g[0] = v_g
            v_g_all = p.c("vsem")
            db = [wbuf[:, k * 4096:(k + 1) * 4096] for k in range(4)]
            p.wait("tensor", "vsem", v_g_all)
            v_d = 0
            for ec in range(4):
                for st in range(NST):
                    sem, val = pf_free[st]
                    p.wait("tensor", sem, val)
                t_d = 0
                for kb in range(8):
                    bi = kb % 4
                    sem, val = wbuf_tok[bi]
                    p.wait("sync", sem, val)
                    wv = p.emit("sync", lambda e, ec=ec, kb=kb, b=db[bi]: e.dma_start(
                        b.rearrange("p (k c) -> p k c", c=512),
                        dnT[kb * 1024:(kb + 1) * 1024, ec * 512:(ec + 1) * 512]
                        .rearrange("(k p) c -> p k c", p=128)), inc=f"sl{bi}", incby=16)
                    p.wait("tensor", f"sl{bi}", wv)
                    for st in range(NST):
                        for k in range(8):
                            it = kb * 8 + k
                            t_d = mm(pf[st][:, 0:512],
                                     arena[:, it * C + st * 128:it * C + (st + 1) * 128],
                                     db[bi][:, k * 512:(k + 1) * 512],
                                     it == 0, it == NIT - 1)
                    wbuf_tok[bi] = ("tsem", t_d)
                p.wait("vector", "tsem", t_d)
                for st in range(NST):
                    p.drain("vector")
                    p.emit("vector", lambda e, st=st, ec=ec: e.tensor_mul(
                        sq[:, 0:512], pf[st][:, 0:512], wscb[:, ec * 512:(ec + 1) * 512]))
                    p.drain("vector")
                    v_d = p.emit("vector", lambda e, st=st, ec=ec: e.tensor_add(
                        x_b[:, st * E + ec * 512:st * E + (ec + 1) * 512],
                        x_b[:, st * E + ec * 512:st * E + (ec + 1) * 512],
                        sq[:, 0:512]), inc="vsem")
                    pf_free[st] = ("vsem", v_d)
                    if write_ch is not None and ec == 3:
                        p.wait("gpsimd", "vsem", v_d)
                        p.emit("gpsimd", lambda e, ch=write_ch, st=st: e.dma_start(
                            y_out[ch * C + st * 128:ch * C + (st + 1) * 128, :],
                            x_b[:, st * E:(st + 1) * E]), inc="osem", incby=16)
                        ow[0] += 1
            p.drain("vector")
            return v_d

        def rms_apply(src_ap_fn, dst_off, w_bcast, v_ready, out_is_nq, stats_done=False):
            p.wait("scalar", "vsem", v_ready)
            p.wait("vector", "vsem", v_ready)
            p.drain("vector")
            if not stats_done:
                for st in range(NST):
                    a_s = src_ap_fn(st)
                    for j in range(4):
                        p.emit("vector", lambda e, a_s=a_s, st=st, j=j: e.bn_stats(
                            stats[:, st * 24 + j * 6:st * 24 + (j + 1) * 6],
                            a_s[:, j * 512:(j + 1) * 512]))
                p.drain("vector")
            for st in range(NST):
                p.emit("vector", lambda e, st=st: e.bn_aggr(
                    stats[:, 96 + 2 * st:98 + 2 * st], stats[:, st * 24:(st + 1) * 24]))
            p.drain("vector")
            mean_v = stats[:, 96:104].rearrange("p (a b) -> p a b", b=2)[:, :, 0:1]
            var_v = stats[:, 97:105].rearrange("p (a b) -> p a b", b=2)[:, :, 0:1]
            msq_v = stats[:, 104:108].rearrange("p (a b) -> p a b", b=1)
            rs_v = stats[:, 108:112].rearrange("p (a b) -> p a b", b=1)
            v_ss = p.emit("vector", lambda e: e.tensor_tensor(
                msq_v, mean_v, mean_v, ALU.mult), inc="vsem")
            p.drain("vector")
            v_ss = p.emit("vector", lambda e: e.tensor_tensor(
                msq_v, msq_v, var_v, ALU.add), inc="vsem")
            p.wait("scalar", "vsem", v_ss)
            p.drain("scalar")
            p.emit("scalar", lambda e: e.activation(msq_v, msq_v, ACT.Ln, bias=stats[:, 122:123]))
            p.drain("scalar")
            s_r = p.emit("scalar", lambda e: e.activation(
                rs_v, msq_v, ACT.Exp, scale=stats[:, 120:121]), inc="ssem")
            p.wait("vector", "ssem", s_r)
            last = 0
            for st in range(NST):
                a_s = src_ap_fn(st)
                if out_is_nq:
                    last = p.emit("vector", lambda e, a_s=a_s, st=st: e.scalar_tensor_tensor(
                        arena[:, dst_off + st * E:dst_off + (st + 1) * E], a_s,
                        stats[:, 108 + st:109 + st], w_bcast[:], ALU.mult, ALU.mult), inc="vsem")
                else:
                    p.drain("vector")
                    p.emit("vector", lambda e, a_s=a_s, st=st: e.scalar_tensor_tensor(
                        sq[:], a_s, stats[:, 108 + st:109 + st], w_bcast[:], ALU.mult, ALU.mult))
                    p.drain("vector")
                    last = p.emit("vector", lambda e, st=st: e.tensor_add(
                        x_b[:, st * E:(st + 1) * E], x_b[:, st * E:(st + 1) * E], sq[:]), inc="vsem")
            return last

        def attn(v_nq_ready):
            s_tr = transpose_tm_to_fm(A_NQ, False, v_nq_ready)
            wb = [wbuf[:, k * 4096:(k + 1) * 4096] for k in range(2)]
            p.wait("tensor", "ssem", s_tr)
            for eb in range(8):
                bi = eb % 2
                sem, val = wbuf_tok[bi]
                p.wait("sync", sem, val)
                p.emit("sync", lambda e, eb=eb, b=wb[bi]: e.dma_start(
                    b.rearrange("p (et c) -> p et c", c=256),
                    winT[eb].rearrange("et p c -> p et c")), inc=f"sl{bi}", incby=16)
                p.wait("tensor", f"sl{bi}", p.c(f"sl{bi}"))
                for il in range(2):
                    ep = eb * 2 + il
                    bix = 4 + (ep % 2)
                    sem, val = pf_free[bix]
                    p.wait("tensor", sem, val)
                    t_p = 0
                    for et in range(NET):
                        t_p = mm(pf[bix][:, 0:C], wb[bi][:, et * 256 + il * 128:et * 256 + (il + 1) * 128],
                                 fmA[:, et * C:(et + 1) * C], et == 0, et == NET - 1)
                    if il == 1:
                        wbuf_tok[bi] = ("tsem", t_p)
                    p.wait("scalar", "tsem", t_p)
                    s_p = p.emit("scalar", lambda e, ep=ep, bix=bix: e.copy(
                        arena[:, A_NQ + ep * C:A_NQ + (ep + 1) * C], pf[bix][:, 0:C]), inc="ssem")
                    pf_free[bix] = ("ssem", s_p)
            p.wait("tensor", "ssem", p.c("ssem"))
            for h in range(H):
                bix = 4 + (h % 2)
                sem, val = pf_free[bix]
                p.wait("tensor", sem, val)
                t_q = mm(pf[bix][:, 0:C], wqT[:], arena[:, A_NQ + h * C:A_NQ + (h + 1) * C],
                         True, True)
                p.wait("scalar", "tsem", t_q)
                s_q = p.emit("scalar", lambda e, h=h, bix=bix: e.copy(
                    fmA[:, h * C:(h + 1) * C], pf[bix][:, 0:C]), inc="ssem")
                pf_free[bix] = ("ssem", s_q)
            p.wait("tensor", "ssem", p.c("ssem"))
            kv_free = kv_tok
            e_guard = e_tok
            s_e_done = [0] * H
            v_o = 0

            def emit_scores(h):
                sl = h % 2
                koff = A_KB + sl * 1024
                voff = A_VB + sl * 1088
                sem, val = kv_free[sl]
                p.wait("sync", sem, val)
                p.emit("sync", lambda e, h=h, koff=koff: e.dma_start(
                    arena[:, koff:koff + 1024], kT[h]), inc=f"kv{sl}", incby=16)
                w_kv = p.emit("sync", lambda e, h=h, voff=voff: e.dma_start(
                    arena[:, voff:voff + 1088], vT[h]), inc=f"kv{sl}", incby=16)
                p.wait("tensor", f"kv{sl}", w_kv)
                eoff = A_E + sl * 4096
                p.wait("scalar", "tsem", e_guard[sl])
                for mtq in range(0, NMT, 4):
                    t_sc = 0
                    for k in range(4):
                        mt = mtq + k
                        sem, val = pf_free[k]
                        p.wait("tensor", sem, val)
                        t_sc = mm(pf[k][:, 0:C], arena[:, koff + mt * 128:koff + (mt + 1) * 128],
                                  fmA[:, h * C:(h + 1) * C], True, True)
                    p.wait("scalar", "tsem", t_sc)
                    s_e = 0
                    for k in range(4):
                        mt = mtq + k
                        s_e = p.emit("scalar", lambda e, mt=mt, k=k, eoff=eoff: e.activation(
                            arena[:, eoff + mt * C:eoff + (mt + 1) * C], pf[k][:, 0:C], ACT.Exp),
                            inc="ssem")
                    for k in range(4):
                        pf_free[k] = ("ssem", s_e)
                    if mtq == 0 and h > 0:
                        emit_out(h - 1)
                s_e_done[h] = p.c("ssem")

            def emit_out(h):
                nonlocal v_o
                sl = h % 2
                voff = A_VB + sl * 1088
                eoff = A_E + sl * 4096
                p.wait("tensor", "ssem", s_e_done[h])
                t_o = 0
                for st in range(NST):
                    bix = 4 + st // 2
                    coff = (st % 2) * 256
                    if st % 2 == 0:
                        sem, val = pf_free[bix]
                        p.wait("tensor", sem, val)
                    for mt in range(NMT):
                        t_o = mm(pf[bix][:, coff:coff + VW],
                                 arena[:, eoff + mt * C + st * 128:eoff + mt * C + (st + 1) * 128],
                                 arena[:, voff + mt * VW:voff + (mt + 1) * VW],
                                 mt == 0, mt == NMT - 1)
                p.wait("vector", "tsem", t_o)
                p.drain("vector")
                for st in range(NST):
                    bix = 4 + st // 2
                    coff = (st % 2) * 256
                    p.emit("vector", lambda e, st=st, bix=bix, coff=coff: e.reciprocal(
                        stats[:, 116 + st:117 + st], pf[bix][:, coff + 128:coff + 129]))
                p.drain("vector")
                for st in range(NST):
                    bix = 4 + st // 2
                    coff = (st % 2) * 256
                    v_o = p.emit("vector", lambda e, h=h, st=st, bix=bix, coff=coff: e.tensor_scalar(
                        arena[:, A_AF + st * E + h * 128:A_AF + st * E + (h + 1) * 128],
                        pf[bix][:, coff:coff + 128], stats[:, 116 + st:117 + st], None, ALU.mult), inc="vsem")
                for st in (0, 2):
                    pf_free[4 + st // 2] = ("vsem", v_o)
                kv_free[sl] = ("tsem", p.c("tsem"))
                e_guard[sl] = p.c("tsem")
                if h % 4 == 3:
                    j = h // 4
                    p.drain("vector")
                    for st in range(NST):
                        p.emit("vector", lambda e, st=st, j=j: e.bn_stats(
                            stats[:, st * 24 + j * 6:st * 24 + (j + 1) * 6],
                            arena[:, A_AF + st * E + j * 512:
                                  A_AF + st * E + (j + 1) * 512]))

            for h in range(H):
                emit_scores(h)
                if h > 0:
                    emit_out(h - 1)
            emit_out(H - 1)
            return rms_apply(lambda st: arena[:, A_AF + st * E:A_AF + (st + 1) * E],
                             0, wret_b, v_o, out_is_nq=False, stats_done=True)

        for ch in range(NCH):
            g_in = 0
            for st in range(NST):
                if ch > 0:
                    p.wait("sync", "osem", (4 * (ch - 1) + st + 1) * 16)
                g_in = p.emit("sync", lambda e, ch=ch, st=st: e.dma_start(
                    x_b[:, st * E:(st + 1) * E],
                    x_in[ch * C + st * 128:ch * C + (st + 1) * 128, :]), inc="gsem", incby=16)
            p.wait("vector", "gsem", g_in)
            p.wait("scalar", "gsem", g_in)
            p.wait("vector", "tsem", p.c("tsem"))

            v_ln = layer_norm(LN_EPS)
            transpose_tm_to_fm(None, True, v_ln)
            fm_ready = fwht2048_fm()
            ffn(fm_ready)

            v_ln2 = layer_norm(LN_EPS)
            v_y = 0
            for st in range(NST):
                p.drain("vector")
                v_y = p.emit("vector", lambda e, st=st: e.tensor_add(
                    x_b[:, st * E:(st + 1) * E], x_b[:, st * E:(st + 1) * E],
                    tmA[:, st * E:(st + 1) * E]), inc="vsem")
            p.wait("vector", "tsem", p.c("tsem"))
            rs_v = stats[:, 108:112].rearrange("p (a b) -> p a b", b=1)
            var_v = stats[:, 97:105].rearrange("p (a b) -> p a b", b=2)[:, :, 0:1]
            w1_v = stats[:, 64:68].rearrange("p (a b) -> p a b", b=1)
            u_v = stats[:, 68:72].rearrange("p (a b) -> p a b", b=1)
            p.drain("vector")
            p.emit("vector", lambda e: e.tensor_tensor(w1_v, rs_v, rs_v, ALU.mult))
            p.drain("vector")
            v_w1 = p.emit("vector", lambda e: e.tensor_tensor(w1_v, w1_v, var_v, ALU.mult), inc="vsem")
            p.wait("scalar", "vsem", v_w1)
            p.drain("scalar")
            p.emit("scalar", lambda e: e.activation(u_v, w1_v, ACT.Ln, bias=stats[:, 122:123]))
            p.drain("scalar")
            s_u = p.emit("scalar", lambda e: e.activation(
                u_v, u_v, ACT.Exp, scale=stats[:, 120:121]), inc="ssem")
            p.wait("vector", "ssem", s_u)
            p.wait("vector", "vsem", v_y)
            v_nq = 0
            for st in range(NST):
                v_nq = p.emit("vector", lambda e, st=st: e.scalar_tensor_tensor(
                    arena[:, A_NQ + st * E:A_NQ + (st + 1) * E], tmA[:, st * E:(st + 1) * E],
                    stats[:, 68 + st:69 + st], wnq_b[:], ALU.mult, ALU.mult), inc="vsem")
            attn(v_nq)

            v_ln3 = layer_norm(LN_EPS)
            transpose_tm_to_fm(None, True, v_ln3)
            fm_ready3 = fwht2048_fm()
            ffn(fm_ready3, write_ch=ch)

        total_out = ow[0] * 16

        with nc.Block() as block:
            @block.sync
            def _(eng):
                for fn in p.plan["sync"]:
                    fn(eng)

            @block.tensor
            def _(eng):
                for fn in p.plan["tensor"]:
                    fn(eng)

            @block.scalar
            def _(eng):
                for fn in p.plan["scalar"]:
                    fn(eng)

            @block.vector
            def _(eng):
                for fn in p.plan["vector"]:
                    fn(eng)

            @block.gpsimd
            def _(eng):
                for fn in p.plan["gpsimd"]:
                    fn(eng)
                eng.wait_ge(p.sems["osem"], total_out)

    return nc


_NC_CACHE = None


def kernel(L0_attn_query_input, L0_attn_W_in, L0_attn_W_q, L0_attn_W_k, L0_attn_W_v,
           L0_attn_stored_patterns, L0_attn_norm_query_w, L0_attn_norm_retrieved_w,
           L0_attn_beta, L0_ffn_SU_ug, L0_ffn_SU_dp, L0_ffn_Wscale_ug, L0_ffn_Wscale_dp,
           L0_ffn_W_up, L0_ffn_W_gate, L0_ffn_W_down):
    global _NC_CACHE
    x = np.ascontiguousarray(np.asarray(L0_attn_query_input, dtype=np.float32).reshape(16384, E))
    consts = dict(
        h128=np.ascontiguousarray(_hadamard(128).astype(BFNP)),
        ident=np.ascontiguousarray(np.eye(128, dtype=np.float32).astype(BFNP)),
        ones1=np.ones((1, 128), dtype=BFNP),
        ones1f=np.ones((1, 128), dtype=np.float32),
    )
    weights = dict(
        w_in=np.ascontiguousarray(np.asarray(L0_attn_W_in, dtype=np.float32)),
        w_q=np.ascontiguousarray(np.asarray(L0_attn_W_q, dtype=np.float32)),
        w_k=np.ascontiguousarray(np.asarray(L0_attn_W_k, dtype=np.float32)),
        w_v=np.ascontiguousarray(np.asarray(L0_attn_W_v, dtype=np.float32)),
        pats=np.ascontiguousarray(np.asarray(L0_attn_stored_patterns, dtype=np.float32)),
        nqw=np.ascontiguousarray(np.asarray(L0_attn_norm_query_w, dtype=np.float32)),
        nrw=np.ascontiguousarray(np.asarray(L0_attn_norm_retrieved_w, dtype=np.float32)),
        beta=np.asarray(L0_attn_beta, dtype=np.float32).reshape(1),
        su_ug=np.ascontiguousarray(np.asarray(L0_ffn_SU_ug, dtype=np.float32)),
        su_dp=np.ascontiguousarray(np.asarray(L0_ffn_SU_dp, dtype=np.float32)),
        wsc_ug=np.ascontiguousarray(np.asarray(L0_ffn_Wscale_ug, dtype=np.float32)),
        wsc_dp=np.ascontiguousarray(np.asarray(L0_ffn_Wscale_dp, dtype=np.float32)),
        w_up=np.ascontiguousarray(np.asarray(L0_ffn_W_up, dtype=np.float32)),
        w_gate=np.ascontiguousarray(np.asarray(L0_ffn_W_gate, dtype=np.float32)),
        w_down=np.ascontiguousarray(np.asarray(L0_ffn_W_down, dtype=np.float32)),
    )
    if _NC_CACHE is None:
        _NC_CACHE = build_nc()
    nc = _NC_CACHE
    in_maps = []
    for c in range(NCORES):
        m = {"x": np.ascontiguousarray(x[c * TCORE:(c + 1) * TCORE])}
        m.update(weights)
        m.update(consts)
        in_maps.append(m)
    res = run_bass_kernel_spmd(nc, in_maps, core_ids=list(range(NCORES)))
    out = np.concatenate([res.results[c]["y"] for c in range(NCORES)], axis=0)
    return np.ascontiguousarray(out.reshape(1, 16384, E).astype(np.float32))


if __name__ == "__main__":
    build_nc()
    print("built ok")


# version 20
# speedup vs baseline: 1.1976x; 1.0431x over previous
import numpy as np
import ml_dtypes
from contextlib import ExitStack

import concourse.bass as bass
import concourse.mybir as mybir
from concourse.bass_utils import run_bass_kernel_spmd

BFNP = ml_dtypes.bfloat16
F32 = mybir.dt.float32
BF16 = mybir.dt.bfloat16
AX = mybir.AxisListType.X
ALU = mybir.AluOpType
ACT = mybir.ActivationFunctionType

NCORES = 8
TCORE = 2048
C = 512
NCH = TCORE // C
NST = C // 128
E = 2048
I = 8192
NET = E // 128
NIT = I // 128
H = 16
D = 128
M = 1024
NMT = M // 128
VW = 136
RS2048 = float(1.0 / np.sqrt(2048.0))
RS8192 = float(1.0 / np.sqrt(8192.0))
LN_EPS = 1e-5
RMS_EPS = 1e-6

ENG_SEM = {"tensor": "tsem", "scalar": "ssem", "vector": "vsem"}


def _hadamard(n):
    h = np.array([[1.0]], dtype=np.float64)
    while h.shape[0] < n:
        h = np.block([[h, h], [h, -h]])
    return h


class P:

    def __init__(self):
        self.plan = {k: [] for k in ("sync", "tensor", "scalar", "vector", "gpsimd")}
        self.cnt = {}
        self.lastw = {}
        self.sems = {}

    def emit(self, eng, build, inc=None, incby=1):
        if inc is not None:
            self.cnt[inc] = self.cnt.get(inc, 0) + incby

            def fn(e, b=build, s=inc, n=incby):
                b(e).then_inc(self.sems[s], n)
        else:
            def fn(e, b=build):
                b(e)
        self.plan[eng].append(fn)
        return self.cnt.get(inc, 0)

    def wait(self, eng, sem, val):
        if not val or val <= 0:
            return
        key = (eng, sem)
        if self.lastw.get(key, 0) >= val:
            return
        self.lastw[key] = val
        self.plan[eng].append(lambda e, s=sem, v=val: e.wait_ge(self.sems[s], v))

    def drain(self, eng):
        self.plan[eng].append(lambda e: e.drain())

    def release(self, eng):
        sem = ENG_SEM[eng]
        return self.emit(eng, lambda e: e.drain(), inc=sem)

    def c(self, sem):
        return self.cnt.get(sem, 0)


def build_nc():
    nc = bass.Bass("TRN2", target_bir_lowering=False)
    p = P()

    x_in = nc.dram_tensor("x", [TCORE, E], F32, kind="ExternalInput").ap()
    w_in = nc.dram_tensor("w_in", [E, E], F32, kind="ExternalInput").ap()
    w_q = nc.dram_tensor("w_q", [D, D], F32, kind="ExternalInput").ap()
    w_k = nc.dram_tensor("w_k", [D, D], F32, kind="ExternalInput").ap()
    w_v = nc.dram_tensor("w_v", [D, D], F32, kind="ExternalInput").ap()
    pats = nc.dram_tensor("pats", [H, M, D], F32, kind="ExternalInput").ap()
    nqw = nc.dram_tensor("nqw", [E], F32, kind="ExternalInput").ap()
    nrw = nc.dram_tensor("nrw", [E], F32, kind="ExternalInput").ap()
    beta = nc.dram_tensor("beta", [1], F32, kind="ExternalInput").ap()
    su_ug = nc.dram_tensor("su_ug", [E], F32, kind="ExternalInput").ap()
    su_dp = nc.dram_tensor("su_dp", [I], F32, kind="ExternalInput").ap()
    wsc_ug = nc.dram_tensor("wsc_ug", [2 * I], F32, kind="ExternalInput").ap()
    wsc_dp = nc.dram_tensor("wsc_dp", [E], F32, kind="ExternalInput").ap()
    w_up = nc.dram_tensor("w_up", [I, E], F32, kind="ExternalInput").ap()
    w_gate = nc.dram_tensor("w_gate", [I, E], F32, kind="ExternalInput").ap()
    w_down = nc.dram_tensor("w_down", [E, I], F32, kind="ExternalInput").ap()
    h128 = nc.dram_tensor("h128", [128, 128], BF16, kind="ExternalInput").ap()
    ident = nc.dram_tensor("ident", [128, 128], BF16, kind="ExternalInput").ap()
    ones1 = nc.dram_tensor("ones1", [1, 128], BF16, kind="ExternalInput").ap()
    ones1f = nc.dram_tensor("ones1f", [1, 128], F32, kind="ExternalInput").ap()

    y_out = nc.dram_tensor("y", [TCORE, E], F32, kind="ExternalOutput").ap()

    upT = nc.dram_tensor("upT", [32, NET, 128, 256], BF16, kind="Internal").ap()
    gateT = nc.dram_tensor("gateT", [32, NET, 128, 256], BF16, kind="Internal").ap()
    winT = nc.dram_tensor("winT", [8, NET, 128, 256], BF16, kind="Internal").ap()
    dnT = nc.dram_tensor("dnT", [I, E], BF16, kind="Internal").ap()
    kT = nc.dram_tensor("kT", [H, 128, M], BF16, kind="Internal").ap()
    vT = nc.dram_tensor("vT", [H, 128, NMT * VW], BF16, kind="Internal").ap()

    with ExitStack() as ctx:
        for s in ("wsem", "gsem", "tsem", "ssem", "vsem", "osem",
                  "wprep", "wprep1", "castg", "castg1", "castg2", "bsem", "dnw", "patg", "gkv",
                  "sl0", "sl1", "sl2", "sl3", "kv0", "kv1"):
            p.sems[s] = ctx.enter_context(nc.semaphore(s))

        x_b = ctx.enter_context(nc.sbuf_tensor("x_b", [128, NST * E], F32))
        tmA = ctx.enter_context(nc.sbuf_tensor("tmA", [128, NST * E], BF16))
        fmA = ctx.enter_context(nc.sbuf_tensor("fmA", [128, NET * C], BF16))
        fwtmp = ctx.enter_context(nc.sbuf_tensor("fwtmp", [128, C], BF16))
        arena = ctx.enter_context(nc.sbuf_tensor("arena", [128, NIT * C], BF16))
        wbuf = ctx.enter_context(nc.sbuf_tensor("wbuf", [128, 4 * 4096], BF16))
        sq = ctx.enter_context(nc.sbuf_tensor("sq", [128, E], F32))
        silu_t = ctx.enter_context(nc.sbuf_tensor("silu_t", [128, C], BF16))
        stats = ctx.enter_context(nc.sbuf_tensor("stats", [128, 128], F32))
        h_sb = ctx.enter_context(nc.sbuf_tensor("h_sb", [128, 128], BF16))
        id_sb = ctx.enter_context(nc.sbuf_tensor("id_sb", [128, 128], BF16))
        on_sb = ctx.enter_context(nc.sbuf_tensor("on_sb", [1, 128], BF16))
        onf_sb = ctx.enter_context(nc.sbuf_tensor("onf_sb", [1, 128], F32))
        su_sb = ctx.enter_context(nc.sbuf_tensor("su_sb", [128, NET], F32))
        wscu_su = ctx.enter_context(nc.sbuf_tensor("wscu_su", [128, NIT], F32))
        wscg_sb = ctx.enter_context(nc.sbuf_tensor("wscg_sb", [128, NIT], F32))
        wscb = ctx.enter_context(nc.sbuf_tensor("wscb", [128, E], F32))
        wnq_b = ctx.enter_context(nc.sbuf_tensor("wnq_b", [128, E], BF16))
        wret_b = ctx.enter_context(nc.sbuf_tensor("wret_b", [128, E], BF16))
        wqT = ctx.enter_context(nc.sbuf_tensor("wqT", [128, 128], BF16))
        wkT = ctx.enter_context(nc.sbuf_tensor("wkT", [128, 128], BF16))
        wvT = ctx.enter_context(nc.sbuf_tensor("wvT", [128, 128], BF16))
        bsig = ctx.enter_context(nc.sbuf_tensor("bsig", [128, 4], F32))

        pf = [ctx.enter_context(nc.psum_tensor(f"pf{i}", [128, 512], F32)) for i in range(6)]
        pt = [ctx.enter_context(nc.psum_tensor(f"pt{i}", [128, 1024], BF16)) for i in range(2)]

        pf_free = [("ssem", 0)] * 6
        pt_free = [("ssem", 0)] * 2

        def mm(out_ap, lhsT_ap, rhs_ap, start, stop):
            return p.emit(
                "tensor",
                lambda e, o=out_ap, l=lhsT_ap, r=rhs_ap, st=start, sp=stop:
                    e.matmul(o, l, r, start=st, stop=sp),
                inc="tsem")

        def tp(out_ap, in_ap):
            return p.emit(
                "tensor",
                lambda e, o=out_ap, i_=in_ap: e.transpose(o, i_, id_sb[:]),
                inc="tsem")

        def sload(dst, src, slow=False):
            return p.emit("sync", lambda e, d=dst, s=src, sl=slow: e.dma_start(
                d, s, allow_slow_non_contiguous=sl), inc="wsem", incby=16)

        sload(h_sb[:], h128)
        sload(id_sb[:], ident)
        sload(on_sb[:], ones1)
        sload(onf_sb[:], ones1f)
        sload(su_sb[:], su_ug.rearrange("(a b) -> b a", b=128), slow=True)
        sload(wscu_su[:], wsc_ug[0:I].rearrange("(a b) -> b a", b=128), slow=True)
        sload(wscg_sb[:], wsc_ug[I:2 * I].rearrange("(a b) -> b a", b=128), slow=True)
        sload(stats[0:1, 112:113], beta.rearrange("(a b) -> a b", b=1))
        sload(x_b[0:1, 0:E], nqw.rearrange("(a b) -> a b", a=1))
        sload(x_b[0:1, E:2 * E], nrw.rearrange("(a b) -> a b", a=1))
        sload(x_b[0:1, 2 * E:3 * E], wsc_dp.rearrange("(a b) -> a b", a=1))
        sload(sq[:, 1024:1024 + NIT], su_dp.rearrange("(a b) -> b a", b=128), slow=True)
        sload(sq[:, 1200:1328], w_q)
        sload(sq[:, 1328:1456], w_k)
        w_all = sload(sq[:, 1456:1584], w_v)

        p.emit("vector", lambda e: e.memset(stats[:, 120:121], -0.5))
        p.emit("vector", lambda e: e.memset(stats[:, 121:122], float(LN_EPS)))
        p.emit("vector", lambda e: e.memset(stats[:, 122:123], float(RMS_EPS)))
        p.emit("vector", lambda e: e.memset(stats[:, 123:124], -1.0))
        v_consts = p.release("vector")
        p.wait("vector", "wsem", w_all)
        p.emit("vector", lambda e: e.tensor_scalar_mul(su_sb[:], su_sb[:], RS2048))
        p.emit("vector", lambda e: e.tensor_mul(wscu_su[:], wscu_su[:], sq[:, 1024:1024 + NIT]))
        p.wait("scalar", "wsem", w_all)
        p.wait("scalar", "vsem", v_consts)
        p.emit("scalar", lambda e: e.activation(
            stats[0:1, 113:114], stats[0:1, 112:113], ACT.Exp, scale=stats[0:1, 123:124]))
        p.emit("scalar", lambda e: e.mul(x_b[0:1, 2 * E:3 * E], x_b[0:1, 2 * E:3 * E], 0.5 * RS8192))
        s_rows0 = p.release("scalar")
        p.wait("vector", "ssem", s_rows0)
        p.emit("vector", lambda e: e.tensor_scalar_add(stats[0:1, 114:115], stats[0:1, 113:114], 1.0))
        p.drain("vector")
        p.emit("vector", lambda e: e.reciprocal(stats[0:1, 115:116], stats[0:1, 114:115]))
        p.emit("vector", lambda e: e.tensor_copy(tmA[0:1, 0:E], x_b[0:1, 0:E]))
        p.emit("vector", lambda e: e.tensor_copy(tmA[:, 2 * E:2 * E + 128], sq[:, 1200:1328]))
        p.emit("vector", lambda e: e.tensor_copy(tmA[:, 2 * E + 128:2 * E + 256], sq[:, 1328:1456]))
        p.emit("vector", lambda e: e.tensor_copy(tmA[:, 2 * E + 256:2 * E + 384], sq[:, 1456:1584]))
        p.emit("vector", lambda e: e.tensor_copy(tmA[0:1, E:2 * E], x_b[0:1, E:2 * E]))
        v_prep0 = p.release("vector")

        p.wait("tensor", "vsem", v_prep0)
        p.wait("tensor", "ssem", s_rows0)
        for j in range(4):
            mm(pf[0][:, 0:512], on_sb[:], tmA[0:1, j * 512:(j + 1) * 512], True, True)
            mm(pf[1][:, 0:512], on_sb[:], tmA[0:1, E + j * 512:E + (j + 1) * 512], True, True)
            t_b = mm(pf[2][:, 0:512], onf_sb[:], x_b[0:1, 2 * E + j * 512:2 * E + (j + 1) * 512], True, True)
            p.wait("scalar", "tsem", t_b)
            p.emit("scalar", lambda e, j=j: e.copy(wnq_b[:, j * 512:(j + 1) * 512], pf[0][:, 0:512]))
            p.emit("scalar", lambda e, j=j: e.copy(wret_b[:, j * 512:(j + 1) * 512], pf[1][:, 0:512]))
            s_b = p.emit("scalar", lambda e, j=j: e.copy(wscb[:, j * 512:(j + 1) * 512], pf[2][:, 0:512]), inc="ssem")
            p.wait("tensor", "ssem", s_b)
        t_bs = mm(pf[3][:, 0:1], onf_sb[:], stats[0:1, 115:116], True, True)
        p.wait("scalar", "tsem", t_bs)
        s_bs = p.emit("scalar", lambda e: e.copy(bsig[:, 0:1], pf[3][:, 0:1]), inc="ssem")

        tp(pt[0][:, 0:128], tmA[:, 2 * E:2 * E + 128])
        tp(pt[0][:, 128:256], tmA[:, 2 * E + 128:2 * E + 256])
        t_w3 = tp(pt[0][:, 256:384], tmA[:, 2 * E + 256:2 * E + 384])
        p.wait("vector", "tsem", t_w3)
        p.wait("vector", "ssem", s_bs)
        p.emit("vector", lambda e: e.tensor_scalar_mul(wqT[:], pt[0][:, 0:128], bsig[:, 0:1]))
        p.emit("vector", lambda e: e.tensor_copy(wkT[:], pt[0][:, 128:256]))
        v_w3 = p.emit("vector", lambda e: e.tensor_copy(wvT[:], pt[0][:, 256:384]), inc="vsem")
        pt_free[0] = ("vsem", v_w3)
        pf_free[3] = ("ssem", s_bs)
        xb_free_t = p.c("tsem")

        pat_guard = [0, 0]
        for h in range(H):
            sl = h % 2
            poff = sl * 1024
            toff = 2048 + sl * 1024
            p.wait("gpsimd", "tsem", pat_guard[sl])
            g_pat = p.emit(
                "gpsimd",
                lambda e, h=h, poff=poff: e.dma_start(
                    fmA[:, poff:poff + 1024].rearrange("p (mt d) -> p mt d", d=128),
                    pats[h].rearrange("(mt p) d -> p mt d", p=128)),
                inc="patg", incby=16)
            p.wait("tensor", "patg", g_pat)
            for eg in range(2):
                bk = eg % 2
                sem, val = pt_free[bk]
                p.wait("tensor", sem, val)
                t_last = 0
                for k in range(8):
                    mt = eg * 8 + k
                    if mt >= 8:
                        break
                    t_last = tp(pt[bk][:, k * 128:(k + 1) * 128],
                                fmA[:, poff + mt * 128:poff + (mt + 1) * 128])
                if eg == 0:
                    p.wait("scalar", "tsem", t_last)
                    sv = p.emit("scalar",
                                lambda e, toff=toff, bk=bk: e.copy(
                                    fmA[:, toff:toff + 1024], pt[bk][:, 0:1024]),
                                inc="ssem")
                    pt_free[bk] = ("ssem", sv)
                    break
            s_patT = p.c("ssem")
            for bix in (0, 1):
                sem, val = pf_free[bix]
                p.wait("tensor", sem, val)
            p.wait("tensor", "ssem", s_patT)
            mm(pf[0][:, 0:512], wkT[:], fmA[:, toff:toff + 512], True, True)
            t_k = mm(pf[1][:, 0:512], wkT[:], fmA[:, toff + 512:toff + 1024], True, True)
            p.wait("vector", "tsem", t_k)
            p.drain("vector")
            p.emit("vector", lambda e: e.reduce_sum(stats[:, 0:1], pf[0][:, 0:512], axis=AX))
            p.emit("vector", lambda e: e.reduce_sum(stats[:, 1:2], pf[1][:, 0:512], axis=AX))
            p.drain("vector")
            p.emit("vector", lambda e: e.tensor_add(stats[:, 2:3], stats[:, 0:1], stats[:, 1:2]))
            p.drain("vector")
            p.emit("vector", lambda e: e.tensor_scalar_mul(stats[:, 3:4], stats[:, 2:3], 1.0 / M))
            p.drain("vector")
            p.emit("vector", lambda e: e.tensor_scalar(
                arena[:, 0:512], pf[0][:, 0:512], stats[:, 3:4], None, ALU.subtract), inc="vsem")
            v_k = p.emit("vector", lambda e: e.tensor_scalar(
                arena[:, 512:1024], pf[1][:, 0:512], stats[:, 3:4], None, ALU.subtract), inc="vsem")
            pf_free[0] = ("vsem", v_k)
            pf_free[1] = ("vsem", v_k)
            p.wait("gpsimd", "vsem", v_k)
            g_k = p.emit("gpsimd", lambda e, h=h: e.dma_start(kT[h], arena[:, 0:1024]),
                         inc="gkv", incby=16)
            p.drain("vector")
            vz = p.emit("vector", lambda e: e.memset(arena[:, 1024:1024 + NMT * VW], 0.0), inc="vsem")
            p.drain("vector")
            ones_ap = arena[:, 1024:1024 + NMT * VW].rearrange(
                "p (mt c) -> p mt c", c=VW)[:, :, 128:129]
            vm_last = p.emit("vector", lambda e, ones_ap=ones_ap: e.memset(ones_ap, 1.0), inc="vsem")
            p.wait("scalar", "vsem", vm_last)
            t_v = 0
            for mtq in range(0, 8, 4):
                for k in range(4):
                    mt = mtq + k
                    sem, val = pf_free[k]
                    p.wait("tensor", sem, val)
                    t_v = mm(pf[k][:, 0:128], fmA[:, toff + mt * 128:toff + (mt + 1) * 128],
                             wvT[:], True, True)
                p.wait("scalar", "tsem", t_v)
                s_v = 0
                for k in range(4):
                    mt = mtq + k
                    s_v = p.emit("scalar", lambda e, mt=mt, k=k: e.copy(
                        arena[:, 1024 + mt * VW:1024 + mt * VW + 128], pf[k][:, 0:128]), inc="ssem")
                for k in range(4):
                    pf_free[k] = ("ssem", s_v)
            p.wait("gpsimd", "ssem", p.c("ssem"))
            g_v = p.emit("gpsimd", lambda e, h=h: e.dma_start(vT[h], arena[:, 1024:1024 + NMT * VW]),
                         inc="gkv", incby=16)
            pat_guard[sl] = p.c("tsem")
            p.wait("vector", "gkv", p.c("gkv"))
            p.wait("scalar", "gkv", p.c("gkv"))

        p.wait("gpsimd", "gkv", p.c("gkv"))
        wT_off = [12288, 4096]
        cast_guard = [0, 0]
        write_guard = [0, 0]

        def prep_wT(w_src, dst, nib):
            for ib in range(nib):
                sset = ib % 2
                base = 0 if sset == 0 else 8192
                woff = wT_off[sset]
                wT_view = arena[:, woff:woff + 4096].rearrange(
                    "p (g two c) -> p g two c", two=2, c=128)
                csem = "castg" if sset == 0 else "castg1"
                p.wait("gpsimd", "tsem", cast_guard[sset])
                p.emit(
                    "gpsimd",
                    lambda e, ib=ib, base=base, w=w_src: e.dma_start(
                        arena[:, base:base + 4096].rearrange("p (il c) -> p il c", c=2048),
                        w[ib * 256:(ib + 1) * 256, :].rearrange("(il p) c -> p il c", p=128)),
                    inc=csem, incby=16)
                g_c = p.c(csem)
                p.wait("tensor", csem, g_c)
                for il in range(2):
                    coff = base + il * 2048
                    for eg in range(2):
                        bk = eg % 2
                        sem, val = pt_free[bk]
                        p.wait("tensor", sem, val)
                        t_last = 0
                        for k in range(8):
                            et = eg * 8 + k
                            t_last = tp(pt[bk][:, k * 128:(k + 1) * 128],
                                        arena[:, coff + et * 128:coff + (et + 1) * 128])
                        p.wait("scalar", "tsem", t_last)
                        p.wait("scalar", "wprep" if sset == 0 else "wprep1", write_guard[sset])
                        s_t = p.emit(
                            "scalar",
                            lambda e, eg=eg, il=il, bk=bk, wT_view=wT_view: e.copy(
                                wT_view[:, eg * 8:(eg + 1) * 8, il, :], pt[bk][:, 0:1024]),
                            inc="ssem")
                        pt_free[bk] = ("ssem", s_t)
                cast_guard[sset] = p.c("tsem")
                p.wait("sync", "ssem", p.c("ssem"))
                write_guard[sset] = p.emit(
                    "sync",
                    lambda e, ib=ib, d=dst, woff=woff: e.dma_start(
                        d[ib].rearrange("et p c -> p et c"),
                        arena[:, woff:woff + 4096].rearrange("p (et c) -> p et c", c=256)),
                    inc="wprep" if sset == 0 else "wprep1", incby=16)

        prep_wT(w_up, upT, 32)
        prep_wT(w_gate, gateT, 32)
        prep_wT(w_in, winT, 8)

        p3_cast_guard = p.c("tsem")
        p3_z_guard = p.c("dnw")
        p.wait("gpsimd", "wprep", p.c("wprep"))
        p.wait("gpsimd", "wprep1", p.c("wprep1"))
        p.wait("vector", "wprep", p.c("wprep"))
        p.wait("vector", "wprep1", p.c("wprep1"))
        p.wait("sync", "tsem", xb_free_t)
        p.wait("sync", "vsem", v_prep0)
        xstage = [("vsem", 0), ("vsem", 0)]

        def pong_ap(it):
            if it < 32:
                return fmA[:, it * 256:(it + 1) * 256]
            return tmA[:, (it - 32) * 256:(it - 31) * 256]

        for eb in range(8):
            p.wait("gpsimd", "tsem", p3_cast_guard)
            p.emit(
                "gpsimd",
                lambda e, eb=eb: e.dma_start(
                    arena[:, 0:8192],
                    w_down[eb * 256:eb * 256 + 128, :]),
                inc="castg", incby=16)
            g_c = p.c("castg")
            v_c = 0
            for hh in range(2):
                sem, val = xstage[hh]
                p.wait("sync", sem, val)
                p.emit(
                    "sync",
                    lambda e, eb=eb, hh=hh: e.dma_start(
                        x_b[:, hh * 4096:(hh + 1) * 4096],
                        w_down[eb * 256 + 128:eb * 256 + 256,
                               hh * 4096:(hh + 1) * 4096]),
                    inc="castg2", incby=16)
                lt = p.c("castg2")
                p.wait("gpsimd", "castg2", lt)
                p.wait("gpsimd", "tsem", p3_cast_guard)
                v_c = p.emit("gpsimd", lambda e, hh=hh: e.tensor_copy(
                    arena[:, 8192 + hh * 4096:8192 + (hh + 1) * 4096],
                    x_b[:, hh * 4096:(hh + 1) * 4096]), inc="bsem")
                xstage[hh] = ("bsem", v_c)
            p.wait("tensor", "castg", g_c)
            p.wait("tensor", "bsem", v_c)
            for tg in range(16):
                bk = tg % 2
                sem, val = pt_free[bk]
                p.wait("tensor", sem, val)
                t_last = 0
                for k in range(8):
                    tix = tg * 8 + k
                    it, j = tix // 2, tix % 2
                    t_last = tp(pt[bk][:, k * 128:(k + 1) * 128],
                                arena[:, j * 8192 + it * 128:j * 8192 + (it + 1) * 128])
                p.wait("scalar", "tsem", t_last)
                p.wait("scalar", "dnw", p3_z_guard)
                s_t = p.emit(
                    "scalar",
                    lambda e, tg=tg, bk=bk: e.copy(
                        arena[:, 16384 + tg * 1024:16384 + (tg + 1) * 1024], pt[bk][:, 0:1024]),
                    inc="ssem")
                pt_free[bk] = ("ssem", s_t)
            p3_cast_guard = p.c("tsem")
            p.wait("tensor", "ssem", p.c("ssem"))
            for itq in range(0, NIT, 4):
                t_s = 0
                for k in range(4):
                    it = itq + k
                    sem, val = pf_free[k]
                    p.wait("tensor", sem, val)
                    t_s = mm(pf[k][:, 0:256], h_sb[:],
                             arena[:, 16384 + it * 256:16384 + (it + 1) * 256],
                             True, True)
                p.wait("scalar", "tsem", t_s)
                s_s = 0
                for k in range(4):
                    it = itq + k
                    s_s = p.emit("scalar", lambda e, it=it, k=k: e.copy(
                        arena[:, 16384 + it * 256:16384 + (it + 1) * 256], pf[k][:, 0:256]),
                        inc="ssem")
                for k in range(4):
                    pf_free[k] = ("ssem", s_s)
            p.wait("vector", "ssem", p.c("ssem"))
            p.wait("vector", "dnw", p3_z_guard)
            for st in range(6):
                from_z = (st % 2 == 0)
                for u in range(NIT):
                    v = u ^ (1 << st)
                    if v < u:
                        continue
                    if from_z:
                        iu = arena[:, 16384 + u * 256:16384 + (u + 1) * 256]
                        iv = arena[:, 16384 + v * 256:16384 + (v + 1) * 256]
                        ou, ov = pong_ap(u), pong_ap(v)
                    else:
                        iu, iv = pong_ap(u), pong_ap(v)
                        ou = arena[:, 16384 + u * 256:16384 + (u + 1) * 256]
                        ov = arena[:, 16384 + v * 256:16384 + (v + 1) * 256]
                    p.emit("vector", lambda e, iu=iu, iv=iv, ou=ou: e.tensor_add(ou, iu, iv))
                    p.emit("vector", lambda e, iu=iu, iv=iv, ov=ov: e.tensor_sub(ov, iu, iv))
                p.drain("vector")
            v_bt = p.release("vector")
            p.wait("gpsimd", "vsem", v_bt)
            for grp in range(8):
                p.emit(
                    "gpsimd",
                    lambda e, eb=eb, grp=grp: e.dma_start(
                        dnT[grp * 1024:(grp + 1) * 1024, eb * 256:(eb + 1) * 256]
                        .rearrange("(k p) c -> p k c", p=128),
                        arena[:, 16384 + grp * 2048:16384 + (grp + 1) * 2048]
                        .rearrange("p (k c) -> p k c", c=256)),
                    inc="dnw", incby=16)
            p3_z_guard = p.c("dnw")

        t_rel = p.release("tensor")
        s_rel = p.release("scalar")
        v_rel = p.release("vector")
        pd = dict(wsem=p.c("wsem"), gsem=p.c("gsem"), tsem=t_rel, ssem=s_rel, vsem=v_rel,
                  wprep=p.c("wprep"), wprep1=p.c("wprep1"), castg=p.c("castg"), castg1=p.c("castg1"),
                  castg2=p.c("castg2"), bsem=p.c("bsem"), dnw=p.c("dnw"),
                  patg=p.c("patg"), gkv=p.c("gkv"))
        for eng in ("sync", "tensor", "scalar", "vector", "gpsimd"):
            for s, v in pd.items():
                p.wait(eng, s, v)

        A_NQ = 0
        A_AF = 8192
        A_E = 16384
        A_KB = 24576
        A_VB = 26624

        pf_free = [("ssem", p.c("ssem"))] * 6
        pt_free = [("ssem", p.c("ssem"))] * 2
        wbuf_tok = [("tsem", p.c("tsem"))] * 4
        kv_tok = [("tsem", p.c("tsem"))] * 2
        e_tok = [p.c("tsem"), p.c("tsem")]

        def layer_norm(eps, skip_stats=False):
            v_in = p.c("vsem")
            p.wait("scalar", "vsem", v_in)
            eps_ap = stats[:, 121:122] if eps == LN_EPS else stats[:, 122:123]
            p.drain("vector")
            if not skip_stats:
                for st in range(NST):
                    xs = x_b[:, st * E:(st + 1) * E]
                    for j in range(4):
                        p.emit("vector", lambda e, xs=xs, st=st, j=j: e.bn_stats(
                            stats[:, st * 24 + j * 6:st * 24 + (j + 1) * 6],
                            xs[:, j * 512:(j + 1) * 512]))
                p.drain("vector")
            v_mu = 0
            for st in range(NST):
                v_mu = p.emit("vector", lambda e, st=st: e.bn_aggr(
                    stats[:, 96 + 2 * st:98 + 2 * st], stats[:, st * 24:(st + 1) * 24]),
                    inc="vsem")
            p.wait("scalar", "vsem", v_mu)
            var_v = stats[:, 97:105].rearrange("p (a b) -> p a b", b=2)[:, :, 0:1]
            ln_v = stats[:, 104:108].rearrange("p (a b) -> p a b", b=1)
            rs_v = stats[:, 108:112].rearrange("p (a b) -> p a b", b=1)
            p.drain("scalar")
            p.emit("scalar", lambda e, eps_ap=eps_ap: e.activation(ln_v, var_v, ACT.Ln, bias=eps_ap))
            p.drain("scalar")
            s_r = p.emit("scalar", lambda e: e.activation(
                rs_v, ln_v, ACT.Exp, scale=stats[:, 120:121]), inc="ssem")
            p.wait("vector", "ssem", s_r)
            v_a = 0
            for st in range(NST):
                xs = x_b[:, st * E:(st + 1) * E]
                v_a = p.emit("vector", lambda e, xs=xs, st=st: e.tensor_scalar(
                    tmA[:, st * E:(st + 1) * E], xs, stats[:, 96 + 2 * st:97 + 2 * st],
                    stats[:, 108 + st:109 + st], ALU.subtract, ALU.mult), inc="vsem")
            return v_a

        def transpose_tm_to_fm(src_off, scaled, v_ready):
            p.wait("tensor", "vsem", v_ready)
            s_last = 0
            for et in range(NET):
                bk = et % 2
                sem, val = pt_free[bk]
                p.wait("tensor", sem, val)
                t_last = 0
                for st in range(NST):
                    src = tmA if src_off is None else arena
                    off = (0 if src_off is None else src_off) + st * E + et * 128
                    t_last = tp(pt[bk][:, st * 128:(st + 1) * 128], src[:, off:off + 128])
                p.wait("scalar", "tsem", t_last)
                if scaled:
                    s_last = p.emit("scalar", lambda e, et=et, bk=bk: e.activation(
                        fmA[:, et * C:(et + 1) * C], pt[bk][:, 0:C],
                        ACT.Copy, scale=su_sb[:, et:et + 1]), inc="ssem")
                else:
                    s_last = p.emit("scalar", lambda e, et=et, bk=bk: e.copy(
                        fmA[:, et * C:(et + 1) * C], pt[bk][:, 0:C]), inc="ssem")
                pt_free[bk] = ("ssem", s_last)
            return s_last

        def fwht2048_fm():
            p.wait("tensor", "ssem", p.c("ssem"))
            for et in range(NET):
                bix = et % 2
                sem, val = pf_free[bix]
                p.wait("tensor", sem, val)
                t_s = mm(pf[bix][:, 0:C], h_sb[:], fmA[:, et * C:(et + 1) * C], True, True)
                p.wait("scalar", "tsem", t_s)
                s_s = p.emit("scalar", lambda e, et=et, bix=bix: e.copy(
                    fmA[:, et * C:(et + 1) * C], pf[bix][:, 0:C]), inc="ssem")
                pf_free[bix] = ("ssem", s_s)
            p.wait("vector", "ssem", p.c("ssem"))
            p.wait("vector", "tsem", p.c("tsem"))
            ready = [0] * NET
            for st in range(4):
                src_fm = (st % 2 == 0)
                for u in range(NET):
                    v = u ^ (1 << st)
                    if v < u:
                        continue
                    if src_fm:
                        iu, iv = fmA[:, u * C:(u + 1) * C], fmA[:, v * C:(v + 1) * C]
                        ou, ov = arena[:, u * C:(u + 1) * C], arena[:, v * C:(v + 1) * C]
                    else:
                        iu, iv = arena[:, u * C:(u + 1) * C], arena[:, v * C:(v + 1) * C]
                        ou, ov = fmA[:, u * C:(u + 1) * C], fmA[:, v * C:(v + 1) * C]
                    p.emit("vector", lambda e, iu=iu, iv=iv, ou=ou: e.tensor_add(ou, iu, iv))
                    last = (st == 3)
                    vv = p.emit("vector", lambda e, iu=iu, iv=iv, ov=ov: e.tensor_sub(ov, iu, iv),
                                inc="vsem" if last else None)
                    if last:
                        ready[u] = vv
                        ready[v] = vv
                p.drain("vector")
            return ready

        ow = [0]

        def ffn(fm_ready, write_ch=None):
            korder = sorted(range(NET), key=lambda k: fm_ready[k])
            wb = [wbuf[:, k * 4096:(k + 1) * 4096] for k in range(4)]
            last_v_g = [p.c("vsem")]
            for ib in range(32):
                ub, gb = wb[(ib % 2) * 2], wb[(ib % 2) * 2 + 1]
                for m_i, (buf, srcT) in enumerate(((ub, upT), (gb, gateT))):
                    slot = (ib % 2) * 2 + m_i
                    sem, val = wbuf_tok[slot]
                    p.wait("sync", sem, val)
                    p.emit("sync", lambda e, ib=ib, buf=buf, srcT=srcT: e.dma_start(
                        buf.rearrange("p (et c) -> p et c", c=256),
                        srcT[ib].rearrange("et p c -> p et c")), inc=f"sl{slot}", incby=16)
                    p.wait("tensor", f"sl{slot}", p.c(f"sl{slot}"))
                for il in range(2):
                    it = ib * 2 + il
                    pu_i, pg_i = (it % 2) * 2, (it % 2) * 2 + 1
                    for bix in (pu_i, pg_i):
                        sem, val = pf_free[bix]
                        p.wait("tensor", sem, val)
                    t_u = 0
                    for ki, et in enumerate(korder):
                        p.wait("tensor", "vsem", fm_ready[et])
                        t_u = mm(pf[pu_i][:, 0:C], ub[:, et * 256 + il * 128:et * 256 + (il + 1) * 128],
                                 fmA[:, et * C:(et + 1) * C], ki == 0, ki == NET - 1)
                    t_g = 0
                    for ki, et in enumerate(korder):
                        t_g = mm(pf[pg_i][:, 0:C], gb[:, et * 256 + il * 128:et * 256 + (il + 1) * 128],
                                 fmA[:, et * C:(et + 1) * C], ki == 0, ki == NET - 1)
                    if il == 1:
                        wbuf_tok[(ib % 2) * 2] = ("tsem", t_g)
                        wbuf_tok[(ib % 2) * 2 + 1] = ("tsem", t_g)
                    p.wait("scalar", "tsem", t_g)
                    p.wait("scalar", "vsem", last_v_g[0])
                    s_si = p.emit("scalar", lambda e, it=it, pg_i=pg_i: e.activation(
                        silu_t[:, 0:C], pf[pg_i][:, 0:C], ACT.Silu, scale=wscg_sb[:, it:it + 1]),
                        inc="ssem")
                    p.wait("vector", "ssem", s_si)
                    p.wait("vector", "tsem", t_u)
                    v_g = p.emit("vector", lambda e, it=it, pu_i=pu_i: e.scalar_tensor_tensor(
                        arena[:, it * C:(it + 1) * C], pf[pu_i][:, 0:C], wscu_su[:, it:it + 1],
                        silu_t[:, 0:C], ALU.mult, ALU.mult), inc="vsem")
                    pf_free[pu_i] = ("vsem", v_g)
                    pf_free[pg_i] = ("ssem", s_si)
                    last_v_g[0] = v_g
            v_g_all = p.c("vsem")
            db = [wbuf[:, k * 4096:(k + 1) * 4096] for k in range(4)]
            p.wait("tensor", "vsem", v_g_all)
            v_d = 0
            for ec in range(4):
                for st in range(NST):
                    sem, val = pf_free[st]
                    p.wait("tensor", sem, val)
                t_d = 0
                for kb in range(8):
                    bi = kb % 4
                    sem, val = wbuf_tok[bi]
                    p.wait("sync", sem, val)
                    wv = p.emit("sync", lambda e, ec=ec, kb=kb, b=db[bi]: e.dma_start(
                        b.rearrange("p (k c) -> p k c", c=512),
                        dnT[kb * 1024:(kb + 1) * 1024, ec * 512:(ec + 1) * 512]
                        .rearrange("(k p) c -> p k c", p=128)), inc=f"sl{bi}", incby=16)
                    p.wait("tensor", f"sl{bi}", wv)
                    for st in range(NST):
                        for k in range(8):
                            it = kb * 8 + k
                            t_d = mm(pf[st][:, 0:512],
                                     arena[:, it * C + st * 128:it * C + (st + 1) * 128],
                                     db[bi][:, k * 512:(k + 1) * 512],
                                     it == 0, it == NIT - 1)
                    wbuf_tok[bi] = ("tsem", t_d)
                p.wait("vector", "tsem", t_d)
                for st in range(NST):
                    p.drain("vector")
                    p.emit("vector", lambda e, st=st, ec=ec: e.tensor_mul(
                        sq[:, 0:512], pf[st][:, 0:512], wscb[:, ec * 512:(ec + 1) * 512]))
                    p.drain("vector")
                    v_d = p.emit("vector", lambda e, st=st, ec=ec: e.tensor_add(
                        x_b[:, st * E + ec * 512:st * E + (ec + 1) * 512],
                        x_b[:, st * E + ec * 512:st * E + (ec + 1) * 512],
                        sq[:, 0:512]), inc="vsem")
                    pf_free[st] = ("vsem", v_d)
                    if write_ch is not None and ec == 3:
                        p.wait("gpsimd", "vsem", v_d)
                        p.emit("gpsimd", lambda e, ch=write_ch, st=st: e.dma_start(
                            y_out[ch * C + st * 128:ch * C + (st + 1) * 128, :],
                            x_b[:, st * E:(st + 1) * E]), inc="osem", incby=16)
                        ow[0] += 1
            p.drain("vector")
            return v_d

        def rms_apply(src_ap_fn, dst_off, w_bcast, v_ready, out_is_nq, stats_done=False):
            p.wait("scalar", "vsem", v_ready)
            p.wait("vector", "vsem", v_ready)
            p.drain("vector")
            if not stats_done:
                for st in range(NST):
                    a_s = src_ap_fn(st)
                    for j in range(4):
                        p.emit("vector", lambda e, a_s=a_s, st=st, j=j: e.bn_stats(
                            stats[:, st * 24 + j * 6:st * 24 + (j + 1) * 6],
                            a_s[:, j * 512:(j + 1) * 512]))
                p.drain("vector")
            for st in range(NST):
                p.emit("vector", lambda e, st=st: e.bn_aggr(
                    stats[:, 96 + 2 * st:98 + 2 * st], stats[:, st * 24:(st + 1) * 24]))
            p.drain("vector")
            mean_v = stats[:, 96:104].rearrange("p (a b) -> p a b", b=2)[:, :, 0:1]
            var_v = stats[:, 97:105].rearrange("p (a b) -> p a b", b=2)[:, :, 0:1]
            msq_v = stats[:, 104:108].rearrange("p (a b) -> p a b", b=1)
            rs_v = stats[:, 108:112].rearrange("p (a b) -> p a b", b=1)
            v_ss = p.emit("vector", lambda e: e.tensor_tensor(
                msq_v, mean_v, mean_v, ALU.mult), inc="vsem")
            p.drain("vector")
            v_ss = p.emit("vector", lambda e: e.tensor_tensor(
                msq_v, msq_v, var_v, ALU.add), inc="vsem")
            p.wait("scalar", "vsem", v_ss)
            p.drain("scalar")
            p.emit("scalar", lambda e: e.activation(msq_v, msq_v, ACT.Ln, bias=stats[:, 122:123]))
            p.drain("scalar")
            s_r = p.emit("scalar", lambda e: e.activation(
                rs_v, msq_v, ACT.Exp, scale=stats[:, 120:121]), inc="ssem")
            p.wait("vector", "ssem", s_r)
            last = 0
            for st in range(NST):
                a_s = src_ap_fn(st)
                if out_is_nq:
                    last = p.emit("vector", lambda e, a_s=a_s, st=st: e.scalar_tensor_tensor(
                        arena[:, dst_off + st * E:dst_off + (st + 1) * E], a_s,
                        stats[:, 108 + st:109 + st], w_bcast[:], ALU.mult, ALU.mult), inc="vsem")
                else:
                    p.drain("vector")
                    p.emit("vector", lambda e, a_s=a_s, st=st: e.scalar_tensor_tensor(
                        sq[:], a_s, stats[:, 108 + st:109 + st], w_bcast[:], ALU.mult, ALU.mult))
                    p.drain("vector")
                    last = p.emit("vector", lambda e, st=st: e.tensor_add(
                        x_b[:, st * E:(st + 1) * E], x_b[:, st * E:(st + 1) * E], sq[:]), inc="vsem")
            return last

        def attn(v_nq_ready):
            s_tr = transpose_tm_to_fm(A_NQ, False, v_nq_ready)
            wb = [wbuf[:, k * 4096:(k + 1) * 4096] for k in range(2)]
            p.wait("tensor", "ssem", s_tr)
            for eb in range(8):
                bi = eb % 2
                sem, val = wbuf_tok[bi]
                p.wait("sync", sem, val)
                p.emit("sync", lambda e, eb=eb, b=wb[bi]: e.dma_start(
                    b.rearrange("p (et c) -> p et c", c=256),
                    winT[eb].rearrange("et p c -> p et c")), inc=f"sl{bi}", incby=16)
                p.wait("tensor", f"sl{bi}", p.c(f"sl{bi}"))
                for il in range(2):
                    ep = eb * 2 + il
                    bix = 4 + (ep % 2)
                    sem, val = pf_free[bix]
                    p.wait("tensor", sem, val)
                    t_p = 0
                    for et in range(NET):
                        t_p = mm(pf[bix][:, 0:C], wb[bi][:, et * 256 + il * 128:et * 256 + (il + 1) * 128],
                                 fmA[:, et * C:(et + 1) * C], et == 0, et == NET - 1)
                    if il == 1:
                        wbuf_tok[bi] = ("tsem", t_p)
                    p.wait("scalar", "tsem", t_p)
                    s_p = p.emit("scalar", lambda e, ep=ep, bix=bix: e.copy(
                        arena[:, A_NQ + ep * C:A_NQ + (ep + 1) * C], pf[bix][:, 0:C]), inc="ssem")
                    pf_free[bix] = ("ssem", s_p)
            p.wait("tensor", "ssem", p.c("ssem"))
            for h in range(H):
                bix = 4 + (h % 2)
                sem, val = pf_free[bix]
                p.wait("tensor", sem, val)
                t_q = mm(pf[bix][:, 0:C], wqT[:], arena[:, A_NQ + h * C:A_NQ + (h + 1) * C],
                         True, True)
                p.wait("scalar", "tsem", t_q)
                s_q = p.emit("scalar", lambda e, h=h, bix=bix: e.copy(
                    fmA[:, h * C:(h + 1) * C], pf[bix][:, 0:C]), inc="ssem")
                pf_free[bix] = ("ssem", s_q)
            p.wait("tensor", "ssem", p.c("ssem"))
            kv_free = kv_tok
            e_guard = e_tok
            s_e_done = [0] * H
            v_o = 0

            def emit_scores(h):
                sl = h % 2
                koff = A_KB + sl * 1024
                voff = A_VB + sl * 1088
                sem, val = kv_free[sl]
                p.wait("sync", sem, val)
                p.emit("sync", lambda e, h=h, koff=koff: e.dma_start(
                    arena[:, koff:koff + 1024], kT[h]), inc=f"kv{sl}", incby=16)
                w_kv = p.emit("sync", lambda e, h=h, voff=voff: e.dma_start(
                    arena[:, voff:voff + 1088], vT[h]), inc=f"kv{sl}", incby=16)
                p.wait("tensor", f"kv{sl}", w_kv)
                eoff = A_E + sl * 4096
                p.wait("scalar", "tsem", e_guard[sl])
                for mtq in range(0, NMT, 4):
                    t_sc = 0
                    for k in range(4):
                        mt = mtq + k
                        sem, val = pf_free[k]
                        p.wait("tensor", sem, val)
                        t_sc = mm(pf[k][:, 0:C], arena[:, koff + mt * 128:koff + (mt + 1) * 128],
                                  fmA[:, h * C:(h + 1) * C], True, True)
                    p.wait("scalar", "tsem", t_sc)
                    s_e = 0
                    for k in range(4):
                        mt = mtq + k
                        s_e = p.emit("scalar", lambda e, mt=mt, k=k, eoff=eoff: e.activation(
                            arena[:, eoff + mt * C:eoff + (mt + 1) * C], pf[k][:, 0:C], ACT.Exp),
                            inc="ssem")
                    for k in range(4):
                        pf_free[k] = ("ssem", s_e)
                    if mtq == 0 and h > 0:
                        emit_out(h - 1)
                s_e_done[h] = p.c("ssem")

            def emit_out(h):
                nonlocal v_o
                sl = h % 2
                voff = A_VB + sl * 1088
                eoff = A_E + sl * 4096
                p.wait("tensor", "ssem", s_e_done[h])
                t_o = 0
                for st in range(NST):
                    bix = 4 + st // 2
                    coff = (st % 2) * 256
                    if st % 2 == 0:
                        sem, val = pf_free[bix]
                        p.wait("tensor", sem, val)
                    for mt in range(NMT):
                        t_o = mm(pf[bix][:, coff:coff + VW],
                                 arena[:, eoff + mt * C + st * 128:eoff + mt * C + (st + 1) * 128],
                                 arena[:, voff + mt * VW:voff + (mt + 1) * VW],
                                 mt == 0, mt == NMT - 1)
                p.wait("vector", "tsem", t_o)
                p.drain("vector")
                for st in range(NST):
                    bix = 4 + st // 2
                    coff = (st % 2) * 256
                    p.emit("vector", lambda e, st=st, bix=bix, coff=coff: e.reciprocal(
                        stats[:, 116 + st:117 + st], pf[bix][:, coff + 128:coff + 129]))
                p.drain("vector")
                for st in range(NST):
                    bix = 4 + st // 2
                    coff = (st % 2) * 256
                    v_o = p.emit("vector", lambda e, h=h, st=st, bix=bix, coff=coff: e.tensor_scalar(
                        arena[:, A_AF + st * E + h * 128:A_AF + st * E + (h + 1) * 128],
                        pf[bix][:, coff:coff + 128], stats[:, 116 + st:117 + st], None, ALU.mult), inc="vsem")
                for st in (0, 2):
                    pf_free[4 + st // 2] = ("vsem", v_o)
                kv_free[sl] = ("tsem", p.c("tsem"))
                e_guard[sl] = p.c("tsem")
                if h % 4 == 3:
                    j = h // 4
                    p.drain("vector")
                    for st in range(NST):
                        p.emit("vector", lambda e, st=st, j=j: e.bn_stats(
                            stats[:, st * 24 + j * 6:st * 24 + (j + 1) * 6],
                            arena[:, A_AF + st * E + j * 512:
                                  A_AF + st * E + (j + 1) * 512]))

            for h in range(H):
                emit_scores(h)
                if h > 0:
                    emit_out(h - 1)
            emit_out(H - 1)
            return rms_apply(lambda st: arena[:, A_AF + st * E:A_AF + (st + 1) * E],
                             0, wret_b, v_o, out_is_nq=False, stats_done=True)

        for ch in range(NCH):
            g_in = 0
            p.wait("vector", "tsem", p.c("tsem"))
            for st in range(NST):
                if ch > 0:
                    p.wait("sync", "osem", (4 * (ch - 1) + st + 1) * 16)
                g_in = p.emit("sync", lambda e, ch=ch, st=st: e.dma_start(
                    x_b[:, st * E:(st + 1) * E],
                    x_in[ch * C + st * 128:ch * C + (st + 1) * 128, :]), inc="gsem", incby=16)
                p.wait("vector", "gsem", g_in)
                for j in range(4):
                    p.emit("vector", lambda e, st=st, j=j: e.bn_stats(
                        stats[:, st * 24 + j * 6:st * 24 + (j + 1) * 6],
                        x_b[:, st * E + j * 512:st * E + (j + 1) * 512]))
            p.wait("scalar", "gsem", g_in)

            v_ln = layer_norm(LN_EPS, skip_stats=True)
            transpose_tm_to_fm(None, True, v_ln)
            fm_ready = fwht2048_fm()
            ffn(fm_ready)

            v_ln2 = layer_norm(LN_EPS)
            v_y = 0
            for st in range(NST):
                p.drain("vector")
                v_y = p.emit("vector", lambda e, st=st: e.tensor_add(
                    x_b[:, st * E:(st + 1) * E], x_b[:, st * E:(st + 1) * E],
                    tmA[:, st * E:(st + 1) * E]), inc="vsem")
            p.wait("vector", "tsem", p.c("tsem"))
            rs_v = stats[:, 108:112].rearrange("p (a b) -> p a b", b=1)
            var_v = stats[:, 97:105].rearrange("p (a b) -> p a b", b=2)[:, :, 0:1]
            w1_v = stats[:, 64:68].rearrange("p (a b) -> p a b", b=1)
            u_v = stats[:, 68:72].rearrange("p (a b) -> p a b", b=1)
            p.drain("vector")
            p.emit("vector", lambda e: e.tensor_tensor(w1_v, rs_v, rs_v, ALU.mult))
            p.drain("vector")
            v_w1 = p.emit("vector", lambda e: e.tensor_tensor(w1_v, w1_v, var_v, ALU.mult), inc="vsem")
            p.wait("scalar", "vsem", v_w1)
            p.drain("scalar")
            p.emit("scalar", lambda e: e.activation(u_v, w1_v, ACT.Ln, bias=stats[:, 122:123]))
            p.drain("scalar")
            s_u = p.emit("scalar", lambda e: e.activation(
                u_v, u_v, ACT.Exp, scale=stats[:, 120:121]), inc="ssem")
            p.wait("vector", "ssem", s_u)
            p.wait("vector", "vsem", v_y)
            v_nq = 0
            for st in range(NST):
                v_nq = p.emit("vector", lambda e, st=st: e.scalar_tensor_tensor(
                    arena[:, A_NQ + st * E:A_NQ + (st + 1) * E], tmA[:, st * E:(st + 1) * E],
                    stats[:, 68 + st:69 + st], wnq_b[:], ALU.mult, ALU.mult), inc="vsem")
            attn(v_nq)

            v_ln3 = layer_norm(LN_EPS)
            transpose_tm_to_fm(None, True, v_ln3)
            fm_ready3 = fwht2048_fm()
            ffn(fm_ready3, write_ch=ch)

        total_out = ow[0] * 16

        with nc.Block() as block:
            @block.sync
            def _(eng):
                for fn in p.plan["sync"]:
                    fn(eng)

            @block.tensor
            def _(eng):
                for fn in p.plan["tensor"]:
                    fn(eng)

            @block.scalar
            def _(eng):
                for fn in p.plan["scalar"]:
                    fn(eng)

            @block.vector
            def _(eng):
                for fn in p.plan["vector"]:
                    fn(eng)

            @block.gpsimd
            def _(eng):
                for fn in p.plan["gpsimd"]:
                    fn(eng)
                eng.wait_ge(p.sems["osem"], total_out)

    return nc


_NC_CACHE = None


def kernel(L0_attn_query_input, L0_attn_W_in, L0_attn_W_q, L0_attn_W_k, L0_attn_W_v,
           L0_attn_stored_patterns, L0_attn_norm_query_w, L0_attn_norm_retrieved_w,
           L0_attn_beta, L0_ffn_SU_ug, L0_ffn_SU_dp, L0_ffn_Wscale_ug, L0_ffn_Wscale_dp,
           L0_ffn_W_up, L0_ffn_W_gate, L0_ffn_W_down):
    global _NC_CACHE
    x = np.ascontiguousarray(np.asarray(L0_attn_query_input, dtype=np.float32).reshape(16384, E))
    consts = dict(
        h128=np.ascontiguousarray(_hadamard(128).astype(BFNP)),
        ident=np.ascontiguousarray(np.eye(128, dtype=np.float32).astype(BFNP)),
        ones1=np.ones((1, 128), dtype=BFNP),
        ones1f=np.ones((1, 128), dtype=np.float32),
    )
    weights = dict(
        w_in=np.ascontiguousarray(np.asarray(L0_attn_W_in, dtype=np.float32)),
        w_q=np.ascontiguousarray(np.asarray(L0_attn_W_q, dtype=np.float32)),
        w_k=np.ascontiguousarray(np.asarray(L0_attn_W_k, dtype=np.float32)),
        w_v=np.ascontiguousarray(np.asarray(L0_attn_W_v, dtype=np.float32)),
        pats=np.ascontiguousarray(np.asarray(L0_attn_stored_patterns, dtype=np.float32)),
        nqw=np.ascontiguousarray(np.asarray(L0_attn_norm_query_w, dtype=np.float32)),
        nrw=np.ascontiguousarray(np.asarray(L0_attn_norm_retrieved_w, dtype=np.float32)),
        beta=np.asarray(L0_attn_beta, dtype=np.float32).reshape(1),
        su_ug=np.ascontiguousarray(np.asarray(L0_ffn_SU_ug, dtype=np.float32)),
        su_dp=np.ascontiguousarray(np.asarray(L0_ffn_SU_dp, dtype=np.float32)),
        wsc_ug=np.ascontiguousarray(np.asarray(L0_ffn_Wscale_ug, dtype=np.float32)),
        wsc_dp=np.ascontiguousarray(np.asarray(L0_ffn_Wscale_dp, dtype=np.float32)),
        w_up=np.ascontiguousarray(np.asarray(L0_ffn_W_up, dtype=np.float32)),
        w_gate=np.ascontiguousarray(np.asarray(L0_ffn_W_gate, dtype=np.float32)),
        w_down=np.ascontiguousarray(np.asarray(L0_ffn_W_down, dtype=np.float32)),
    )
    if _NC_CACHE is None:
        _NC_CACHE = build_nc()
    nc = _NC_CACHE
    in_maps = []
    for c in range(NCORES):
        m = {"x": np.ascontiguousarray(x[c * TCORE:(c + 1) * TCORE])}
        m.update(weights)
        m.update(consts)
        in_maps.append(m)
    res = run_bass_kernel_spmd(nc, in_maps, core_ids=list(range(NCORES)))
    out = np.concatenate([res.results[c]["y"] for c in range(NCORES)], axis=0)
    return np.ascontiguousarray(out.reshape(1, 16384, E).astype(np.float32))


if __name__ == "__main__":
    build_nc()
    print("built ok")


# version 21
# speedup vs baseline: 1.2032x; 1.0047x over previous
import numpy as np
import ml_dtypes
from contextlib import ExitStack

import concourse.bass as bass
import concourse.mybir as mybir
from concourse.bass_utils import run_bass_kernel_spmd

BFNP = ml_dtypes.bfloat16
F32 = mybir.dt.float32
BF16 = mybir.dt.bfloat16
AX = mybir.AxisListType.X
ALU = mybir.AluOpType
ACT = mybir.ActivationFunctionType

NCORES = 8
TCORE = 2048
C = 512
NCH = TCORE // C
NST = C // 128
E = 2048
I = 8192
NET = E // 128
NIT = I // 128
H = 16
D = 128
M = 1024
NMT = M // 128
VW = 136
RS2048 = float(1.0 / np.sqrt(2048.0))
RS8192 = float(1.0 / np.sqrt(8192.0))
LN_EPS = 1e-5
RMS_EPS = 1e-6

ENG_SEM = {"tensor": "tsem", "scalar": "ssem", "vector": "vsem"}


def _hadamard(n):
    h = np.array([[1.0]], dtype=np.float64)
    while h.shape[0] < n:
        h = np.block([[h, h], [h, -h]])
    return h


class P:

    def __init__(self):
        self.plan = {k: [] for k in ("sync", "tensor", "scalar", "vector", "gpsimd")}
        self.cnt = {}
        self.lastw = {}
        self.sems = {}

    def emit(self, eng, build, inc=None, incby=1):
        if inc is not None:
            self.cnt[inc] = self.cnt.get(inc, 0) + incby

            def fn(e, b=build, s=inc, n=incby):
                b(e).then_inc(self.sems[s], n)
        else:
            def fn(e, b=build):
                b(e)
        self.plan[eng].append(fn)
        return self.cnt.get(inc, 0)

    def wait(self, eng, sem, val):
        if not val or val <= 0:
            return
        key = (eng, sem)
        if self.lastw.get(key, 0) >= val:
            return
        self.lastw[key] = val
        self.plan[eng].append(lambda e, s=sem, v=val: e.wait_ge(self.sems[s], v))

    def drain(self, eng):
        self.plan[eng].append(lambda e: e.drain())

    def release(self, eng):
        sem = ENG_SEM[eng]
        return self.emit(eng, lambda e: e.drain(), inc=sem)

    def c(self, sem):
        return self.cnt.get(sem, 0)


def build_nc():
    nc = bass.Bass("TRN2", target_bir_lowering=False)
    p = P()

    x_in = nc.dram_tensor("x", [TCORE, E], F32, kind="ExternalInput").ap()
    w_in = nc.dram_tensor("w_in", [E, E], F32, kind="ExternalInput").ap()
    w_q = nc.dram_tensor("w_q", [D, D], F32, kind="ExternalInput").ap()
    w_k = nc.dram_tensor("w_k", [D, D], F32, kind="ExternalInput").ap()
    w_v = nc.dram_tensor("w_v", [D, D], F32, kind="ExternalInput").ap()
    pats = nc.dram_tensor("pats", [H, M, D], F32, kind="ExternalInput").ap()
    nqw = nc.dram_tensor("nqw", [E], F32, kind="ExternalInput").ap()
    nrw = nc.dram_tensor("nrw", [E], F32, kind="ExternalInput").ap()
    beta = nc.dram_tensor("beta", [1], F32, kind="ExternalInput").ap()
    su_ug = nc.dram_tensor("su_ug", [E], F32, kind="ExternalInput").ap()
    su_dp = nc.dram_tensor("su_dp", [I], F32, kind="ExternalInput").ap()
    wsc_ug = nc.dram_tensor("wsc_ug", [2 * I], F32, kind="ExternalInput").ap()
    wsc_dp = nc.dram_tensor("wsc_dp", [E], F32, kind="ExternalInput").ap()
    w_up = nc.dram_tensor("w_up", [I, E], F32, kind="ExternalInput").ap()
    w_gate = nc.dram_tensor("w_gate", [I, E], F32, kind="ExternalInput").ap()
    w_down = nc.dram_tensor("w_down", [E, I], F32, kind="ExternalInput").ap()
    h128 = nc.dram_tensor("h128", [128, 128], BF16, kind="ExternalInput").ap()
    ident = nc.dram_tensor("ident", [128, 128], BF16, kind="ExternalInput").ap()
    ones1 = nc.dram_tensor("ones1", [1, 128], BF16, kind="ExternalInput").ap()
    ones1f = nc.dram_tensor("ones1f", [1, 128], F32, kind="ExternalInput").ap()

    y_out = nc.dram_tensor("y", [TCORE, E], F32, kind="ExternalOutput").ap()

    upT = nc.dram_tensor("upT", [32, NET, 128, 256], BF16, kind="Internal").ap()
    gateT = nc.dram_tensor("gateT", [32, NET, 128, 256], BF16, kind="Internal").ap()
    winT = nc.dram_tensor("winT", [8, NET, 128, 256], BF16, kind="Internal").ap()
    dnT = nc.dram_tensor("dnT", [I, E], BF16, kind="Internal").ap()
    kT = nc.dram_tensor("kT", [H, 128, M], BF16, kind="Internal").ap()
    vT = nc.dram_tensor("vT", [H, 128, NMT * VW], BF16, kind="Internal").ap()

    with ExitStack() as ctx:
        for s in ("wsem", "gsem", "tsem", "ssem", "vsem", "osem",
                  "wprep", "wprep1", "castg", "castg1", "castg2", "bsem", "dnw", "patg", "gkv",
                  "sl0", "sl1", "sl2", "sl3", "kv0", "kv1"):
            p.sems[s] = ctx.enter_context(nc.semaphore(s))

        x_b = ctx.enter_context(nc.sbuf_tensor("x_b", [128, NST * E], F32))
        tmA = ctx.enter_context(nc.sbuf_tensor("tmA", [128, NST * E], BF16))
        fmA = ctx.enter_context(nc.sbuf_tensor("fmA", [128, NET * C], BF16))
        fwtmp = ctx.enter_context(nc.sbuf_tensor("fwtmp", [128, C], BF16))
        arena = ctx.enter_context(nc.sbuf_tensor("arena", [128, NIT * C], BF16))
        wbuf = ctx.enter_context(nc.sbuf_tensor("wbuf", [128, 4 * 4096], BF16))
        sq = ctx.enter_context(nc.sbuf_tensor("sq", [128, E], F32))
        silu_t = ctx.enter_context(nc.sbuf_tensor("silu_t", [128, C], BF16))
        stats = ctx.enter_context(nc.sbuf_tensor("stats", [128, 128], F32))
        h_sb = ctx.enter_context(nc.sbuf_tensor("h_sb", [128, 128], BF16))
        id_sb = ctx.enter_context(nc.sbuf_tensor("id_sb", [128, 128], BF16))
        on_sb = ctx.enter_context(nc.sbuf_tensor("on_sb", [1, 128], BF16))
        onf_sb = ctx.enter_context(nc.sbuf_tensor("onf_sb", [1, 128], F32))
        su_sb = ctx.enter_context(nc.sbuf_tensor("su_sb", [128, NET], F32))
        wscu_su = ctx.enter_context(nc.sbuf_tensor("wscu_su", [128, NIT], F32))
        wscg_sb = ctx.enter_context(nc.sbuf_tensor("wscg_sb", [128, NIT], F32))
        wscb = ctx.enter_context(nc.sbuf_tensor("wscb", [128, E], F32))
        wnq_b = ctx.enter_context(nc.sbuf_tensor("wnq_b", [128, E], BF16))
        wret_b = ctx.enter_context(nc.sbuf_tensor("wret_b", [128, E], BF16))
        wqT = ctx.enter_context(nc.sbuf_tensor("wqT", [128, 128], BF16))
        wkT = ctx.enter_context(nc.sbuf_tensor("wkT", [128, 128], BF16))
        wvT = ctx.enter_context(nc.sbuf_tensor("wvT", [128, 128], BF16))
        bsig = ctx.enter_context(nc.sbuf_tensor("bsig", [128, 4], F32))

        pf = [ctx.enter_context(nc.psum_tensor(f"pf{i}", [128, 512], F32)) for i in range(6)]
        pt = [ctx.enter_context(nc.psum_tensor(f"pt{i}", [128, 1024], BF16)) for i in range(2)]

        pf_free = [("ssem", 0)] * 6
        pt_free = [("ssem", 0)] * 2

        def mm(out_ap, lhsT_ap, rhs_ap, start, stop):
            return p.emit(
                "tensor",
                lambda e, o=out_ap, l=lhsT_ap, r=rhs_ap, st=start, sp=stop:
                    e.matmul(o, l, r, start=st, stop=sp),
                inc="tsem")

        def tp(out_ap, in_ap):
            return p.emit(
                "tensor",
                lambda e, o=out_ap, i_=in_ap: e.transpose(o, i_, id_sb[:]),
                inc="tsem")

        def sload(dst, src, slow=False):
            return p.emit("sync", lambda e, d=dst, s=src, sl=slow: e.dma_start(
                d, s, allow_slow_non_contiguous=sl), inc="wsem", incby=16)

        sload(h_sb[:], h128)
        sload(id_sb[:], ident)
        sload(on_sb[:], ones1)
        sload(onf_sb[:], ones1f)
        sload(su_sb[:], su_ug.rearrange("(a b) -> b a", b=128), slow=True)
        sload(wscu_su[:], wsc_ug[0:I].rearrange("(a b) -> b a", b=128), slow=True)
        sload(wscg_sb[:], wsc_ug[I:2 * I].rearrange("(a b) -> b a", b=128), slow=True)
        sload(stats[0:1, 112:113], beta.rearrange("(a b) -> a b", b=1))
        sload(x_b[0:1, 0:E], nqw.rearrange("(a b) -> a b", a=1))
        sload(x_b[0:1, E:2 * E], nrw.rearrange("(a b) -> a b", a=1))
        sload(x_b[0:1, 2 * E:3 * E], wsc_dp.rearrange("(a b) -> a b", a=1))
        sload(sq[:, 1024:1024 + NIT], su_dp.rearrange("(a b) -> b a", b=128), slow=True)
        sload(sq[:, 1200:1328], w_q)
        sload(sq[:, 1328:1456], w_k)
        w_all = sload(sq[:, 1456:1584], w_v)

        p.emit("vector", lambda e: e.memset(stats[:, 120:121], -0.5))
        p.emit("vector", lambda e: e.memset(stats[:, 121:122], float(LN_EPS)))
        p.emit("vector", lambda e: e.memset(stats[:, 122:123], float(RMS_EPS)))
        p.emit("vector", lambda e: e.memset(stats[:, 123:124], -1.0))
        v_consts = p.release("vector")
        p.wait("vector", "wsem", w_all)
        p.emit("vector", lambda e: e.tensor_scalar_mul(su_sb[:], su_sb[:], RS2048))
        p.emit("vector", lambda e: e.tensor_mul(wscu_su[:], wscu_su[:], sq[:, 1024:1024 + NIT]))
        p.wait("scalar", "wsem", w_all)
        p.wait("scalar", "vsem", v_consts)
        p.emit("scalar", lambda e: e.activation(
            stats[0:1, 113:114], stats[0:1, 112:113], ACT.Exp, scale=stats[0:1, 123:124]))
        p.emit("scalar", lambda e: e.mul(x_b[0:1, 2 * E:3 * E], x_b[0:1, 2 * E:3 * E], 0.5 * RS8192))
        s_rows0 = p.release("scalar")
        p.wait("vector", "ssem", s_rows0)
        p.emit("vector", lambda e: e.tensor_scalar_add(stats[0:1, 114:115], stats[0:1, 113:114], 1.0))
        p.drain("vector")
        p.emit("vector", lambda e: e.reciprocal(stats[0:1, 115:116], stats[0:1, 114:115]))
        p.emit("vector", lambda e: e.tensor_copy(tmA[0:1, 0:E], x_b[0:1, 0:E]))
        p.emit("vector", lambda e: e.tensor_copy(tmA[:, 2 * E:2 * E + 128], sq[:, 1200:1328]))
        p.emit("vector", lambda e: e.tensor_copy(tmA[:, 2 * E + 128:2 * E + 256], sq[:, 1328:1456]))
        p.emit("vector", lambda e: e.tensor_copy(tmA[:, 2 * E + 256:2 * E + 384], sq[:, 1456:1584]))
        p.emit("vector", lambda e: e.tensor_copy(tmA[0:1, E:2 * E], x_b[0:1, E:2 * E]))
        v_prep0 = p.release("vector")

        p.wait("tensor", "vsem", v_prep0)
        p.wait("tensor", "ssem", s_rows0)
        for j in range(4):
            mm(pf[0][:, 0:512], on_sb[:], tmA[0:1, j * 512:(j + 1) * 512], True, True)
            mm(pf[1][:, 0:512], on_sb[:], tmA[0:1, E + j * 512:E + (j + 1) * 512], True, True)
            t_b = mm(pf[2][:, 0:512], onf_sb[:], x_b[0:1, 2 * E + j * 512:2 * E + (j + 1) * 512], True, True)
            p.wait("scalar", "tsem", t_b)
            p.emit("scalar", lambda e, j=j: e.copy(wnq_b[:, j * 512:(j + 1) * 512], pf[0][:, 0:512]))
            p.emit("scalar", lambda e, j=j: e.copy(wret_b[:, j * 512:(j + 1) * 512], pf[1][:, 0:512]))
            s_b = p.emit("scalar", lambda e, j=j: e.copy(wscb[:, j * 512:(j + 1) * 512], pf[2][:, 0:512]), inc="ssem")
            p.wait("tensor", "ssem", s_b)
        t_bs = mm(pf[3][:, 0:1], onf_sb[:], stats[0:1, 115:116], True, True)
        p.wait("scalar", "tsem", t_bs)
        s_bs = p.emit("scalar", lambda e: e.copy(bsig[:, 0:1], pf[3][:, 0:1]), inc="ssem")

        tp(pt[0][:, 0:128], tmA[:, 2 * E:2 * E + 128])
        tp(pt[0][:, 128:256], tmA[:, 2 * E + 128:2 * E + 256])
        t_w3 = tp(pt[0][:, 256:384], tmA[:, 2 * E + 256:2 * E + 384])
        p.wait("vector", "tsem", t_w3)
        p.wait("vector", "ssem", s_bs)
        p.emit("vector", lambda e: e.tensor_scalar_mul(wqT[:], pt[0][:, 0:128], bsig[:, 0:1]))
        p.emit("vector", lambda e: e.tensor_copy(wkT[:], pt[0][:, 128:256]))
        v_w3 = p.emit("vector", lambda e: e.tensor_copy(wvT[:], pt[0][:, 256:384]), inc="vsem")
        pt_free[0] = ("vsem", v_w3)
        pf_free[3] = ("ssem", s_bs)
        xb_free_t = p.c("tsem")

        pat_guard = [0, 0]
        for h in range(H):
            sl = h % 2
            poff = sl * 1024
            toff = 2048 + sl * 1024
            p.wait("gpsimd", "tsem", pat_guard[sl])
            g_pat = p.emit(
                "gpsimd",
                lambda e, h=h, poff=poff: e.dma_start(
                    fmA[:, poff:poff + 1024].rearrange("p (mt d) -> p mt d", d=128),
                    pats[h].rearrange("(mt p) d -> p mt d", p=128)),
                inc="patg", incby=16)
            p.wait("tensor", "patg", g_pat)
            for eg in range(2):
                bk = eg % 2
                sem, val = pt_free[bk]
                p.wait("tensor", sem, val)
                t_last = 0
                for k in range(8):
                    mt = eg * 8 + k
                    if mt >= 8:
                        break
                    t_last = tp(pt[bk][:, k * 128:(k + 1) * 128],
                                fmA[:, poff + mt * 128:poff + (mt + 1) * 128])
                if eg == 0:
                    p.wait("scalar", "tsem", t_last)
                    sv = p.emit("scalar",
                                lambda e, toff=toff, bk=bk: e.copy(
                                    fmA[:, toff:toff + 1024], pt[bk][:, 0:1024]),
                                inc="ssem")
                    pt_free[bk] = ("ssem", sv)
                    break
            s_patT = p.c("ssem")
            for bix in (0, 1):
                sem, val = pf_free[bix]
                p.wait("tensor", sem, val)
            p.wait("tensor", "ssem", s_patT)
            mm(pf[0][:, 0:512], wkT[:], fmA[:, toff:toff + 512], True, True)
            t_k = mm(pf[1][:, 0:512], wkT[:], fmA[:, toff + 512:toff + 1024], True, True)
            p.wait("vector", "tsem", t_k)
            p.drain("vector")
            p.emit("vector", lambda e: e.reduce_sum(stats[:, 0:1], pf[0][:, 0:512], axis=AX))
            p.emit("vector", lambda e: e.reduce_sum(stats[:, 1:2], pf[1][:, 0:512], axis=AX))
            p.drain("vector")
            p.emit("vector", lambda e: e.tensor_add(stats[:, 2:3], stats[:, 0:1], stats[:, 1:2]))
            p.drain("vector")
            p.emit("vector", lambda e: e.tensor_scalar_mul(stats[:, 3:4], stats[:, 2:3], 1.0 / M))
            p.drain("vector")
            p.emit("vector", lambda e: e.tensor_scalar(
                arena[:, 0:512], pf[0][:, 0:512], stats[:, 3:4], None, ALU.subtract), inc="vsem")
            v_k = p.emit("vector", lambda e: e.tensor_scalar(
                arena[:, 512:1024], pf[1][:, 0:512], stats[:, 3:4], None, ALU.subtract), inc="vsem")
            pf_free[0] = ("vsem", v_k)
            pf_free[1] = ("vsem", v_k)
            p.wait("gpsimd", "vsem", v_k)
            g_k = p.emit("gpsimd", lambda e, h=h: e.dma_start(kT[h], arena[:, 0:1024]),
                         inc="gkv", incby=16)
            p.drain("vector")
            vz = p.emit("vector", lambda e: e.memset(arena[:, 1024:1024 + NMT * VW], 0.0), inc="vsem")
            p.drain("vector")
            ones_ap = arena[:, 1024:1024 + NMT * VW].rearrange(
                "p (mt c) -> p mt c", c=VW)[:, :, 128:129]
            vm_last = p.emit("vector", lambda e, ones_ap=ones_ap: e.memset(ones_ap, 1.0), inc="vsem")
            p.wait("scalar", "vsem", vm_last)
            t_v = 0
            for mtq in range(0, 8, 4):
                for k in range(4):
                    mt = mtq + k
                    sem, val = pf_free[k]
                    p.wait("tensor", sem, val)
                    t_v = mm(pf[k][:, 0:128], fmA[:, toff + mt * 128:toff + (mt + 1) * 128],
                             wvT[:], True, True)
                p.wait("scalar", "tsem", t_v)
                s_v = 0
                for k in range(4):
                    mt = mtq + k
                    s_v = p.emit("scalar", lambda e, mt=mt, k=k: e.copy(
                        arena[:, 1024 + mt * VW:1024 + mt * VW + 128], pf[k][:, 0:128]), inc="ssem")
                for k in range(4):
                    pf_free[k] = ("ssem", s_v)
            p.wait("gpsimd", "ssem", p.c("ssem"))
            g_v = p.emit("gpsimd", lambda e, h=h: e.dma_start(vT[h], arena[:, 1024:1024 + NMT * VW]),
                         inc="gkv", incby=16)
            pat_guard[sl] = p.c("tsem")
            p.wait("vector", "gkv", p.c("gkv"))
            p.wait("scalar", "gkv", p.c("gkv"))

        p.wait("gpsimd", "gkv", p.c("gkv"))
        wT_off = [12288, 4096]
        cast_guard = [0, 0]
        write_guard = [0, 0]

        def prep_wT(w_src, dst, nib):
            for ib in range(nib):
                sset = ib % 2
                base = 0 if sset == 0 else 8192
                woff = wT_off[sset]
                wT_view = arena[:, woff:woff + 4096].rearrange(
                    "p (g two c) -> p g two c", two=2, c=128)
                csem = "castg" if sset == 0 else "castg1"
                p.wait("gpsimd", "tsem", cast_guard[sset])
                p.emit(
                    "gpsimd",
                    lambda e, ib=ib, base=base, w=w_src: e.dma_start(
                        arena[:, base:base + 4096].rearrange("p (il c) -> p il c", c=2048),
                        w[ib * 256:(ib + 1) * 256, :].rearrange("(il p) c -> p il c", p=128)),
                    inc=csem, incby=16)
                g_c = p.c(csem)
                p.wait("tensor", csem, g_c)
                for il in range(2):
                    coff = base + il * 2048
                    for eg in range(2):
                        bk = eg % 2
                        sem, val = pt_free[bk]
                        p.wait("tensor", sem, val)
                        t_last = 0
                        for k in range(8):
                            et = eg * 8 + k
                            t_last = tp(pt[bk][:, k * 128:(k + 1) * 128],
                                        arena[:, coff + et * 128:coff + (et + 1) * 128])
                        p.wait("scalar", "tsem", t_last)
                        p.wait("scalar", "wprep" if sset == 0 else "wprep1", write_guard[sset])
                        s_t = p.emit(
                            "scalar",
                            lambda e, eg=eg, il=il, bk=bk, wT_view=wT_view: e.copy(
                                wT_view[:, eg * 8:(eg + 1) * 8, il, :], pt[bk][:, 0:1024]),
                            inc="ssem")
                        pt_free[bk] = ("ssem", s_t)
                cast_guard[sset] = p.c("tsem")
                p.wait("sync", "ssem", p.c("ssem"))
                write_guard[sset] = p.emit(
                    "sync",
                    lambda e, ib=ib, d=dst, woff=woff: e.dma_start(
                        d[ib].rearrange("et p c -> p et c"),
                        arena[:, woff:woff + 4096].rearrange("p (et c) -> p et c", c=256)),
                    inc="wprep" if sset == 0 else "wprep1", incby=16)

        prep_wT(w_up, upT, 32)
        prep_wT(w_gate, gateT, 32)
        prep_wT(w_in, winT, 8)

        p3_cast_guard = p.c("tsem")
        p3_z_guard = p.c("dnw")
        p.wait("gpsimd", "wprep", p.c("wprep"))
        p.wait("gpsimd", "wprep1", p.c("wprep1"))
        p.wait("vector", "wprep", p.c("wprep"))
        p.wait("vector", "wprep1", p.c("wprep1"))
        p.wait("sync", "tsem", xb_free_t)
        p.wait("sync", "vsem", v_prep0)
        xstage = [("vsem", 0), ("vsem", 0)]

        def pong_ap(it):
            if it < 32:
                return fmA[:, it * 256:(it + 1) * 256]
            return tmA[:, (it - 32) * 256:(it - 31) * 256]

        def p3_j1(eb, guard):
            v_c = 0
            for hh in range(2):
                sem, val = xstage[hh]
                p.wait("sync", sem, val)
                p.emit(
                    "sync",
                    lambda e, eb=eb, hh=hh: e.dma_start(
                        x_b[:, hh * 4096:(hh + 1) * 4096],
                        w_down[eb * 256 + 128:eb * 256 + 256,
                               hh * 4096:(hh + 1) * 4096]),
                    inc="castg2", incby=16)
                lt = p.c("castg2")
                p.wait("gpsimd", "castg2", lt)
                p.wait("gpsimd", "tsem", guard)
                v_c = p.emit("gpsimd", lambda e, hh=hh: e.tensor_copy(
                    arena[:, 8192 + hh * 4096:8192 + (hh + 1) * 4096],
                    x_b[:, hh * 4096:(hh + 1) * 4096]), inc="bsem")
                xstage[hh] = ("bsem", v_c)
            return v_c

        j1_tok = p3_j1(0, p3_cast_guard)
        for eb in range(8):
            p.wait("gpsimd", "tsem", p3_cast_guard)
            p.emit(
                "gpsimd",
                lambda e, eb=eb: e.dma_start(
                    arena[:, 0:8192],
                    w_down[eb * 256:eb * 256 + 128, :]),
                inc="castg", incby=16)
            g_c = p.c("castg")
            p.wait("tensor", "castg", g_c)
            p.wait("tensor", "bsem", j1_tok)
            for tg in range(16):
                bk = tg % 2
                sem, val = pt_free[bk]
                p.wait("tensor", sem, val)
                t_last = 0
                for k in range(8):
                    tix = tg * 8 + k
                    it, j = tix // 2, tix % 2
                    t_last = tp(pt[bk][:, k * 128:(k + 1) * 128],
                                arena[:, j * 8192 + it * 128:j * 8192 + (it + 1) * 128])
                p.wait("scalar", "tsem", t_last)
                p.wait("scalar", "dnw", p3_z_guard)
                s_t = p.emit(
                    "scalar",
                    lambda e, tg=tg, bk=bk: e.copy(
                        arena[:, 16384 + tg * 1024:16384 + (tg + 1) * 1024], pt[bk][:, 0:1024]),
                    inc="ssem")
                pt_free[bk] = ("ssem", s_t)
            p3_cast_guard = p.c("tsem")
            if eb < 7:
                j1_tok = p3_j1(eb + 1, p3_cast_guard)
            p.wait("tensor", "ssem", p.c("ssem"))
            for itq in range(0, NIT, 4):
                t_s = 0
                for k in range(4):
                    it = itq + k
                    sem, val = pf_free[k]
                    p.wait("tensor", sem, val)
                    t_s = mm(pf[k][:, 0:256], h_sb[:],
                             arena[:, 16384 + it * 256:16384 + (it + 1) * 256],
                             True, True)
                p.wait("scalar", "tsem", t_s)
                s_s = 0
                for k in range(4):
                    it = itq + k
                    s_s = p.emit("scalar", lambda e, it=it, k=k: e.copy(
                        arena[:, 16384 + it * 256:16384 + (it + 1) * 256], pf[k][:, 0:256]),
                        inc="ssem")
                for k in range(4):
                    pf_free[k] = ("ssem", s_s)
            p.wait("vector", "ssem", p.c("ssem"))
            p.wait("vector", "dnw", p3_z_guard)
            for st in range(6):
                from_z = (st % 2 == 0)
                for u in range(NIT):
                    v = u ^ (1 << st)
                    if v < u:
                        continue
                    if from_z:
                        iu = arena[:, 16384 + u * 256:16384 + (u + 1) * 256]
                        iv = arena[:, 16384 + v * 256:16384 + (v + 1) * 256]
                        ou, ov = pong_ap(u), pong_ap(v)
                    else:
                        iu, iv = pong_ap(u), pong_ap(v)
                        ou = arena[:, 16384 + u * 256:16384 + (u + 1) * 256]
                        ov = arena[:, 16384 + v * 256:16384 + (v + 1) * 256]
                    p.emit("vector", lambda e, iu=iu, iv=iv, ou=ou: e.tensor_add(ou, iu, iv))
                    p.emit("vector", lambda e, iu=iu, iv=iv, ov=ov: e.tensor_sub(ov, iu, iv))
                p.drain("vector")
            v_bt = p.release("vector")
            p.wait("sync", "vsem", v_bt)
            for grp in range(8):
                p.emit(
                    "sync",
                    lambda e, eb=eb, grp=grp: e.dma_start(
                        dnT[grp * 1024:(grp + 1) * 1024, eb * 256:(eb + 1) * 256]
                        .rearrange("(k p) c -> p k c", p=128),
                        arena[:, 16384 + grp * 2048:16384 + (grp + 1) * 2048]
                        .rearrange("p (k c) -> p k c", c=256)),
                    inc="dnw", incby=16)
            p3_z_guard = p.c("dnw")

        t_rel = p.release("tensor")
        s_rel = p.release("scalar")
        v_rel = p.release("vector")
        pd = dict(wsem=p.c("wsem"), gsem=p.c("gsem"), tsem=t_rel, ssem=s_rel, vsem=v_rel,
                  wprep=p.c("wprep"), wprep1=p.c("wprep1"), castg=p.c("castg"), castg1=p.c("castg1"),
                  castg2=p.c("castg2"), bsem=p.c("bsem"), dnw=p.c("dnw"),
                  patg=p.c("patg"), gkv=p.c("gkv"))
        for eng in ("sync", "tensor", "scalar", "vector", "gpsimd"):
            for s, v in pd.items():
                p.wait(eng, s, v)

        A_NQ = 0
        A_AF = 8192
        A_E = 16384
        A_KB = 24576
        A_VB = 26624

        pf_free = [("ssem", p.c("ssem"))] * 6
        pt_free = [("ssem", p.c("ssem"))] * 2
        wbuf_tok = [("tsem", p.c("tsem"))] * 4
        kv_tok = [("tsem", p.c("tsem"))] * 2
        e_tok = [p.c("tsem"), p.c("tsem")]

        def layer_norm(eps, skip_stats=False):
            v_in = p.c("vsem")
            p.wait("scalar", "vsem", v_in)
            eps_ap = stats[:, 121:122] if eps == LN_EPS else stats[:, 122:123]
            p.drain("vector")
            if not skip_stats:
                for st in range(NST):
                    xs = x_b[:, st * E:(st + 1) * E]
                    for j in range(4):
                        p.emit("vector", lambda e, xs=xs, st=st, j=j: e.bn_stats(
                            stats[:, st * 24 + j * 6:st * 24 + (j + 1) * 6],
                            xs[:, j * 512:(j + 1) * 512]))
                p.drain("vector")
            v_mu = 0
            for st in range(NST):
                v_mu = p.emit("vector", lambda e, st=st: e.bn_aggr(
                    stats[:, 96 + 2 * st:98 + 2 * st], stats[:, st * 24:(st + 1) * 24]),
                    inc="vsem")
            p.wait("scalar", "vsem", v_mu)
            var_v = stats[:, 97:105].rearrange("p (a b) -> p a b", b=2)[:, :, 0:1]
            ln_v = stats[:, 104:108].rearrange("p (a b) -> p a b", b=1)
            rs_v = stats[:, 108:112].rearrange("p (a b) -> p a b", b=1)
            p.drain("scalar")
            p.emit("scalar", lambda e, eps_ap=eps_ap: e.activation(ln_v, var_v, ACT.Ln, bias=eps_ap))
            p.drain("scalar")
            s_r = p.emit("scalar", lambda e: e.activation(
                rs_v, ln_v, ACT.Exp, scale=stats[:, 120:121]), inc="ssem")
            p.wait("vector", "ssem", s_r)
            v_a = 0
            for st in range(NST):
                xs = x_b[:, st * E:(st + 1) * E]
                v_a = p.emit("vector", lambda e, xs=xs, st=st: e.tensor_scalar(
                    tmA[:, st * E:(st + 1) * E], xs, stats[:, 96 + 2 * st:97 + 2 * st],
                    stats[:, 108 + st:109 + st], ALU.subtract, ALU.mult), inc="vsem")
            return v_a

        def transpose_tm_to_fm(src_off, scaled, v_ready):
            p.wait("tensor", "vsem", v_ready)
            s_last = 0
            for et in range(NET):
                bk = et % 2
                sem, val = pt_free[bk]
                p.wait("tensor", sem, val)
                t_last = 0
                for st in range(NST):
                    src = tmA if src_off is None else arena
                    off = (0 if src_off is None else src_off) + st * E + et * 128
                    t_last = tp(pt[bk][:, st * 128:(st + 1) * 128], src[:, off:off + 128])
                p.wait("scalar", "tsem", t_last)
                if scaled:
                    s_last = p.emit("scalar", lambda e, et=et, bk=bk: e.activation(
                        fmA[:, et * C:(et + 1) * C], pt[bk][:, 0:C],
                        ACT.Copy, scale=su_sb[:, et:et + 1]), inc="ssem")
                else:
                    s_last = p.emit("scalar", lambda e, et=et, bk=bk: e.copy(
                        fmA[:, et * C:(et + 1) * C], pt[bk][:, 0:C]), inc="ssem")
                pt_free[bk] = ("ssem", s_last)
            return s_last

        def fwht2048_fm():
            p.wait("tensor", "ssem", p.c("ssem"))
            for et in range(NET):
                bix = et % 2
                sem, val = pf_free[bix]
                p.wait("tensor", sem, val)
                t_s = mm(pf[bix][:, 0:C], h_sb[:], fmA[:, et * C:(et + 1) * C], True, True)
                p.wait("scalar", "tsem", t_s)
                s_s = p.emit("scalar", lambda e, et=et, bix=bix: e.copy(
                    fmA[:, et * C:(et + 1) * C], pf[bix][:, 0:C]), inc="ssem")
                pf_free[bix] = ("ssem", s_s)
            p.wait("vector", "ssem", p.c("ssem"))
            p.wait("vector", "tsem", p.c("tsem"))
            ready = [0] * NET
            for st in range(4):
                src_fm = (st % 2 == 0)
                for u in range(NET):
                    v = u ^ (1 << st)
                    if v < u:
                        continue
                    if src_fm:
                        iu, iv = fmA[:, u * C:(u + 1) * C], fmA[:, v * C:(v + 1) * C]
                        ou, ov = arena[:, u * C:(u + 1) * C], arena[:, v * C:(v + 1) * C]
                    else:
                        iu, iv = arena[:, u * C:(u + 1) * C], arena[:, v * C:(v + 1) * C]
                        ou, ov = fmA[:, u * C:(u + 1) * C], fmA[:, v * C:(v + 1) * C]
                    p.emit("vector", lambda e, iu=iu, iv=iv, ou=ou: e.tensor_add(ou, iu, iv))
                    last = (st == 3)
                    vv = p.emit("vector", lambda e, iu=iu, iv=iv, ov=ov: e.tensor_sub(ov, iu, iv),
                                inc="vsem" if last else None)
                    if last:
                        ready[u] = vv
                        ready[v] = vv
                p.drain("vector")
            return ready

        ow = [0]

        def ffn(fm_ready, write_ch=None):
            korder = sorted(range(NET), key=lambda k: fm_ready[k])
            wb = [wbuf[:, k * 4096:(k + 1) * 4096] for k in range(4)]
            last_v_g = [p.c("vsem")]
            for ib in range(32):
                ub, gb = wb[(ib % 2) * 2], wb[(ib % 2) * 2 + 1]
                for m_i, (buf, srcT) in enumerate(((ub, upT), (gb, gateT))):
                    slot = (ib % 2) * 2 + m_i
                    sem, val = wbuf_tok[slot]
                    p.wait("sync", sem, val)
                    p.emit("sync", lambda e, ib=ib, buf=buf, srcT=srcT: e.dma_start(
                        buf.rearrange("p (et c) -> p et c", c=256),
                        srcT[ib].rearrange("et p c -> p et c")), inc=f"sl{slot}", incby=16)
                    p.wait("tensor", f"sl{slot}", p.c(f"sl{slot}"))
                for il in range(2):
                    it = ib * 2 + il
                    pu_i, pg_i = (it % 2) * 2, (it % 2) * 2 + 1
                    for bix in (pu_i, pg_i):
                        sem, val = pf_free[bix]
                        p.wait("tensor", sem, val)
                    t_u = 0
                    for ki, et in enumerate(korder):
                        p.wait("tensor", "vsem", fm_ready[et])
                        t_u = mm(pf[pu_i][:, 0:C], ub[:, et * 256 + il * 128:et * 256 + (il + 1) * 128],
                                 fmA[:, et * C:(et + 1) * C], ki == 0, ki == NET - 1)
                    t_g = 0
                    for ki, et in enumerate(korder):
                        t_g = mm(pf[pg_i][:, 0:C], gb[:, et * 256 + il * 128:et * 256 + (il + 1) * 128],
                                 fmA[:, et * C:(et + 1) * C], ki == 0, ki == NET - 1)
                    if il == 1:
                        wbuf_tok[(ib % 2) * 2] = ("tsem", t_g)
                        wbuf_tok[(ib % 2) * 2 + 1] = ("tsem", t_g)
                    p.wait("scalar", "tsem", t_g)
                    p.wait("scalar", "vsem", last_v_g[0])
                    s_si = p.emit("scalar", lambda e, it=it, pg_i=pg_i: e.activation(
                        silu_t[:, 0:C], pf[pg_i][:, 0:C], ACT.Silu, scale=wscg_sb[:, it:it + 1]),
                        inc="ssem")
                    p.wait("vector", "ssem", s_si)
                    p.wait("vector", "tsem", t_u)
                    v_g = p.emit("vector", lambda e, it=it, pu_i=pu_i: e.scalar_tensor_tensor(
                        arena[:, it * C:(it + 1) * C], pf[pu_i][:, 0:C], wscu_su[:, it:it + 1],
                        silu_t[:, 0:C], ALU.mult, ALU.mult), inc="vsem")
                    pf_free[pu_i] = ("vsem", v_g)
                    pf_free[pg_i] = ("ssem", s_si)
                    last_v_g[0] = v_g
            v_g_all = p.c("vsem")
            db = [wbuf[:, k * 4096:(k + 1) * 4096] for k in range(4)]
            p.wait("tensor", "vsem", v_g_all)
            v_d = 0
            for ec in range(4):
                for st in range(NST):
                    sem, val = pf_free[st]
                    p.wait("tensor", sem, val)
                t_d = 0
                for kb in range(8):
                    bi = kb % 4
                    sem, val = wbuf_tok[bi]
                    p.wait("sync", sem, val)
                    wv = p.emit("sync", lambda e, ec=ec, kb=kb, b=db[bi]: e.dma_start(
                        b.rearrange("p (k c) -> p k c", c=512),
                        dnT[kb * 1024:(kb + 1) * 1024, ec * 512:(ec + 1) * 512]
                        .rearrange("(k p) c -> p k c", p=128)), inc=f"sl{bi}", incby=16)
                    p.wait("tensor", f"sl{bi}", wv)
                    for st in range(NST):
                        for k in range(8):
                            it = kb * 8 + k
                            t_d = mm(pf[st][:, 0:512],
                                     arena[:, it * C + st * 128:it * C + (st + 1) * 128],
                                     db[bi][:, k * 512:(k + 1) * 512],
                                     it == 0, it == NIT - 1)
                    wbuf_tok[bi] = ("tsem", t_d)
                p.wait("vector", "tsem", t_d)
                for st in range(NST):
                    p.drain("vector")
                    p.emit("vector", lambda e, st=st, ec=ec: e.tensor_mul(
                        sq[:, 0:512], pf[st][:, 0:512], wscb[:, ec * 512:(ec + 1) * 512]))
                    p.drain("vector")
                    v_d = p.emit("vector", lambda e, st=st, ec=ec: e.tensor_add(
                        x_b[:, st * E + ec * 512:st * E + (ec + 1) * 512],
                        x_b[:, st * E + ec * 512:st * E + (ec + 1) * 512],
                        sq[:, 0:512]), inc="vsem")
                    pf_free[st] = ("vsem", v_d)
                    if write_ch is not None and ec == 3:
                        p.wait("gpsimd", "vsem", v_d)
                        p.emit("gpsimd", lambda e, ch=write_ch, st=st: e.dma_start(
                            y_out[ch * C + st * 128:ch * C + (st + 1) * 128, :],
                            x_b[:, st * E:(st + 1) * E]), inc="osem", incby=16)
                        ow[0] += 1
            p.drain("vector")
            return v_d

        def rms_apply(src_ap_fn, dst_off, w_bcast, v_ready, out_is_nq, stats_done=False):
            p.wait("scalar", "vsem", v_ready)
            p.wait("vector", "vsem", v_ready)
            p.drain("vector")
            if not stats_done:
                for st in range(NST):
                    a_s = src_ap_fn(st)
                    for j in range(4):
                        p.emit("vector", lambda e, a_s=a_s, st=st, j=j: e.bn_stats(
                            stats[:, st * 24 + j * 6:st * 24 + (j + 1) * 6],
                            a_s[:, j * 512:(j + 1) * 512]))
                p.drain("vector")
            for st in range(NST):
                p.emit("vector", lambda e, st=st: e.bn_aggr(
                    stats[:, 96 + 2 * st:98 + 2 * st], stats[:, st * 24:(st + 1) * 24]))
            p.drain("vector")
            mean_v = stats[:, 96:104].rearrange("p (a b) -> p a b", b=2)[:, :, 0:1]
            var_v = stats[:, 97:105].rearrange("p (a b) -> p a b", b=2)[:, :, 0:1]
            msq_v = stats[:, 104:108].rearrange("p (a b) -> p a b", b=1)
            rs_v = stats[:, 108:112].rearrange("p (a b) -> p a b", b=1)
            v_ss = p.emit("vector", lambda e: e.tensor_tensor(
                msq_v, mean_v, mean_v, ALU.mult), inc="vsem")
            p.drain("vector")
            v_ss = p.emit("vector", lambda e: e.tensor_tensor(
                msq_v, msq_v, var_v, ALU.add), inc="vsem")
            p.wait("scalar", "vsem", v_ss)
            p.drain("scalar")
            p.emit("scalar", lambda e: e.activation(msq_v, msq_v, ACT.Ln, bias=stats[:, 122:123]))
            p.drain("scalar")
            s_r = p.emit("scalar", lambda e: e.activation(
                rs_v, msq_v, ACT.Exp, scale=stats[:, 120:121]), inc="ssem")
            p.wait("vector", "ssem", s_r)
            last = 0
            for st in range(NST):
                a_s = src_ap_fn(st)
                if out_is_nq:
                    last = p.emit("vector", lambda e, a_s=a_s, st=st: e.scalar_tensor_tensor(
                        arena[:, dst_off + st * E:dst_off + (st + 1) * E], a_s,
                        stats[:, 108 + st:109 + st], w_bcast[:], ALU.mult, ALU.mult), inc="vsem")
                else:
                    p.drain("vector")
                    p.emit("vector", lambda e, a_s=a_s, st=st: e.scalar_tensor_tensor(
                        sq[:], a_s, stats[:, 108 + st:109 + st], w_bcast[:], ALU.mult, ALU.mult))
                    p.drain("vector")
                    last = p.emit("vector", lambda e, st=st: e.tensor_add(
                        x_b[:, st * E:(st + 1) * E], x_b[:, st * E:(st + 1) * E], sq[:]), inc="vsem")
            return last

        def attn(v_nq_ready):
            s_tr = transpose_tm_to_fm(A_NQ, False, v_nq_ready)
            wb = [wbuf[:, k * 4096:(k + 1) * 4096] for k in range(2)]
            p.wait("tensor", "ssem", s_tr)
            for eb in range(8):
                bi = eb % 2
                sem, val = wbuf_tok[bi]
                p.wait("sync", sem, val)
                p.emit("sync", lambda e, eb=eb, b=wb[bi]: e.dma_start(
                    b.rearrange("p (et c) -> p et c", c=256),
                    winT[eb].rearrange("et p c -> p et c")), inc=f"sl{bi}", incby=16)
                p.wait("tensor", f"sl{bi}", p.c(f"sl{bi}"))
                for il in range(2):
                    ep = eb * 2 + il
                    bix = 4 + (ep % 2)
                    sem, val = pf_free[bix]
                    p.wait("tensor", sem, val)
                    t_p = 0
                    for et in range(NET):
                        t_p = mm(pf[bix][:, 0:C], wb[bi][:, et * 256 + il * 128:et * 256 + (il + 1) * 128],
                                 fmA[:, et * C:(et + 1) * C], et == 0, et == NET - 1)
                    if il == 1:
                        wbuf_tok[bi] = ("tsem", t_p)
                    p.wait("scalar", "tsem", t_p)
                    s_p = p.emit("scalar", lambda e, ep=ep, bix=bix: e.copy(
                        arena[:, A_NQ + ep * C:A_NQ + (ep + 1) * C], pf[bix][:, 0:C]), inc="ssem")
                    pf_free[bix] = ("ssem", s_p)
            p.wait("tensor", "ssem", p.c("ssem"))
            for h in range(H):
                bix = 4 + (h % 2)
                sem, val = pf_free[bix]
                p.wait("tensor", sem, val)
                t_q = mm(pf[bix][:, 0:C], wqT[:], arena[:, A_NQ + h * C:A_NQ + (h + 1) * C],
                         True, True)
                p.wait("scalar", "tsem", t_q)
                s_q = p.emit("scalar", lambda e, h=h, bix=bix: e.copy(
                    fmA[:, h * C:(h + 1) * C], pf[bix][:, 0:C]), inc="ssem")
                pf_free[bix] = ("ssem", s_q)
            p.wait("tensor", "ssem", p.c("ssem"))
            kv_free = kv_tok
            e_guard = e_tok
            s_e_done = [0] * H
            v_o = 0

            def emit_scores(h):
                sl = h % 2
                koff = A_KB + sl * 1024
                voff = A_VB + sl * 1088
                sem, val = kv_free[sl]
                p.wait("sync", sem, val)
                p.emit("sync", lambda e, h=h, koff=koff: e.dma_start(
                    arena[:, koff:koff + 1024], kT[h]), inc=f"kv{sl}", incby=16)
                w_kv = p.emit("sync", lambda e, h=h, voff=voff: e.dma_start(
                    arena[:, voff:voff + 1088], vT[h]), inc=f"kv{sl}", incby=16)
                p.wait("tensor", f"kv{sl}", w_kv)
                eoff = A_E + sl * 4096
                p.wait("scalar", "tsem", e_guard[sl])
                for mtq in range(0, NMT, 4):
                    t_sc = 0
                    for k in range(4):
                        mt = mtq + k
                        sem, val = pf_free[k]
                        p.wait("tensor", sem, val)
                        t_sc = mm(pf[k][:, 0:C], arena[:, koff + mt * 128:koff + (mt + 1) * 128],
                                  fmA[:, h * C:(h + 1) * C], True, True)
                    p.wait("scalar", "tsem", t_sc)
                    s_e = 0
                    for k in range(4):
                        mt = mtq + k
                        s_e = p.emit("scalar", lambda e, mt=mt, k=k, eoff=eoff: e.activation(
                            arena[:, eoff + mt * C:eoff + (mt + 1) * C], pf[k][:, 0:C], ACT.Exp),
                            inc="ssem")
                    for k in range(4):
                        pf_free[k] = ("ssem", s_e)
                    if mtq == 0 and h > 0:
                        emit_out(h - 1)
                s_e_done[h] = p.c("ssem")

            def emit_out(h):
                nonlocal v_o
                sl = h % 2
                voff = A_VB + sl * 1088
                eoff = A_E + sl * 4096
                p.wait("tensor", "ssem", s_e_done[h])
                t_o = 0
                for st in range(NST):
                    bix = 4 + st // 2
                    coff = (st % 2) * 256
                    if st % 2 == 0:
                        sem, val = pf_free[bix]
                        p.wait("tensor", sem, val)
                    for mt in range(NMT):
                        t_o = mm(pf[bix][:, coff:coff + VW],
                                 arena[:, eoff + mt * C + st * 128:eoff + mt * C + (st + 1) * 128],
                                 arena[:, voff + mt * VW:voff + (mt + 1) * VW],
                                 mt == 0, mt == NMT - 1)
                p.wait("vector", "tsem", t_o)
                p.drain("vector")
                for st in range(NST):
                    bix = 4 + st // 2
                    coff = (st % 2) * 256
                    p.emit("vector", lambda e, st=st, bix=bix, coff=coff: e.reciprocal(
                        stats[:, 116 + st:117 + st], pf[bix][:, coff + 128:coff + 129]))
                p.drain("vector")
                for st in range(NST):
                    bix = 4 + st // 2
                    coff = (st % 2) * 256
                    v_o = p.emit("vector", lambda e, h=h, st=st, bix=bix, coff=coff: e.tensor_scalar(
                        arena[:, A_AF + st * E + h * 128:A_AF + st * E + (h + 1) * 128],
                        pf[bix][:, coff:coff + 128], stats[:, 116 + st:117 + st], None, ALU.mult), inc="vsem")
                for st in (0, 2):
                    pf_free[4 + st // 2] = ("vsem", v_o)
                kv_free[sl] = ("tsem", p.c("tsem"))
                e_guard[sl] = p.c("tsem")
                if h % 4 == 3:
                    j = h // 4
                    p.drain("vector")
                    for st in range(NST):
                        p.emit("vector", lambda e, st=st, j=j: e.bn_stats(
                            stats[:, st * 24 + j * 6:st * 24 + (j + 1) * 6],
                            arena[:, A_AF + st * E + j * 512:
                                  A_AF + st * E + (j + 1) * 512]))

            for h in range(H):
                emit_scores(h)
                if h > 0:
                    emit_out(h - 1)
            emit_out(H - 1)
            return rms_apply(lambda st: arena[:, A_AF + st * E:A_AF + (st + 1) * E],
                             0, wret_b, v_o, out_is_nq=False, stats_done=True)

        for ch in range(NCH):
            g_in = 0
            p.wait("vector", "tsem", p.c("tsem"))
            for st in range(NST):
                if ch > 0:
                    p.wait("sync", "osem", (4 * (ch - 1) + st + 1) * 16)
                g_in = p.emit("sync", lambda e, ch=ch, st=st: e.dma_start(
                    x_b[:, st * E:(st + 1) * E],
                    x_in[ch * C + st * 128:ch * C + (st + 1) * 128, :]), inc="gsem", incby=16)
                p.wait("vector", "gsem", g_in)
                for j in range(4):
                    p.emit("vector", lambda e, st=st, j=j: e.bn_stats(
                        stats[:, st * 24 + j * 6:st * 24 + (j + 1) * 6],
                        x_b[:, st * E + j * 512:st * E + (j + 1) * 512]))
            p.wait("scalar", "gsem", g_in)

            v_ln = layer_norm(LN_EPS, skip_stats=True)
            transpose_tm_to_fm(None, True, v_ln)
            fm_ready = fwht2048_fm()
            ffn(fm_ready)

            v_ln2 = layer_norm(LN_EPS)
            v_y = 0
            for st in range(NST):
                p.drain("vector")
                v_y = p.emit("vector", lambda e, st=st: e.tensor_add(
                    x_b[:, st * E:(st + 1) * E], x_b[:, st * E:(st + 1) * E],
                    tmA[:, st * E:(st + 1) * E]), inc="vsem")
            p.wait("vector", "tsem", p.c("tsem"))
            rs_v = stats[:, 108:112].rearrange("p (a b) -> p a b", b=1)
            var_v = stats[:, 97:105].rearrange("p (a b) -> p a b", b=2)[:, :, 0:1]
            w1_v = stats[:, 64:68].rearrange("p (a b) -> p a b", b=1)
            u_v = stats[:, 68:72].rearrange("p (a b) -> p a b", b=1)
            p.drain("vector")
            p.emit("vector", lambda e: e.tensor_tensor(w1_v, rs_v, rs_v, ALU.mult))
            p.drain("vector")
            v_w1 = p.emit("vector", lambda e: e.tensor_tensor(w1_v, w1_v, var_v, ALU.mult), inc="vsem")
            p.wait("scalar", "vsem", v_w1)
            p.drain("scalar")
            p.emit("scalar", lambda e: e.activation(u_v, w1_v, ACT.Ln, bias=stats[:, 122:123]))
            p.drain("scalar")
            s_u = p.emit("scalar", lambda e: e.activation(
                u_v, u_v, ACT.Exp, scale=stats[:, 120:121]), inc="ssem")
            p.wait("vector", "ssem", s_u)
            p.wait("vector", "vsem", v_y)
            v_nq = 0
            for st in range(NST):
                v_nq = p.emit("vector", lambda e, st=st: e.scalar_tensor_tensor(
                    arena[:, A_NQ + st * E:A_NQ + (st + 1) * E], tmA[:, st * E:(st + 1) * E],
                    stats[:, 68 + st:69 + st], wnq_b[:], ALU.mult, ALU.mult), inc="vsem")
            attn(v_nq)

            v_ln3 = layer_norm(LN_EPS)
            transpose_tm_to_fm(None, True, v_ln3)
            fm_ready3 = fwht2048_fm()
            ffn(fm_ready3, write_ch=ch)

        total_out = ow[0] * 16

        with nc.Block() as block:
            @block.sync
            def _(eng):
                for fn in p.plan["sync"]:
                    fn(eng)

            @block.tensor
            def _(eng):
                for fn in p.plan["tensor"]:
                    fn(eng)

            @block.scalar
            def _(eng):
                for fn in p.plan["scalar"]:
                    fn(eng)

            @block.vector
            def _(eng):
                for fn in p.plan["vector"]:
                    fn(eng)

            @block.gpsimd
            def _(eng):
                for fn in p.plan["gpsimd"]:
                    fn(eng)
                eng.wait_ge(p.sems["osem"], total_out)

    return nc


_NC_CACHE = None


def kernel(L0_attn_query_input, L0_attn_W_in, L0_attn_W_q, L0_attn_W_k, L0_attn_W_v,
           L0_attn_stored_patterns, L0_attn_norm_query_w, L0_attn_norm_retrieved_w,
           L0_attn_beta, L0_ffn_SU_ug, L0_ffn_SU_dp, L0_ffn_Wscale_ug, L0_ffn_Wscale_dp,
           L0_ffn_W_up, L0_ffn_W_gate, L0_ffn_W_down):
    global _NC_CACHE
    x = np.ascontiguousarray(np.asarray(L0_attn_query_input, dtype=np.float32).reshape(16384, E))
    consts = dict(
        h128=np.ascontiguousarray(_hadamard(128).astype(BFNP)),
        ident=np.ascontiguousarray(np.eye(128, dtype=np.float32).astype(BFNP)),
        ones1=np.ones((1, 128), dtype=BFNP),
        ones1f=np.ones((1, 128), dtype=np.float32),
    )
    weights = dict(
        w_in=np.ascontiguousarray(np.asarray(L0_attn_W_in, dtype=np.float32)),
        w_q=np.ascontiguousarray(np.asarray(L0_attn_W_q, dtype=np.float32)),
        w_k=np.ascontiguousarray(np.asarray(L0_attn_W_k, dtype=np.float32)),
        w_v=np.ascontiguousarray(np.asarray(L0_attn_W_v, dtype=np.float32)),
        pats=np.ascontiguousarray(np.asarray(L0_attn_stored_patterns, dtype=np.float32)),
        nqw=np.ascontiguousarray(np.asarray(L0_attn_norm_query_w, dtype=np.float32)),
        nrw=np.ascontiguousarray(np.asarray(L0_attn_norm_retrieved_w, dtype=np.float32)),
        beta=np.asarray(L0_attn_beta, dtype=np.float32).reshape(1),
        su_ug=np.ascontiguousarray(np.asarray(L0_ffn_SU_ug, dtype=np.float32)),
        su_dp=np.ascontiguousarray(np.asarray(L0_ffn_SU_dp, dtype=np.float32)),
        wsc_ug=np.ascontiguousarray(np.asarray(L0_ffn_Wscale_ug, dtype=np.float32)),
        wsc_dp=np.ascontiguousarray(np.asarray(L0_ffn_Wscale_dp, dtype=np.float32)),
        w_up=np.ascontiguousarray(np.asarray(L0_ffn_W_up, dtype=np.float32)),
        w_gate=np.ascontiguousarray(np.asarray(L0_ffn_W_gate, dtype=np.float32)),
        w_down=np.ascontiguousarray(np.asarray(L0_ffn_W_down, dtype=np.float32)),
    )
    if _NC_CACHE is None:
        _NC_CACHE = build_nc()
    nc = _NC_CACHE
    in_maps = []
    for c in range(NCORES):
        m = {"x": np.ascontiguousarray(x[c * TCORE:(c + 1) * TCORE])}
        m.update(weights)
        m.update(consts)
        in_maps.append(m)
    res = run_bass_kernel_spmd(nc, in_maps, core_ids=list(range(NCORES)))
    out = np.concatenate([res.results[c]["y"] for c in range(NCORES)], axis=0)
    return np.ascontiguousarray(out.reshape(1, 16384, E).astype(np.float32))


if __name__ == "__main__":
    build_nc()
    print("built ok")


# version 22 (final)
# speedup vs baseline: 1.2165x; 1.0110x over previous
import numpy as np
import ml_dtypes
from contextlib import ExitStack

import concourse.bass as bass
import concourse.mybir as mybir
from concourse.bass_utils import run_bass_kernel_spmd

BFNP = ml_dtypes.bfloat16
F32 = mybir.dt.float32
BF16 = mybir.dt.bfloat16
AX = mybir.AxisListType.X
ALU = mybir.AluOpType
ACT = mybir.ActivationFunctionType

NCORES = 8
TCORE = 2048
C = 512
NCH = TCORE // C
NST = C // 128
E = 2048
I = 8192
NET = E // 128
NIT = I // 128
H = 16
D = 128
M = 1024
NMT = M // 128
VW = 136
RS2048 = float(1.0 / np.sqrt(2048.0))
RS8192 = float(1.0 / np.sqrt(8192.0))
LN_EPS = 1e-5
RMS_EPS = 1e-6

ENG_SEM = {"tensor": "tsem", "scalar": "ssem", "vector": "vsem"}


def _hadamard(n):
    h = np.array([[1.0]], dtype=np.float64)
    while h.shape[0] < n:
        h = np.block([[h, h], [h, -h]])
    return h


class P:

    def __init__(self):
        self.plan = {k: [] for k in ("sync", "tensor", "scalar", "vector", "gpsimd")}
        self.cnt = {}
        self.lastw = {}
        self.sems = {}

    def emit(self, eng, build, inc=None, incby=1):
        if inc is not None:
            self.cnt[inc] = self.cnt.get(inc, 0) + incby

            def fn(e, b=build, s=inc, n=incby):
                b(e).then_inc(self.sems[s], n)
        else:
            def fn(e, b=build):
                b(e)
        self.plan[eng].append(fn)
        return self.cnt.get(inc, 0)

    def wait(self, eng, sem, val):
        if not val or val <= 0:
            return
        key = (eng, sem)
        if self.lastw.get(key, 0) >= val:
            return
        self.lastw[key] = val
        self.plan[eng].append(lambda e, s=sem, v=val: e.wait_ge(self.sems[s], v))

    def drain(self, eng):
        self.plan[eng].append(lambda e: e.drain())

    def release(self, eng):
        sem = ENG_SEM[eng]
        return self.emit(eng, lambda e: e.drain(), inc=sem)

    def c(self, sem):
        return self.cnt.get(sem, 0)


def build_nc():
    nc = bass.Bass("TRN2", target_bir_lowering=False)
    p = P()

    x_in = nc.dram_tensor("x", [TCORE, E], F32, kind="ExternalInput").ap()
    w_in = nc.dram_tensor("w_in", [E, E], F32, kind="ExternalInput").ap()
    w_q = nc.dram_tensor("w_q", [D, D], F32, kind="ExternalInput").ap()
    w_k = nc.dram_tensor("w_k", [D, D], F32, kind="ExternalInput").ap()
    w_v = nc.dram_tensor("w_v", [D, D], F32, kind="ExternalInput").ap()
    pats = nc.dram_tensor("pats", [H, M, D], F32, kind="ExternalInput").ap()
    nqw = nc.dram_tensor("nqw", [E], F32, kind="ExternalInput").ap()
    nrw = nc.dram_tensor("nrw", [E], F32, kind="ExternalInput").ap()
    beta = nc.dram_tensor("beta", [1], F32, kind="ExternalInput").ap()
    su_ug = nc.dram_tensor("su_ug", [E], F32, kind="ExternalInput").ap()
    su_dp = nc.dram_tensor("su_dp", [I], F32, kind="ExternalInput").ap()
    wsc_ug = nc.dram_tensor("wsc_ug", [2 * I], F32, kind="ExternalInput").ap()
    wsc_dp = nc.dram_tensor("wsc_dp", [E], F32, kind="ExternalInput").ap()
    w_up = nc.dram_tensor("w_up", [I, E], F32, kind="ExternalInput").ap()
    w_gate = nc.dram_tensor("w_gate", [I, E], F32, kind="ExternalInput").ap()
    w_down = nc.dram_tensor("w_down", [E, I], F32, kind="ExternalInput").ap()
    h128 = nc.dram_tensor("h128", [128, 128], BF16, kind="ExternalInput").ap()
    ident = nc.dram_tensor("ident", [128, 128], BF16, kind="ExternalInput").ap()
    ones1 = nc.dram_tensor("ones1", [1, 128], BF16, kind="ExternalInput").ap()
    ones1f = nc.dram_tensor("ones1f", [1, 128], F32, kind="ExternalInput").ap()

    y_out = nc.dram_tensor("y", [TCORE, E], F32, kind="ExternalOutput").ap()

    upT = nc.dram_tensor("upT", [32, NET, 128, 256], BF16, kind="Internal").ap()
    gateT = nc.dram_tensor("gateT", [32, NET, 128, 256], BF16, kind="Internal").ap()
    winT = nc.dram_tensor("winT", [8, NET, 128, 256], BF16, kind="Internal").ap()
    dnT = nc.dram_tensor("dnT", [I, E], BF16, kind="Internal").ap()
    kT = nc.dram_tensor("kT", [H, 128, M], BF16, kind="Internal").ap()
    vT = nc.dram_tensor("vT", [H, 128, NMT * VW], BF16, kind="Internal").ap()

    with ExitStack() as ctx:
        for s in ("wsem", "gsem", "tsem", "ssem", "vsem", "osem",
                  "wprep", "wprep1", "castg", "castg1", "castg2", "bsem", "dnw", "patg", "gkv",
                  "sl0", "sl1", "sl2", "sl3", "kv0", "kv1"):
            p.sems[s] = ctx.enter_context(nc.semaphore(s))

        x_b = ctx.enter_context(nc.sbuf_tensor("x_b", [128, NST * E], F32))
        tmA = ctx.enter_context(nc.sbuf_tensor("tmA", [128, NST * E], BF16))
        fmA = ctx.enter_context(nc.sbuf_tensor("fmA", [128, NET * C], BF16))
        fwtmp = ctx.enter_context(nc.sbuf_tensor("fwtmp", [128, C], BF16))
        arena = ctx.enter_context(nc.sbuf_tensor("arena", [128, NIT * C], BF16))
        wbuf = ctx.enter_context(nc.sbuf_tensor("wbuf", [128, 4 * 4096], BF16))
        sq = ctx.enter_context(nc.sbuf_tensor("sq", [128, E], F32))
        silu_t = ctx.enter_context(nc.sbuf_tensor("silu_t", [128, C], BF16))
        stats = ctx.enter_context(nc.sbuf_tensor("stats", [128, 128], F32))
        h_sb = ctx.enter_context(nc.sbuf_tensor("h_sb", [128, 128], BF16))
        id_sb = ctx.enter_context(nc.sbuf_tensor("id_sb", [128, 128], BF16))
        on_sb = ctx.enter_context(nc.sbuf_tensor("on_sb", [1, 128], BF16))
        onf_sb = ctx.enter_context(nc.sbuf_tensor("onf_sb", [1, 128], F32))
        su_sb = ctx.enter_context(nc.sbuf_tensor("su_sb", [128, NET], F32))
        wscu_su = ctx.enter_context(nc.sbuf_tensor("wscu_su", [128, NIT], F32))
        wscg_sb = ctx.enter_context(nc.sbuf_tensor("wscg_sb", [128, NIT], F32))
        wscb = ctx.enter_context(nc.sbuf_tensor("wscb", [128, E], F32))
        wnq_b = ctx.enter_context(nc.sbuf_tensor("wnq_b", [128, E], BF16))
        wret_b = ctx.enter_context(nc.sbuf_tensor("wret_b", [128, E], BF16))
        wqT = ctx.enter_context(nc.sbuf_tensor("wqT", [128, 128], BF16))
        wkT = ctx.enter_context(nc.sbuf_tensor("wkT", [128, 128], BF16))
        wvT = ctx.enter_context(nc.sbuf_tensor("wvT", [128, 128], BF16))
        bsig = ctx.enter_context(nc.sbuf_tensor("bsig", [128, 4], F32))

        pf = [ctx.enter_context(nc.psum_tensor(f"pf{i}", [128, 512], F32)) for i in range(6)]
        pt = [ctx.enter_context(nc.psum_tensor(f"pt{i}", [128, 1024], BF16)) for i in range(2)]

        pf_free = [("ssem", 0)] * 6
        pt_free = [("ssem", 0)] * 2

        def mm(out_ap, lhsT_ap, rhs_ap, start, stop):
            return p.emit(
                "tensor",
                lambda e, o=out_ap, l=lhsT_ap, r=rhs_ap, st=start, sp=stop:
                    e.matmul(o, l, r, start=st, stop=sp),
                inc="tsem")

        def tp(out_ap, in_ap):
            return p.emit(
                "tensor",
                lambda e, o=out_ap, i_=in_ap: e.transpose(o, i_, id_sb[:]),
                inc="tsem")

        def sload(dst, src, slow=False):
            return p.emit("sync", lambda e, d=dst, s=src, sl=slow: e.dma_start(
                d, s, allow_slow_non_contiguous=sl), inc="wsem", incby=16)

        sload(h_sb[:], h128)
        sload(id_sb[:], ident)
        sload(on_sb[:], ones1)
        sload(onf_sb[:], ones1f)
        sload(su_sb[:], su_ug.rearrange("(a b) -> b a", b=128), slow=True)
        sload(wscu_su[:], wsc_ug[0:I].rearrange("(a b) -> b a", b=128), slow=True)
        sload(wscg_sb[:], wsc_ug[I:2 * I].rearrange("(a b) -> b a", b=128), slow=True)
        sload(stats[0:1, 112:113], beta.rearrange("(a b) -> a b", b=1))
        sload(x_b[0:1, 0:E], nqw.rearrange("(a b) -> a b", a=1))
        sload(x_b[0:1, E:2 * E], nrw.rearrange("(a b) -> a b", a=1))
        sload(x_b[0:1, 2 * E:3 * E], wsc_dp.rearrange("(a b) -> a b", a=1))
        sload(sq[:, 1024:1024 + NIT], su_dp.rearrange("(a b) -> b a", b=128), slow=True)
        sload(sq[:, 1200:1328], w_q)
        sload(sq[:, 1328:1456], w_k)
        w_all = sload(sq[:, 1456:1584], w_v)

        p.emit("vector", lambda e: e.memset(stats[:, 120:121], -0.5))
        p.emit("vector", lambda e: e.memset(stats[:, 121:122], float(LN_EPS)))
        p.emit("vector", lambda e: e.memset(stats[:, 122:123], float(RMS_EPS)))
        p.emit("vector", lambda e: e.memset(stats[:, 123:124], -1.0))
        v_consts = p.release("vector")
        p.wait("vector", "wsem", w_all)
        p.emit("vector", lambda e: e.tensor_scalar_mul(su_sb[:], su_sb[:], RS2048))
        p.emit("vector", lambda e: e.tensor_mul(wscu_su[:], wscu_su[:], sq[:, 1024:1024 + NIT]))
        p.wait("scalar", "wsem", w_all)
        p.wait("scalar", "vsem", v_consts)
        p.emit("scalar", lambda e: e.activation(
            stats[0:1, 113:114], stats[0:1, 112:113], ACT.Exp, scale=stats[0:1, 123:124]))
        p.emit("scalar", lambda e: e.mul(x_b[0:1, 2 * E:3 * E], x_b[0:1, 2 * E:3 * E], 0.5 * RS8192))
        s_rows0 = p.release("scalar")
        p.wait("vector", "ssem", s_rows0)
        p.emit("vector", lambda e: e.tensor_scalar_add(stats[0:1, 114:115], stats[0:1, 113:114], 1.0))
        p.drain("vector")
        p.emit("vector", lambda e: e.reciprocal(stats[0:1, 115:116], stats[0:1, 114:115]))
        p.emit("vector", lambda e: e.tensor_copy(tmA[0:1, 0:E], x_b[0:1, 0:E]))
        p.emit("vector", lambda e: e.tensor_copy(tmA[:, 2 * E:2 * E + 128], sq[:, 1200:1328]))
        p.emit("vector", lambda e: e.tensor_copy(tmA[:, 2 * E + 128:2 * E + 256], sq[:, 1328:1456]))
        p.emit("vector", lambda e: e.tensor_copy(tmA[:, 2 * E + 256:2 * E + 384], sq[:, 1456:1584]))
        p.emit("vector", lambda e: e.tensor_copy(tmA[0:1, E:2 * E], x_b[0:1, E:2 * E]))
        v_prep0 = p.release("vector")

        p.wait("tensor", "vsem", v_prep0)
        p.wait("tensor", "ssem", s_rows0)
        for j in range(4):
            mm(pf[0][:, 0:512], on_sb[:], tmA[0:1, j * 512:(j + 1) * 512], True, True)
            mm(pf[1][:, 0:512], on_sb[:], tmA[0:1, E + j * 512:E + (j + 1) * 512], True, True)
            t_b = mm(pf[2][:, 0:512], onf_sb[:], x_b[0:1, 2 * E + j * 512:2 * E + (j + 1) * 512], True, True)
            p.wait("scalar", "tsem", t_b)
            p.emit("scalar", lambda e, j=j: e.copy(wnq_b[:, j * 512:(j + 1) * 512], pf[0][:, 0:512]))
            p.emit("scalar", lambda e, j=j: e.copy(wret_b[:, j * 512:(j + 1) * 512], pf[1][:, 0:512]))
            s_b = p.emit("scalar", lambda e, j=j: e.copy(wscb[:, j * 512:(j + 1) * 512], pf[2][:, 0:512]), inc="ssem")
            p.wait("tensor", "ssem", s_b)
        t_bs = mm(pf[3][:, 0:1], onf_sb[:], stats[0:1, 115:116], True, True)
        p.wait("scalar", "tsem", t_bs)
        s_bs = p.emit("scalar", lambda e: e.copy(bsig[:, 0:1], pf[3][:, 0:1]), inc="ssem")

        tp(pt[0][:, 0:128], tmA[:, 2 * E:2 * E + 128])
        tp(pt[0][:, 128:256], tmA[:, 2 * E + 128:2 * E + 256])
        t_w3 = tp(pt[0][:, 256:384], tmA[:, 2 * E + 256:2 * E + 384])
        p.wait("vector", "tsem", t_w3)
        p.wait("vector", "ssem", s_bs)
        p.emit("vector", lambda e: e.tensor_scalar_mul(wqT[:], pt[0][:, 0:128], bsig[:, 0:1]))
        p.emit("vector", lambda e: e.tensor_copy(wkT[:], pt[0][:, 128:256]))
        v_w3 = p.emit("vector", lambda e: e.tensor_copy(wvT[:], pt[0][:, 256:384]), inc="vsem")
        pt_free[0] = ("vsem", v_w3)
        pf_free[3] = ("ssem", s_bs)
        xb_free_t = p.c("tsem")

        pat_guard = [0, 0]
        for h in range(H):
            sl = h % 2
            poff = sl * 1024
            toff = 2048 + sl * 1024
            p.wait("gpsimd", "tsem", pat_guard[sl])
            g_pat = p.emit(
                "gpsimd",
                lambda e, h=h, poff=poff: e.dma_start(
                    fmA[:, poff:poff + 1024].rearrange("p (mt d) -> p mt d", d=128),
                    pats[h].rearrange("(mt p) d -> p mt d", p=128)),
                inc="patg", incby=16)
            p.wait("tensor", "patg", g_pat)
            for eg in range(2):
                bk = eg % 2
                sem, val = pt_free[bk]
                p.wait("tensor", sem, val)
                t_last = 0
                for k in range(8):
                    mt = eg * 8 + k
                    if mt >= 8:
                        break
                    t_last = tp(pt[bk][:, k * 128:(k + 1) * 128],
                                fmA[:, poff + mt * 128:poff + (mt + 1) * 128])
                if eg == 0:
                    p.wait("scalar", "tsem", t_last)
                    sv = p.emit("scalar",
                                lambda e, toff=toff, bk=bk: e.copy(
                                    fmA[:, toff:toff + 1024], pt[bk][:, 0:1024]),
                                inc="ssem")
                    pt_free[bk] = ("ssem", sv)
                    break
            s_patT = p.c("ssem")
            for bix in (0, 1):
                sem, val = pf_free[bix]
                p.wait("tensor", sem, val)
            p.wait("tensor", "ssem", s_patT)
            mm(pf[0][:, 0:512], wkT[:], fmA[:, toff:toff + 512], True, True)
            t_k = mm(pf[1][:, 0:512], wkT[:], fmA[:, toff + 512:toff + 1024], True, True)
            p.wait("vector", "tsem", t_k)
            p.drain("vector")
            p.emit("vector", lambda e: e.reduce_sum(stats[:, 0:1], pf[0][:, 0:512], axis=AX))
            p.emit("vector", lambda e: e.reduce_sum(stats[:, 1:2], pf[1][:, 0:512], axis=AX))
            p.drain("vector")
            p.emit("vector", lambda e: e.tensor_add(stats[:, 2:3], stats[:, 0:1], stats[:, 1:2]))
            p.drain("vector")
            p.emit("vector", lambda e: e.tensor_scalar_mul(stats[:, 3:4], stats[:, 2:3], 1.0 / M))
            p.drain("vector")
            p.emit("vector", lambda e: e.tensor_scalar(
                arena[:, 0:512], pf[0][:, 0:512], stats[:, 3:4], None, ALU.subtract), inc="vsem")
            v_k = p.emit("vector", lambda e: e.tensor_scalar(
                arena[:, 512:1024], pf[1][:, 0:512], stats[:, 3:4], None, ALU.subtract), inc="vsem")
            pf_free[0] = ("vsem", v_k)
            pf_free[1] = ("vsem", v_k)
            p.wait("gpsimd", "vsem", v_k)
            g_k = p.emit("gpsimd", lambda e, h=h: e.dma_start(kT[h], arena[:, 0:1024]),
                         inc="gkv", incby=16)
            p.drain("vector")
            vz = p.emit("vector", lambda e: e.memset(arena[:, 1024:1024 + NMT * VW], 0.0), inc="vsem")
            p.drain("vector")
            ones_ap = arena[:, 1024:1024 + NMT * VW].rearrange(
                "p (mt c) -> p mt c", c=VW)[:, :, 128:129]
            vm_last = p.emit("vector", lambda e, ones_ap=ones_ap: e.memset(ones_ap, 1.0), inc="vsem")
            p.wait("scalar", "vsem", vm_last)
            t_v = 0
            for mtq in range(0, 8, 4):
                for k in range(4):
                    mt = mtq + k
                    sem, val = pf_free[k]
                    p.wait("tensor", sem, val)
                    t_v = mm(pf[k][:, 0:128], fmA[:, toff + mt * 128:toff + (mt + 1) * 128],
                             wvT[:], True, True)
                p.wait("scalar", "tsem", t_v)
                s_v = 0
                for k in range(4):
                    mt = mtq + k
                    s_v = p.emit("scalar", lambda e, mt=mt, k=k: e.copy(
                        arena[:, 1024 + mt * VW:1024 + mt * VW + 128], pf[k][:, 0:128]), inc="ssem")
                for k in range(4):
                    pf_free[k] = ("ssem", s_v)
            p.wait("gpsimd", "ssem", p.c("ssem"))
            g_v = p.emit("gpsimd", lambda e, h=h: e.dma_start(vT[h], arena[:, 1024:1024 + NMT * VW]),
                         inc="gkv", incby=16)
            pat_guard[sl] = p.c("tsem")
            p.wait("vector", "gkv", p.c("gkv"))
            p.wait("scalar", "gkv", p.c("gkv"))

        p.wait("gpsimd", "gkv", p.c("gkv"))
        wT_off = [12288, 4096]
        cast_guard = [0, 0]
        write_guard = [0, 0]

        def prep_wT(w_src, dst, nib):
            for ib in range(nib):
                sset = ib % 2
                base = 0 if sset == 0 else 8192
                woff = wT_off[sset]
                wT_view = arena[:, woff:woff + 4096].rearrange(
                    "p (g two c) -> p g two c", two=2, c=128)
                csem = "castg" if sset == 0 else "castg1"
                p.wait("gpsimd", "tsem", cast_guard[sset])
                p.emit(
                    "gpsimd",
                    lambda e, ib=ib, base=base, w=w_src: e.dma_start(
                        arena[:, base:base + 4096].rearrange("p (il c) -> p il c", c=2048),
                        w[ib * 256:(ib + 1) * 256, :].rearrange("(il p) c -> p il c", p=128)),
                    inc=csem, incby=16)
                g_c = p.c(csem)
                p.wait("tensor", csem, g_c)
                for il in range(2):
                    coff = base + il * 2048
                    for eg in range(2):
                        bk = eg % 2
                        sem, val = pt_free[bk]
                        p.wait("tensor", sem, val)
                        t_last = 0
                        for k in range(8):
                            et = eg * 8 + k
                            t_last = tp(pt[bk][:, k * 128:(k + 1) * 128],
                                        arena[:, coff + et * 128:coff + (et + 1) * 128])
                        p.wait("scalar", "tsem", t_last)
                        p.wait("scalar", "wprep" if sset == 0 else "wprep1", write_guard[sset])
                        s_t = p.emit(
                            "scalar",
                            lambda e, eg=eg, il=il, bk=bk, wT_view=wT_view: e.copy(
                                wT_view[:, eg * 8:(eg + 1) * 8, il, :], pt[bk][:, 0:1024]),
                            inc="ssem")
                        pt_free[bk] = ("ssem", s_t)
                cast_guard[sset] = p.c("tsem")
                p.wait("sync", "ssem", p.c("ssem"))
                write_guard[sset] = p.emit(
                    "sync",
                    lambda e, ib=ib, d=dst, woff=woff: e.dma_start(
                        d[ib].rearrange("et p c -> p et c"),
                        arena[:, woff:woff + 4096].rearrange("p (et c) -> p et c", c=256)),
                    inc="wprep" if sset == 0 else "wprep1", incby=16)

        prep_wT(w_up, upT, 32)
        prep_wT(w_gate, gateT, 32)
        prep_wT(w_in, winT, 8)

        p3_cast_guard = p.c("tsem")
        p3_z_guard = p.c("dnw")
        p.wait("gpsimd", "wprep", p.c("wprep"))
        p.wait("gpsimd", "wprep1", p.c("wprep1"))
        p.wait("vector", "wprep", p.c("wprep"))
        p.wait("vector", "wprep1", p.c("wprep1"))
        p.wait("sync", "tsem", xb_free_t)
        p.wait("sync", "vsem", v_prep0)
        xstage = [("vsem", 0), ("vsem", 0)]

        def pong_ap(it):
            if it < 32:
                return fmA[:, it * 256:(it + 1) * 256]
            return tmA[:, (it - 32) * 256:(it - 31) * 256]

        for eb in range(8):
            p.wait("gpsimd", "tsem", p3_cast_guard)
            p.emit(
                "gpsimd",
                lambda e, eb=eb: e.dma_start(
                    arena[:, 0:8192],
                    w_down[eb * 256:eb * 256 + 128, :]),
                inc="castg", incby=16)
            g_c = p.c("castg")
            v_c = 0
            for hh in range(2):
                sem, val = xstage[hh]
                p.wait("sync", sem, val)
                p.emit(
                    "sync",
                    lambda e, eb=eb, hh=hh: e.dma_start(
                        x_b[:, hh * 4096:(hh + 1) * 4096],
                        w_down[eb * 256 + 128:eb * 256 + 256,
                               hh * 4096:(hh + 1) * 4096]),
                    inc="castg2", incby=16)
                lt = p.c("castg2")
                p.wait("gpsimd", "castg2", lt)
                p.wait("gpsimd", "tsem", p3_cast_guard)
                v_c = p.emit("gpsimd", lambda e, hh=hh: e.tensor_copy(
                    arena[:, 8192 + hh * 4096:8192 + (hh + 1) * 4096],
                    x_b[:, hh * 4096:(hh + 1) * 4096]), inc="bsem")
                xstage[hh] = ("bsem", v_c)
            p.wait("tensor", "castg", g_c)
            p.wait("tensor", "bsem", v_c)
            for tg in range(16):
                bk = tg % 2
                sem, val = pt_free[bk]
                p.wait("tensor", sem, val)
                t_last = 0
                for k in range(8):
                    tix = tg * 8 + k
                    it, j = tix // 2, tix % 2
                    t_last = tp(pt[bk][:, k * 128:(k + 1) * 128],
                                arena[:, j * 8192 + it * 128:j * 8192 + (it + 1) * 128])
                p.wait("scalar", "tsem", t_last)
                p.wait("scalar", "dnw", p3_z_guard)
                s_t = p.emit(
                    "scalar",
                    lambda e, tg=tg, bk=bk: e.copy(
                        arena[:, 16384 + tg * 1024:16384 + (tg + 1) * 1024], pt[bk][:, 0:1024]),
                    inc="ssem")
                pt_free[bk] = ("ssem", s_t)
            p3_cast_guard = p.c("tsem")
            p.wait("tensor", "ssem", p.c("ssem"))
            for itq in range(0, NIT, 4):
                t_s = 0
                for k in range(4):
                    it = itq + k
                    sem, val = pf_free[k]
                    p.wait("tensor", sem, val)
                    t_s = mm(pf[k][:, 0:256], h_sb[:],
                             arena[:, 16384 + it * 256:16384 + (it + 1) * 256],
                             True, True)
                p.wait("scalar", "tsem", t_s)
                s_s = 0
                for k in range(4):
                    it = itq + k
                    s_s = p.emit("scalar", lambda e, it=it, k=k: e.copy(
                        arena[:, 16384 + it * 256:16384 + (it + 1) * 256], pf[k][:, 0:256]),
                        inc="ssem")
                for k in range(4):
                    pf_free[k] = ("ssem", s_s)
            p.wait("vector", "ssem", p.c("ssem"))
            p.wait("vector", "dnw", p3_z_guard)
            for st in range(6):
                from_z = (st % 2 == 0)
                for u in range(NIT):
                    v = u ^ (1 << st)
                    if v < u:
                        continue
                    if from_z:
                        iu = arena[:, 16384 + u * 256:16384 + (u + 1) * 256]
                        iv = arena[:, 16384 + v * 256:16384 + (v + 1) * 256]
                        ou, ov = pong_ap(u), pong_ap(v)
                    else:
                        iu, iv = pong_ap(u), pong_ap(v)
                        ou = arena[:, 16384 + u * 256:16384 + (u + 1) * 256]
                        ov = arena[:, 16384 + v * 256:16384 + (v + 1) * 256]
                    p.emit("vector", lambda e, iu=iu, iv=iv, ou=ou: e.tensor_add(ou, iu, iv))
                    p.emit("vector", lambda e, iu=iu, iv=iv, ov=ov: e.tensor_sub(ov, iu, iv))
                p.drain("vector")
            v_bt = p.release("vector")
            p.wait("scalar", "vsem", v_bt)
            for grp in range(8):
                p.emit(
                    "scalar",
                    lambda e, eb=eb, grp=grp: e.dma_start(
                        dnT[grp * 1024:(grp + 1) * 1024, eb * 256:(eb + 1) * 256]
                        .rearrange("(k p) c -> p k c", p=128),
                        arena[:, 16384 + grp * 2048:16384 + (grp + 1) * 2048]
                        .rearrange("p (k c) -> p k c", c=256)),
                    inc="dnw", incby=16)
            p3_z_guard = p.c("dnw")

        t_rel = p.release("tensor")
        s_rel = p.release("scalar")
        v_rel = p.release("vector")
        pd = dict(wsem=p.c("wsem"), gsem=p.c("gsem"), tsem=t_rel, ssem=s_rel, vsem=v_rel,
                  wprep=p.c("wprep"), wprep1=p.c("wprep1"), castg=p.c("castg"), castg1=p.c("castg1"),
                  castg2=p.c("castg2"), bsem=p.c("bsem"), dnw=p.c("dnw"),
                  patg=p.c("patg"), gkv=p.c("gkv"))
        for eng in ("sync", "tensor", "scalar", "vector", "gpsimd"):
            for s, v in pd.items():
                p.wait(eng, s, v)

        A_NQ = 0
        A_AF = 8192
        A_E = 16384
        A_KB = 24576
        A_VB = 26624

        pf_free = [("ssem", p.c("ssem"))] * 6
        pt_free = [("ssem", p.c("ssem"))] * 2
        wbuf_tok = [("tsem", p.c("tsem"))] * 4
        kv_tok = [("tsem", p.c("tsem"))] * 2
        e_tok = [p.c("tsem"), p.c("tsem")]

        def layer_norm(eps, skip_stats=False):
            v_in = p.c("vsem")
            p.wait("scalar", "vsem", v_in)
            eps_ap = stats[:, 121:122] if eps == LN_EPS else stats[:, 122:123]
            p.drain("vector")
            if not skip_stats:
                for st in range(NST):
                    xs = x_b[:, st * E:(st + 1) * E]
                    for j in range(4):
                        p.emit("vector", lambda e, xs=xs, st=st, j=j: e.bn_stats(
                            stats[:, st * 24 + j * 6:st * 24 + (j + 1) * 6],
                            xs[:, j * 512:(j + 1) * 512]))
                p.drain("vector")
            v_mu = 0
            for st in range(NST):
                v_mu = p.emit("vector", lambda e, st=st: e.bn_aggr(
                    stats[:, 96 + 2 * st:98 + 2 * st], stats[:, st * 24:(st + 1) * 24]),
                    inc="vsem")
            p.wait("scalar", "vsem", v_mu)
            var_v = stats[:, 97:105].rearrange("p (a b) -> p a b", b=2)[:, :, 0:1]
            ln_v = stats[:, 104:108].rearrange("p (a b) -> p a b", b=1)
            rs_v = stats[:, 108:112].rearrange("p (a b) -> p a b", b=1)
            p.drain("scalar")
            p.emit("scalar", lambda e, eps_ap=eps_ap: e.activation(ln_v, var_v, ACT.Ln, bias=eps_ap))
            p.drain("scalar")
            s_r = p.emit("scalar", lambda e: e.activation(
                rs_v, ln_v, ACT.Exp, scale=stats[:, 120:121]), inc="ssem")
            p.wait("vector", "ssem", s_r)
            v_a = 0
            for st in range(NST):
                xs = x_b[:, st * E:(st + 1) * E]
                v_a = p.emit("vector", lambda e, xs=xs, st=st: e.tensor_scalar(
                    tmA[:, st * E:(st + 1) * E], xs, stats[:, 96 + 2 * st:97 + 2 * st],
                    stats[:, 108 + st:109 + st], ALU.subtract, ALU.mult), inc="vsem")
            return v_a

        def transpose_tm_to_fm(src_off, scaled, v_ready):
            p.wait("tensor", "vsem", v_ready)
            s_last = 0
            for et in range(NET):
                bk = et % 2
                sem, val = pt_free[bk]
                p.wait("tensor", sem, val)
                t_last = 0
                for st in range(NST):
                    src = tmA if src_off is None else arena
                    off = (0 if src_off is None else src_off) + st * E + et * 128
                    t_last = tp(pt[bk][:, st * 128:(st + 1) * 128], src[:, off:off + 128])
                p.wait("scalar", "tsem", t_last)
                if scaled:
                    s_last = p.emit("scalar", lambda e, et=et, bk=bk: e.activation(
                        fmA[:, et * C:(et + 1) * C], pt[bk][:, 0:C],
                        ACT.Copy, scale=su_sb[:, et:et + 1]), inc="ssem")
                else:
                    s_last = p.emit("scalar", lambda e, et=et, bk=bk: e.copy(
                        fmA[:, et * C:(et + 1) * C], pt[bk][:, 0:C]), inc="ssem")
                pt_free[bk] = ("ssem", s_last)
            return s_last

        def fwht2048_fm():
            p.wait("tensor", "ssem", p.c("ssem"))
            for et in range(NET):
                bix = et % 2
                sem, val = pf_free[bix]
                p.wait("tensor", sem, val)
                t_s = mm(pf[bix][:, 0:C], h_sb[:], fmA[:, et * C:(et + 1) * C], True, True)
                p.wait("scalar", "tsem", t_s)
                s_s = p.emit("scalar", lambda e, et=et, bix=bix: e.copy(
                    fmA[:, et * C:(et + 1) * C], pf[bix][:, 0:C]), inc="ssem")
                pf_free[bix] = ("ssem", s_s)
            p.wait("vector", "ssem", p.c("ssem"))
            p.wait("vector", "tsem", p.c("tsem"))
            ready = [0] * NET
            for st in range(4):
                src_fm = (st % 2 == 0)
                for u in range(NET):
                    v = u ^ (1 << st)
                    if v < u:
                        continue
                    if src_fm:
                        iu, iv = fmA[:, u * C:(u + 1) * C], fmA[:, v * C:(v + 1) * C]
                        ou, ov = arena[:, u * C:(u + 1) * C], arena[:, v * C:(v + 1) * C]
                    else:
                        iu, iv = arena[:, u * C:(u + 1) * C], arena[:, v * C:(v + 1) * C]
                        ou, ov = fmA[:, u * C:(u + 1) * C], fmA[:, v * C:(v + 1) * C]
                    p.emit("vector", lambda e, iu=iu, iv=iv, ou=ou: e.tensor_add(ou, iu, iv))
                    last = (st == 3)
                    vv = p.emit("vector", lambda e, iu=iu, iv=iv, ov=ov: e.tensor_sub(ov, iu, iv),
                                inc="vsem" if last else None)
                    if last:
                        ready[u] = vv
                        ready[v] = vv
                p.drain("vector")
            return ready

        ow = [0]

        def ffn(fm_ready, write_ch=None):
            korder = sorted(range(NET), key=lambda k: fm_ready[k])
            wb = [wbuf[:, k * 4096:(k + 1) * 4096] for k in range(4)]
            last_v_g = [p.c("vsem")]
            for ib in range(32):
                ub, gb = wb[(ib % 2) * 2], wb[(ib % 2) * 2 + 1]
                for m_i, (buf, srcT) in enumerate(((ub, upT), (gb, gateT))):
                    slot = (ib % 2) * 2 + m_i
                    sem, val = wbuf_tok[slot]
                    p.wait("sync", sem, val)
                    p.emit("sync", lambda e, ib=ib, buf=buf, srcT=srcT: e.dma_start(
                        buf.rearrange("p (et c) -> p et c", c=256),
                        srcT[ib].rearrange("et p c -> p et c")), inc=f"sl{slot}", incby=16)
                    p.wait("tensor", f"sl{slot}", p.c(f"sl{slot}"))
                for il in range(2):
                    it = ib * 2 + il
                    pu_i, pg_i = (it % 2) * 2, (it % 2) * 2 + 1
                    for bix in (pu_i, pg_i):
                        sem, val = pf_free[bix]
                        p.wait("tensor", sem, val)
                    t_u = 0
                    for ki, et in enumerate(korder):
                        p.wait("tensor", "vsem", fm_ready[et])
                        t_u = mm(pf[pu_i][:, 0:C], ub[:, et * 256 + il * 128:et * 256 + (il + 1) * 128],
                                 fmA[:, et * C:(et + 1) * C], ki == 0, ki == NET - 1)
                    t_g = 0
                    for ki, et in enumerate(korder):
                        t_g = mm(pf[pg_i][:, 0:C], gb[:, et * 256 + il * 128:et * 256 + (il + 1) * 128],
                                 fmA[:, et * C:(et + 1) * C], ki == 0, ki == NET - 1)
                    if il == 1:
                        wbuf_tok[(ib % 2) * 2] = ("tsem", t_g)
                        wbuf_tok[(ib % 2) * 2 + 1] = ("tsem", t_g)
                    p.wait("scalar", "tsem", t_g)
                    p.wait("scalar", "vsem", last_v_g[0])
                    s_si = p.emit("scalar", lambda e, it=it, pg_i=pg_i: e.activation(
                        silu_t[:, 0:C], pf[pg_i][:, 0:C], ACT.Silu, scale=wscg_sb[:, it:it + 1]),
                        inc="ssem")
                    p.wait("vector", "ssem", s_si)
                    p.wait("vector", "tsem", t_u)
                    v_g = p.emit("vector", lambda e, it=it, pu_i=pu_i: e.scalar_tensor_tensor(
                        arena[:, it * C:(it + 1) * C], pf[pu_i][:, 0:C], wscu_su[:, it:it + 1],
                        silu_t[:, 0:C], ALU.mult, ALU.mult), inc="vsem")
                    pf_free[pu_i] = ("vsem", v_g)
                    pf_free[pg_i] = ("ssem", s_si)
                    last_v_g[0] = v_g
            v_g_all = p.c("vsem")
            db = [wbuf[:, k * 4096:(k + 1) * 4096] for k in range(4)]
            p.wait("tensor", "vsem", v_g_all)
            v_d = 0
            for ec in range(4):
                for st in range(NST):
                    sem, val = pf_free[st]
                    p.wait("tensor", sem, val)
                t_d = 0
                for kb in range(8):
                    bi = kb % 4
                    sem, val = wbuf_tok[bi]
                    p.wait("sync", sem, val)
                    wv = p.emit("sync", lambda e, ec=ec, kb=kb, b=db[bi]: e.dma_start(
                        b.rearrange("p (k c) -> p k c", c=512),
                        dnT[kb * 1024:(kb + 1) * 1024, ec * 512:(ec + 1) * 512]
                        .rearrange("(k p) c -> p k c", p=128)), inc=f"sl{bi}", incby=16)
                    p.wait("tensor", f"sl{bi}", wv)
                    for st in range(NST):
                        for k in range(8):
                            it = kb * 8 + k
                            t_d = mm(pf[st][:, 0:512],
                                     arena[:, it * C + st * 128:it * C + (st + 1) * 128],
                                     db[bi][:, k * 512:(k + 1) * 512],
                                     it == 0, it == NIT - 1)
                    wbuf_tok[bi] = ("tsem", t_d)
                p.wait("vector", "tsem", t_d)
                for st in range(NST):
                    p.drain("vector")
                    p.emit("vector", lambda e, st=st, ec=ec: e.tensor_mul(
                        sq[:, 0:512], pf[st][:, 0:512], wscb[:, ec * 512:(ec + 1) * 512]))
                    p.drain("vector")
                    v_d = p.emit("vector", lambda e, st=st, ec=ec: e.tensor_add(
                        x_b[:, st * E + ec * 512:st * E + (ec + 1) * 512],
                        x_b[:, st * E + ec * 512:st * E + (ec + 1) * 512],
                        sq[:, 0:512]), inc="vsem")
                    pf_free[st] = ("vsem", v_d)
                    if write_ch is not None and ec == 3:
                        p.wait("gpsimd", "vsem", v_d)
                        p.emit("gpsimd", lambda e, ch=write_ch, st=st: e.dma_start(
                            y_out[ch * C + st * 128:ch * C + (st + 1) * 128, :],
                            x_b[:, st * E:(st + 1) * E]), inc="osem", incby=16)
                        ow[0] += 1
            p.drain("vector")
            return v_d

        def rms_apply(src_ap_fn, dst_off, w_bcast, v_ready, out_is_nq, stats_done=False):
            p.wait("scalar", "vsem", v_ready)
            p.wait("vector", "vsem", v_ready)
            p.drain("vector")
            if not stats_done:
                for st in range(NST):
                    a_s = src_ap_fn(st)
                    for j in range(4):
                        p.emit("vector", lambda e, a_s=a_s, st=st, j=j: e.bn_stats(
                            stats[:, st * 24 + j * 6:st * 24 + (j + 1) * 6],
                            a_s[:, j * 512:(j + 1) * 512]))
                p.drain("vector")
            for st in range(NST):
                p.emit("vector", lambda e, st=st: e.bn_aggr(
                    stats[:, 96 + 2 * st:98 + 2 * st], stats[:, st * 24:(st + 1) * 24]))
            p.drain("vector")
            mean_v = stats[:, 96:104].rearrange("p (a b) -> p a b", b=2)[:, :, 0:1]
            var_v = stats[:, 97:105].rearrange("p (a b) -> p a b", b=2)[:, :, 0:1]
            msq_v = stats[:, 104:108].rearrange("p (a b) -> p a b", b=1)
            rs_v = stats[:, 108:112].rearrange("p (a b) -> p a b", b=1)
            v_ss = p.emit("vector", lambda e: e.tensor_tensor(
                msq_v, mean_v, mean_v, ALU.mult), inc="vsem")
            p.drain("vector")
            v_ss = p.emit("vector", lambda e: e.tensor_tensor(
                msq_v, msq_v, var_v, ALU.add), inc="vsem")
            p.wait("scalar", "vsem", v_ss)
            p.drain("scalar")
            p.emit("scalar", lambda e: e.activation(msq_v, msq_v, ACT.Ln, bias=stats[:, 122:123]))
            p.drain("scalar")
            s_r = p.emit("scalar", lambda e: e.activation(
                rs_v, msq_v, ACT.Exp, scale=stats[:, 120:121]), inc="ssem")
            p.wait("vector", "ssem", s_r)
            last = 0
            for st in range(NST):
                a_s = src_ap_fn(st)
                if out_is_nq:
                    last = p.emit("vector", lambda e, a_s=a_s, st=st: e.scalar_tensor_tensor(
                        arena[:, dst_off + st * E:dst_off + (st + 1) * E], a_s,
                        stats[:, 108 + st:109 + st], w_bcast[:], ALU.mult, ALU.mult), inc="vsem")
                else:
                    p.drain("vector")
                    p.emit("vector", lambda e, a_s=a_s, st=st: e.scalar_tensor_tensor(
                        sq[:], a_s, stats[:, 108 + st:109 + st], w_bcast[:], ALU.mult, ALU.mult))
                    p.drain("vector")
                    last = p.emit("vector", lambda e, st=st: e.tensor_add(
                        x_b[:, st * E:(st + 1) * E], x_b[:, st * E:(st + 1) * E], sq[:]), inc="vsem")
            return last

        def attn(v_nq_ready):
            s_tr = transpose_tm_to_fm(A_NQ, False, v_nq_ready)
            wb = [wbuf[:, k * 4096:(k + 1) * 4096] for k in range(2)]
            p.wait("tensor", "ssem", s_tr)
            for eb in range(8):
                bi = eb % 2
                sem, val = wbuf_tok[bi]
                p.wait("sync", sem, val)
                p.emit("sync", lambda e, eb=eb, b=wb[bi]: e.dma_start(
                    b.rearrange("p (et c) -> p et c", c=256),
                    winT[eb].rearrange("et p c -> p et c")), inc=f"sl{bi}", incby=16)
                p.wait("tensor", f"sl{bi}", p.c(f"sl{bi}"))
                for il in range(2):
                    ep = eb * 2 + il
                    bix = 4 + (ep % 2)
                    sem, val = pf_free[bix]
                    p.wait("tensor", sem, val)
                    t_p = 0
                    for et in range(NET):
                        t_p = mm(pf[bix][:, 0:C], wb[bi][:, et * 256 + il * 128:et * 256 + (il + 1) * 128],
                                 fmA[:, et * C:(et + 1) * C], et == 0, et == NET - 1)
                    if il == 1:
                        wbuf_tok[bi] = ("tsem", t_p)
                    p.wait("scalar", "tsem", t_p)
                    s_p = p.emit("scalar", lambda e, ep=ep, bix=bix: e.copy(
                        arena[:, A_NQ + ep * C:A_NQ + (ep + 1) * C], pf[bix][:, 0:C]), inc="ssem")
                    pf_free[bix] = ("ssem", s_p)
            p.wait("tensor", "ssem", p.c("ssem"))
            for h in range(H):
                bix = 4 + (h % 2)
                sem, val = pf_free[bix]
                p.wait("tensor", sem, val)
                t_q = mm(pf[bix][:, 0:C], wqT[:], arena[:, A_NQ + h * C:A_NQ + (h + 1) * C],
                         True, True)
                p.wait("scalar", "tsem", t_q)
                s_q = p.emit("scalar", lambda e, h=h, bix=bix: e.copy(
                    fmA[:, h * C:(h + 1) * C], pf[bix][:, 0:C]), inc="ssem")
                pf_free[bix] = ("ssem", s_q)
            p.wait("tensor", "ssem", p.c("ssem"))
            kv_free = kv_tok
            e_guard = e_tok
            s_e_done = [0] * H
            v_o = 0

            def emit_scores(h):
                sl = h % 2
                koff = A_KB + sl * 1024
                voff = A_VB + sl * 1088
                sem, val = kv_free[sl]
                p.wait("sync", sem, val)
                p.emit("sync", lambda e, h=h, koff=koff: e.dma_start(
                    arena[:, koff:koff + 1024], kT[h]), inc=f"kv{sl}", incby=16)
                w_kv = p.emit("sync", lambda e, h=h, voff=voff: e.dma_start(
                    arena[:, voff:voff + 1088], vT[h]), inc=f"kv{sl}", incby=16)
                p.wait("tensor", f"kv{sl}", w_kv)
                eoff = A_E + sl * 4096
                p.wait("scalar", "tsem", e_guard[sl])
                for mtq in range(0, NMT, 4):
                    t_sc = 0
                    for k in range(4):
                        mt = mtq + k
                        sem, val = pf_free[k]
                        p.wait("tensor", sem, val)
                        t_sc = mm(pf[k][:, 0:C], arena[:, koff + mt * 128:koff + (mt + 1) * 128],
                                  fmA[:, h * C:(h + 1) * C], True, True)
                    p.wait("scalar", "tsem", t_sc)
                    s_e = 0
                    for k in range(4):
                        mt = mtq + k
                        s_e = p.emit("scalar", lambda e, mt=mt, k=k, eoff=eoff: e.activation(
                            arena[:, eoff + mt * C:eoff + (mt + 1) * C], pf[k][:, 0:C], ACT.Exp),
                            inc="ssem")
                    for k in range(4):
                        pf_free[k] = ("ssem", s_e)
                    if mtq == 0 and h > 0:
                        emit_out(h - 1)
                s_e_done[h] = p.c("ssem")

            def emit_out(h):
                nonlocal v_o
                sl = h % 2
                voff = A_VB + sl * 1088
                eoff = A_E + sl * 4096
                p.wait("tensor", "ssem", s_e_done[h])
                t_o = 0
                for st in range(NST):
                    bix = 4 + st // 2
                    coff = (st % 2) * 256
                    if st % 2 == 0:
                        sem, val = pf_free[bix]
                        p.wait("tensor", sem, val)
                    for mt in range(NMT):
                        t_o = mm(pf[bix][:, coff:coff + VW],
                                 arena[:, eoff + mt * C + st * 128:eoff + mt * C + (st + 1) * 128],
                                 arena[:, voff + mt * VW:voff + (mt + 1) * VW],
                                 mt == 0, mt == NMT - 1)
                p.wait("vector", "tsem", t_o)
                p.drain("vector")
                for st in range(NST):
                    bix = 4 + st // 2
                    coff = (st % 2) * 256
                    p.emit("vector", lambda e, st=st, bix=bix, coff=coff: e.reciprocal(
                        stats[:, 116 + st:117 + st], pf[bix][:, coff + 128:coff + 129]))
                p.drain("vector")
                for st in range(NST):
                    bix = 4 + st // 2
                    coff = (st % 2) * 256
                    v_o = p.emit("vector", lambda e, h=h, st=st, bix=bix, coff=coff: e.tensor_scalar(
                        arena[:, A_AF + st * E + h * 128:A_AF + st * E + (h + 1) * 128],
                        pf[bix][:, coff:coff + 128], stats[:, 116 + st:117 + st], None, ALU.mult), inc="vsem")
                for st in (0, 2):
                    pf_free[4 + st // 2] = ("vsem", v_o)
                kv_free[sl] = ("tsem", p.c("tsem"))
                e_guard[sl] = p.c("tsem")
                if h % 4 == 3:
                    j = h // 4
                    p.drain("vector")
                    for st in range(NST):
                        p.emit("vector", lambda e, st=st, j=j: e.bn_stats(
                            stats[:, st * 24 + j * 6:st * 24 + (j + 1) * 6],
                            arena[:, A_AF + st * E + j * 512:
                                  A_AF + st * E + (j + 1) * 512]))

            for h in range(H):
                emit_scores(h)
                if h > 0:
                    emit_out(h - 1)
            emit_out(H - 1)
            return rms_apply(lambda st: arena[:, A_AF + st * E:A_AF + (st + 1) * E],
                             0, wret_b, v_o, out_is_nq=False, stats_done=True)

        for ch in range(NCH):
            g_in = 0
            p.wait("vector", "tsem", p.c("tsem"))
            for st in range(NST):
                if ch > 0:
                    p.wait("sync", "osem", (4 * (ch - 1) + st + 1) * 16)
                g_in = p.emit("sync", lambda e, ch=ch, st=st: e.dma_start(
                    x_b[:, st * E:(st + 1) * E],
                    x_in[ch * C + st * 128:ch * C + (st + 1) * 128, :]), inc="gsem", incby=16)
                p.wait("vector", "gsem", g_in)
                for j in range(4):
                    p.emit("vector", lambda e, st=st, j=j: e.bn_stats(
                        stats[:, st * 24 + j * 6:st * 24 + (j + 1) * 6],
                        x_b[:, st * E + j * 512:st * E + (j + 1) * 512]))
            p.wait("scalar", "gsem", g_in)

            v_ln = layer_norm(LN_EPS, skip_stats=True)
            transpose_tm_to_fm(None, True, v_ln)
            fm_ready = fwht2048_fm()
            ffn(fm_ready)

            v_ln2 = layer_norm(LN_EPS)
            v_y = 0
            for st in range(NST):
                p.drain("vector")
                v_y = p.emit("vector", lambda e, st=st: e.tensor_add(
                    x_b[:, st * E:(st + 1) * E], x_b[:, st * E:(st + 1) * E],
                    tmA[:, st * E:(st + 1) * E]), inc="vsem")
            p.wait("vector", "tsem", p.c("tsem"))
            rs_v = stats[:, 108:112].rearrange("p (a b) -> p a b", b=1)
            var_v = stats[:, 97:105].rearrange("p (a b) -> p a b", b=2)[:, :, 0:1]
            w1_v = stats[:, 64:68].rearrange("p (a b) -> p a b", b=1)
            u_v = stats[:, 68:72].rearrange("p (a b) -> p a b", b=1)
            p.drain("vector")
            p.emit("vector", lambda e: e.tensor_tensor(w1_v, rs_v, rs_v, ALU.mult))
            p.drain("vector")
            v_w1 = p.emit("vector", lambda e: e.tensor_tensor(w1_v, w1_v, var_v, ALU.mult), inc="vsem")
            p.wait("scalar", "vsem", v_w1)
            p.drain("scalar")
            p.emit("scalar", lambda e: e.activation(u_v, w1_v, ACT.Ln, bias=stats[:, 122:123]))
            p.drain("scalar")
            s_u = p.emit("scalar", lambda e: e.activation(
                u_v, u_v, ACT.Exp, scale=stats[:, 120:121]), inc="ssem")
            p.wait("vector", "ssem", s_u)
            p.wait("vector", "vsem", v_y)
            v_nq = 0
            for st in range(NST):
                v_nq = p.emit("vector", lambda e, st=st: e.scalar_tensor_tensor(
                    arena[:, A_NQ + st * E:A_NQ + (st + 1) * E], tmA[:, st * E:(st + 1) * E],
                    stats[:, 68 + st:69 + st], wnq_b[:], ALU.mult, ALU.mult), inc="vsem")
            attn(v_nq)

            v_ln3 = layer_norm(LN_EPS)
            transpose_tm_to_fm(None, True, v_ln3)
            fm_ready3 = fwht2048_fm()
            ffn(fm_ready3, write_ch=ch)

        total_out = ow[0] * 16

        with nc.Block() as block:
            @block.sync
            def _(eng):
                for fn in p.plan["sync"]:
                    fn(eng)

            @block.tensor
            def _(eng):
                for fn in p.plan["tensor"]:
                    fn(eng)

            @block.scalar
            def _(eng):
                for fn in p.plan["scalar"]:
                    fn(eng)

            @block.vector
            def _(eng):
                for fn in p.plan["vector"]:
                    fn(eng)

            @block.gpsimd
            def _(eng):
                for fn in p.plan["gpsimd"]:
                    fn(eng)
                eng.wait_ge(p.sems["osem"], total_out)

    return nc


_NC_CACHE = None


def kernel(L0_attn_query_input, L0_attn_W_in, L0_attn_W_q, L0_attn_W_k, L0_attn_W_v,
           L0_attn_stored_patterns, L0_attn_norm_query_w, L0_attn_norm_retrieved_w,
           L0_attn_beta, L0_ffn_SU_ug, L0_ffn_SU_dp, L0_ffn_Wscale_ug, L0_ffn_Wscale_dp,
           L0_ffn_W_up, L0_ffn_W_gate, L0_ffn_W_down):
    global _NC_CACHE
    x = np.ascontiguousarray(np.asarray(L0_attn_query_input, dtype=np.float32).reshape(16384, E))
    consts = dict(
        h128=np.ascontiguousarray(_hadamard(128).astype(BFNP)),
        ident=np.ascontiguousarray(np.eye(128, dtype=np.float32).astype(BFNP)),
        ones1=np.ones((1, 128), dtype=BFNP),
        ones1f=np.ones((1, 128), dtype=np.float32),
    )
    weights = dict(
        w_in=np.ascontiguousarray(np.asarray(L0_attn_W_in, dtype=np.float32)),
        w_q=np.ascontiguousarray(np.asarray(L0_attn_W_q, dtype=np.float32)),
        w_k=np.ascontiguousarray(np.asarray(L0_attn_W_k, dtype=np.float32)),
        w_v=np.ascontiguousarray(np.asarray(L0_attn_W_v, dtype=np.float32)),
        pats=np.ascontiguousarray(np.asarray(L0_attn_stored_patterns, dtype=np.float32)),
        nqw=np.ascontiguousarray(np.asarray(L0_attn_norm_query_w, dtype=np.float32)),
        nrw=np.ascontiguousarray(np.asarray(L0_attn_norm_retrieved_w, dtype=np.float32)),
        beta=np.asarray(L0_attn_beta, dtype=np.float32).reshape(1),
        su_ug=np.ascontiguousarray(np.asarray(L0_ffn_SU_ug, dtype=np.float32)),
        su_dp=np.ascontiguousarray(np.asarray(L0_ffn_SU_dp, dtype=np.float32)),
        wsc_ug=np.ascontiguousarray(np.asarray(L0_ffn_Wscale_ug, dtype=np.float32)),
        wsc_dp=np.ascontiguousarray(np.asarray(L0_ffn_Wscale_dp, dtype=np.float32)),
        w_up=np.ascontiguousarray(np.asarray(L0_ffn_W_up, dtype=np.float32)),
        w_gate=np.ascontiguousarray(np.asarray(L0_ffn_W_gate, dtype=np.float32)),
        w_down=np.ascontiguousarray(np.asarray(L0_ffn_W_down, dtype=np.float32)),
    )
    if _NC_CACHE is None:
        _NC_CACHE = build_nc()
    nc = _NC_CACHE
    in_maps = []
    for c in range(NCORES):
        m = {"x": np.ascontiguousarray(x[c * TCORE:(c + 1) * TCORE])}
        m.update(weights)
        m.update(consts)
        in_maps.append(m)
    res = run_bass_kernel_spmd(nc, in_maps, core_ids=list(range(NCORES)))
    out = np.concatenate([res.results[c]["y"] for c in range(NCORES)], axis=0)
    return np.ascontiguousarray(out.reshape(1, 16384, E).astype(np.float32))


if __name__ == "__main__":
    build_nc()
    print("built ok")
